# version 01
# baseline (speedup 1.0000x reference)
import numpy as np
import ml_dtypes
import concourse.bass as bass
import concourse.mybir as mybir
from concourse.bass_utils import run_bass_kernel_spmd

F32 = mybir.dt.float32
BF16 = mybir.dt.bfloat16
I32 = mybir.dt.int32
AF = mybir.ActivationFunctionType
ALU = mybir.AluOpType
AX = mybir.AxisListType

NCORES = 8
BB, S, DM = 2, 4096, 2048
T = BB * S
TPC = T // NCORES
KC = DM // 128

bf16 = ml_dtypes.bfloat16

ENGINES = ("sync", "gpsimd", "vector", "scalar", "tensor")


class P:

    def __init__(self, nc):
        self.nc = nc
        self.ops = {e: [] for e in ENGINES}
        self.last_writer = {}
        self.readers = {}
        self.groups = {}
        self.sems = {}

    def add(self, eng, fn, reads=(), writes=(), dma=False, group=None, tag=""):
        idx = len(self.ops[eng])
        me = (eng, idx)
        assert (group is not None) == bool(dma), f"dma ops need a group ({tag})"
        deps = set()
        for r in reads:
            w = self.last_writer.get(r)
            if w is not None:
                deps.add(w)
        for r in writes:
            w = self.last_writer.get(r)
            if w is not None:
                deps.add(w)
            for rd in self.readers.get(r, ()):
                deps.add(rd)
        if dma and group in self.groups:
            deps.add(self.groups[group])
        deps.discard(me)
        self.ops[eng].append(dict(fn=fn, deps=deps, dma=dma, group=group,
                                  marked=False, tag=tag))
        for r in reads:
            self.readers.setdefault(r, set()).add(me)
        for r in writes:
            self.last_writer[r] = me
            self.readers[r] = set()
        if dma:
            self.groups[group] = me
        return me

    def finalize(self):
        for e in ENGINES:
            for op in self.ops[e]:
                if op["dma"]:
                    op["marked"] = True
                for (de, di) in op["deps"]:
                    self.ops[de][di]["marked"] = True
        self.incval = {}
        counts = {}
        for e in ENGINES:
            for i, op in enumerate(self.ops[e]):
                key = op["group"] if op["dma"] else e
                if op["marked"]:
                    counts[key] = counts.get(key, 0) + (16 if op["dma"] else 1)
                self.incval[(e, i)] = (key, counts.get(key, 0))

    def make_sems(self, stack):
        keys = set()
        for e in ENGINES:
            for op in self.ops[e]:
                keys.add(op["group"] if op["dma"] else e)
        for k in sorted(keys):
            self.sems[k] = stack.enter_context(self.nc.semaphore(f"sp_{k}"))

    def emit(self, eng, h):
        waited = {}
        for i, op in enumerate(self.ops[eng]):
            wants = {}
            for (de, di) in op["deps"]:
                key, v = self.incval[(de, di)]
                wants[key] = max(wants.get(key, 0), v)
            for key in sorted(wants):
                v = wants[key]
                if v > waited.get(key, 0):
                    h.wait_ge(self.sems[key], v)
                    waited[key] = v
            last = op["fn"](h)
            if op["marked"]:
                assert last is not None, f"op {eng}:{i} ({op['tag']}) returned None"
                key = op["group"] if op["dma"] else eng
                last.then_inc(self.sems[key], 16 if op["dma"] else 1)

    def emit_all(self, block, final_engine="gpsimd", final_waits=()):
        def _noop(h):
            return h.nop(nofuse=True)
        self.add(final_engine, _noop, reads=list(final_waits), tag="finalwait")
        self.finalize()

        def mk(e):
            def _(h):
                self.emit(e, h)
            return _
        block.sync(mk("sync"))
        block.gpsimd(mk("gpsimd"))
        block.vector(mk("vector"))
        block.scalar(mk("scalar"))
        block.tensor(mk("tensor"))

    def emit_with_sems(self, stack, block, final_engine="gpsimd", final_waits=()):
        def _noop(h):
            return h.nop(nofuse=True)
        self.add(final_engine, _noop, reads=list(final_waits), tag="finalwait")
        self.finalize()
        self.make_sems(stack)

        def mk(e):
            def _(h):
                self.emit(e, h)
            return _
        block.sync(mk("sync"))
        block.gpsimd(mk("gpsimd"))
        block.vector(mk("vector"))
        block.scalar(mk("scalar"))
        block.tensor(mk("tensor"))


def _f32(x):
    return np.ascontiguousarray(x, dtype=np.float32)



LBS = 256
NBLK = S // LBS


def build_A():
    nc = bass.Bass("TRN2", target_bir_lowering=False)
    id64c = nc.inline_tensor(np.vstack([np.eye(64), np.eye(64)]).astype(bf16), name="identity64a")
    x0T_in = nc.dram_tensor("x0T_in", [128, KC, T], F32, kind="ExternalInput").ap()
    wqkv_in = nc.dram_tensor("wqkv_in", [128, KC, 768], F32, kind="ExternalInput").ap()
    wgate_in = nc.dram_tensor("wgate_in", [128, KC, 256], F32, kind="ExternalInput").ap()
    wout_in = nc.dram_tensor("wout_in", [128, 2, DM], F32, kind="ExternalInput").ap()
    normw_in = nc.dram_tensor("normw_in", [128, 2], F32, kind="ExternalInput").ap()
    ddT_in = nc.dram_tensor("ddT_in", [128, 4, 2, LBS], F32, kind="ExternalInput").ap()
    qd_in = nc.dram_tensor("qd_in", [128, 4, LBS], F32, kind="ExternalInput").ap()
    kd_in = nc.dram_tensor("kd_in", [128, 8], F32, kind="ExternalInput").ap()
    bd_in = nc.dram_tensor("bd_in", [64, 4], F32, kind="ExternalInput").ap()
    partial_out = nc.dram_tensor("partial_out", [128, KC, T], F32, kind="ExternalOutput").ap()
    sumsq_out = nc.dram_tensor("sumsq_out", [1, T], F32, kind="ExternalOutput").ap()

    TTA = 256
    NPT = T // TTA

    import contextlib
    with contextlib.ExitStack() as st:
        xtb = st.enter_context(nc.sbuf_tensor("xtb", [128, 2 * KC * TTA], BF16))
        wqkv = st.enter_context(nc.sbuf_tensor("wqkv", [128, KC * 768], BF16))
        wgate = st.enter_context(nc.sbuf_tensor("wgate", [128, KC * 256], BF16))
        woutw = st.enter_context(nc.sbuf_tensor("woutw", [128, 2 * DM], BF16))
        normw = st.enter_context(nc.sbuf_tensor("normw", [128, 2], F32))
        Q = st.enter_context(nc.sbuf_tensor("Q", [128, 2 * T], BF16))
        Kt = st.enter_context(nc.sbuf_tensor("Kt", [128, 2 * T], BF16))
        V = st.enter_context(nc.sbuf_tensor("V", [128, (T // 128) * 256], BF16))
        ga = st.enter_context(nc.sbuf_tensor("ga", [128, 2 * T], BF16))
        ddT = st.enter_context(nc.sbuf_tensor("ddT", [128, 8 * LBS], F32))
        qd = st.enter_context(nc.sbuf_tensor("qd", [128, 4 * LBS], F32))
        kd = st.enter_context(nc.sbuf_tensor("kd", [128, 8], F32))
        bd = st.enter_context(nc.sbuf_tensor("bd", [64, 4], F32))
        id64 = st.enter_context(nc.sbuf_tensor("id64", [128, 64], BF16))
        onesf = st.enter_context(nc.sbuf_tensor("onesf", [64, 1], F32))
        qp = st.enter_context(nc.sbuf_tensor("qp", [64, LBS], BF16))
        awm = st.enter_context(nc.sbuf_tensor("awm", [128, 2 * LBS], BF16))
        ktm = st.enter_context(nc.sbuf_tensor("ktm", [128, 128], BF16))
        states = st.enter_context(nc.sbuf_tensor("states", [64, 8 * 64], F32))
        statesbf = st.enter_context(nc.sbuf_tensor("statesbf", [64, 8 * 64], BF16))
        sqscr = st.enter_context(nc.sbuf_tensor("sqscr", [64, LBS], F32))
        ssqrow = st.enter_context(nc.sbuf_tensor("ssqrow", [1, LBS], F32))
        ostage = st.enter_context(nc.sbuf_tensor("ostage", [128, 4 * LBS], F32))
        pool = st.enter_context(nc.psum_tensor("pool", [128, 3584], F32))
        pbf = st.enter_context(nc.psum_tensor("pbf", [128, 1024], BF16))

        p = P(nc)

        def reg(off, width, parts=128):
            return pool[0:parts, off:off + width], psbk(off, width)

        ps_qk = [reg(i * 512, TTA) for i in range(4)]
        ps_v = [reg(2048 + i * 256, TTA) for i in range(2)]
        ps_gt = [reg(2560 + i * 256, TTA) for i in range(2)]
        ps_aw = [reg(0, LBS), reg(512, LBS)]
        ps_att = reg(1024, LBS, parts=64)
        ps_state = reg(1536, 64, parts=64)
        ps_sq = reg(2048, LBS, parts=1)
        ps_out = [reg(2560, LBS), reg(3072, LBS)]
        k_pbf = ["psB7"]

        p.add("vector", lambda h: h.memset(onesf[:], 1.0), writes=["onesf"])
        p.add("sync", lambda h: h.dma_start(out=id64[:], in_=id64c.ap()),
              writes=["id64"], dma=True, group="gc")
        p.add("sync", lambda h: h.dma_start(
            out=ddT[:].rearrange("p (i jc l) -> p i jc l", i=4, jc=2), in_=ddT_in),
            writes=["ddT"], dma=True, group="gc")
        p.add("sync", lambda h: h.dma_start(
            out=qd[:].rearrange("p (i l) -> p i l", i=4), in_=qd_in),
            writes=["qd"], dma=True, group="gc")
        p.add("sync", lambda h: h.dma_start(out=kd[:], in_=kd_in),
              writes=["kd"], dma=True, group="gc")
        p.add("sync", lambda h: h.dma_start(out=bd[:], in_=bd_in),
              writes=["bd"], dma=True, group="gc")
        p.add("sync", lambda h: h.dma_start(out=normw[:], in_=normw_in),
              writes=["normw"], dma=True, group="gc")
        p.add("gpsimd", lambda h: h.dma_start(
            out=woutw[:].rearrange("p (c d) -> p c d", c=2), in_=wout_in),
            writes=["woutw"], dma=True, group="gc2")
        p.add("gpsimd", lambda h: h.dma_start(
            out=wqkv[:].rearrange("p (c m) -> p c m", m=768), in_=wqkv_in),
            writes=["wqkv"], dma=True, group="gw")
        p.add("gpsimd", lambda h: h.dma_start(
            out=wgate[:].rearrange("p (c m) -> p c m", m=256), in_=wgate_in),
            writes=["wgate"], dma=True, group="gw")
        for c2 in range(2):
            p.add("scalar", (lambda c2=c2: lambda h: h.activation(
                woutw[:, c2 * DM:(c2 + 1) * DM], woutw[:, c2 * DM:(c2 + 1) * DM],
                AF.Copy, scale=normw[:, c2:c2 + 1]))(),
                reads=["woutw", "normw"], writes=["woutw"], tag=f"woscale{c2}")

        for tt in range(NPT):
            par = tt % 2
            t0 = tt * TTA
            xv = lambda par=par: xtb[:, par * KC * TTA:(par + 1) * KC * TTA]
            p.add("gpsimd", (lambda tt=tt, par=par: lambda h: h.dma_start(
                out=xtb[:, par * KC * TTA:(par + 1) * KC * TTA]
                    .rearrange("p (c t) -> p c t", c=KC),
                in_=x0T_in[:, :, tt * TTA:(tt + 1) * TTA]))(),
                reads=[f"xtu{par}"], writes=[f"xtb{par}"], dma=True, group=f"gx{par}",
                tag=f"xld{tt}")

            for mi in range(4):
                def mm_qk(tt=tt, par=par, mi=mi):
                    def g(h):
                        last = None
                        for kc in range(KC):
                            last = h.matmul(
                                ps_qk[mi][0],
                                wqkv[:, kc * 768 + mi * 128: kc * 768 + (mi + 1) * 128],
                                xtb[:, par * KC * TTA + kc * TTA:
                                    par * KC * TTA + (kc + 1) * TTA],
                                start=(kc == 0), stop=(kc == KC - 1))
                        return last
                    return g
                p.add("tensor", mm_qk(), reads=["wqkv", f"xtb{par}"] + ps_qk[mi][1],
                      writes=ps_qk[mi][1], tag=f"mmqk{tt}_{mi}")
                dst = Q if mi < 2 else Kt
                dsn = "Q" if mi < 2 else "Kt"
                qi = mi % 2
                p.add("scalar", (lambda dst=dst, qi=qi, t0=t0, mi=mi: lambda h:
                      h.activation(dst[:, qi * T + t0: qi * T + t0 + TTA],
                                   ps_qk[mi][0], AF.Silu))(),
                      reads=ps_qk[mi][1], writes=[dsn], tag=f"evqk{tt}_{mi}")

            for ts in range(2):
                def mm_v(tt=tt, par=par, ts=ts):
                    def g(h):
                        last = None
                        for kc in range(KC):
                            last = h.matmul(
                                ps_v[ts][0],
                                xtb[:, par * KC * TTA + kc * TTA + ts * 128:
                                    par * KC * TTA + kc * TTA + (ts + 1) * 128],
                                wqkv[:, kc * 768 + 512: kc * 768 + 768],
                                start=(kc == 0), stop=(kc == KC - 1))
                        return last
                    return g
                p.add("tensor", mm_v(), reads=["wqkv", f"xtb{par}"] + ps_v[ts][1],
                      writes=ps_v[ts][1], tag=f"mmv{tt}_{ts}")
                ch = (t0 + ts * 128) // 128
                p.add("scalar", (lambda ch=ch, ts=ts: lambda h: h.activation(
                    V[:, ch * 256:(ch + 1) * 256], ps_v[ts][0], AF.Silu))(),
                    reads=ps_v[ts][1], writes=["V"], tag=f"evv{tt}_{ts}")

            for gi in range(2):
                def mm_gt(tt=tt, par=par, gi=gi):
                    def g(h):
                        last = None
                        for kc in range(KC):
                            last = h.matmul(
                                ps_gt[gi][0],
                                wgate[:, kc * 256 + gi * 128: kc * 256 + (gi + 1) * 128],
                                xtb[:, par * KC * TTA + kc * TTA:
                                    par * KC * TTA + (kc + 1) * TTA],
                                start=(kc == 0), stop=(kc == KC - 1))
                        return last
                    return g
                p.add("tensor", mm_gt(), reads=["wgate", f"xtb{par}"] + ps_gt[gi][1],
                      writes=ps_gt[gi][1] + ([f"xtu{par}"] if gi == 1 else []),
                      tag=f"mmgt{tt}_{gi}")
                p.add("scalar", (lambda gi=gi, t0=t0: lambda h: h.activation(
                    ga[:, gi * T + t0: gi * T + t0 + TTA], ps_gt[gi][0], AF.Sigmoid))(),
                    reads=ps_gt[gi][1], writes=["gate"], tag=f"evgt{tt}_{gi}")


        for b in range(2):
            for blk in range(NBLK):
                bt0 = b * S + blk * LBS
                gch = bt0 // 128
                for i in range(4):
                    qi, pr = i // 2, i % 2
                    rows = slice(pr * 64, pr * 64 + 64)
                    sslot = slice((b * 4 + i) * 64, (b * 4 + i) * 64 + 64)

                    for jc in range(2):
                        p.add("tensor", (lambda qi=qi, rows=rows, bt0=bt0, jc=jc:
                              lambda h: h.matmul(
                                  ps_aw[jc][0],
                                  Kt[rows, qi * T + bt0 + jc * 128:
                                     qi * T + bt0 + (jc + 1) * 128],
                                  Q[rows, qi * T + bt0: qi * T + bt0 + LBS],
                                  start=True, stop=True))(),
                              reads=["Q", "Kt"] + ps_aw[jc][1], writes=ps_aw[jc][1],
                              tag=f"aw{b}_{blk}_{i}_{jc}")
                        p.add("vector", (lambda i=i, jc=jc: lambda h: h.tensor_tensor(
                            awm[:, jc * LBS:(jc + 1) * LBS], ps_aw[jc][0],
                            ddT[:, (i * 2 + jc) * LBS:(i * 2 + jc + 1) * LBS],
                            ALU.mult))(),
                            reads=ps_aw[jc][1] + ["ddT", f"awmu{jc}"],
                            writes=[f"awm{jc}"],
                            tag=f"awm{b}_{blk}_{i}_{jc}")

                    if blk > 0:
                        p.add("vector", (lambda i=i, qi=qi, rows=rows, bt0=bt0:
                              lambda h: h.tensor_tensor(
                                  qp[:], Q[rows, qi * T + bt0: qi * T + bt0 + LBS],
                                  qd[rows, i * LBS:(i + 1) * LBS], ALU.mult))(),
                              reads=["Q", "qd", "qpu"], writes=["qp"],
                              tag=f"qp{b}_{blk}_{i}")

                    def mm_att(b=b, blk=blk, i=i, gch=gch, sslot=sslot):
                        def g(h):
                            last = None
                            for jc in range(2):
                                last = h.matmul(
                                    ps_att[0], V[:, (gch + jc) * 256 + i * 64:
                                                 (gch + jc) * 256 + (i + 1) * 64],
                                    awm[:, jc * LBS:(jc + 1) * LBS],
                                    start=(jc == 0), stop=(blk == 0 and jc == 1))
                            if blk > 0:
                                last = h.matmul(
                                    ps_att[0], statesbf[:, sslot], qp[:],
                                    start=False, stop=True)
                            return last
                        return g
                    rd = ["V", "awm0", "awm1"] + ps_att[1]
                    if blk > 0:
                        rd += ["qp", f"stbf{b}{i}"]
                    p.add("tensor", mm_att(), reads=rd,
                          writes=ps_att[1] + ["awmu0", "awmu1", "qpu"],
                          tag=f"att{b}_{blk}_{i}")

                    for jc in range(2):
                        p.add("tensor", (lambda qi=qi, rows=rows, bt0=bt0, jc=jc:
                              lambda h: h.transpose(
                                  pbf[:, jc * 64:(jc + 1) * 64],
                                  Kt[rows, qi * T + bt0 + jc * 128:
                                     qi * T + bt0 + (jc + 1) * 128], id64[rows, :]))(),
                              reads=["Kt", "id64"] + k_pbf, writes=k_pbf,
                              tag=f"ktr{b}_{blk}_{i}_{jc}")
                        p.add("scalar", (lambda i=i, jc=jc: lambda h: h.activation(
                            ktm[:, jc * 64:(jc + 1) * 64], pbf[:, jc * 64:(jc + 1) * 64],
                            AF.Copy, scale=kd[:, i * 2 + jc: i * 2 + jc + 1]))(),
                            reads=k_pbf + ["kd", f"ktmu{jc}"], writes=[f"ktm{jc}"],
                            tag=f"ktev{b}_{blk}_{i}_{jc}")

                    def mm_st(i=i, gch=gch):
                        def g(h):
                            last = None
                            for jc in range(2):
                                last = h.matmul(
                                    ps_state[0], ktm[:, jc * 64:(jc + 1) * 64],
                                    V[:, (gch + jc) * 256 + i * 64:
                                      (gch + jc) * 256 + (i + 1) * 64],
                                    start=(jc == 0), stop=(jc == 1))
                            return last
                        return g
                    p.add("tensor", mm_st(), reads=["ktm0", "ktm1", "V"] + ps_state[1],
                          writes=ps_state[1] + ["ktmu0", "ktmu1"],
                          tag=f"stmm{b}_{blk}_{i}")

                    if blk == 0:
                        p.add("vector", (lambda sslot=sslot: lambda h: h.tensor_copy(
                            states[:, sslot], ps_state[0]))(),
                            reads=ps_state[1], writes=[f"st{b}{i}"],
                            tag=f"stini{b}_{i}")
                    else:
                        p.add("vector", (lambda sslot=sslot, i=i: lambda h:
                              h.scalar_tensor_tensor(
                                  states[:, sslot], states[:, sslot],
                                  bd[:, i:i + 1], ps_state[0],
                                  ALU.mult, ALU.add))(),
                              reads=ps_state[1] + [f"st{b}{i}", "bd"],
                              writes=[f"st{b}{i}"], tag=f"stup{b}_{blk}_{i}")
                    p.add("scalar", (lambda sslot=sslot: lambda h: h.copy(
                        statesbf[:, sslot], states[:, sslot]))(),
                        reads=[f"st{b}{i}"], writes=[f"stbf{b}{i}"],
                        tag=f"stbf{b}_{blk}_{i}")

                    p.add("scalar", (lambda: lambda h: h.activation(
                        sqscr[:], ps_att[0], AF.Square))(),
                        reads=ps_att[1], writes=["sqscr"], tag=f"sq{b}_{blk}_{i}")
                    p.add("tensor", (lambda i=i: lambda h: h.matmul(
                        ps_sq[0], onesf[:, 0:1], sqscr[:],
                        start=(i == 0), stop=(i == 3)))(),
                        reads=["sqscr", "onesf"] + ps_sq[1], writes=ps_sq[1],
                        tag=f"sqmm{b}_{blk}_{i}")

                    p.add("vector", (lambda qi=qi, rows=rows, bt0=bt0: lambda h:
                          h.tensor_tensor(
                              ga[rows, qi * T + bt0: qi * T + bt0 + LBS],
                              ps_att[0],
                              ga[rows, qi * T + bt0: qi * T + bt0 + LBS],
                              ALU.mult))(),
                          reads=ps_att[1] + ["gate", "sqscr"], writes=["gate"],
                          tag=f"gmul{b}_{blk}_{i}")

                p.add("scalar", (lambda: lambda h: h.activation(
                    ssqrow[:], ps_sq[0], AF.Copy))(),
                    reads=ps_sq[1] + ["ssqu"], writes=["ssqrow"],
                    tag=f"ssqev{b}_{blk}")
                p.add("gpsimd", (lambda bt0=bt0: lambda h: h.dma_start(
                    out=sumsq_out[0:1, bt0:bt0 + LBS], in_=ssqrow[:]))(),
                    reads=["ssqrow"], writes=["ssqu"], dma=True, group="gsq",
                    tag=f"ssqst{b}_{blk}")

                for oc in range(KC):
                    opar = oc % 2
                    spar = oc % 4

                    def mm_o(oc=oc, opar=opar, bt0=bt0):
                        def g(h):
                            last = None
                            for c2 in range(2):
                                last = h.matmul(
                                    ps_out[opar][0],
                                    woutw[:, c2 * DM + oc * 128: c2 * DM + (oc + 1) * 128],
                                    ga[:, c2 * T + bt0: c2 * T + bt0 + LBS],
                                    start=(c2 == 0), stop=(c2 == 1))
                            return last
                        return g
                    p.add("tensor", mm_o(), reads=["woutw", "gate"] + ps_out[opar][1],
                          writes=ps_out[opar][1], tag=f"mmo{b}_{blk}_{oc}")
                    p.add("vector", (lambda oc=oc, opar=opar, spar=spar: lambda h:
                          h.tensor_copy(ostage[:, spar * LBS:(spar + 1) * LBS],
                                        ps_out[opar][0]))(),
                          reads=ps_out[opar][1] + [f"osu{spar}"],
                          writes=[f"ost{spar}"], tag=f"oev{b}_{blk}_{oc}")
                    p.add("gpsimd", (lambda oc=oc, spar=spar, bt0=bt0: lambda h:
                          h.dma_start(
                              out=partial_out[:, oc, bt0:bt0 + LBS],
                              in_=ostage[:, spar * LBS:(spar + 1) * LBS]))(),
                          reads=[f"ost{spar}"], writes=[f"osu{spar}", "pout"],
                          dma=True, group=f"go{spar}", tag=f"ost{b}_{blk}_{oc}")

        with nc.Block() as block:
            p.emit_with_sems(st, block, final_engine="gpsimd",
                             final_waits=["pout", "ssqu"])
    return nc


def lightning_consts(heads):
    Hh = 32
    base = 1.0 / 2.0 ** (8.0 / Hh)
    rate = np.float32(base) ** (np.arange(Hh, dtype=np.float32) + 1.0)
    factor = np.float32(1.0 - 0.0 / (32 - 1 + 1e-5) + 1e-5)
    slope = (rate * factor).astype(np.float32)
    r = np.arange(LBS, dtype=np.float32) + 1.0
    ddT = np.zeros((128, 4, 2, LBS), np.float32)
    qdv = np.zeros((128, 4, LBS), np.float32)
    kdv = np.zeros((128, 8), np.float32)
    bdv = np.zeros((64, 4), np.float32)
    for ii, h in enumerate(heads):
        s = slope[h]
        qdv[:, ii, :] = np.exp(-s * r)[None, :]
        dd = r[:, None] - r[None, :]
        ddm = np.where(dd >= 0, np.exp(-s * dd), 0.0).astype(np.float32)
        ddTm = ddm.T
        for jc in range(2):
            ddT[:, ii, jc, :] = ddTm[jc * 128:(jc + 1) * 128, :]
        kdfull = np.exp(-s * (LBS - r)).astype(np.float32)
        for jc in range(2):
            kdv[:, ii * 2 + jc] = kdfull[jc * 128:(jc + 1) * 128]
        bdv[:, ii] = np.exp(np.float32(-s * LBS))
    return ddT, qdv, kdv, bdv


def host_A(x0, qkv_w, gate_w, out_w, norm_w):
    x0T = np.ascontiguousarray(
        x0.reshape(T, DM).T.reshape(KC, 128, T).transpose(1, 0, 2), dtype=np.float32)
    maps = []
    for c in range(NCORES):
        heads = [4 * c + i for i in range(4)]
        cols = []
        for grp in range(3):
            for i in range(4):
                h0 = heads[i] * 192 + grp * 64
                cols.extend(range(h0, h0 + 64))
        wq = qkv_w[cols, :]
        wqkv = np.ascontiguousarray(
            wq.T.reshape(KC, 128, 768).transpose(1, 0, 2), dtype=np.float32)
        gsl = gate_w[256 * c:256 * c + 256, :]
        wgate = np.ascontiguousarray(
            gsl.T.reshape(KC, 128, 256).transpose(1, 0, 2), dtype=np.float32)
        osl = out_w[:, 256 * c:256 * c + 256]
        wout = np.ascontiguousarray(
            osl.T.reshape(2, 128, DM).transpose(1, 0, 2), dtype=np.float32)
        nsl = np.ascontiguousarray(
            norm_w[256 * c:256 * c + 256].reshape(2, 128).T, dtype=np.float32)
        ddT, qdv, kdv, bdv = lightning_consts(heads)
        maps.append({"x0T_in": x0T, "wqkv_in": wqkv, "wgate_in": wgate,
                     "wout_in": wout, "normw_in": nsl, "ddT_in": ddT,
                     "qd_in": qdv, "kd_in": kdv, "bd_in": bdv})
    return maps


def run_A(x0, qkv_w, gate_w, out_w, norm_w):
    nc = build_A()
    res = run_bass_kernel_spmd(nc, host_A(x0, qkv_w, gate_w, out_w, norm_w),
                               core_ids=list(range(NCORES)))
    partials = [r["partial_out"] for r in res.results]
    sumsqs = [r["sumsq_out"][0] for r in res.results]
    return partials, sumsqs



PSBANK = 512


def psbk(off, width):
    return [f"psB{k}" for k in range(off // PSBANK, (off + width - 1) // PSBANK + 1)]


COMMON_ROWSLOTS = ("MEAN", "VAR", "AROW", "BROW", "ZERO", "SCR1")


def make_common(nc, st, p, pool, NT=TPC, offs=(0, 1024, 2048, 2560)):
    HT = NT // 2
    ct = dict(NT=NT, HT=HT)
    ct["ones"] = st.enter_context(nc.sbuf_tensor("c_ones", [128, 1], F32))
    ct["onesrow"] = st.enter_context(nc.sbuf_tensor("c_onesrow", [1, 128], F32))
    ct["eps"] = st.enter_context(nc.sbuf_tensor("c_eps", [1, 1], F32))
    ct["rows"] = st.enter_context(nc.sbuf_tensor("c_rows", [1, NT * len(COMMON_ROWSLOTS)], F32))
    ct["sqscr"] = st.enter_context(nc.sbuf_tensor("c_sqscr", [128, 2 * NT], F32))
    o_r, o_r2, o_a, o_b = offs
    ct["ps_row"] = pool[0:1, o_r:o_r + NT]
    ct["ps_row2"] = pool[0:1, o_r2:o_r2 + NT]
    ct["ps_a"] = pool[:, o_a:o_a + HT]
    ct["ps_b"] = pool[:, o_b:o_b + HT]
    ct["k_row"] = psbk(o_r, NT)
    ct["k_row2"] = psbk(o_r2, NT)
    ct["k_a"] = psbk(o_a, HT)
    ct["k_b"] = psbk(o_b, HT)
    for i, nm in enumerate(COMMON_ROWSLOTS):
        ct[nm] = i * NT
    p.add("vector", lambda h: h.memset(ct["ones"][:], 1.0), writes=["c_ones"])
    p.add("vector", lambda h: h.memset(ct["onesrow"][:], 1.0), writes=["c_onesrow"])
    p.add("vector", lambda h: h.memset(ct["eps"][:], 1e-5), writes=["c_eps"])
    p.add("vector", lambda h: h.memset(ct["rows"][0:1, ct["ZERO"] * 1: ct["ZERO"] + NT], 0.0),
          writes=["c_zero"])

    def row(slot, lo=0, hi=None):
        off = ct[slot]
        return ct["rows"][0:1, off + lo: off + (NT if hi is None else hi)]
    ct["row"] = row
    return ct


def add_ln_fm(p, ct, acc, acc_res, tag=""):
    NT, HT = ct["NT"], ct["HT"]
    row, ones, onesrow, eps = ct["row"], ct["ones"], ct["onesrow"], ct["eps"]
    sqscr, ps_row, ps_row2, ps_a, ps_b = (
        ct["sqscr"], ct["ps_row"], ct["ps_row2"], ct["ps_a"], ct["ps_b"])
    k_row, k_row2, k_a, k_b = ct["k_row"], ct["k_row2"], ct["k_a"], ct["k_b"]

    def mm_sum(h):
        last = None
        for c in range(KC):
            for hh in range(2):
                last = h.matmul(
                    ps_row[0:1, hh * HT:(hh + 1) * HT], ones[:, 0:1],
                    acc[:, c * NT + hh * HT: c * NT + (hh + 1) * HT],
                    start=(c == 0), stop=(c == KC - 1))
        return last
    p.add("tensor", mm_sum, reads=list(acc_res) + ["c_ones"], writes=k_row,
          tag=f"lnsum{tag}")

    for c in range(KC):
        buf = c % 2
        p.add("scalar",
              (lambda c=c, buf=buf: lambda h: h.activation(
                  sqscr[:, buf * NT:(buf + 1) * NT],
                  acc[:, c * NT:(c + 1) * NT], AF.Square))(),
              reads=list(acc_res), writes=[f"c_sqscr{buf}"], tag=f"sq{tag}{c}")

        def mmq(c=c, buf=buf):
            def g(h):
                last = None
                for hh in range(2):
                    last = h.matmul(
                        ps_row2[0:1, hh * HT:(hh + 1) * HT], ones[:, 0:1],
                        sqscr[:, buf * NT + hh * HT: buf * NT + (hh + 1) * HT],
                        start=(c == 0), stop=(c == KC - 1))
                return last
            return g
        p.add("tensor", mmq(), reads=[f"c_sqscr{buf}", "c_ones"],
              writes=k_row2, tag=f"mmq{tag}{c}")

    p.add("scalar", lambda h: h.activation(
        row("MEAN"), ps_row[0:1, :], AF.Copy, scale=1.0 / DM),
        reads=k_row, writes=["c_rMEAN"])
    p.add("scalar", lambda h: h.activation(
        row("VAR"), ps_row2[0:1, :], AF.Copy, scale=1.0 / DM),
        reads=k_row2, writes=["c_rVAR"])
    p.add("scalar", lambda h: h.activation(row("SCR1"), row("MEAN"), AF.Square),
          reads=["c_rMEAN"], writes=["c_rSCR1"])
    p.add("vector", lambda h: h.tensor_tensor(
        row("VAR"), row("VAR"), row("SCR1"), ALU.subtract),
        reads=["c_rVAR", "c_rSCR1"], writes=["c_rVAR"])
    p.add("scalar", lambda h: h.activation(
        row("VAR"), row("VAR"), AF.Sqrt, bias=eps[0:1, 0:1]),
        reads=["c_rVAR", "c_eps"], writes=["c_rVAR"])
    p.add("vector", lambda h: h.reciprocal(row("AROW"), row("VAR")),
          reads=["c_rVAR"], writes=["c_rAROW"])
    p.add("vector", lambda h: h.tensor_tensor(
        row("BROW"), row("MEAN"), row("AROW"), ALU.mult),
        reads=["c_rMEAN", "c_rAROW"], writes=["c_rBROW"])
    p.add("vector", lambda h: h.tensor_tensor(
        row("BROW"), row("ZERO"), row("BROW"), ALU.subtract),
        reads=["c_rBROW", "c_zero"], writes=["c_rBROW"])

    for hh in range(2):
        lo = hh * HT
        p.add("tensor",
              (lambda lo=lo: lambda h: h.matmul(
                  ps_a[:], onesrow[:], row("AROW", lo, lo + HT),
                  start=True, stop=True))(),
              reads=["c_rAROW", "c_onesrow"], writes=k_a, tag=f"bca{tag}{hh}")
        p.add("tensor",
              (lambda lo=lo: lambda h: h.matmul(
                  ps_b[:], onesrow[:], row("BROW", lo, lo + HT),
                  start=True, stop=True))(),
              reads=["c_rBROW", "c_onesrow"], writes=k_b, tag=f"bcb{tag}{hh}")
        for c in range(KC):
            p.add("vector",
                  (lambda c=c, lo=lo: lambda h: h.tensor_tensor(
                      acc[:, c * NT + lo: c * NT + lo + HT],
                      acc[:, c * NT + lo: c * NT + lo + HT],
                      ps_a[:], ALU.mult))(),
                  reads=list(acc_res) + k_a, writes=list(acc_res),
                  tag=f"lnm{tag}{hh}_{c}")
            p.add("vector",
                  (lambda c=c, lo=lo: lambda h: h.tensor_tensor(
                      acc[:, c * NT + lo: c * NT + lo + HT],
                      acc[:, c * NT + lo: c * NT + lo + HT],
                      ps_b[:], ALU.add))(),
                  reads=list(acc_res) + k_b, writes=list(acc_res),
                  tag=f"lna{tag}{hh}_{c}")


def build_B(upto=99):
    nc = bass.Bass("TRN2", target_bir_lowering=False)
    p8 = nc.dram_tensor("p8", [8, 128, KC, TPC], F32, kind="ExternalInput").ap()
    sq8 = nc.dram_tensor("sq8", [8, TPC], F32, kind="ExternalInput").ap()
    x0c = nc.dram_tensor("x0c", [128, KC, TPC], F32, kind="ExternalInput").ap()
    x1_out = nc.dram_tensor("x1_out", [128, KC, TPC], F32, kind="ExternalOutput").ap()
    srp_out = nc.dram_tensor("srp_out", [128, KC], F32, kind="ExternalOutput").ap()

    HT = TPC // 2

    import contextlib
    with contextlib.ExitStack() as st:
        acc = st.enter_context(nc.sbuf_tensor("acc", [128, KC * TPC], F32))
        x0 = st.enter_context(nc.sbuf_tensor("x0", [128, KC * TPC], F32))
        sqs = st.enter_context(nc.sbuf_tensor("sq", [8, TPC], F32))
        srp = st.enter_context(nc.sbuf_tensor("srp", [128, KC], F32))
        pool = st.enter_context(nc.psum_tensor("pool", [128, 4096], F32))

        p = P(nc)
        ct = make_common(nc, st, p, pool, NT=TPC)
        row, ones, onesrow, eps = ct["row"], ct["ones"], ct["onesrow"], ct["eps"]
        ps_row, ps_a = ct["ps_row"], ct["ps_a"]
        k_row, k_a = ct["k_row"], ct["k_a"]
        ACCV = lambda: acc[:].rearrange("p (c t) -> p c t", c=KC)

        p.add("vector", lambda h: h.memset(srp[:], 0.0), writes=["srp"])

        p.add("sync", lambda h: h.dma_start(
            out=x0[:].rearrange("p (c t) -> p c t", c=KC), in_=x0c),
            writes=["x0"], dma=True, group="gx0")
        p.add("sync", lambda h: h.dma_start(out=sqs[:], in_=sq8),
              writes=["sq"], dma=True, group="gsq")

        NSL = (KC * TPC) // 2048
        for s in range(NSL):
            sl_out = acc[:, s * 2048:(s + 1) * 2048]
            p.add("gpsimd", (lambda s=s, sl_out=sl_out: lambda h: h.dma_start(
                out=sl_out, in_=p8[0, :, 2 * s:2 * s + 2, :]))(),
                writes=[f"acc{s}"], dma=True, group=f"gacc{s}")
            for j in range(1, 8):
                p.add("gpsimd", (lambda j=j, s=s, sl_out=sl_out: lambda h: h.dma_start(
                    out=sl_out, in_=p8[j, :, 2 * s:2 * s + 2, :], accum_op=ALU.add))(),
                    reads=[f"acc{s}"], writes=[f"acc{s}"], dma=True, group=f"gacc{s}")
        ACC_RES = [f"acc{s}" for s in range(NSL)]

        if upto >= 2:
            def mm_sqsum(h):
                h.matmul(ps_row[0:1, 0:HT], ones[0:8, 0:1], sqs[:, 0:HT],
                         start=True, stop=True)
                return h.matmul(ps_row[0:1, HT:TPC], ones[0:8, 0:1], sqs[:, HT:TPC],
                                start=True, stop=True)
            p.add("tensor", mm_sqsum, reads=["sq", "c_ones"], writes=k_row)
            p.add("scalar", lambda h: h.activation(
                row("VAR"), ps_row[0:1, :], AF.Sqrt, bias=eps[0:1, 0:1], scale=1.0 / DM),
                reads=k_row + ["c_eps"], writes=["c_rVAR"])
            p.add("vector", lambda h: h.reciprocal(row("MEAN"), row("VAR")),
                  reads=["c_rVAR"], writes=["c_rRSTD1"])

            for hh in range(2):
                lo = hh * HT
                p.add("tensor",
                      (lambda lo=lo: lambda h: h.matmul(
                          ps_a[:], onesrow[:], row("MEAN", lo, lo + HT),
                          start=True, stop=True))(),
                      reads=["c_rRSTD1", "c_onesrow"], writes=k_a)
                for c in range(KC):
                    p.add("vector",
                          (lambda c=c, lo=lo: lambda h: h.tensor_tensor(
                              acc[:, c * TPC + lo: c * TPC + lo + HT],
                              acc[:, c * TPC + lo: c * TPC + lo + HT],
                              ps_a[:], ALU.mult))(),
                          reads=ACC_RES + k_a, writes=ACC_RES, tag=f"sc{hh}_{c}")
                    p.add("vector",
                          (lambda c=c, lo=lo: lambda h: h.tensor_tensor(
                              acc[:, c * TPC + lo: c * TPC + lo + HT],
                              acc[:, c * TPC + lo: c * TPC + lo + HT],
                              x0[:, c * TPC + lo: c * TPC + lo + HT], ALU.add))(),
                          reads=ACC_RES + ["x0"], writes=ACC_RES, tag=f"ax{hh}_{c}")

        if upto >= 3:
            add_ln_fm(p, ct, acc, ACC_RES, tag="B")

        if upto >= 4:
            def srp_f(h):
                last = None
                for c in range(KC):
                    last = h.reduce_sum(srp[:, c:c + 1],
                                        acc[:, c * TPC:(c + 1) * TPC], axis=AX.X)
                return last
            p.add("vector", srp_f, reads=ACC_RES, writes=["srp"])

        p.add("gpsimd", lambda h: h.dma_start(out=x1_out, in_=ACCV()),
              reads=ACC_RES, writes=["out1"], dma=True, group="gout")
        p.add("gpsimd", lambda h: h.dma_start(out=srp_out, in_=srp[:]),
              reads=["srp"], writes=["out2"], dma=True, group="gout")

        with nc.Block() as block:
            p.emit_with_sems(st, block, final_engine="gpsimd",
                             final_waits=["out1", "out2"])
    return nc



def build_C():
    nc = bass.Bass("TRN2", target_bir_lowering=False)
    ident64 = nc.inline_tensor(np.eye(64, dtype=bf16), name="identity64c")
    x1_in = nc.dram_tensor("x1_in", [128, KC, TPC], F32, kind="ExternalInput").ap()
    srp4 = nc.dram_tensor("srp4", [128, 4 * KC], F32, kind="ExternalInput").ap()
    simm = nc.dram_tensor("simm", [128, KC, 32], F32, kind="ExternalInput").ap()
    gates_in = nc.dram_tensor("gates_in", [1, 32], F32, kind="ExternalInput").ap()
    wv_in = nc.dram_tensor("wv_in", [32, 64, DM], F32, kind="ExternalInput").ap()
    ow_in = nc.dram_tensor("ow_in", [32, 64, DM], F32, kind="ExternalInput").ap()
    x2_out = nc.dram_tensor("x2_out", [128, KC, TPC], F32, kind="ExternalOutput").ap()
    rt_out = nc.dram_tensor("rt_out", [1, 32], F32, kind="ExternalOutput").ap()

    HT = TPC // 2
    NE = 32

    import contextlib
    with contextlib.ExitStack() as st:
        x1 = st.enter_context(nc.sbuf_tensor("x1", [128, KC * TPC], F32))
        x1b = st.enter_context(nc.sbuf_tensor("x1b", [128, KC * TPC], BF16))
        srp_sb = st.enter_context(nc.sbuf_tensor("srp_sb", [128, 4 * KC], F32))
        s_sb = st.enter_context(nc.sbuf_tensor("s_sb", [128, KC], F32))
        s2_sb = st.enter_context(nc.sbuf_tensor("s2_sb", [128, KC], F32))
        sim_sb = st.enter_context(nc.sbuf_tensor("sim_sb", [128, KC * 32], F32))
        sim2_sb = st.enter_context(nc.sbuf_tensor("sim2_sb", [128, KC * 32], F32))
        id64 = st.enter_context(nc.sbuf_tensor("id64", [64, 64], BF16))
        g_sb = st.enter_context(nc.sbuf_tensor("g_sb", [1, 32 * 12], F32))
        r_bc = st.enter_context(nc.sbuf_tensor("r_bc", [64, 32], F32))
        rIe = st.enter_context(nc.sbuf_tensor("rIe", [64, 2 * 64], BF16))
        wvb = st.enter_context(nc.sbuf_tensor("wvb", [64, 2 * DM], BF16))
        owb = st.enter_context(nc.sbuf_tensor("owb", [64, 2 * DM], BF16))
        weT = st.enter_context(nc.sbuf_tensor("weT", [128, KC * 64], BF16))
        weTacc = st.enter_context(nc.sbuf_tensor("weTacc", [128, KC * 64], F32))
        wod = st.enter_context(nc.sbuf_tensor("wod", [64, DM], BF16))
        comb = st.enter_context(nc.sbuf_tensor("comb", [64, TPC], BF16))
        pool = st.enter_context(nc.psum_tensor("pool", [128, 4096], F32))
        ps_wo = pool[0:64, 1024:1024 + DM]
        k_wo = psbk(1024, DM)
        ps_c = pool[0:64, 3072:3072 + HT]
        k_c = psbk(3072, HT)
        ps_f = pool[:, 3584:3584 + HT]
        k_f = psbk(3584, HT)
        ps_g = pool[0:1, 3072:3072 + 128]
        k_g = psbk(3072, 128)
        ps_rb = pool[0:64, 3584:3584 + 32]
        k_rb = psbk(3584, 32)

        p = P(nc)
        ct = make_common(nc, st, p, pool, NT=TPC)
        row, ones, onesrow, eps = ct["row"], ct["ones"], ct["onesrow"], ct["eps"]

        X2V = lambda: x1[:].rearrange("p (c t) -> p c t", c=KC)

        GS = {nm: i * 32 for i, nm in enumerate(
            ["AFF", "LOG", "GATED", "MASKA", "WORK", "FB", "IND", "SCR",
             "SEL", "ROUT", "EXP", "SC1"])}

        def grow(nm, lo=0, hi=32):
            off = GS[nm]
            return g_sb[0:1, off + lo: off + hi]

        p.add("sync", lambda h: h.dma_start(out=X2V(), in_=x1_in),
              writes=["x1"], dma=True, group="gx1")
        p.add("gpsimd", lambda h: h.dma_start(
            out=x1b[:].rearrange("p (c t) -> p c t", c=KC), in_=x1_in),
            writes=["x1b"], dma=True, group="gx1b")
        p.add("sync", lambda h: h.dma_start(out=srp_sb[:], in_=srp4),
              writes=["srp_sb"], dma=True, group="gsrp")
        p.add("sync", lambda h: h.dma_start(
            out=sim_sb[:].rearrange("p (c e) -> p c e", e=32), in_=simm),
            writes=["sim_sb"], dma=True, group="gsim")
        p.add("sync", lambda h: h.dma_start(out=grow("LOG"), in_=gates_in),
              writes=["ggates", "gLOG"], dma=True, group="ggate")
        p.add("sync", lambda h: h.dma_start(out=id64[:], in_=ident64.ap()),
              writes=["id64"], dma=True, group="gid")

        def T(*slots):
            return [f"g{s}" for s in slots]

        p.add("vector", lambda h: h.tensor_tensor(
            s_sb[:], srp_sb[:, 0:KC], srp_sb[:, KC:2 * KC], ALU.add),
            reads=["srp_sb"], writes=["s_sb"])
        p.add("vector", lambda h: h.tensor_tensor(
            s2_sb[:], srp_sb[:, 2 * KC:3 * KC], srp_sb[:, 3 * KC:4 * KC], ALU.add),
            reads=["srp_sb"], writes=["s2_sb"])
        p.add("vector", lambda h: h.tensor_tensor(
            s_sb[:], s_sb[:], s2_sb[:], ALU.add),
            reads=["s_sb", "s2_sb"], writes=["s_sb"])
        p.add("scalar", lambda h: h.activation(s2_sb[:], s_sb[:], AF.Square),
              reads=["s_sb"], writes=["s2_sb"])
        p.add("tensor", lambda h: h.matmul(ps_g[0:1, 0:KC], ones[:, 0:1], s2_sb[:],
                                           start=True, stop=True),
              reads=["s2_sb", "c_ones"], writes=k_g)
        p.add("vector", lambda h: h.reduce_sum(grow("SC1", 0, 1), ps_g[0:1, 0:KC], axis=AX.X),
              reads=k_g, writes=T("SC1_0"))
        p.add("scalar", lambda h: h.activation(sim2_sb[:], sim_sb[:], AF.Square),
              reads=["sim_sb"], writes=["sim2_sb"])

        def mm_cn(h):
            last = None
            for c in range(KC):
                last = h.matmul(ps_g[0:1, 32:64], ones[:, 0:1],
                                sim2_sb[:, c * 32:(c + 1) * 32],
                                start=(c == 0), stop=(c == KC - 1))
            return last
        p.add("tensor", mm_cn, reads=["sim2_sb", "c_ones"] + k_g, writes=k_g)

        def mm_num(h):
            last = None
            for c in range(KC):
                last = h.matmul(ps_g[0:1, 64:96], s_sb[:, c:c + 1],
                                sim_sb[:, c * 32:(c + 1) * 32],
                                start=(c == 0), stop=(c == KC - 1))
            return last
        p.add("tensor", mm_num, reads=["sim_sb", "s_sb"] + k_g, writes=k_g)

        p.add("vector", lambda h: h.memset(grow("SC1", 3, 4), 0.0), writes=T("SC1_3"))
        p.add("vector", lambda h: h.memset(grow("SC1", 5, 6), 1.0e30), writes=T("SC1_5"))
        p.add("vector", lambda h: h.memset(grow("SC1", 8, 9), 1.0), writes=T("SC1_8"))
        p.add("vector", lambda h: h.memset(grow("SC1", 9, 10), 3.4e38), writes=T("SC1_9"))

        p.add("scalar", lambda h: h.activation(grow("SC1", 1, 2), grow("SC1", 0, 1), AF.Sqrt),
              reads=T("SC1_0"), writes=T("SC1_1"))
        p.add("vector", lambda h: h.reciprocal(grow("SC1", 2, 3), grow("SC1", 1, 2)),
              reads=T("SC1_1"), writes=T("SC1_2"))
        p.add("scalar", lambda h: h.activation(grow("SCR"), ps_g[0:1, 32:64], AF.Sqrt),
              reads=k_g, writes=T("SCR"))
        p.add("vector", lambda h: h.reciprocal(grow("IND"), grow("SCR")),
              reads=T("SCR"), writes=T("IND"))
        p.add("vector", lambda h: h.tensor_tensor(
            grow("AFF"), ps_g[0:1, 64:96], grow("IND"), ALU.mult),
            reads=k_g + T("IND"), writes=T("AFF"))
        p.add("vector", lambda h: h.tensor_single_scalar(
            grow("AFF"), grow("AFF"), grow("SC1", 2, 3), ALU.mult),
            reads=T("AFF", "SC1_2"), writes=T("AFF"))
        p.add("scalar", lambda h: h.activation(grow("SCR"), grow("LOG"), AF.Sigmoid),
              reads=["ggates"] + T("LOG"), writes=T("SCR"))
        p.add("vector", lambda h: h.tensor_tensor(
            grow("LOG"), grow("AFF"), grow("SCR"), ALU.subtract),
            reads=T("AFF", "SCR", "LOG") + ["ggates"], writes=T("LOG"))
        p.add("vector", lambda h: h.tensor_relu(grow("GATED"), grow("LOG")),
              reads=T("LOG"), writes=T("GATED"))
        p.add("vector", lambda h: h.tensor_single_scalar(
            grow("MASKA"), grow("GATED"), grow("SC1", 3, 4), ALU.is_gt),
            reads=T("GATED", "SC1_3"), writes=T("MASKA"))

        p.add("vector", lambda h: h.tensor_copy(grow("WORK"), grow("LOG")),
              reads=T("LOG"), writes=T("WORK"))
        p.add("vector", lambda h: h.memset(grow("FB"), 0.0), writes=T("FB"))
        for it in range(16):
            p.add("vector", lambda h: h.reduce_max(grow("SC1", 4, 5), grow("WORK"), axis=AX.X),
                  reads=T("WORK"), writes=T("SC1_4"), tag=f"tk{it}")
            p.add("vector", lambda h: h.tensor_single_scalar(
                grow("IND"), grow("WORK"), grow("SC1", 4, 5), ALU.is_ge),
                reads=T("WORK", "SC1_4"), writes=T("IND"))
            p.add("vector", lambda h: h.tensor_tensor(
                grow("FB"), grow("FB"), grow("IND"), ALU.max),
                reads=T("FB", "IND"), writes=T("FB"))
            p.add("vector", lambda h: h.tensor_single_scalar(
                grow("IND"), grow("IND"), grow("SC1", 5, 6), ALU.mult),
                reads=T("IND", "SC1_5"), writes=T("IND"))
            p.add("vector", lambda h: h.tensor_tensor(
                grow("WORK"), grow("WORK"), grow("IND"), ALU.subtract),
                reads=T("WORK", "IND"), writes=T("WORK"))
        p.add("vector", lambda h: h.reduce_sum(grow("SC1", 6, 7), grow("MASKA"), axis=AX.X),
              reads=T("MASKA"), writes=T("SC1_6"))
        p.add("vector", lambda h: h.tensor_single_scalar(
            grow("SC1", 7, 8), grow("SC1", 6, 7), grow("SC1", 3, 4), ALU.is_le),
            reads=T("SC1_6", "SC1_3"), writes=T("SC1_7"))
        p.add("vector", lambda h: h.tensor_tensor(
            grow("SCR"), grow("MASKA"), grow("FB"), ALU.max),
            reads=T("MASKA", "FB"), writes=T("SCR"))
        p.add("vector", lambda h: h.tensor_tensor(
            grow("SCR"), grow("SCR"), grow("MASKA"), ALU.subtract),
            reads=T("SCR", "MASKA"), writes=T("SCR"))
        p.add("vector", lambda h: h.tensor_single_scalar(
            grow("SCR"), grow("SCR"), grow("SC1", 7, 8), ALU.mult),
            reads=T("SCR", "SC1_7"), writes=T("SCR"))
        p.add("vector", lambda h: h.tensor_tensor(
            grow("MASKA"), grow("MASKA"), grow("SCR"), ALU.add),
            reads=T("MASKA", "SCR"), writes=T("MASKA"))
        p.add("vector", lambda h: h.tensor_tensor(
            grow("SEL"), grow("GATED"), grow("MASKA"), ALU.mult),
            reads=T("GATED", "MASKA"), writes=T("SEL"))
        p.add("vector", lambda h: h.tensor_single_scalar(
            grow("SCR"), grow("MASKA"), grow("SC1", 8, 9), ALU.subtract),
            reads=T("MASKA", "SC1_8"), writes=T("SCR"))
        p.add("vector", lambda h: h.tensor_single_scalar(
            grow("SCR"), grow("SCR"), grow("SC1", 9, 10), ALU.mult),
            reads=T("SCR", "SC1_9"), writes=T("SCR"))
        p.add("vector", lambda h: h.tensor_tensor(
            grow("SEL"), grow("SEL"), grow("SCR"), ALU.add),
            reads=T("SEL", "SCR"), writes=T("SEL"))
        p.add("vector", lambda h: h.reduce_max(grow("SC1", 10, 11), grow("SEL"),
                                               axis=AX.X, negate=True),
              reads=T("SEL"), writes=T("SC1_10"))
        p.add("scalar", lambda h: h.activation(
            grow("EXP"), grow("SEL"), AF.Exp, bias=grow("SC1", 10, 11),
            accum_out=grow("SC1", 11, 12)),
            reads=T("SEL", "SC1_10"), writes=T("EXP", "SC1_11"))
        p.add("vector", lambda h: h.reciprocal(grow("SC1", 10, 11), grow("SC1", 11, 12)),
              reads=T("SC1_11"), writes=T("SC1_10"))
        p.add("vector", lambda h: h.tensor_single_scalar(
            grow("ROUT"), grow("EXP"), grow("SC1", 10, 11), ALU.mult),
            reads=T("EXP", "SC1_10"), writes=T("ROUT") + ["grout"])
        p.add("tensor", lambda h: h.matmul(
            ps_rb, onesrow[0:1, 0:64], grow("ROUT"), start=True, stop=True),
            reads=T("ROUT") + ["c_onesrow"] + k_rb, writes=k_rb, tag="rbc")
        p.add("vector", lambda h: h.tensor_copy(r_bc[:], ps_rb),
            reads=k_rb, writes=["r_bc"])

        p.add("vector", lambda h: h.memset(weTacc[:], 0.0), writes=["weTacc"])
        for e in range(NE):
            par = e % 2
            p.add("gpsimd", (lambda e=e, par=par: lambda h: h.dma_start(
                out=wvb[:, par * DM:(par + 1) * DM], in_=wv_in[e]))(),
                reads=[f"wvuse{par}"], writes=[f"wvb{par}"], dma=True, group="gwv",
                tag=f"wvld{e}")
            p.add("gpsimd", (lambda e=e, par=par: lambda h: h.dma_start(
                out=owb[:, par * DM:(par + 1) * DM], in_=ow_in[e]))(),
                reads=[f"owuse{par}"], writes=[f"owb{par}"], dma=True, group="gow",
                tag=f"owld{e}")
            p.add("vector", (lambda e=e, par=par: lambda h: h.tensor_single_scalar(
                rIe[:, par * 64:(par + 1) * 64], id64[:], r_bc[0:64, e:e + 1], ALU.mult))(),
                reads=["id64", "r_bc", f"rIuse{par}"], writes=[f"rIe{par}"],
                tag=f"rIe{e}")

            for cc in range(KC):
                slot = cc % 8
                kslot = [f"psB{(slot * 64) // PSBANK}"]
                p.add("tensor", (lambda e=e, par=par, cc=cc, slot=slot: lambda h: h.matmul(
                    pool[:, slot * 64:(slot + 1) * 64],
                    wvb[:, par * DM + cc * 128: par * DM + (cc + 1) * 128],
                    rIe[:, par * 64:(par + 1) * 64],
                    start=True, stop=True))(),
                    reads=[f"wvb{par}", f"rIe{par}"] + kslot, writes=kslot,
                    tag=f"mmwe{e}_{cc}")
                p.add("vector", (lambda cc=cc, slot=slot: lambda h: h.tensor_tensor(
                    weTacc[:, cc * 64:(cc + 1) * 64],
                    weTacc[:, cc * 64:(cc + 1) * 64],
                    pool[:, slot * 64:(slot + 1) * 64], ALU.add))(),
                    reads=["weTacc"] + kslot, writes=["weTacc"] + kslot,
                    tag=f"weadd{e}_{cc}")
            p.add("gpsimd", (lambda par=par: lambda h: h.nop(nofuse=True))(),
                  reads=[f"wvb{par}"], writes=[f"wvuse{par}"], tag=f"wvfree{e}")

            def mm_wo(e=e, par=par):
                def g(h):
                    last = None
                    for oc in range(4):
                        last = h.matmul(
                            ps_wo[:, oc * 512:(oc + 1) * 512],
                            rIe[:, par * 64:(par + 1) * 64],
                            owb[:, par * DM + oc * 512: par * DM + (oc + 1) * 512],
                            start=(e == 0), stop=(e == NE - 1))
                    return last
                return g
            p.add("tensor", mm_wo(), reads=[f"owb{par}", f"rIe{par}"] + k_wo,
                  writes=k_wo + [f"owuse{par}", f"rIuse{par}"],
                  tag=f"mmwo{e}")

        p.add("scalar", lambda h: h.activation(weT[:], weTacc[:], AF.Copy),
              reads=["weTacc"], writes=["weT"])
        p.add("scalar", lambda h: h.activation(wod[:], ps_wo, AF.Copy),
              reads=k_wo, writes=["wod"])

        for th in range(2):
            lo = th * HT

            def mm_c(th=th, lo=lo):
                def g(h):
                    last = None
                    for cc in range(KC):
                        last = h.matmul(
                            ps_c, weT[:, cc * 64:(cc + 1) * 64],
                            x1b[:, cc * TPC + lo: cc * TPC + lo + HT],
                            start=(cc == 0), stop=(cc == KC - 1))
                    return last
                return g
            p.add("tensor", mm_c(), reads=["weT", "x1b"] + k_c, writes=k_c,
                  tag=f"mmc{th}")
            p.add("scalar", (lambda lo=lo: lambda h: h.activation(
                comb[:, lo:lo + HT], ps_c, AF.Copy))(),
                reads=k_c, writes=["comb"], tag=f"combev{th}")

            for oc in range(KC):
                p.add("tensor", (lambda oc=oc, lo=lo: lambda h: h.matmul(
                    ps_f, wod[:, oc * 128:(oc + 1) * 128],
                    comb[:, lo:lo + HT], start=True, stop=True))(),
                    reads=["wod", "comb"] + k_f, writes=k_f, tag=f"mmf{th}_{oc}")
                p.add("vector", (lambda oc=oc, lo=lo: lambda h: h.tensor_tensor(
                    x1[:, oc * TPC + lo: oc * TPC + lo + HT],
                    ps_f,
                    x1[:, oc * TPC + lo: oc * TPC + lo + HT], ALU.add))(),
                    reads=k_f + ["x1"], writes=["x1"], tag=f"resf{th}_{oc}")

        add_ln_fm(p, ct, x1, ["x1"], tag="C")

        p.add("gpsimd", lambda h: h.dma_start(out=x2_out, in_=X2V()),
              reads=["x1"], writes=["o1"], dma=True, group="gout")
        p.add("gpsimd", lambda h: h.dma_start(out=rt_out, in_=grow("ROUT")),
              reads=["grout"], writes=["o2"], dma=True, group="gout")

        with nc.Block() as block:
            p.emit_with_sems(st, block, final_engine="gpsimd",
                             final_waits=["o1", "o2"])
    return nc


def host_C(x1T, srps, sim_matrix, gates, w_v, o_weights):
    maps = []
    simm = _f32(sim_matrix.reshape(KC, 128, 32).transpose(1, 0, 2))
    wv = _f32(w_v)
    ow = _f32(o_weights)
    g1 = _f32(np.asarray(gates).reshape(1, 32))
    for c in range(NCORES):
        b = c // 4
        sl = slice(c * TPC, (c + 1) * TPC)
        x1c = _f32(x1T[:, sl].reshape(KC, 128, TPC).transpose(1, 0, 2))
        srp4 = np.concatenate([srps[4 * b + j] for j in range(4)], axis=1)
        maps.append({"x1_in": x1c, "srp4": _f32(srp4), "simm": simm,
                     "gates_in": g1, "wv_in": wv, "ow_in": ow})
    return maps


def run_C(x1T, srps, sim_matrix, gates, w_v, o_weights):
    nc = build_C()
    res = run_bass_kernel_spmd(nc, host_C(x1T, srps, sim_matrix, gates, w_v, o_weights),
                               core_ids=list(range(NCORES)))
    x2T = np.concatenate(
        [r["x2_out"].transpose(1, 0, 2).reshape(DM, TPC) for r in res.results], axis=1)
    return x2T, [r["rt_out"] for r in res.results]




def build_D():
    nc = bass.Bass("TRN2", target_bir_lowering=False)
    id128c = nc.inline_tensor(np.eye(128, dtype=np.float32), name="identity128d")
    x2T_in = nc.dram_tensor("x2T_in", [128, KC, T], F32, kind="ExternalInput").ap()
    wq_in = nc.dram_tensor("wq_in", [128, KC, 256], F32, kind="ExternalInput").ap()
    wk_in = nc.dram_tensor("wk_in", [128, KC, 256], F32, kind="ExternalInput").ap()
    wv_in = nc.dram_tensor("wv_in", [128, KC, 256], F32, kind="ExternalInput").ap()
    bq_in = nc.dram_tensor("bq_in", [128, 2], F32, kind="ExternalInput").ap()
    bk_in = nc.dram_tensor("bk_in", [128, 2], F32, kind="ExternalInput").ap()
    bv_in = nc.dram_tensor("bv_in", [1, 256], F32, kind="ExternalInput").ap()
    maskc_in = nc.dram_tensor("maskc_in", [128, 4, 512], F32, kind="ExternalInput").ap()
    maskcT_in = nc.dram_tensor("maskcT_in", [128, 4, 512], F32, kind="ExternalInput").ap()
    attn_out = nc.dram_tensor("attn_out", [2, 2, 128, S], BF16, kind="ExternalOutput").ap()
    trbounce = nc.dram_tensor("trbounce", [2, NCORES and (S // 128), 128], F32,
                              kind="Internal").ap()

    TTD = 256
    NPT = T // TTD
    NIG = S // 512
    NIC = S // 128
    RS = 0.08838834764831845

    import contextlib
    with contextlib.ExitStack() as st:
        xtb = st.enter_context(nc.sbuf_tensor("xtb", [128, 2 * KC * TTD], BF16))
        wq = st.enter_context(nc.sbuf_tensor("wq", [128, KC * 256], BF16))
        wk = st.enter_context(nc.sbuf_tensor("wk", [128, KC * 256], BF16))
        wv = st.enter_context(nc.sbuf_tensor("wv", [128, KC * 256], BF16))
        bq = st.enter_context(nc.sbuf_tensor("bq", [128, 2], F32))
        bk = st.enter_context(nc.sbuf_tensor("bk", [128, 2], F32))
        bqs = st.enter_context(nc.sbuf_tensor("bqs", [128, 2], F32))
        rs_t = st.enter_context(nc.sbuf_tensor("rs_t", [128, 1], F32))
        bvb = st.enter_context(nc.sbuf_tensor("bvb", [128, 256], F32))
        onesrow = st.enter_context(nc.sbuf_tensor("onesrow", [1, 128], F32))
        onesbf1 = st.enter_context(nc.sbuf_tensor("onesbf1", [1, 128], BF16))
        maskc = st.enter_context(nc.sbuf_tensor("maskc", [128, 4 * 512], F32))
        maskcT = st.enter_context(nc.sbuf_tensor("maskcT", [128, 4 * 512], F32))
        Q = st.enter_context(nc.sbuf_tensor("Q", [128, 2 * T], BF16))
        Kt = st.enter_context(nc.sbuf_tensor("Kt", [128, 2 * T], BF16))
        V = st.enter_context(nc.sbuf_tensor("V", [128, (T // 128) * 256], BF16))
        ptile = st.enter_context(nc.sbuf_tensor("ptile", [128, 512], BF16))
        pscr = st.enter_context(nc.sbuf_tensor("pscr", [128, 512], BF16))
        nmarr = st.enter_context(nc.sbuf_tensor("nmarr", [128, 16], F32))
        lsarr = st.enter_context(nc.sbuf_tensor("lsarr", [128, 16], F32))
        mst = st.enter_context(nc.sbuf_tensor("mst", [128, 8], F32))
        mcol = st.enter_context(nc.sbuf_tensor("mcol", [128, NIC], F32))
        rlcol = st.enter_context(nc.sbuf_tensor("rlcol", [128, NIC], F32))
        mtr = st.enter_context(nc.sbuf_tensor("mtr", [NIC, 2 * 128], F32))
        nmrow = st.enter_context(nc.sbuf_tensor("nmrow", [1, S], F32))
        nmrowb = st.enter_context(nc.sbuf_tensor("nmrowb", [1, S], BF16))
        rlrow = st.enter_context(nc.sbuf_tensor("rlrow", [1, S], F32))
        rlb = st.enter_context(nc.sbuf_tensor("rlb", [128, 512], F32))
        astage = st.enter_context(nc.sbuf_tensor("astage", [128, 2 * 512], BF16))
        id128 = st.enter_context(nc.sbuf_tensor("id128", [128, 128], F32))
        pool = st.enter_context(nc.psum_tensor("pool", [128, 4096], F32))

        p = P(nc)

        def reg(off, width, parts=128):
            return pool[0:parts, off:off + width], psbk(off, width)

        ps_s = [reg(0, 512), reg(512, 512)]
        ps_st = [reg(1024, 512), reg(1536, 512)]
        ps_pv = [reg(2048, 512), reg(2560, 512)]
        ps_rl = reg(3072, 512)
        ps_tr = reg(3584, 128, parts=NIC)
        ps_pj = [reg(0, 256), reg(512, 256), reg(1024, 256),
                 reg(1536, 256), reg(2048, 256), reg(2560, 256)]

        p.add("vector", lambda h: h.memset(onesrow[:], 1.0), writes=["onesrow"])
        p.add("vector", lambda h: h.memset(onesbf1[:], 1.0), writes=["onesbf1"])
        p.add("vector", lambda h: h.memset(rs_t[:], RS), writes=["rs_t"])
        p.add("sync", lambda h: h.dma_start(out=id128[:], in_=id128c.ap()),
              writes=["id128"], dma=True, group="gc")
        p.add("sync", lambda h: h.dma_start(out=bq[:], in_=bq_in),
              writes=["bq"], dma=True, group="gc")
        p.add("sync", lambda h: h.dma_start(out=bk[:], in_=bk_in),
              writes=["bk"], dma=True, group="gc")
        p.add("sync", lambda h: h.dma_start(
            out=maskc[:].rearrange("p (f j) -> p f j", f=4), in_=maskc_in),
            writes=["maskc"], dma=True, group="gc")
        p.add("sync", lambda h: h.dma_start(
            out=maskcT[:].rearrange("p (f j) -> p f j", f=4), in_=maskcT_in),
            writes=["maskcT"], dma=True, group="gc")
        p.add("gpsimd", lambda h: h.dma_start(
            out=wq[:].rearrange("p (c m) -> p c m", m=256), in_=wq_in),
            writes=["wq"], dma=True, group="gw")
        p.add("gpsimd", lambda h: h.dma_start(
            out=wk[:].rearrange("p (c m) -> p c m", m=256), in_=wk_in),
            writes=["wk"], dma=True, group="gw")
        p.add("gpsimd", lambda h: h.dma_start(
            out=wv[:].rearrange("p (c m) -> p c m", m=256), in_=wv_in),
            writes=["wv"], dma=True, group="gw")
        p.add("vector", lambda h: h.tensor_scalar(
            bqs[:], bq[:], rs_t[:, 0:1], None, ALU.mult),
            reads=["bq", "rs_t"], writes=["bqs"])
        p.add("sync", lambda h: h.dma_start(out=nmrow[0:1, 0:256], in_=bv_in),
              writes=["bvrow"], dma=True, group="gc")
        p.add("tensor", lambda h: h.matmul(
            pool[:, 3072:3072 + 256], onesrow[:], nmrow[0:1, 0:256],
            start=True, stop=True),
            reads=["bvrow", "onesrow"] + psbk(3072, 256), writes=psbk(3072, 256),
            tag="bvbc")
        p.add("vector", lambda h: h.tensor_copy(bvb[:], pool[:, 3072:3072 + 256]),
              reads=psbk(3072, 256), writes=["bvb"])

        for tt in range(NPT):
            par = tt % 2
            t0 = tt * TTD
            p.add("gpsimd", (lambda tt=tt, par=par: lambda h: h.dma_start(
                out=xtb[:, par * KC * TTD:(par + 1) * KC * TTD]
                    .rearrange("p (c t) -> p c t", c=KC),
                in_=x2T_in[:, :, tt * TTD:(tt + 1) * TTD]))(),
                reads=[f"xtu{par}"], writes=[f"xtb{par}"], dma=True, group=f"gx{par}",
                tag=f"xld{tt}")

            for mi in range(2):
                for wname, wsb, dst, dsn, extras in (
                        ("wq", wq, Q, "Q", "q"), ("wk", wk, Kt, "Kt", "k")):
                    pj = ps_pj[mi if wname == "wq" else 2 + mi]

                    def mm_p(par=par, mi=mi, wsb=wsb, pj=pj):
                        def g(h):
                            last = None
                            for kc in range(KC):
                                last = h.matmul(
                                    pj[0],
                                    wsb[:, kc * 256 + mi * 128: kc * 256 + (mi + 1) * 128],
                                    xtb[:, par * KC * TTD + kc * TTD:
                                        par * KC * TTD + (kc + 1) * TTD],
                                    start=(kc == 0), stop=(kc == KC - 1))
                            return last
                        return g
                    p.add("tensor", mm_p(), reads=[wname, f"xtb{par}"] + pj[1],
                          writes=pj[1], tag=f"mm{extras}{tt}_{mi}")
                    if wname == "wq":
                        p.add("vector", (lambda mi=mi, t0=t0, pj=pj: lambda h:
                              h.tensor_scalar(Q[:, mi * T + t0: mi * T + t0 + TTD],
                                              pj[0], rs_t[:, 0:1], bqs[:, mi:mi + 1],
                                              ALU.mult, ALU.add))(),
                              reads=pj[1] + ["bqs", "rs_t"], writes=["Q"],
                              tag=f"evq{tt}_{mi}")
                    else:
                        p.add("vector", (lambda mi=mi, t0=t0, pj=pj: lambda h:
                              h.tensor_scalar(Kt[:, mi * T + t0: mi * T + t0 + TTD],
                                              pj[0], bk[:, mi:mi + 1], None,
                                              ALU.add))(),
                              reads=pj[1] + ["bk"], writes=["Kt"],
                              tag=f"evk{tt}_{mi}")

            for ts in range(2):
                pj = ps_pj[4 + ts]

                def mm_v(par=par, ts=ts, pj=pj):
                    def g(h):
                        last = None
                        for kc in range(KC):
                            last = h.matmul(
                                pj[0],
                                xtb[:, par * KC * TTD + kc * TTD + ts * 128:
                                    par * KC * TTD + kc * TTD + (ts + 1) * 128],
                                wv[:, kc * 256:(kc + 1) * 256],
                                start=(kc == 0), stop=(kc == KC - 1))
                        return last
                    return g
                p.add("tensor", mm_v(), reads=["wv", f"xtb{par}"] + pj[1],
                      writes=pj[1] + ([f"xtu{par}"] if ts == 1 else []),
                      tag=f"mmv{tt}_{ts}")
                ch = (t0 + ts * 128) // 128
                p.add("vector", (lambda ch=ch, pj=pj: lambda h: h.tensor_tensor(
                    V[:, ch * 256:(ch + 1) * 256], pj[0], bvb[:], ALU.add))(),
                    reads=pj[1] + ["bvb"], writes=["V"], tag=f"evv{tt}_{ts}")

        for b in range(2):
            for hh in range(2):
                base = hh * T + b * S
                vbase = (b * S) // 128
                for ic in range(NIC):
                    i0 = ic * 128
                    njb = (i0 + 128 + 511) // 512
                    for jb in range(njb):
                        par = jb % 2
                        p.add("tensor", (lambda base=base, i0=i0, jb=jb, par=par:
                              lambda h: h.matmul(
                                  ps_s[par][0],
                                  Q[:, base + i0: base + i0 + 128],
                                  Kt[:, base + jb * 512: base + (jb + 1) * 512],
                                  start=True, stop=True))(),
                              reads=["Q", "Kt"] + ps_s[par][1], writes=ps_s[par][1],
                              tag=f"sa{b}{hh}_{ic}_{jb}")
                        if jb == i0 // 512:
                            ph = (i0 % 512) // 128
                            p.add("vector", (lambda par=par, ph=ph: lambda h:
                                  h.tensor_tensor(
                                      ps_s[par][0], ps_s[par][0],
                                      maskc[:, ph * 512:(ph + 1) * 512], ALU.add))(),
                                  reads=ps_s[par][1] + ["maskc"],
                                  writes=ps_s[par][1], tag=f"msk{b}{hh}_{ic}_{jb}")
                        p.add("vector", (lambda par=par, jb=jb: lambda h:
                              h.reduce_max(nmarr[:, jb:jb + 1], ps_s[par][0],
                                           axis=AX.X, negate=True))(),
                              reads=ps_s[par][1] + ["nmarr_u"], writes=[f"nm{jb}"],
                              tag=f"nmx{b}{hh}_{ic}_{jb}")
                        p.add("scalar", (lambda par=par, jb=jb: lambda h:
                              h.activation(pscr[:], ps_s[par][0], AF.Exp,
                                           bias=nmarr[:, jb:jb + 1],
                                           accum_out=lsarr[:, jb:jb + 1]))(),
                              reads=ps_s[par][1] + [f"nm{jb}", "lsarr_u"],
                              writes=[f"ls{jb}", "pscr"],
                              tag=f"aexp{b}{hh}_{ic}_{jb}")
                    NM = [f"nm{jb}" for jb in range(njb)]
                    LS = [f"ls{jb}" for jb in range(njb)]
                    p.add("vector", (lambda njb=njb: lambda h: h.tensor_reduce(
                        mst[:, 0:1], nmarr[:, 0:njb], AX.X, ALU.min))(),
                        reads=NM, writes=["mmin"], tag=f"mmin{b}{hh}_{ic}")
                    p.add("vector", lambda h: h.tensor_copy(pscr[0:128, 0:1], mst[:, 0:1]),
                          reads=["mmin", "pscr"], writes=["mbf"], tag=f"rnd1{b}{hh}_{ic}")
                    p.add("vector", lambda h: h.tensor_copy(mst[:, 1:2], pscr[0:128, 0:1]),
                          reads=["mbf"], writes=["mrnd"], tag=f"rnd2{b}{hh}_{ic}")
                    p.add("vector", (lambda ic=ic: lambda h: h.tensor_copy(
                        mcol[:, ic:ic + 1], mst[:, 1:2]))(),
                        reads=["mrnd"], writes=["mcol", f"mcol{ic}"],
                        tag=f"mc{b}{hh}_{ic}")
                    p.add("vector", (lambda njb=njb: lambda h: h.tensor_scalar(
                        mst[:, 8 - njb:8], nmarr[:, 0:njb], mst[:, 1:2], None,
                        ALU.subtract))(),
                        reads=NM + ["mrnd"], writes=["fac0"], tag=f"fs{b}{hh}_{ic}")
                    p.add("scalar", (lambda njb=njb: lambda h: h.activation(
                        mst[:, 8 - njb:8], mst[:, 8 - njb:8], AF.Exp, scale=-1.0))(),
                        reads=["fac0"], writes=["fac"], tag=f"fe{b}{hh}_{ic}")
                    p.add("vector", (lambda njb=njb: lambda h: h.tensor_tensor(
                        mst[:, 8 - njb:8], mst[:, 8 - njb:8], lsarr[:, 0:njb],
                        ALU.mult))(),
                        reads=LS + ["fac"], writes=["facl"], tag=f"fl{b}{hh}_{ic}")
                    p.add("vector", (lambda njb=njb: lambda h: h.reduce_sum(
                        mst[:, 2:3], mst[:, 8 - njb:8], axis=AX.X))(),
                        reads=["facl"], writes=["lsum"], tag=f"lsm{b}{hh}_{ic}")
                    p.add("vector", (lambda ic=ic: lambda h: h.reciprocal(
                        rlcol[:, ic:ic + 1], mst[:, 2:3]))(),
                        reads=["lsum"], writes=["rlcol", f"rlcol{ic}"],
                        tag=f"rl{b}{hh}_{ic}")
                    p.add("vector", lambda h: h.nop(nofuse=True),
                          reads=NM + LS, writes=["nmarr_u", "lsarr_u"],
                          tag=f"free{b}{hh}_{ic}")

                MC = [f"mcol{ic}" for ic in range(NIC)]
                RC = [f"rlcol{ic}" for ic in range(NIC)]
                p.add("tensor", lambda h: h.transpose(
                    ps_tr[0][0:NIC, 0:128], mcol[:], id128[:]),
                    reads=MC + ["mcol", "id128"] + ps_tr[1], writes=ps_tr[1],
                    tag=f"trm{b}{hh}")
                p.add("vector", lambda h: h.tensor_copy(mtr[:, 0:128], ps_tr[0][0:NIC, 0:128]),
                      reads=ps_tr[1], writes=["mtr1"], tag=f"trev{b}{hh}")
                p.add("tensor", lambda h: h.transpose(
                    ps_tr[0][0:NIC, 0:128], rlcol[:], id128[:]),
                    reads=RC + ["rlcol", "id128", "mtr1"] + ps_tr[1], writes=ps_tr[1],
                    tag=f"trr{b}{hh}")
                p.add("vector", lambda h: h.tensor_copy(mtr[:, 128:256], ps_tr[0][0:NIC, 0:128]),
                      reads=ps_tr[1], writes=["mtr2"], tag=f"trev2{b}{hh}")
                p.add("gpsimd", lambda h: h.dma_start(
                    out=trbounce[0], in_=mtr[0:NIC, 0:128]),
                    reads=["mtr1"], writes=["trb1"], dma=True, group="gtr",
                    tag=f"trd1a{b}{hh}")
                p.add("gpsimd", lambda h: h.dma_start(
                    out=trbounce[1], in_=mtr[0:NIC, 128:256]),
                    reads=["mtr2"], writes=["trb2"], dma=True, group="gtr",
                    tag=f"trd2a{b}{hh}")
                p.add("gpsimd", lambda h: h.dma_start(
                    out=nmrow[:], in_=trbounce[0].rearrange("c t -> (c t)")[None, :]),
                    reads=["trb1", "nmrow_u"], writes=["nmrow"], dma=True, group="gtr",
                    tag=f"trd1b{b}{hh}")
                p.add("gpsimd", lambda h: h.dma_start(
                    out=rlrow[:], in_=trbounce[1].rearrange("c t -> (c t)")[None, :]),
                    reads=["trb2", "rlrow_u"], writes=["rlrow"], dma=True, group="gtr",
                    tag=f"trd2b{b}{hh}")
                p.add("vector", lambda h: h.tensor_copy(nmrowb[:], nmrow[:]),
                      reads=["nmrow"], writes=["nmrowb"], tag=f"nmb{b}{hh}")

                for ig in range(NIG):
                    i0 = ig * 512
                    njc = (i0 + 512) // 128
                    pvp = ig % 2
                    for jc in range(njc):
                        par = jc % 2

                        def mm_st(base=base, i0=i0, jc=jc, par=par):
                            def g(h):
                                h.matmul(
                                    ps_st[par][0],
                                    Kt[:, base + jc * 128: base + (jc + 1) * 128],
                                    Q[:, base + i0: base + i0 + 512],
                                    start=True, stop=False)
                                return h.matmul(
                                    ps_st[par][0], onesbf1[:],
                                    nmrowb[0:1, i0:i0 + 512],
                                    start=False, stop=True)
                            return g
                        p.add("tensor", mm_st(), reads=["Q", "Kt", "nmrowb"] + ps_st[par][1],
                              writes=ps_st[par][1], tag=f"sb{b}{hh}_{ig}_{jc}")
                        rj = jc - ig * 4
                        if rj >= 0:
                            p.add("vector", (lambda par=par, rj=rj: lambda h:
                                  h.tensor_tensor(
                                      ps_st[par][0], ps_st[par][0],
                                      maskcT[:, rj * 512:(rj + 1) * 512], ALU.add))(),
                                  reads=ps_st[par][1] + ["maskcT"],
                                  writes=ps_st[par][1], tag=f"mskb{b}{hh}_{ig}_{jc}")
                        p.add("scalar", (lambda par=par: lambda h: h.activation(
                            ptile[:], ps_st[par][0], AF.Exp))(),
                            reads=ps_st[par][1] + ["ptile_u"], writes=["ptile"],
                            tag=f"bexp{b}{hh}_{ig}_{jc}")
                        p.add("tensor", (lambda hh=hh, vbase=vbase, jc=jc, pvp=pvp,
                              njc=njc: lambda h: h.matmul(
                                  ps_pv[pvp][0],
                                  V[:, (vbase + jc) * 256 + hh * 128:
                                    (vbase + jc) * 256 + (hh + 1) * 128],
                                  ptile[:],
                                  start=(jc == 0), stop=(jc == njc - 1)))(),
                              reads=["V", "ptile"] + ps_pv[pvp][1],
                              writes=ps_pv[pvp][1] + ["ptile_u"],
                              tag=f"pv{b}{hh}_{ig}_{jc}")

                    p.add("tensor", (lambda i0=i0: lambda h: h.matmul(
                        ps_rl[0], onesrow[:], rlrow[0:1, i0:i0 + 512],
                        start=True, stop=True))(),
                        reads=["rlrow", "onesrow"] + ps_rl[1], writes=ps_rl[1],
                        tag=f"rlb{b}{hh}_{ig}")
                    p.add("vector", (lambda: lambda h: h.tensor_copy(
                        rlb[:], ps_rl[0]))(),
                        reads=ps_rl[1] + ["rlb_u"], writes=["rlb"],
                        tag=f"rlev{b}{hh}_{ig}")
                    sp = ig % 2
                    p.add("vector", (lambda pvp=pvp, sp=sp: lambda h: h.tensor_tensor(
                        astage[:, sp * 512:(sp + 1) * 512], ps_pv[pvp][0], rlb[:],
                        ALU.mult))(),
                        reads=ps_pv[pvp][1] + ["rlb", f"asu{sp}"],
                        writes=[f"ast{sp}", "rlb_u"], tag=f"oev{b}{hh}_{ig}")
                    p.add("gpsimd", (lambda b=b, hh=hh, i0=i0, sp=sp: lambda h:
                          h.dma_start(
                              out=attn_out[b, hh, :, i0:i0 + 512],
                              in_=astage[:, sp * 512:(sp + 1) * 512]))(),
                          reads=[f"ast{sp}"], writes=[f"asu{sp}", "aout"],
                          dma=True, group=f"gao{sp}", tag=f"ao{b}{hh}_{ig}")

        with nc.Block() as block:
            p.emit_with_sems(st, block, final_engine="gpsimd", final_waits=["aout"])
    return nc


def host_D(x2T, w_q, b_q, w_k, b_k, w_v, b_v):
    x2L = np.ascontiguousarray(
        x2T.reshape(KC, 128, T).transpose(1, 0, 2), dtype=np.float32)
    maskc = np.zeros((128, 4, 512), np.float32)
    maskcT = np.zeros((128, 4, 512), np.float32)
    for ph in range(4):
        for pp in range(128):
            i = ph * 128 + pp
            maskc[pp, ph, :] = np.where(np.arange(512) <= i, 0.0, -1e10)
        for pp in range(128):
            maskcT[pp, ph, :] = np.where(np.arange(512) >= ph * 128 + pp, 0.0, -1e10)
    maps = []
    for c in range(NCORES):
        sl = slice(256 * c, 256 * (c + 1))
        wqL = np.ascontiguousarray(
            w_q[:, sl].reshape(KC, 128, 256).transpose(1, 0, 2), dtype=np.float32)
        wkL = np.ascontiguousarray(
            w_k[:, sl].reshape(KC, 128, 256).transpose(1, 0, 2), dtype=np.float32)
        wvL = np.ascontiguousarray(
            w_v[:, sl].reshape(KC, 128, 256).transpose(1, 0, 2), dtype=np.float32)
        bqL = np.ascontiguousarray(b_q[sl].reshape(2, 128).T, dtype=np.float32)
        bkL = np.ascontiguousarray(b_k[sl].reshape(2, 128).T, dtype=np.float32)
        bvL = np.ascontiguousarray(b_v[sl].reshape(1, 256), dtype=np.float32)
        maps.append({"x2T_in": x2L, "wq_in": wqL, "wk_in": wkL, "wv_in": wvL,
                     "bq_in": bqL, "bk_in": bkL, "bv_in": bvL,
                     "maskc_in": maskc, "maskcT_in": maskcT})
    return maps


def run_D(x2T, w_q, b_q, w_k, b_k, w_v, b_v):
    nc = build_D()
    res = run_bass_kernel_spmd(nc, host_D(x2T, w_q, b_q, w_k, b_k, w_v, b_v),
                               core_ids=list(range(NCORES)))
    attnT = np.zeros((DM, T), dtype=bf16)
    for c in range(NCORES):
        a = res.results[c]["attn_out"]
        for hh in range(2):
            head = 2 * c + hh
            for b in range(2):
                attnT[head * 128:(head + 1) * 128, b * S:(b + 1) * S] = a[b, hh]
    return attnT



HID = 5632
HC = HID // 128


def build_E():
    nc = bass.Bass("TRN2", target_bir_lowering=False)
    attn_in = nc.dram_tensor("attn_in", [128, KC, TPC], BF16, kind="ExternalInput").ap()
    x2_in = nc.dram_tensor("x2_in", [128, KC, TPC], F32, kind="ExternalInput").ap()
    wo_in = nc.dram_tensor("wo_in", [128, KC, DM], F32, kind="ExternalInput").ap()
    bo_in = nc.dram_tensor("bo_in", [128, KC], F32, kind="ExternalInput").ap()
    wqg_in = nc.dram_tensor("wqg_in", [128, KC, HID], I32, kind="ExternalInput").ap()
    wqu_in = nc.dram_tensor("wqu_in", [128, KC, HID], I32, kind="ExternalInput").ap()
    wqd_in = nc.dram_tensor("wqd_in", [128, HC, DM], I32, kind="ExternalInput").ap()
    sg_in = nc.dram_tensor("sg_in", [128, HC], F32, kind="ExternalInput").ap()
    su_in = nc.dram_tensor("su_in", [128, HC], F32, kind="ExternalInput").ap()
    sd_in = nc.dram_tensor("sd_in", [128, KC], F32, kind="ExternalInput").ap()
    x4_out = nc.dram_tensor("x4_out", [128, KC, TPC], F32, kind="ExternalOutput").ap()

    TT2 = 256
    NTT = TPC // TT2
    HT = TPC // 2

    import contextlib
    with contextlib.ExitStack() as st:
        x = st.enter_context(nc.sbuf_tensor("x", [128, KC * TPC], F32))
        attn = st.enter_context(nc.sbuf_tensor("attn", [128, KC * TPC], BF16))
        wo = st.enter_context(nc.sbuf_tensor("wo", [128, KC * DM], BF16))
        bo = st.enter_context(nc.sbuf_tensor("bo", [128, KC], F32))
        sg = st.enter_context(nc.sbuf_tensor("sg", [128, HC], F32))
        su = st.enter_context(nc.sbuf_tensor("su", [128, HC], F32))
        sd = st.enter_context(nc.sbuf_tensor("sd", [128, KC], F32))
        onesbf = st.enter_context(nc.sbuf_tensor("onesbf", [128, 1], BF16))
        csb = st.enter_context(nc.sbuf_tensor("csb", [128, TT2], F32))
        csb2 = st.enter_context(nc.sbuf_tensor("csb2", [128, TT2], F32))
        tscr = st.enter_context(nc.sbuf_tensor("tscr", [128, TT2], F32))
        ga = st.enter_context(nc.sbuf_tensor("ga", [128, TT2], BF16))
        ua = st.enter_context(nc.sbuf_tensor("ua", [128, TT2], BF16))
        pool = st.enter_context(nc.psum_tensor("pool", [128, 4096], F32))

        x3bf = wo[:, 0:KC * TT2]
        wt = [wo[:, 4096 + i * 2048: 4096 + (i + 1) * 2048] for i in range(4)]
        wdt = [wo[:, 12288 + i * HC * 128: 12288 + (i + 1) * HC * 128] for i in range(2)]
        assert 12288 + 2 * HC * 128 <= KC * DM
        hid = attn[:, 0:HC * TT2]

        p = P(nc)
        ct = make_common(nc, st, p, pool, NT=TPC)
        row, ones, onesrow, eps = ct["row"], ct["ones"], ct["onesrow"], ct["eps"]

        XV = lambda: x[:].rearrange("p (c t) -> p c t", c=KC)

        ps_o = [pool[:, 0:512], pool[:, 512:1024]]
        k_o = [psbk(0, 512), psbk(512, 512)]
        ps_cs = pool[0:1, 1024:1024 + TT2]
        k_cs = psbk(1024, TT2)
        ps_cb = pool[:, 1536:1536 + TT2]
        k_cb = psbk(1536, TT2)
        ps_g = pool[:, 2048:2048 + TT2]
        k_g = psbk(2048, TT2)
        ps_u = pool[:, 2560:2560 + TT2]
        k_u = psbk(2560, TT2)
        ps_d = [pool[:, 3072:3072 + TT2], pool[:, 3584:3584 + TT2]]
        k_d = [psbk(3072, TT2), psbk(3584, TT2)]

        p.add("vector", lambda h: h.memset(onesbf[:], 1.0), writes=["onesbf"])
        p.add("sync", lambda h: h.dma_start(out=XV(), in_=x2_in),
              writes=["x"], dma=True, group="gx2")
        p.add("sync", lambda h: h.dma_start(
            out=attn[:].rearrange("p (c t) -> p c t", c=KC), in_=attn_in),
            writes=["attn"], dma=True, group="gattn")
        p.add("gpsimd", lambda h: h.dma_start(
            out=wo[:].rearrange("p (c d) -> p c d", d=DM), in_=wo_in),
            writes=["wo"], dma=True, group="gwo")
        p.add("sync", lambda h: h.dma_start(out=bo[:], in_=bo_in),
              writes=["bo"], dma=True, group="gsc")
        p.add("sync", lambda h: h.dma_start(out=sg[:], in_=sg_in),
              writes=["sg"], dma=True, group="gsc")
        p.add("sync", lambda h: h.dma_start(out=su[:], in_=su_in),
              writes=["su"], dma=True, group="gsc")
        p.add("sync", lambda h: h.dma_start(out=sd[:], in_=sd_in),
              writes=["sd"], dma=True, group="gsc")

        for th in range(2):
            for oc in range(KC):
                par = (th * KC + oc) % 2

                def mm_o(th=th, oc=oc, par=par):
                    def g(h):
                        last = None
                        for kc in range(KC):
                            last = h.matmul(
                                ps_o[par],
                                wo[:, kc * DM + oc * 128: kc * DM + (oc + 1) * 128],
                                attn[:, kc * TPC + th * HT: kc * TPC + th * HT + HT],
                                start=(kc == 0), stop=(kc == KC - 1))
                        return last
                    return g
                p.add("tensor", mm_o(), reads=["wo", "attn"] + k_o[par],
                      writes=k_o[par], tag=f"mmo{th}_{oc}")
                p.add("vector", (lambda th=th, oc=oc, par=par: lambda h:
                      h.scalar_tensor_tensor(
                          x[:, oc * TPC + th * HT: oc * TPC + th * HT + HT],
                          ps_o[par], bo[:, oc:oc + 1],
                          x[:, oc * TPC + th * HT: oc * TPC + th * HT + HT],
                          ALU.add, ALU.add))(),
                      reads=k_o[par] + ["bo", "x"], writes=["x"], tag=f"evo{th}_{oc}")

        p.add("gpsimd", lambda h: h.nop(nofuse=True),
              writes=["wo", "attn", "woarena", "attnarena"], tag="arena_sw")

        add_ln_fm(p, ct, x, ["x"], tag="E3")

        for tt in range(NTT):
            t0 = tt * TT2

            def cast_x3(tt=tt, t0=t0):
                def g(h):
                    last = None
                    for kc in range(KC):
                        last = h.tensor_copy(
                            x3bf[:, kc * TT2:(kc + 1) * TT2],
                            x[:, kc * TPC + t0: kc * TPC + t0 + TT2])
                    return last
                return g
            p.add("vector", cast_x3(), reads=["x", "woarena"], writes=["x3bf"],
                  tag=f"cx3_{tt}")

            def mm_cs(tt=tt):
                def g(h):
                    last = None
                    for kc in range(KC):
                        last = h.matmul(ps_cs, onesbf[:, 0:1],
                                        x3bf[:, kc * TT2:(kc + 1) * TT2],
                                        start=(kc == 0), stop=(kc == KC - 1))
                    return last
                return g
            p.add("tensor", mm_cs(), reads=["x3bf", "onesbf"] + k_cs, writes=k_cs,
                  tag=f"mmcs{tt}")
            p.add("scalar", (lambda: lambda h: h.activation(
                row("SCR1", 0, TT2), ps_cs, AF.Copy, scale=128.0))(),
                reads=k_cs, writes=["r128cs"], tag=f"cs{tt}")
            p.add("tensor", lambda h: h.matmul(
                ps_cb, onesrow[:], row("SCR1", 0, TT2), start=True, stop=True),
                reads=["r128cs", "c_onesrow"] + k_cb, writes=k_cb, tag=f"bccs{tt}")
            p.add("vector", lambda h: h.tensor_copy(csb[:], ps_cb),
                  reads=k_cb, writes=["csb"], tag=f"evcs{tt}")

            for hc in range(HC):
                par = hc % 2
                p.add("gpsimd", (lambda hc=hc, par=par: lambda h: h.dma_start(
                    out=wt[par].rearrange("p (c m) -> p c m", m=128),
                    in_=wqg_in[:, :, hc * 128:(hc + 1) * 128]))(),
                    reads=[f"wtu{par}", "woarena"], writes=[f"wt{par}"], dma=True,
                    group=f"gwt{par}", tag=f"ldg{tt}_{hc}")
                p.add("gpsimd", (lambda hc=hc, par=par: lambda h: h.dma_start(
                    out=wt[2 + par].rearrange("p (c m) -> p c m", m=128),
                    in_=wqu_in[:, :, hc * 128:(hc + 1) * 128]))(),
                    reads=[f"wuu{par}", "woarena"], writes=[f"wu{par}"], dma=True,
                    group=f"gwu{par}", tag=f"ldu{tt}_{hc}")

                def mm_g(hc=hc, par=par):
                    def g(h):
                        last = None
                        for kc in range(KC):
                            last = h.matmul(
                                ps_g, wt[par][:, kc * 128:(kc + 1) * 128],
                                x3bf[:, kc * TT2:(kc + 1) * TT2],
                                start=(kc == 0), stop=(kc == KC - 1))
                        return last
                    return g
                p.add("tensor", mm_g(), reads=[f"wt{par}", "x3bf"] + k_g,
                      writes=k_g + [f"wtu{par}"], tag=f"mmg{tt}_{hc}")
                p.add("vector", (lambda: lambda h: h.tensor_tensor(
                    tscr[:], ps_g, csb[:], ALU.subtract))(),
                    reads=k_g + ["csb", "tscr_free"], writes=["tscr"],
                    tag=f"gsub{tt}_{hc}")
                p.add("scalar", (lambda hc=hc: lambda h: h.activation(
                    ga[:], tscr[:], AF.Silu, scale=sg[:, hc:hc + 1]))(),
                    reads=["tscr", "sg", "ga_free"], writes=["ga"], tag=f"gact{tt}_{hc}")

                def mm_u(hc=hc, par=par):
                    def g(h):
                        last = None
                        for kc in range(KC):
                            last = h.matmul(
                                ps_u, wt[2 + par][:, kc * 128:(kc + 1) * 128],
                                x3bf[:, kc * TT2:(kc + 1) * TT2],
                                start=(kc == 0), stop=(kc == KC - 1))
                        return last
                    return g
                p.add("tensor", mm_u(), reads=[f"wu{par}", "x3bf"] + k_u,
                      writes=k_u + [f"wuu{par}"], tag=f"mmu{tt}_{hc}")
                p.add("vector", (lambda: lambda h: h.tensor_tensor(
                    tscr[:], ps_u, csb[:], ALU.subtract))(),
                    reads=k_u + ["csb", "ga"], writes=["tscr"],
                    tag=f"usub{tt}_{hc}")
                p.add("scalar", (lambda hc=hc: lambda h: h.activation(
                    ua[:], tscr[:], AF.Copy, scale=su[:, hc:hc + 1]))(),
                    reads=["tscr", "su", "ua_free"], writes=["ua"], tag=f"uact{tt}_{hc}")
                p.add("vector", (lambda hc=hc: lambda h: h.tensor_tensor(
                    hid[:, hc * TT2:(hc + 1) * TT2], ga[:], ua[:], ALU.mult))(),
                    reads=["ga", "ua", "attnarena"], writes=[f"hid{hc}", "ga_free",
                                                             "ua_free", "tscr_free"],
                    tag=f"hmul{tt}_{hc}")
            HID_RES = [f"hid{hc}" for hc in range(HC)]

            def mm_cs2(tt=tt):
                def g(h):
                    last = None
                    for hc in range(HC):
                        last = h.matmul(ps_cs, onesbf[:, 0:1],
                                        hid[:, hc * TT2:(hc + 1) * TT2],
                                        start=(hc == 0), stop=(hc == HC - 1))
                    return last
                return g
            p.add("tensor", mm_cs2(), reads=HID_RES + ["onesbf"] + k_cs, writes=k_cs,
                  tag=f"mmcs2{tt}")
            p.add("scalar", (lambda: lambda h: h.activation(
                row("SCR1", 0, TT2), ps_cs, AF.Copy, scale=128.0))(),
                reads=k_cs + ["r128cs"], writes=["r128cs"], tag=f"cs2{tt}")
            p.add("tensor", lambda h: h.matmul(
                ps_cb, onesrow[:], row("SCR1", 0, TT2), start=True, stop=True),
                reads=["r128cs", "c_onesrow"] + k_cb, writes=k_cb, tag=f"bccs2{tt}")
            p.add("vector", lambda h: h.tensor_copy(csb2[:], ps_cb),
                  reads=k_cb + ["csb"], writes=["csb2"], tag=f"evcs2{tt}")

            for dc in range(KC):
                par = dc % 2
                p.add("gpsimd", (lambda dc=dc, par=par: lambda h: h.dma_start(
                    out=wdt[par].rearrange("p (c m) -> p c m", m=128),
                    in_=wqd_in[:, :, dc * 128:(dc + 1) * 128]))(),
                    reads=[f"wdu{par}", "woarena"], writes=[f"wd{par}"], dma=True,
                    group=f"gwd{par}", tag=f"ldd{tt}_{dc}")

                def mm_d(dc=dc, par=par):
                    def g(h):
                        last = None
                        for hc in range(HC):
                            last = h.matmul(
                                ps_d[par], wdt[par][:, hc * 128:(hc + 1) * 128],
                                hid[:, hc * TT2:(hc + 1) * TT2],
                                start=(hc == 0), stop=(hc == HC - 1))
                        return last
                    return g
                p.add("tensor", mm_d(), reads=[f"wd{par}"] + HID_RES + k_d[par],
                      writes=k_d[par] + [f"wdu{par}"], tag=f"mmd{tt}_{dc}")
                p.add("vector", (lambda dc=dc, par=par: lambda h: h.tensor_tensor(
                    tscr[:], ps_d[par], csb2[:], ALU.subtract))(),
                    reads=k_d[par] + ["csb2", "tscr_free"], writes=["tscr2"],
                    tag=f"dsub{tt}_{dc}")
                p.add("vector", (lambda dc=dc, t0=t0: lambda h:
                      h.scalar_tensor_tensor(
                          x[:, dc * TPC + t0: dc * TPC + t0 + TT2],
                          tscr[:], sd[:, dc:dc + 1],
                          x[:, dc * TPC + t0: dc * TPC + t0 + TT2],
                          ALU.mult, ALU.add))(),
                      reads=["tscr2", "sd", "x"], writes=["x", "tscr_free"],
                      tag=f"dres{tt}_{dc}")

        add_ln_fm(p, ct, x, ["x"], tag="E4")

        p.add("gpsimd", lambda h: h.dma_start(out=x4_out, in_=XV()),
              reads=["x"], writes=["o1"], dma=True, group="gout")

        with nc.Block() as block:
            p.emit_with_sems(st, block, final_engine="gpsimd", final_waits=["o1"])
    return nc


def host_E(attnT, x2T, w_o, b_o, wq_gate, s_gate, wq_up, s_up, wq_down, s_down):
    woL = _f32(w_o).reshape(KC, 128, DM).transpose(1, 0, 2)
    woL = np.ascontiguousarray(woL)
    boL = _f32(b_o).reshape(KC, 128).T
    wqgL = np.ascontiguousarray(
        wq_gate.T.reshape(KC, 128, HID).transpose(1, 0, 2), dtype=np.int32)
    wquL = np.ascontiguousarray(
        wq_up.T.reshape(KC, 128, HID).transpose(1, 0, 2), dtype=np.int32)
    wqdL = np.ascontiguousarray(
        wq_down.T.reshape(HC, 128, DM).transpose(1, 0, 2), dtype=np.int32)
    sgL = _f32(s_gate).reshape(HC, 128).T
    suL = _f32(s_up).reshape(HC, 128).T
    sdL = _f32(s_down).reshape(KC, 128).T
    maps = []
    for c in range(NCORES):
        sl = slice(c * TPC, (c + 1) * TPC)
        at = np.ascontiguousarray(
            attnT[:, sl].reshape(KC, 128, TPC).transpose(1, 0, 2))
        x2c = _f32(x2T[:, sl].reshape(KC, 128, TPC).transpose(1, 0, 2))
        maps.append({"attn_in": at.astype(bf16), "x2_in": x2c, "wo_in": woL,
                     "bo_in": np.ascontiguousarray(boL),
                     "wqg_in": wqgL, "wqu_in": wquL, "wqd_in": wqdL,
                     "sg_in": np.ascontiguousarray(sgL),
                     "su_in": np.ascontiguousarray(suL),
                     "sd_in": np.ascontiguousarray(sdL)})
    return maps


def run_E(attnT, x2T, w_o, b_o, wq_gate, s_gate, wq_up, s_up, wq_down, s_down):
    nc = build_E()
    res = run_bass_kernel_spmd(
        nc, host_E(attnT, x2T, w_o, b_o, wq_gate, s_gate, wq_up, s_up, wq_down, s_down),
        core_ids=list(range(NCORES)))
    return np.concatenate(
        [r["x4_out"].transpose(1, 0, 2).reshape(DM, TPC) for r in res.results], axis=1)


def host_B(partials, sumsqs, x0T):
    maps = []
    for c in range(NCORES):
        sl = slice(c * TPC, (c + 1) * TPC)
        p8 = np.stack([partials[j][:, :, sl] for j in range(8)])
        sq8 = np.stack([sumsqs[j][sl] for j in range(8)])
        x0c = np.ascontiguousarray(
            x0T[:, sl].reshape(KC, 128, TPC).transpose(1, 0, 2))
        maps.append({"p8": _f32(p8), "sq8": _f32(sq8), "x0c": _f32(x0c)})
    return maps


def run_B(partials, sumsqs, x0T):
    nc = build_B()
    res = run_bass_kernel_spmd(nc, host_B(partials, sumsqs, x0T),
                               core_ids=list(range(NCORES)))
    x1T = np.concatenate(
        [r["x1_out"].transpose(1, 0, 2).reshape(DM, TPC) for r in res.results], axis=1)
    srp = [r["srp_out"] for r in res.results]
    return x1T, srp


def kernel(L0_attn_hidden_states, L0_attn_qkv_w, L0_attn_out_w, L0_attn_gate_w,
           L0_attn_norm_w, L0_ffn_sim_matrix, L0_ffn_gates, L0_ffn_w_q,
           L0_ffn_w_k, L0_ffn_w_v, L0_ffn_o_weights, L1_attn_w_q, L1_attn_b_q,
           L1_attn_w_k, L1_attn_b_k, L1_attn_w_v, L1_attn_b_v, L1_attn_w_o,
           L1_attn_b_o, L1_ffn_wq_gate, L1_ffn_s_gate, L1_ffn_wq_up,
           L1_ffn_s_up, L1_ffn_wq_down, L1_ffn_s_down):
    x0 = np.ascontiguousarray(np.asarray(L0_attn_hidden_states), dtype=np.float32)
    x0T = np.ascontiguousarray(x0.reshape(T, DM).T)

    partials, sumsqs = run_A(x0, np.asarray(L0_attn_qkv_w),
                             np.asarray(L0_attn_gate_w), np.asarray(L0_attn_out_w),
                             np.asarray(L0_attn_norm_w))
    x1T, srps = run_B(partials, sumsqs, x0T)
    x2T, _ = run_C(x1T, srps, np.asarray(L0_ffn_sim_matrix),
                   np.asarray(L0_ffn_gates), np.asarray(L0_ffn_w_v),
                   np.asarray(L0_ffn_o_weights))
    attnT = run_D(x2T, np.asarray(L1_attn_w_q), np.asarray(L1_attn_b_q),
                  np.asarray(L1_attn_w_k), np.asarray(L1_attn_b_k),
                  np.asarray(L1_attn_w_v), np.asarray(L1_attn_b_v))
    x4T = run_E(attnT, x2T, np.asarray(L1_attn_w_o), np.asarray(L1_attn_b_o),
                np.asarray(L1_ffn_wq_gate), np.asarray(L1_ffn_s_gate),
                np.asarray(L1_ffn_wq_up), np.asarray(L1_ffn_s_up),
                np.asarray(L1_ffn_wq_down), np.asarray(L1_ffn_s_down))
    return np.ascontiguousarray(x4T.T).reshape(BB, S, DM)


# version 02
# speedup vs baseline: 1.3863x; 1.3863x over previous
import numpy as np
import ml_dtypes
import concourse.bass as bass
import concourse.mybir as mybir
from concourse.bass_utils import run_bass_kernel_spmd

F32 = mybir.dt.float32
BF16 = mybir.dt.bfloat16
I32 = mybir.dt.int32
AF = mybir.ActivationFunctionType
ALU = mybir.AluOpType
AX = mybir.AxisListType

NCORES = 8
BB, S, DM = 2, 4096, 2048
T = BB * S
TPC = T // NCORES
KC = DM // 128

bf16 = ml_dtypes.bfloat16

ENGINES = ("sync", "gpsimd", "vector", "scalar", "tensor")


class P:

    def __init__(self, nc):
        self.nc = nc
        self.ops = {e: [] for e in ENGINES}
        self.last_writer = {}
        self.readers = {}
        self.groups = {}
        self.sems = {}

    def add(self, eng, fn, reads=(), writes=(), dma=False, group=None, tag=""):
        idx = len(self.ops[eng])
        me = (eng, idx)
        assert (group is not None) == bool(dma), f"dma ops need a group ({tag})"
        deps = set()
        for r in reads:
            w = self.last_writer.get(r)
            if w is not None:
                deps.add(w)
        for r in writes:
            w = self.last_writer.get(r)
            if w is not None:
                deps.add(w)
            for rd in self.readers.get(r, ()):
                deps.add(rd)
        if dma and group in self.groups:
            deps.add(self.groups[group])
        deps.discard(me)
        self.ops[eng].append(dict(fn=fn, deps=deps, dma=dma, group=group,
                                  marked=False, tag=tag))
        for r in reads:
            self.readers.setdefault(r, set()).add(me)
        for r in writes:
            self.last_writer[r] = me
            self.readers[r] = set()
        if dma:
            self.groups[group] = me
        return me

    def finalize(self):
        for e in ENGINES:
            for op in self.ops[e]:
                if op["dma"]:
                    op["marked"] = True
                for (de, di) in op["deps"]:
                    self.ops[de][di]["marked"] = True
        self.incval = {}
        counts = {}
        for e in ENGINES:
            for i, op in enumerate(self.ops[e]):
                key = op["group"] if op["dma"] else e
                if op["marked"]:
                    counts[key] = counts.get(key, 0) + (16 if op["dma"] else 1)
                self.incval[(e, i)] = (key, counts.get(key, 0))

    def make_sems(self, stack):
        keys = set()
        for e in ENGINES:
            for op in self.ops[e]:
                keys.add(op["group"] if op["dma"] else e)
        for k in sorted(keys):
            self.sems[k] = stack.enter_context(self.nc.semaphore(f"sp_{k}"))

    def emit(self, eng, h):
        waited = {}
        for i, op in enumerate(self.ops[eng]):
            wants = {}
            for (de, di) in op["deps"]:
                key, v = self.incval[(de, di)]
                wants[key] = max(wants.get(key, 0), v)
            for key in sorted(wants):
                v = wants[key]
                if v > waited.get(key, 0):
                    h.wait_ge(self.sems[key], v)
                    waited[key] = v
            last = op["fn"](h)
            if op["marked"]:
                assert last is not None, f"op {eng}:{i} ({op['tag']}) returned None"
                key = op["group"] if op["dma"] else eng
                last.then_inc(self.sems[key], 16 if op["dma"] else 1)

    def emit_all(self, block, final_engine="gpsimd", final_waits=()):
        def _noop(h):
            return h.nop(nofuse=True)
        self.add(final_engine, _noop, reads=list(final_waits), tag="finalwait")
        self.finalize()

        def mk(e):
            def _(h):
                self.emit(e, h)
            return _
        block.sync(mk("sync"))
        block.gpsimd(mk("gpsimd"))
        block.vector(mk("vector"))
        block.scalar(mk("scalar"))
        block.tensor(mk("tensor"))

    def emit_with_sems(self, stack, block, final_engine="gpsimd", final_waits=()):
        def _noop(h):
            return h.nop(nofuse=True)
        self.add(final_engine, _noop, reads=list(final_waits), tag="finalwait")
        self.finalize()
        self.make_sems(stack)

        def mk(e):
            def _(h):
                self.emit(e, h)
            return _
        block.sync(mk("sync"))
        block.gpsimd(mk("gpsimd"))
        block.vector(mk("vector"))
        block.scalar(mk("scalar"))
        block.tensor(mk("tensor"))


def _f32(x):
    return np.ascontiguousarray(x, dtype=np.float32)



LBS = 256
NBLK = S // LBS


def build_A():
    nc = bass.Bass("TRN2", target_bir_lowering=False)
    id64c = nc.inline_tensor(np.vstack([np.eye(64), np.eye(64)]).astype(bf16), name="identity64a")
    x0T_in = nc.dram_tensor("x0T_in", [128, KC, T], F32, kind="ExternalInput").ap()
    wqkv_in = nc.dram_tensor("wqkv_in", [128, KC, 768], F32, kind="ExternalInput").ap()
    wgate_in = nc.dram_tensor("wgate_in", [128, KC, 256], F32, kind="ExternalInput").ap()
    wout_in = nc.dram_tensor("wout_in", [128, 2, DM], F32, kind="ExternalInput").ap()
    normw_in = nc.dram_tensor("normw_in", [128, 2], F32, kind="ExternalInput").ap()
    ddT_in = nc.dram_tensor("ddT_in", [128, 4, 2, LBS], F32, kind="ExternalInput").ap()
    qd_in = nc.dram_tensor("qd_in", [128, 4, LBS], F32, kind="ExternalInput").ap()
    kd_in = nc.dram_tensor("kd_in", [128, 8], F32, kind="ExternalInput").ap()
    bd_in = nc.dram_tensor("bd_in", [64, 4], F32, kind="ExternalInput").ap()
    partial_out = nc.dram_tensor("partial_out", [128, KC, T], F32, kind="ExternalOutput").ap()
    sumsq_out = nc.dram_tensor("sumsq_out", [1, T], F32, kind="ExternalOutput").ap()

    TTA = 256
    NPT = T // TTA

    import contextlib
    with contextlib.ExitStack() as st:
        xtb = st.enter_context(nc.sbuf_tensor("xtb", [128, 2 * KC * TTA], BF16))
        wqkv = st.enter_context(nc.sbuf_tensor("wqkv", [128, KC * 768], BF16))
        wgate = st.enter_context(nc.sbuf_tensor("wgate", [128, KC * 256], BF16))
        woutw = st.enter_context(nc.sbuf_tensor("woutw", [128, 2 * DM], BF16))
        normw = st.enter_context(nc.sbuf_tensor("normw", [128, 2], F32))
        Q = st.enter_context(nc.sbuf_tensor("Q", [128, 2 * T], BF16))
        Kt = st.enter_context(nc.sbuf_tensor("Kt", [128, 2 * T], BF16))
        V = st.enter_context(nc.sbuf_tensor("V", [128, (T // 128) * 256], BF16))
        ga = st.enter_context(nc.sbuf_tensor("ga", [128, 2 * T], BF16))

        kd = st.enter_context(nc.sbuf_tensor("kd", [128, 8], F32))
        bd = st.enter_context(nc.sbuf_tensor("bd", [64, 4], F32))
        id64 = st.enter_context(nc.sbuf_tensor("id64", [128, 64], BF16))
        onesf = st.enter_context(nc.sbuf_tensor("onesf", [64, 1], F32))
        ktm = st.enter_context(nc.sbuf_tensor("ktm", [128, 8 * 64], BF16))
        states = st.enter_context(nc.sbuf_tensor("states", [64, 8 * 64], F32))
        statesbf = st.enter_context(nc.sbuf_tensor("statesbf", [64, 8 * 64], BF16))
        sqscr = st.enter_context(nc.sbuf_tensor("sqscr", [64, 4 * LBS], F32))
        ssqrow = st.enter_context(nc.sbuf_tensor("ssqrow", [1, 2 * LBS], F32))
        ostage = st.enter_context(nc.sbuf_tensor("ostage", [128, 8 * LBS], F32))
        ddT = wqkv[:, 0:8 * LBS]
        qd = wqkv[:, 8 * LBS:12 * LBS]
        awm = wgate[:, 0:8 * LBS]
        qp = wgate[0:64, 8 * LBS:12 * LBS]
        pool = st.enter_context(nc.psum_tensor("pool", [128, 3584], F32))
        pbf = st.enter_context(nc.psum_tensor("pbf", [128, 1024], BF16))

        p = P(nc)

        def reg(off, width, parts=128):
            return pool[0:parts, off:off + width], psbk(off, width)

        def aset(s):
            return dict(
                qk=[reg(s * 1024 + i * 256, TTA) for i in range(4)],
                v=[reg(3072 + i * 256, TTA) for i in range(2)],
                gt=[reg(2048 + s * 512 + i * 256, TTA) for i in range(2)])
        asets = [aset(0), aset(1)]
        ps_aw = [reg(0, LBS), reg(512, LBS)]
        ps_att = reg(1024, LBS, parts=64)
        ps_state = reg(1536, 64, parts=64)
        ps_sq = reg(2048, LBS, parts=1)
        ps_out = [reg(2560, LBS), reg(3072, LBS)]
        k_pbf = ["psB7"]

        p.add("vector", lambda h: h.memset(onesf[:], 1.0), writes=["onesf"])
        p.add("sync", lambda h: h.dma_start(out=id64[:], in_=id64c.ap()),
              writes=["id64"], dma=True, group="gc")
        p.add("sync", lambda h: h.dma_start(out=kd[:], in_=kd_in),
              writes=["kd"], dma=True, group="gc")
        p.add("sync", lambda h: h.dma_start(out=bd[:], in_=bd_in),
              writes=["bd"], dma=True, group="gc")
        p.add("sync", lambda h: h.dma_start(out=normw[:], in_=normw_in),
              writes=["normw"], dma=True, group="gc")
        p.add("gpsimd", lambda h: h.dma_start(
            out=woutw[:].rearrange("p (c d) -> p c d", c=2), in_=wout_in),
            writes=["woutw"], dma=True, group="gc2")
        p.add("gpsimd", lambda h: h.dma_start(
            out=wqkv[:].rearrange("p (c m) -> p c m", m=768), in_=wqkv_in),
            writes=["wqkv"], dma=True, group="gw")
        p.add("gpsimd", lambda h: h.dma_start(
            out=wgate[:].rearrange("p (c m) -> p c m", m=256), in_=wgate_in),
            writes=["wgate"], dma=True, group="gw")
        for c2 in range(2):
            p.add("scalar", (lambda c2=c2: lambda h: h.activation(
                woutw[:, c2 * DM:(c2 + 1) * DM], woutw[:, c2 * DM:(c2 + 1) * DM],
                AF.Copy, scale=normw[:, c2:c2 + 1]))(),
                reads=["woutw", "normw"], writes=["woutw"], tag=f"woscale{c2}")

        for tt in range(NPT):
            par = tt % 2
            t0 = tt * TTA
            xv = lambda par=par: xtb[:, par * KC * TTA:(par + 1) * KC * TTA]
            p.add("gpsimd", (lambda tt=tt, par=par: lambda h: h.dma_start(
                out=xtb[:, par * KC * TTA:(par + 1) * KC * TTA]
                    .rearrange("p (c t) -> p c t", c=KC),
                in_=x0T_in[:, :, tt * TTA:(tt + 1) * TTA]))(),
                reads=[f"xtu{par}"], writes=[f"xtb{par}"], dma=True, group=f"gx{par}",
                tag=f"xld{tt}")

            ps_qk = asets[par]["qk"]
            ps_v = asets[par]["v"]
            ps_gt = asets[par]["gt"]
            for mi in range(4):
                def mm_qk(tt=tt, par=par, mi=mi, ps_qk=ps_qk):
                    def g(h):
                        last = None
                        for kc in range(KC):
                            last = h.matmul(
                                ps_qk[mi][0],
                                wqkv[:, kc * 768 + mi * 128: kc * 768 + (mi + 1) * 128],
                                xtb[:, par * KC * TTA + kc * TTA:
                                    par * KC * TTA + (kc + 1) * TTA],
                                start=(kc == 0), stop=(kc == KC - 1))
                        return last
                    return g
                p.add("tensor", mm_qk(), reads=["wqkv", f"xtb{par}"] + ps_qk[mi][1],
                      writes=ps_qk[mi][1], tag=f"mmqk{tt}_{mi}")
                dst = Q if mi < 2 else Kt
                dsn = "Q" if mi < 2 else "Kt"
                qi = mi % 2
                p.add("scalar", (lambda dst=dst, qi=qi, t0=t0, mi=mi, ps_qk=ps_qk:
                      lambda h: h.activation(dst[:, qi * T + t0: qi * T + t0 + TTA],
                                             ps_qk[mi][0], AF.Silu))(),
                      reads=ps_qk[mi][1], writes=[dsn], tag=f"evqk{tt}_{mi}")

            for ts in range(2):
                def mm_v(tt=tt, par=par, ts=ts, ps_v=ps_v):
                    def g(h):
                        last = None
                        for kc in range(KC):
                            last = h.matmul(
                                ps_v[ts][0],
                                xtb[:, par * KC * TTA + kc * TTA + ts * 128:
                                    par * KC * TTA + kc * TTA + (ts + 1) * 128],
                                wqkv[:, kc * 768 + 512: kc * 768 + 768],
                                start=(kc == 0), stop=(kc == KC - 1))
                        return last
                    return g
                p.add("tensor", mm_v(), reads=["wqkv", f"xtb{par}"] + ps_v[ts][1],
                      writes=ps_v[ts][1], tag=f"mmv{tt}_{ts}")
                ch = (t0 + ts * 128) // 128
                p.add("scalar", (lambda ch=ch, ts=ts, ps_v=ps_v: lambda h: h.activation(
                    V[:, ch * 256:(ch + 1) * 256], ps_v[ts][0], AF.Silu))(),
                    reads=ps_v[ts][1], writes=["V"], tag=f"evv{tt}_{ts}")

            for gi in range(2):
                def mm_gt(tt=tt, par=par, gi=gi, ps_gt=ps_gt):
                    def g(h):
                        last = None
                        for kc in range(KC):
                            last = h.matmul(
                                ps_gt[gi][0],
                                wgate[:, kc * 256 + gi * 128: kc * 256 + (gi + 1) * 128],
                                xtb[:, par * KC * TTA + kc * TTA:
                                    par * KC * TTA + (kc + 1) * TTA],
                                start=(kc == 0), stop=(kc == KC - 1))
                        return last
                    return g
                p.add("tensor", mm_gt(), reads=["wgate", f"xtb{par}"] + ps_gt[gi][1],
                      writes=ps_gt[gi][1] + ([f"xtu{par}"] if gi == 1 else []),
                      tag=f"mmgt{tt}_{gi}")
                p.add("scalar", (lambda gi=gi, t0=t0, ps_gt=ps_gt: lambda h: h.activation(
                    ga[:, gi * T + t0: gi * T + t0 + TTA], ps_gt[gi][0], AF.Sigmoid))(),
                    reads=ps_gt[gi][1], writes=["gate"], tag=f"evgt{tt}_{gi}")


        ps_aw = [[reg(i * 512 + jc * 256, LBS) for jc in range(2)] for i in range(4)]
        ps_att = [reg(2048 + i * 256, LBS, parts=64) for i in range(4)]
        ps_state = [reg(3072 + i * 64, 64, parts=64) for i in range(4)]
        ps_sq = reg(3328, LBS, parts=1)
        ps_out = [reg((oc % 4) * 512 + 256 * 0, LBS) for oc in range(4)]

        p.add("gpsimd", lambda h: h.nop(nofuse=True),
              writes=["wqkv", "wgate", "warena"], tag="a_arena_sw")
        p.add("gpsimd", lambda h: h.dma_start(
            out=ddT.rearrange("p (i jc l) -> p i jc l", i=4, jc=2), in_=ddT_in),
            reads=["warena"], writes=["ddT"], dma=True, group="gcc")
        p.add("gpsimd", lambda h: h.dma_start(
            out=qd.rearrange("p (i l) -> p i l", i=4), in_=qd_in),
            reads=["warena"], writes=["qd"], dma=True, group="gcc")

        for b2 in range(2):
            for blk in range(NBLK):
                bt0 = b2 * S + blk * LBS
                gch = bt0 // 128
                gt = b2 * NBLK + blk

                for i in range(4):
                    qi, pr = i // 2, i % 2
                    rows = slice(pr * 64, pr * 64 + 64)
                    for jc in range(2):
                        p.add("tensor", (lambda qi=qi, rows=rows, bt0=bt0, jc=jc, i=i:
                              lambda h: h.matmul(
                                  ps_aw[i][jc][0],
                                  Kt[rows, qi * T + bt0 + jc * 128:
                                     qi * T + bt0 + (jc + 1) * 128],
                                  Q[rows, qi * T + bt0: qi * T + bt0 + LBS],
                                  start=True, stop=True))(),
                              reads=["Q", "Kt"] + ps_aw[i][jc][1],
                              writes=ps_aw[i][jc][1],
                              tag=f"aw{b2}_{blk}_{i}_{jc}")
                for i in range(4):
                    qi, pr = i // 2, i % 2
                    rows = slice(pr * 64, pr * 64 + 64)
                    for jc in range(2):
                        p.add("vector", (lambda i=i, jc=jc: lambda h: h.tensor_tensor(
                            awm[:, (i * 2 + jc) * LBS:(i * 2 + jc + 1) * LBS],
                            ps_aw[i][jc][0],
                            ddT[:, (i * 2 + jc) * LBS:(i * 2 + jc + 1) * LBS],
                            ALU.mult))(),
                            reads=ps_aw[i][jc][1] + ["ddT", "warena", f"awmu{i}_{jc}"],
                            writes=[f"awm{i}_{jc}"] + ps_aw[i][jc][1],
                            tag=f"awm{b2}_{blk}_{i}_{jc}")
                    if blk > 0:
                        p.add("vector", (lambda i=i, qi=qi, rows=rows, bt0=bt0:
                              lambda h: h.tensor_tensor(
                                  qp[0:64, i * LBS:(i + 1) * LBS],
                                  Q[rows, qi * T + bt0: qi * T + bt0 + LBS],
                                  qd[rows, i * LBS:(i + 1) * LBS], ALU.mult))(),
                              reads=["Q", "qd", "warena", f"qpu{i}"], writes=[f"qp{i}"],
                              tag=f"qp{b2}_{blk}_{i}")
                for i in range(4):
                    pr = i % 2
                    rows = slice(pr * 64, pr * 64 + 64)
                    sslot = slice((b2 * 4 + i) * 64, (b2 * 4 + i) * 64 + 64)

                    def mm_att(b2=b2, blk=blk, i=i, gch=gch, sslot=sslot, rows=rows):
                        def g(h):
                            last = None
                            for jc in range(2):
                                last = h.matmul(
                                    ps_att[i][0], V[:, (gch + jc) * 256 + i * 64:
                                                    (gch + jc) * 256 + (i + 1) * 64],
                                    awm[:, (i * 2 + jc) * LBS:(i * 2 + jc + 1) * LBS],
                                    start=(jc == 0), stop=(blk == 0 and jc == 1))
                            if blk > 0:
                                last = h.matmul(
                                    ps_att[i][0], statesbf[:, sslot],
                                    qp[0:64, i * LBS:(i + 1) * LBS],
                                    start=False, stop=True)
                            return last
                        return g
                    rd = ["V", f"awm{i}_0", f"awm{i}_1"] + ps_att[i][1]
                    wr = ps_att[i][1] + [f"awmu{i}_0", f"awmu{i}_1"]
                    if blk > 0:
                        rd += [f"qp{i}", f"stbf{b2}{i}"]
                        wr += [f"qpu{i}"]
                    p.add("tensor", mm_att(), reads=rd, writes=wr,
                          tag=f"att{b2}_{blk}_{i}")
                for i in range(4):
                    qi, pr = i // 2, i % 2
                    rows = slice(pr * 64, pr * 64 + 64)
                    for jc in range(2):
                        p.add("tensor", (lambda qi=qi, rows=rows, bt0=bt0, jc=jc, i=i:
                              lambda h: h.transpose(
                                  pbf[:, (i * 2 + jc) * 64:(i * 2 + jc + 1) * 64],
                                  Kt[rows, qi * T + bt0 + jc * 128:
                                     qi * T + bt0 + (jc + 1) * 128], id64[rows, :]))(),
                              reads=["Kt", "id64"] + k_pbf, writes=k_pbf,
                              tag=f"ktr{b2}_{blk}_{i}_{jc}")
                for i in range(4):
                    for jc in range(2):
                        p.add("scalar", (lambda i=i, jc=jc: lambda h: h.activation(
                            ktm[:, (i * 2 + jc) * 64:(i * 2 + jc + 1) * 64],
                            pbf[:, (i * 2 + jc) * 64:(i * 2 + jc + 1) * 64],
                            AF.Copy, scale=kd[:, i * 2 + jc: i * 2 + jc + 1]))(),
                            reads=k_pbf + ["kd", f"ktmu{i}_{jc}"],
                            writes=[f"ktm{i}_{jc}"] + k_pbf,
                            tag=f"ktev{b2}_{blk}_{i}_{jc}")
                for i in range(4):
                    def mm_st(i=i, gch=gch):
                        def g(h):
                            last = None
                            for jc in range(2):
                                last = h.matmul(
                                    ps_state[i][0],
                                    ktm[:, (i * 2 + jc) * 64:(i * 2 + jc + 1) * 64],
                                    V[:, (gch + jc) * 256 + i * 64:
                                      (gch + jc) * 256 + (i + 1) * 64],
                                    start=(jc == 0), stop=(jc == 1))
                            return last
                        return g
                    p.add("tensor", mm_st(), reads=[f"ktm{i}_0", f"ktm{i}_1", "V"]
                          + ps_state[i][1],
                          writes=ps_state[i][1] + [f"ktmu{i}_0", f"ktmu{i}_1"],
                          tag=f"stmm{b2}_{blk}_{i}")
                for i in range(4):
                    sslot = slice((b2 * 4 + i) * 64, (b2 * 4 + i) * 64 + 64)
                    if blk == 0:
                        p.add("vector", (lambda sslot=sslot, i=i: lambda h:
                              h.tensor_copy(states[:, sslot], ps_state[i][0]))(),
                              reads=ps_state[i][1], writes=[f"st{b2}{i}"] + ps_state[i][1],
                              tag=f"stini{b2}_{i}")
                    else:
                        p.add("vector", (lambda sslot=sslot, i=i: lambda h:
                              h.scalar_tensor_tensor(
                                  states[:, sslot], states[:, sslot],
                                  bd[:, i:i + 1], ps_state[i][0],
                                  ALU.mult, ALU.add))(),
                              reads=ps_state[i][1] + [f"st{b2}{i}", "bd"],
                              writes=[f"st{b2}{i}"] + ps_state[i][1],
                              tag=f"stup{b2}_{blk}_{i}")
                    p.add("scalar", (lambda sslot=sslot: lambda h: h.copy(
                        statesbf[:, sslot], states[:, sslot]))(),
                        reads=[f"st{b2}{i}"], writes=[f"stbf{b2}{i}"],
                        tag=f"stbf{b2}_{blk}_{i}")
                for i in range(4):
                    p.add("scalar", (lambda i=i: lambda h: h.activation(
                        sqscr[:, i * LBS:(i + 1) * LBS], ps_att[i][0], AF.Square))(),
                        reads=ps_att[i][1] + [f"sqsu{i}"], writes=[f"sqscr{i}"],
                        tag=f"sq{b2}_{blk}_{i}")
                for i in range(4):
                    p.add("tensor", (lambda i=i: lambda h: h.matmul(
                        ps_sq[0], onesf[:, 0:1], sqscr[:, i * LBS:(i + 1) * LBS],
                        start=(i == 0), stop=(i == 3)))(),
                        reads=[f"sqscr{i}", "onesf"] + ps_sq[1],
                        writes=ps_sq[1] + [f"sqsu{i}"],
                        tag=f"sqmm{b2}_{blk}_{i}")
                for i in range(4):
                    qi, pr = i // 2, i % 2
                    rows = slice(pr * 64, pr * 64 + 64)
                    p.add("vector", (lambda qi=qi, rows=rows, bt0=bt0, i=i: lambda h:
                          h.tensor_tensor(
                              ga[rows, qi * T + bt0: qi * T + bt0 + LBS],
                              ps_att[i][0],
                              ga[rows, qi * T + bt0: qi * T + bt0 + LBS],
                              ALU.mult))(),
                          reads=ps_att[i][1] + [f"ga{gt}", f"sqscr{i}"],
                          writes=[f"ga{gt}"] + ps_att[i][1],
                          tag=f"gmul{b2}_{blk}_{i}")
                sqp = blk % 2
                p.add("scalar", (lambda sqp=sqp: lambda h: h.activation(
                    ssqrow[0:1, sqp * LBS:(sqp + 1) * LBS], ps_sq[0], AF.Copy))(),
                    reads=ps_sq[1] + [f"ssqu{sqp}"], writes=[f"ssqrow{sqp}"] + ps_sq[1],
                    tag=f"ssqev{b2}_{blk}")
                p.add("gpsimd", (lambda bt0=bt0, sqp=sqp: lambda h: h.dma_start(
                    out=sumsq_out[0:1, bt0:bt0 + LBS],
                    in_=ssqrow[0:1, sqp * LBS:(sqp + 1) * LBS]))(),
                    reads=[f"ssqrow{sqp}"], writes=[f"ssqu{sqp}", "ssqout"],
                    dma=True, group="gsq", tag=f"ssqst{b2}_{blk}")

                stp = blk % 2
                for oc in range(KC):
                    opar = oc % 4
                    grp = oc // 4

                    def mm_o(oc=oc, opar=opar, bt0=bt0):
                        def g(h):
                            last = None
                            for c2 in range(2):
                                last = h.matmul(
                                    ps_aw[opar][0][0],
                                    woutw[:, c2 * DM + oc * 128: c2 * DM + (oc + 1) * 128],
                                    ga[:, c2 * T + bt0: c2 * T + bt0 + LBS],
                                    start=(c2 == 0), stop=(c2 == 1))
                            return last
                        return g
                    p.add("tensor", mm_o(), reads=["woutw", f"ga{gt}"]
                          + ps_aw[opar][0][1] + ps_aw[opar][1][1],
                          writes=ps_aw[opar][0][1] + ps_aw[opar][1][1],
                          tag=f"mmo{b2}_{blk}_{oc}")
                    p.add("scalar", (lambda oc=oc, opar=opar, stp=stp: lambda h:
                          h.activation(ostage[:, (stp * 4 + oc % 4) * LBS:
                                              (stp * 4 + oc % 4 + 1) * LBS],
                                       ps_aw[opar][0][0], AF.Copy))(),
                          reads=ps_aw[opar][0][1] + ps_aw[opar][1][1]
                          + [f"osu{stp}_{grp}"],
                          writes=[f"ost{stp}_{oc % 4}"] + ps_aw[opar][0][1]
                          + ps_aw[opar][1][1],
                          tag=f"oev{b2}_{blk}_{oc}")
                    if oc % 4 == 3:
                        p.add("gpsimd", (lambda bt0=bt0, stp=stp, grp=grp: lambda h:
                              h.dma_start(
                                  out=partial_out[:, grp * 4:(grp + 1) * 4,
                                                  bt0:bt0 + LBS],
                                  in_=ostage[:, stp * 4 * LBS:(stp + 1) * 4 * LBS]
                                      .rearrange("p (c t) -> p c t", c=4)))(),
                              reads=[f"ost{stp}_{g4}" for g4 in range(4)],
                              writes=[f"osu{stp}_{grp}", "pout"]
                              + [f"ost{stp}_{g4}" for g4 in range(4)],
                              dma=True, group=f"go{stp}", tag=f"ost{b2}_{blk}_{grp}")

        with nc.Block() as block:
            p.emit_with_sems(st, block, final_engine="gpsimd",
                             final_waits=["pout", "ssqu"])
    return nc


def lightning_consts(heads):
    Hh = 32
    base = 1.0 / 2.0 ** (8.0 / Hh)
    rate = np.float32(base) ** (np.arange(Hh, dtype=np.float32) + 1.0)
    factor = np.float32(1.0 - 0.0 / (32 - 1 + 1e-5) + 1e-5)
    slope = (rate * factor).astype(np.float32)
    r = np.arange(LBS, dtype=np.float32) + 1.0
    ddT = np.zeros((128, 4, 2, LBS), np.float32)
    qdv = np.zeros((128, 4, LBS), np.float32)
    kdv = np.zeros((128, 8), np.float32)
    bdv = np.zeros((64, 4), np.float32)
    for ii, h in enumerate(heads):
        s = slope[h]
        qdv[:, ii, :] = np.exp(-s * r)[None, :]
        dd = r[:, None] - r[None, :]
        ddm = np.where(dd >= 0, np.exp(-s * dd), 0.0).astype(np.float32)
        ddTm = ddm.T
        for jc in range(2):
            ddT[:, ii, jc, :] = ddTm[jc * 128:(jc + 1) * 128, :]
        kdfull = np.exp(-s * (LBS - r)).astype(np.float32)
        for jc in range(2):
            kdv[:, ii * 2 + jc] = kdfull[jc * 128:(jc + 1) * 128]
        bdv[:, ii] = np.exp(np.float32(-s * LBS))
    return ddT, qdv, kdv, bdv


def host_A(x0, qkv_w, gate_w, out_w, norm_w):
    x0T = np.ascontiguousarray(
        x0.reshape(T, DM).T.reshape(KC, 128, T).transpose(1, 0, 2), dtype=np.float32)
    maps = []
    for c in range(NCORES):
        heads = [4 * c + i for i in range(4)]
        cols = []
        for grp in range(3):
            for i in range(4):
                h0 = heads[i] * 192 + grp * 64
                cols.extend(range(h0, h0 + 64))
        wq = qkv_w[cols, :]
        wqkv = np.ascontiguousarray(
            wq.T.reshape(KC, 128, 768).transpose(1, 0, 2), dtype=np.float32)
        gsl = gate_w[256 * c:256 * c + 256, :]
        wgate = np.ascontiguousarray(
            gsl.T.reshape(KC, 128, 256).transpose(1, 0, 2), dtype=np.float32)
        osl = out_w[:, 256 * c:256 * c + 256]
        wout = np.ascontiguousarray(
            osl.T.reshape(2, 128, DM).transpose(1, 0, 2), dtype=np.float32)
        nsl = np.ascontiguousarray(
            norm_w[256 * c:256 * c + 256].reshape(2, 128).T, dtype=np.float32)
        ddT, qdv, kdv, bdv = lightning_consts(heads)
        maps.append({"x0T_in": x0T, "wqkv_in": wqkv, "wgate_in": wgate,
                     "wout_in": wout, "normw_in": nsl, "ddT_in": ddT,
                     "qd_in": qdv, "kd_in": kdv, "bd_in": bdv})
    return maps


def run_A(x0, qkv_w, gate_w, out_w, norm_w):
    nc = build_A()
    res = run_bass_kernel_spmd(nc, host_A(x0, qkv_w, gate_w, out_w, norm_w),
                               core_ids=list(range(NCORES)))
    partials = [r["partial_out"] for r in res.results]
    sumsqs = [r["sumsq_out"][0] for r in res.results]
    return partials, sumsqs



PSBANK = 512


def psbk(off, width):
    return [f"psB{k}" for k in range(off // PSBANK, (off + width - 1) // PSBANK + 1)]


COMMON_ROWSLOTS = ("MEAN", "VAR", "AROW", "BROW", "ZERO", "SCR1")


def make_common(nc, st, p, pool, NT=TPC, offs=(0, 1024, 2048, 2560)):
    HT = NT // 2
    ct = dict(NT=NT, HT=HT)
    ct["ones"] = st.enter_context(nc.sbuf_tensor("c_ones", [128, 1], F32))
    ct["onesrow"] = st.enter_context(nc.sbuf_tensor("c_onesrow", [1, 128], F32))
    ct["eps"] = st.enter_context(nc.sbuf_tensor("c_eps", [1, 1], F32))
    ct["rows"] = st.enter_context(nc.sbuf_tensor("c_rows", [1, NT * len(COMMON_ROWSLOTS)], F32))
    ct["sqscr"] = st.enter_context(nc.sbuf_tensor("c_sqscr", [128, 2 * NT], F32))
    o_r, o_r2, o_a, o_b = offs
    ct["ps_row"] = pool[0:1, o_r:o_r + NT]
    ct["ps_row2"] = pool[0:1, o_r2:o_r2 + NT]
    ct["ps_a"] = pool[:, o_a:o_a + HT]
    ct["ps_b"] = pool[:, o_b:o_b + HT]
    ct["k_row"] = psbk(o_r, NT)
    ct["k_row2"] = psbk(o_r2, NT)
    ct["k_a"] = psbk(o_a, HT)
    ct["k_b"] = psbk(o_b, HT)
    for i, nm in enumerate(COMMON_ROWSLOTS):
        ct[nm] = i * NT
    p.add("vector", lambda h: h.memset(ct["ones"][:], 1.0), writes=["c_ones"])
    p.add("vector", lambda h: h.memset(ct["onesrow"][:], 1.0), writes=["c_onesrow"])
    p.add("vector", lambda h: h.memset(ct["eps"][:], 1e-5), writes=["c_eps"])
    p.add("vector", lambda h: h.memset(ct["rows"][0:1, ct["ZERO"] * 1: ct["ZERO"] + NT], 0.0),
          writes=["c_zero"])

    def row(slot, lo=0, hi=None):
        off = ct[slot]
        return ct["rows"][0:1, off + lo: off + (NT if hi is None else hi)]
    ct["row"] = row
    return ct


def add_ln_fm(p, ct, acc, acc_res, tag=""):
    NT, HT = ct["NT"], ct["HT"]
    row, ones, onesrow, eps = ct["row"], ct["ones"], ct["onesrow"], ct["eps"]
    sqscr, ps_row, ps_row2, ps_a, ps_b = (
        ct["sqscr"], ct["ps_row"], ct["ps_row2"], ct["ps_a"], ct["ps_b"])
    k_row, k_row2, k_a, k_b = ct["k_row"], ct["k_row2"], ct["k_a"], ct["k_b"]

    def mm_sum(h):
        last = None
        for c in range(KC):
            for hh in range(2):
                last = h.matmul(
                    ps_row[0:1, hh * HT:(hh + 1) * HT], ones[:, 0:1],
                    acc[:, c * NT + hh * HT: c * NT + (hh + 1) * HT],
                    start=(c == 0), stop=(c == KC - 1))
        return last
    p.add("tensor", mm_sum, reads=list(acc_res) + ["c_ones"], writes=k_row,
          tag=f"lnsum{tag}")

    for c in range(KC):
        buf = c % 2
        p.add("scalar",
              (lambda c=c, buf=buf: lambda h: h.activation(
                  sqscr[:, buf * NT:(buf + 1) * NT],
                  acc[:, c * NT:(c + 1) * NT], AF.Square))(),
              reads=list(acc_res), writes=[f"c_sqscr{buf}"], tag=f"sq{tag}{c}")

        def mmq(c=c, buf=buf):
            def g(h):
                last = None
                for hh in range(2):
                    last = h.matmul(
                        ps_row2[0:1, hh * HT:(hh + 1) * HT], ones[:, 0:1],
                        sqscr[:, buf * NT + hh * HT: buf * NT + (hh + 1) * HT],
                        start=(c == 0), stop=(c == KC - 1))
                return last
            return g
        p.add("tensor", mmq(), reads=[f"c_sqscr{buf}", "c_ones"],
              writes=k_row2, tag=f"mmq{tag}{c}")

    p.add("scalar", lambda h: h.activation(
        row("MEAN"), ps_row[0:1, :], AF.Copy, scale=1.0 / DM),
        reads=k_row, writes=["c_rMEAN"])
    p.add("scalar", lambda h: h.activation(
        row("VAR"), ps_row2[0:1, :], AF.Copy, scale=1.0 / DM),
        reads=k_row2, writes=["c_rVAR"])
    p.add("scalar", lambda h: h.activation(row("SCR1"), row("MEAN"), AF.Square),
          reads=["c_rMEAN"], writes=["c_rSCR1"])
    p.add("vector", lambda h: h.tensor_tensor(
        row("VAR"), row("VAR"), row("SCR1"), ALU.subtract),
        reads=["c_rVAR", "c_rSCR1"], writes=["c_rVAR"])
    p.add("scalar", lambda h: h.activation(
        row("VAR"), row("VAR"), AF.Sqrt, bias=eps[0:1, 0:1]),
        reads=["c_rVAR", "c_eps"], writes=["c_rVAR"])
    p.add("vector", lambda h: h.reciprocal(row("AROW"), row("VAR")),
          reads=["c_rVAR"], writes=["c_rAROW"])
    p.add("vector", lambda h: h.tensor_tensor(
        row("BROW"), row("MEAN"), row("AROW"), ALU.mult),
        reads=["c_rMEAN", "c_rAROW"], writes=["c_rBROW"])
    p.add("vector", lambda h: h.tensor_tensor(
        row("BROW"), row("ZERO"), row("BROW"), ALU.subtract),
        reads=["c_rBROW", "c_zero"], writes=["c_rBROW"])

    for hh in range(2):
        lo = hh * HT
        p.add("tensor",
              (lambda lo=lo: lambda h: h.matmul(
                  ps_a[:], onesrow[:], row("AROW", lo, lo + HT),
                  start=True, stop=True))(),
              reads=["c_rAROW", "c_onesrow"], writes=k_a, tag=f"bca{tag}{hh}")
        p.add("tensor",
              (lambda lo=lo: lambda h: h.matmul(
                  ps_b[:], onesrow[:], row("BROW", lo, lo + HT),
                  start=True, stop=True))(),
              reads=["c_rBROW", "c_onesrow"], writes=k_b, tag=f"bcb{tag}{hh}")
        for c in range(KC):
            p.add("vector",
                  (lambda c=c, lo=lo: lambda h: h.tensor_tensor(
                      acc[:, c * NT + lo: c * NT + lo + HT],
                      acc[:, c * NT + lo: c * NT + lo + HT],
                      ps_a[:], ALU.mult))(),
                  reads=list(acc_res) + k_a, writes=list(acc_res),
                  tag=f"lnm{tag}{hh}_{c}")
            p.add("vector",
                  (lambda c=c, lo=lo: lambda h: h.tensor_tensor(
                      acc[:, c * NT + lo: c * NT + lo + HT],
                      acc[:, c * NT + lo: c * NT + lo + HT],
                      ps_b[:], ALU.add))(),
                  reads=list(acc_res) + k_b, writes=list(acc_res),
                  tag=f"lna{tag}{hh}_{c}")


def build_B(upto=99):
    nc = bass.Bass("TRN2", target_bir_lowering=False)
    p8 = nc.dram_tensor("p8", [8, 128, KC, TPC], F32, kind="ExternalInput").ap()
    sq8 = nc.dram_tensor("sq8", [8, TPC], F32, kind="ExternalInput").ap()
    x0c = nc.dram_tensor("x0c", [128, KC, TPC], F32, kind="ExternalInput").ap()
    x1_out = nc.dram_tensor("x1_out", [128, KC, TPC], F32, kind="ExternalOutput").ap()
    srp_out = nc.dram_tensor("srp_out", [128, KC], F32, kind="ExternalOutput").ap()

    HT = TPC // 2

    import contextlib
    with contextlib.ExitStack() as st:
        acc = st.enter_context(nc.sbuf_tensor("acc", [128, KC * TPC], F32))
        x0 = st.enter_context(nc.sbuf_tensor("x0", [128, KC * TPC], F32))
        sqs = st.enter_context(nc.sbuf_tensor("sq", [8, TPC], F32))
        srp = st.enter_context(nc.sbuf_tensor("srp", [128, KC], F32))
        pool = st.enter_context(nc.psum_tensor("pool", [128, 4096], F32))

        p = P(nc)
        ct = make_common(nc, st, p, pool, NT=TPC)
        row, ones, onesrow, eps = ct["row"], ct["ones"], ct["onesrow"], ct["eps"]
        ps_row, ps_a = ct["ps_row"], ct["ps_a"]
        k_row, k_a = ct["k_row"], ct["k_a"]
        ACCV = lambda: acc[:].rearrange("p (c t) -> p c t", c=KC)

        p.add("vector", lambda h: h.memset(srp[:], 0.0), writes=["srp"])

        p.add("sync", lambda h: h.dma_start(
            out=x0[:].rearrange("p (c t) -> p c t", c=KC), in_=x0c),
            writes=["x0"], dma=True, group="gx0")
        p.add("sync", lambda h: h.dma_start(out=sqs[:], in_=sq8),
              writes=["sq"], dma=True, group="gsq")

        NSL = (KC * TPC) // 2048
        for s in range(NSL):
            sl_out = acc[:, s * 2048:(s + 1) * 2048]
            p.add("gpsimd", (lambda s=s, sl_out=sl_out: lambda h: h.dma_start(
                out=sl_out, in_=p8[0, :, 2 * s:2 * s + 2, :]))(),
                writes=[f"acc{s}"], dma=True, group=f"gacc{s}")
            for j in range(1, 8):
                p.add("gpsimd", (lambda j=j, s=s, sl_out=sl_out: lambda h: h.dma_start(
                    out=sl_out, in_=p8[j, :, 2 * s:2 * s + 2, :], accum_op=ALU.add))(),
                    reads=[f"acc{s}"], writes=[f"acc{s}"], dma=True, group=f"gacc{s}")
        ACC_RES = [f"acc{s}" for s in range(NSL)]

        if upto >= 2:
            def mm_sqsum(h):
                h.matmul(ps_row[0:1, 0:HT], ones[0:8, 0:1], sqs[:, 0:HT],
                         start=True, stop=True)
                return h.matmul(ps_row[0:1, HT:TPC], ones[0:8, 0:1], sqs[:, HT:TPC],
                                start=True, stop=True)
            p.add("tensor", mm_sqsum, reads=["sq", "c_ones"], writes=k_row)
            p.add("scalar", lambda h: h.activation(
                row("VAR"), ps_row[0:1, :], AF.Sqrt, bias=eps[0:1, 0:1], scale=1.0 / DM),
                reads=k_row + ["c_eps"], writes=["c_rVAR"])
            p.add("vector", lambda h: h.reciprocal(row("MEAN"), row("VAR")),
                  reads=["c_rVAR"], writes=["c_rRSTD1"])

            for hh in range(2):
                lo = hh * HT
                p.add("tensor",
                      (lambda lo=lo: lambda h: h.matmul(
                          ps_a[:], onesrow[:], row("MEAN", lo, lo + HT),
                          start=True, stop=True))(),
                      reads=["c_rRSTD1", "c_onesrow"], writes=k_a)
                for c in range(KC):
                    p.add("vector",
                          (lambda c=c, lo=lo: lambda h: h.tensor_tensor(
                              acc[:, c * TPC + lo: c * TPC + lo + HT],
                              acc[:, c * TPC + lo: c * TPC + lo + HT],
                              ps_a[:], ALU.mult))(),
                          reads=ACC_RES + k_a, writes=ACC_RES, tag=f"sc{hh}_{c}")
                    p.add("vector",
                          (lambda c=c, lo=lo: lambda h: h.tensor_tensor(
                              acc[:, c * TPC + lo: c * TPC + lo + HT],
                              acc[:, c * TPC + lo: c * TPC + lo + HT],
                              x0[:, c * TPC + lo: c * TPC + lo + HT], ALU.add))(),
                          reads=ACC_RES + ["x0"], writes=ACC_RES, tag=f"ax{hh}_{c}")

        if upto >= 3:
            add_ln_fm(p, ct, acc, ACC_RES, tag="B")

        if upto >= 4:
            def srp_f(h):
                last = None
                for c in range(KC):
                    last = h.reduce_sum(srp[:, c:c + 1],
                                        acc[:, c * TPC:(c + 1) * TPC], axis=AX.X)
                return last
            p.add("vector", srp_f, reads=ACC_RES, writes=["srp"])

        p.add("gpsimd", lambda h: h.dma_start(out=x1_out, in_=ACCV()),
              reads=ACC_RES, writes=["out1"], dma=True, group="gout")
        p.add("gpsimd", lambda h: h.dma_start(out=srp_out, in_=srp[:]),
              reads=["srp"], writes=["out2"], dma=True, group="gout")

        with nc.Block() as block:
            p.emit_with_sems(st, block, final_engine="gpsimd",
                             final_waits=["out1", "out2"])
    return nc



def build_C():
    nc = bass.Bass("TRN2", target_bir_lowering=False)
    ident64 = nc.inline_tensor(np.eye(64, dtype=bf16), name="identity64c")
    x1_in = nc.dram_tensor("x1_in", [128, KC, TPC], F32, kind="ExternalInput").ap()
    srp4 = nc.dram_tensor("srp4", [128, 4 * KC], F32, kind="ExternalInput").ap()
    simm = nc.dram_tensor("simm", [128, KC, 32], F32, kind="ExternalInput").ap()
    gates_in = nc.dram_tensor("gates_in", [1, 32], F32, kind="ExternalInput").ap()
    wv_in = nc.dram_tensor("wv_in", [32, 64, DM], F32, kind="ExternalInput").ap()
    ow_in = nc.dram_tensor("ow_in", [32, 64, DM], F32, kind="ExternalInput").ap()
    x2_out = nc.dram_tensor("x2_out", [128, KC, TPC], F32, kind="ExternalOutput").ap()
    rt_out = nc.dram_tensor("rt_out", [1, 32], F32, kind="ExternalOutput").ap()

    HT = TPC // 2
    NE = 32

    import contextlib
    with contextlib.ExitStack() as st:
        x1 = st.enter_context(nc.sbuf_tensor("x1", [128, KC * TPC], F32))
        x1b = st.enter_context(nc.sbuf_tensor("x1b", [128, KC * TPC], BF16))
        srp_sb = st.enter_context(nc.sbuf_tensor("srp_sb", [128, 4 * KC], F32))
        s_sb = st.enter_context(nc.sbuf_tensor("s_sb", [128, KC], F32))
        s2_sb = st.enter_context(nc.sbuf_tensor("s2_sb", [128, KC], F32))
        sim_sb = st.enter_context(nc.sbuf_tensor("sim_sb", [128, KC * 32], F32))
        sim2_sb = st.enter_context(nc.sbuf_tensor("sim2_sb", [128, KC * 32], F32))
        id64 = st.enter_context(nc.sbuf_tensor("id64", [64, 64], BF16))
        g_sb = st.enter_context(nc.sbuf_tensor("g_sb", [1, 32 * 12], F32))
        r_bc = st.enter_context(nc.sbuf_tensor("r_bc", [64, 32], F32))
        rIe = st.enter_context(nc.sbuf_tensor("rIe", [64, 2 * 64], BF16))
        wvb = st.enter_context(nc.sbuf_tensor("wvb", [64, 2 * DM], BF16))
        owb = st.enter_context(nc.sbuf_tensor("owb", [64, 2 * DM], BF16))
        weT = st.enter_context(nc.sbuf_tensor("weT", [128, KC * 64], BF16))
        weTacc = st.enter_context(nc.sbuf_tensor("weTacc", [128, KC * 64], F32))
        wod = st.enter_context(nc.sbuf_tensor("wod", [64, DM], BF16))
        comb = st.enter_context(nc.sbuf_tensor("comb", [64, TPC], BF16))
        pool = st.enter_context(nc.psum_tensor("pool", [128, 4096], F32))
        ps_wo = pool[0:64, 1024:1024 + DM]
        k_wo = psbk(1024, DM)
        ps_c = pool[0:64, 3072:3072 + HT]
        k_c = psbk(3072, HT)
        ps_f = pool[:, 3584:3584 + HT]
        k_f = psbk(3584, HT)
        ps_g = pool[0:1, 3072:3072 + 128]
        k_g = psbk(3072, 128)
        ps_rb = pool[0:64, 3584:3584 + 32]
        k_rb = psbk(3584, 32)

        p = P(nc)
        ct = make_common(nc, st, p, pool, NT=TPC)
        row, ones, onesrow, eps = ct["row"], ct["ones"], ct["onesrow"], ct["eps"]

        X2V = lambda: x1[:].rearrange("p (c t) -> p c t", c=KC)

        GS = {nm: i * 32 for i, nm in enumerate(
            ["AFF", "LOG", "GATED", "MASKA", "WORK", "FB", "IND", "SCR",
             "SEL", "ROUT", "EXP", "SC1"])}

        def grow(nm, lo=0, hi=32):
            off = GS[nm]
            return g_sb[0:1, off + lo: off + hi]

        p.add("sync", lambda h: h.dma_start(out=X2V(), in_=x1_in),
              writes=["x1"], dma=True, group="gx1")
        p.add("gpsimd", lambda h: h.dma_start(
            out=x1b[:].rearrange("p (c t) -> p c t", c=KC), in_=x1_in),
            writes=["x1b"], dma=True, group="gx1b")
        p.add("sync", lambda h: h.dma_start(out=srp_sb[:], in_=srp4),
              writes=["srp_sb"], dma=True, group="gsrp")
        p.add("sync", lambda h: h.dma_start(
            out=sim_sb[:].rearrange("p (c e) -> p c e", e=32), in_=simm),
            writes=["sim_sb"], dma=True, group="gsim")
        p.add("sync", lambda h: h.dma_start(out=grow("LOG"), in_=gates_in),
              writes=["ggates", "gLOG"], dma=True, group="ggate")
        p.add("sync", lambda h: h.dma_start(out=id64[:], in_=ident64.ap()),
              writes=["id64"], dma=True, group="gid")

        def T(*slots):
            return [f"g{s}" for s in slots]

        p.add("vector", lambda h: h.tensor_tensor(
            s_sb[:], srp_sb[:, 0:KC], srp_sb[:, KC:2 * KC], ALU.add),
            reads=["srp_sb"], writes=["s_sb"])
        p.add("vector", lambda h: h.tensor_tensor(
            s2_sb[:], srp_sb[:, 2 * KC:3 * KC], srp_sb[:, 3 * KC:4 * KC], ALU.add),
            reads=["srp_sb"], writes=["s2_sb"])
        p.add("vector", lambda h: h.tensor_tensor(
            s_sb[:], s_sb[:], s2_sb[:], ALU.add),
            reads=["s_sb", "s2_sb"], writes=["s_sb"])
        p.add("scalar", lambda h: h.activation(s2_sb[:], s_sb[:], AF.Square),
              reads=["s_sb"], writes=["s2_sb"])
        p.add("tensor", lambda h: h.matmul(ps_g[0:1, 0:KC], ones[:, 0:1], s2_sb[:],
                                           start=True, stop=True),
              reads=["s2_sb", "c_ones"], writes=k_g)
        p.add("vector", lambda h: h.reduce_sum(grow("SC1", 0, 1), ps_g[0:1, 0:KC], axis=AX.X),
              reads=k_g, writes=T("SC1_0"))
        p.add("scalar", lambda h: h.activation(sim2_sb[:], sim_sb[:], AF.Square),
              reads=["sim_sb"], writes=["sim2_sb"])

        def mm_cn(h):
            last = None
            for c in range(KC):
                last = h.matmul(ps_g[0:1, 32:64], ones[:, 0:1],
                                sim2_sb[:, c * 32:(c + 1) * 32],
                                start=(c == 0), stop=(c == KC - 1))
            return last
        p.add("tensor", mm_cn, reads=["sim2_sb", "c_ones"] + k_g, writes=k_g)

        def mm_num(h):
            last = None
            for c in range(KC):
                last = h.matmul(ps_g[0:1, 64:96], s_sb[:, c:c + 1],
                                sim_sb[:, c * 32:(c + 1) * 32],
                                start=(c == 0), stop=(c == KC - 1))
            return last
        p.add("tensor", mm_num, reads=["sim_sb", "s_sb"] + k_g, writes=k_g)

        p.add("vector", lambda h: h.memset(grow("SC1", 3, 4), 0.0), writes=T("SC1_3"))
        p.add("vector", lambda h: h.memset(grow("SC1", 5, 6), 1.0e30), writes=T("SC1_5"))
        p.add("vector", lambda h: h.memset(grow("SC1", 8, 9), 1.0), writes=T("SC1_8"))
        p.add("vector", lambda h: h.memset(grow("SC1", 9, 10), 3.4e38), writes=T("SC1_9"))

        p.add("scalar", lambda h: h.activation(grow("SC1", 1, 2), grow("SC1", 0, 1), AF.Sqrt),
              reads=T("SC1_0"), writes=T("SC1_1"))
        p.add("vector", lambda h: h.reciprocal(grow("SC1", 2, 3), grow("SC1", 1, 2)),
              reads=T("SC1_1"), writes=T("SC1_2"))
        p.add("scalar", lambda h: h.activation(grow("SCR"), ps_g[0:1, 32:64], AF.Sqrt),
              reads=k_g, writes=T("SCR"))
        p.add("vector", lambda h: h.reciprocal(grow("IND"), grow("SCR")),
              reads=T("SCR"), writes=T("IND"))
        p.add("vector", lambda h: h.tensor_tensor(
            grow("AFF"), ps_g[0:1, 64:96], grow("IND"), ALU.mult),
            reads=k_g + T("IND"), writes=T("AFF"))
        p.add("vector", lambda h: h.tensor_single_scalar(
            grow("AFF"), grow("AFF"), grow("SC1", 2, 3), ALU.mult),
            reads=T("AFF", "SC1_2"), writes=T("AFF"))
        p.add("scalar", lambda h: h.activation(grow("SCR"), grow("LOG"), AF.Sigmoid),
              reads=["ggates"] + T("LOG"), writes=T("SCR"))
        p.add("vector", lambda h: h.tensor_tensor(
            grow("LOG"), grow("AFF"), grow("SCR"), ALU.subtract),
            reads=T("AFF", "SCR", "LOG") + ["ggates"], writes=T("LOG"))
        p.add("vector", lambda h: h.tensor_relu(grow("GATED"), grow("LOG")),
              reads=T("LOG"), writes=T("GATED"))
        p.add("vector", lambda h: h.tensor_single_scalar(
            grow("MASKA"), grow("GATED"), grow("SC1", 3, 4), ALU.is_gt),
            reads=T("GATED", "SC1_3"), writes=T("MASKA"))

        p.add("vector", lambda h: h.tensor_copy(grow("WORK"), grow("LOG")),
              reads=T("LOG"), writes=T("WORK"))
        p.add("vector", lambda h: h.memset(grow("FB"), 0.0), writes=T("FB"))
        for it in range(16):
            p.add("vector", lambda h: h.reduce_max(grow("SC1", 4, 5), grow("WORK"), axis=AX.X),
                  reads=T("WORK"), writes=T("SC1_4"), tag=f"tk{it}")
            p.add("vector", lambda h: h.tensor_single_scalar(
                grow("IND"), grow("WORK"), grow("SC1", 4, 5), ALU.is_ge),
                reads=T("WORK", "SC1_4"), writes=T("IND"))
            p.add("vector", lambda h: h.tensor_tensor(
                grow("FB"), grow("FB"), grow("IND"), ALU.max),
                reads=T("FB", "IND"), writes=T("FB"))
            p.add("vector", lambda h: h.tensor_single_scalar(
                grow("IND"), grow("IND"), grow("SC1", 5, 6), ALU.mult),
                reads=T("IND", "SC1_5"), writes=T("IND"))
            p.add("vector", lambda h: h.tensor_tensor(
                grow("WORK"), grow("WORK"), grow("IND"), ALU.subtract),
                reads=T("WORK", "IND"), writes=T("WORK"))
        p.add("vector", lambda h: h.reduce_sum(grow("SC1", 6, 7), grow("MASKA"), axis=AX.X),
              reads=T("MASKA"), writes=T("SC1_6"))
        p.add("vector", lambda h: h.tensor_single_scalar(
            grow("SC1", 7, 8), grow("SC1", 6, 7), grow("SC1", 3, 4), ALU.is_le),
            reads=T("SC1_6", "SC1_3"), writes=T("SC1_7"))
        p.add("vector", lambda h: h.tensor_tensor(
            grow("SCR"), grow("MASKA"), grow("FB"), ALU.max),
            reads=T("MASKA", "FB"), writes=T("SCR"))
        p.add("vector", lambda h: h.tensor_tensor(
            grow("SCR"), grow("SCR"), grow("MASKA"), ALU.subtract),
            reads=T("SCR", "MASKA"), writes=T("SCR"))
        p.add("vector", lambda h: h.tensor_single_scalar(
            grow("SCR"), grow("SCR"), grow("SC1", 7, 8), ALU.mult),
            reads=T("SCR", "SC1_7"), writes=T("SCR"))
        p.add("vector", lambda h: h.tensor_tensor(
            grow("MASKA"), grow("MASKA"), grow("SCR"), ALU.add),
            reads=T("MASKA", "SCR"), writes=T("MASKA"))
        p.add("vector", lambda h: h.tensor_tensor(
            grow("SEL"), grow("GATED"), grow("MASKA"), ALU.mult),
            reads=T("GATED", "MASKA"), writes=T("SEL"))
        p.add("vector", lambda h: h.tensor_single_scalar(
            grow("SCR"), grow("MASKA"), grow("SC1", 8, 9), ALU.subtract),
            reads=T("MASKA", "SC1_8"), writes=T("SCR"))
        p.add("vector", lambda h: h.tensor_single_scalar(
            grow("SCR"), grow("SCR"), grow("SC1", 9, 10), ALU.mult),
            reads=T("SCR", "SC1_9"), writes=T("SCR"))
        p.add("vector", lambda h: h.tensor_tensor(
            grow("SEL"), grow("SEL"), grow("SCR"), ALU.add),
            reads=T("SEL", "SCR"), writes=T("SEL"))
        p.add("vector", lambda h: h.reduce_max(grow("SC1", 10, 11), grow("SEL"),
                                               axis=AX.X, negate=True),
              reads=T("SEL"), writes=T("SC1_10"))
        p.add("scalar", lambda h: h.activation(
            grow("EXP"), grow("SEL"), AF.Exp, bias=grow("SC1", 10, 11),
            accum_out=grow("SC1", 11, 12)),
            reads=T("SEL", "SC1_10"), writes=T("EXP", "SC1_11"))
        p.add("vector", lambda h: h.reciprocal(grow("SC1", 10, 11), grow("SC1", 11, 12)),
              reads=T("SC1_11"), writes=T("SC1_10"))
        p.add("vector", lambda h: h.tensor_single_scalar(
            grow("ROUT"), grow("EXP"), grow("SC1", 10, 11), ALU.mult),
            reads=T("EXP", "SC1_10"), writes=T("ROUT") + ["grout"])
        p.add("tensor", lambda h: h.matmul(
            ps_rb, onesrow[0:1, 0:64], grow("ROUT"), start=True, stop=True),
            reads=T("ROUT") + ["c_onesrow"] + k_rb, writes=k_rb, tag="rbc")
        p.add("vector", lambda h: h.tensor_copy(r_bc[:], ps_rb),
            reads=k_rb, writes=["r_bc"])

        p.add("vector", lambda h: h.memset(weTacc[:], 0.0), writes=["weTacc"])
        for e in range(NE):
            par = e % 2
            p.add("gpsimd", (lambda e=e, par=par: lambda h: h.dma_start(
                out=wvb[:, par * DM:(par + 1) * DM], in_=wv_in[e]))(),
                reads=[f"wvuse{par}"], writes=[f"wvb{par}"], dma=True, group="gwv",
                tag=f"wvld{e}")
            p.add("gpsimd", (lambda e=e, par=par: lambda h: h.dma_start(
                out=owb[:, par * DM:(par + 1) * DM], in_=ow_in[e]))(),
                reads=[f"owuse{par}"], writes=[f"owb{par}"], dma=True, group="gow",
                tag=f"owld{e}")
            p.add("vector", (lambda e=e, par=par: lambda h: h.tensor_single_scalar(
                rIe[:, par * 64:(par + 1) * 64], id64[:], r_bc[0:64, e:e + 1], ALU.mult))(),
                reads=["id64", "r_bc", f"rIuse{par}"], writes=[f"rIe{par}"],
                tag=f"rIe{e}")

            for cc in range(KC):
                slot_off = (cc % 2) * 512 + ((cc // 2) % 8) * 64
                kslot = [f"psB{slot_off // PSBANK}"]
                p.add("tensor", (lambda e=e, par=par, cc=cc, slot_off=slot_off: lambda h: h.matmul(
                    pool[:, slot_off:slot_off + 64],
                    wvb[:, par * DM + cc * 128: par * DM + (cc + 1) * 128],
                    rIe[:, par * 64:(par + 1) * 64],
                    start=True, stop=True))(),
                    reads=[f"wvb{par}", f"rIe{par}"] + kslot, writes=kslot,
                    tag=f"mmwe{e}_{cc}")
                p.add("vector", (lambda cc=cc, slot_off=slot_off: lambda h: h.tensor_tensor(
                    weTacc[:, cc * 64:(cc + 1) * 64],
                    weTacc[:, cc * 64:(cc + 1) * 64],
                    pool[:, slot_off:slot_off + 64], ALU.add))(),
                    reads=["weTacc"] + kslot, writes=["weTacc"] + kslot,
                    tag=f"weadd{e}_{cc}")
            p.add("gpsimd", (lambda par=par: lambda h: h.nop(nofuse=True))(),
                  reads=[f"wvb{par}"], writes=[f"wvuse{par}"], tag=f"wvfree{e}")

            def mm_wo(e=e, par=par):
                def g(h):
                    last = None
                    for oc in range(4):
                        last = h.matmul(
                            ps_wo[:, oc * 512:(oc + 1) * 512],
                            rIe[:, par * 64:(par + 1) * 64],
                            owb[:, par * DM + oc * 512: par * DM + (oc + 1) * 512],
                            start=(e == 0), stop=(e == NE - 1))
                    return last
                return g
            p.add("tensor", mm_wo(), reads=[f"owb{par}", f"rIe{par}"] + k_wo,
                  writes=k_wo + [f"owuse{par}", f"rIuse{par}"],
                  tag=f"mmwo{e}")

        p.add("scalar", lambda h: h.activation(weT[:], weTacc[:], AF.Copy),
              reads=["weTacc"], writes=["weT"])
        p.add("scalar", lambda h: h.activation(wod[:], ps_wo, AF.Copy),
              reads=k_wo, writes=["wod"])

        for th in range(2):
            lo = th * HT

            def mm_c(th=th, lo=lo):
                def g(h):
                    last = None
                    for cc in range(KC):
                        last = h.matmul(
                            ps_c, weT[:, cc * 64:(cc + 1) * 64],
                            x1b[:, cc * TPC + lo: cc * TPC + lo + HT],
                            start=(cc == 0), stop=(cc == KC - 1))
                    return last
                return g
            p.add("tensor", mm_c(), reads=["weT", "x1b"] + k_c, writes=k_c,
                  tag=f"mmc{th}")
            p.add("scalar", (lambda lo=lo: lambda h: h.activation(
                comb[:, lo:lo + HT], ps_c, AF.Copy))(),
                reads=k_c, writes=["comb"], tag=f"combev{th}")

            for oc in range(KC):
                p.add("tensor", (lambda oc=oc, lo=lo: lambda h: h.matmul(
                    ps_f, wod[:, oc * 128:(oc + 1) * 128],
                    comb[:, lo:lo + HT], start=True, stop=True))(),
                    reads=["wod", "comb"] + k_f, writes=k_f, tag=f"mmf{th}_{oc}")
                p.add("vector", (lambda oc=oc, lo=lo: lambda h: h.tensor_tensor(
                    x1[:, oc * TPC + lo: oc * TPC + lo + HT],
                    ps_f,
                    x1[:, oc * TPC + lo: oc * TPC + lo + HT], ALU.add))(),
                    reads=k_f + ["x1"], writes=["x1"], tag=f"resf{th}_{oc}")

        add_ln_fm(p, ct, x1, ["x1"], tag="C")

        p.add("gpsimd", lambda h: h.dma_start(out=x2_out, in_=X2V()),
              reads=["x1"], writes=["o1"], dma=True, group="gout")
        p.add("gpsimd", lambda h: h.dma_start(out=rt_out, in_=grow("ROUT")),
              reads=["grout"], writes=["o2"], dma=True, group="gout")

        with nc.Block() as block:
            p.emit_with_sems(st, block, final_engine="gpsimd",
                             final_waits=["o1", "o2"])
    return nc


def host_C(x1T, srps, sim_matrix, gates, w_v, o_weights):
    maps = []
    simm = _f32(sim_matrix.reshape(KC, 128, 32).transpose(1, 0, 2))
    wv = _f32(w_v)
    ow = _f32(o_weights)
    g1 = _f32(np.asarray(gates).reshape(1, 32))
    for c in range(NCORES):
        b = c // 4
        sl = slice(c * TPC, (c + 1) * TPC)
        x1c = _f32(x1T[:, sl].reshape(KC, 128, TPC).transpose(1, 0, 2))
        srp4 = np.concatenate([srps[4 * b + j] for j in range(4)], axis=1)
        maps.append({"x1_in": x1c, "srp4": _f32(srp4), "simm": simm,
                     "gates_in": g1, "wv_in": wv, "ow_in": ow})
    return maps


def run_C(x1T, srps, sim_matrix, gates, w_v, o_weights):
    nc = build_C()
    res = run_bass_kernel_spmd(nc, host_C(x1T, srps, sim_matrix, gates, w_v, o_weights),
                               core_ids=list(range(NCORES)))
    x2T = np.concatenate(
        [r["x2_out"].transpose(1, 0, 2).reshape(DM, TPC) for r in res.results], axis=1)
    return x2T, [r["rt_out"] for r in res.results]




def build_D():
    nc = bass.Bass("TRN2", target_bir_lowering=False)
    x2T_in = nc.dram_tensor("x2T_in", [128, KC, T], F32, kind="ExternalInput").ap()
    wq_in = nc.dram_tensor("wq_in", [128, KC, 256], F32, kind="ExternalInput").ap()
    wk_in = nc.dram_tensor("wk_in", [128, KC, 256], F32, kind="ExternalInput").ap()
    wv_in = nc.dram_tensor("wv_in", [128, KC, 256], F32, kind="ExternalInput").ap()
    bq_in = nc.dram_tensor("bq_in", [128, 2], F32, kind="ExternalInput").ap()
    bk_in = nc.dram_tensor("bk_in", [128, 2], F32, kind="ExternalInput").ap()
    bv_in = nc.dram_tensor("bv_in", [1, 256], F32, kind="ExternalInput").ap()
    maskcT_in = nc.dram_tensor("maskcT_in", [128, 4, 512], F32, kind="ExternalInput").ap()
    attn_out = nc.dram_tensor("attn_out", [2, 2, 128, S], BF16, kind="ExternalOutput").ap()

    TTD = 256
    NPT = T // TTD
    NIG = S // 512
    RS = 0.08838834764831845

    import contextlib
    with contextlib.ExitStack() as st:
        xtb = st.enter_context(nc.sbuf_tensor("xtb", [128, 2 * KC * TTD], BF16))
        wq = st.enter_context(nc.sbuf_tensor("wq", [128, KC * 256], BF16))
        wk = st.enter_context(nc.sbuf_tensor("wk", [128, KC * 256], BF16))
        wv = st.enter_context(nc.sbuf_tensor("wv", [128, KC * 256], BF16))
        bq = st.enter_context(nc.sbuf_tensor("bq", [128, 2], F32))
        bk = st.enter_context(nc.sbuf_tensor("bk", [128, 2], F32))
        bqs = st.enter_context(nc.sbuf_tensor("bqs", [128, 2], F32))
        rs_t = st.enter_context(nc.sbuf_tensor("rs_t", [128, 1], F32))
        rsneg = st.enter_context(nc.sbuf_tensor("rsneg", [1, 1], F32))
        bvb = st.enter_context(nc.sbuf_tensor("bvb", [128, 256], F32))
        onesrow = st.enter_context(nc.sbuf_tensor("onesrow", [1, 128], F32))
        onesbf1 = st.enter_context(nc.sbuf_tensor("onesbf1", [1, 128], BF16))
        onesbc = st.enter_context(nc.sbuf_tensor("onesbc", [128, 1], BF16))
        maskcT = st.enter_context(nc.sbuf_tensor("maskcT", [128, 4 * 512], F32))
        Q = st.enter_context(nc.sbuf_tensor("Q", [128, 2 * T], BF16))
        Kt = st.enter_context(nc.sbuf_tensor("Kt", [128, 2 * T], BF16))
        V = st.enter_context(nc.sbuf_tensor("V", [128, (T // 128) * 256], BF16))
        ptile = st.enter_context(nc.sbuf_tensor("ptile", [128, 2 * 512], BF16))
        sqt = st.enter_context(nc.sbuf_tensor("sqt", [128, 2 * 512], BF16))
        rowq = st.enter_context(nc.sbuf_tensor("rowq", [1, S], F32))
        rowk = st.enter_context(nc.sbuf_tensor("rowk", [1, 512], F32))
        kmx = st.enter_context(nc.sbuf_tensor("kmx", [1, 4], F32))
        nbb = st.enter_context(nc.sbuf_tensor("nbb", [1, S], BF16))
        lrow = st.enter_context(nc.sbuf_tensor("lrow", [1, 2 * 512], F32))
        rlrow = st.enter_context(nc.sbuf_tensor("rlrow", [1, 2 * 512], F32))
        rlb = st.enter_context(nc.sbuf_tensor("rlb", [128, 2 * 512], F32))
        astage = st.enter_context(nc.sbuf_tensor("astage", [128, 2 * 512], BF16))
        pool = st.enter_context(nc.psum_tensor("pool", [128, 4096], F32))

        p = P(nc)

        def reg(off, width, parts=128):
            return pool[0:parts, off:off + width], psbk(off, width)

        ps_st = [reg(0, 512), reg(512, 512)]
        ps_pv = [reg(1024, 512), reg(1536, 512)]
        ps_l = [reg(2048, 512, parts=1), reg(2560, 512, parts=1)]
        ps_rl = reg(3072, 512)
        ps_qn = [reg(3584, 512, parts=1)]
        k_qn = ps_qn[0][1]
        ps_pj = [reg(0, 256), reg(512, 256), reg(1024, 256),
                 reg(1536, 256), reg(2048, 256), reg(2560, 256)]

        p.add("vector", lambda h: h.memset(onesrow[:], 1.0), writes=["onesrow"])
        p.add("vector", lambda h: h.memset(onesbf1[:], 1.0), writes=["onesbf1"])
        p.add("vector", lambda h: h.memset(onesbc[:], 1.0), writes=["onesbc"])
        p.add("vector", lambda h: h.memset(rs_t[:], RS), writes=["rs_t"])
        p.add("vector", lambda h: h.memset(rsneg[:], -RS), writes=["rsneg"])
        p.add("sync", lambda h: h.dma_start(out=bq[:], in_=bq_in),
              writes=["bq"], dma=True, group="gc")
        p.add("sync", lambda h: h.dma_start(out=bk[:], in_=bk_in),
              writes=["bk"], dma=True, group="gc")
        p.add("sync", lambda h: h.dma_start(
            out=maskcT[:].rearrange("p (f j) -> p f j", f=4), in_=maskcT_in),
            writes=["maskcT"], dma=True, group="gc")
        p.add("gpsimd", lambda h: h.dma_start(
            out=wq[:].rearrange("p (c m) -> p c m", m=256), in_=wq_in),
            writes=["wq"], dma=True, group="gw")
        p.add("gpsimd", lambda h: h.dma_start(
            out=wk[:].rearrange("p (c m) -> p c m", m=256), in_=wk_in),
            writes=["wk"], dma=True, group="gw")
        p.add("gpsimd", lambda h: h.dma_start(
            out=wv[:].rearrange("p (c m) -> p c m", m=256), in_=wv_in),
            writes=["wv"], dma=True, group="gw")
        p.add("vector", lambda h: h.tensor_scalar(
            bqs[:], bq[:], rs_t[:, 0:1], None, ALU.mult),
            reads=["bq", "rs_t"], writes=["bqs"])
        p.add("sync", lambda h: h.dma_start(out=rowq[0:1, 0:256], in_=bv_in),
              writes=["bvrow"], dma=True, group="gc")
        p.add("tensor", lambda h: h.matmul(
            pool[:, 3072:3072 + 256], onesrow[:], rowq[0:1, 0:256],
            start=True, stop=True),
            reads=["bvrow", "onesrow"] + psbk(3072, 256), writes=psbk(3072, 256),
            tag="bvbc")
        p.add("vector", lambda h: h.tensor_copy(bvb[:], pool[:, 3072:3072 + 256]),
              reads=psbk(3072, 256), writes=["bvb", "rowq_free"])

        for tt in range(NPT):
            par = tt % 2
            t0 = tt * TTD
            p.add("gpsimd", (lambda tt=tt, par=par: lambda h: h.dma_start(
                out=xtb[:, par * KC * TTD:(par + 1) * KC * TTD]
                    .rearrange("p (c t) -> p c t", c=KC),
                in_=x2T_in[:, :, tt * TTD:(tt + 1) * TTD]))(),
                reads=[f"xtu{par}"], writes=[f"xtb{par}"], dma=True, group=f"gx{par}",
                tag=f"xld{tt}")

            for mi in range(2):
                for wname, wsb in (("wq", wq), ("wk", wk)):
                    pj = ps_pj[mi if wname == "wq" else 2 + mi]

                    def mm_p(par=par, mi=mi, wsb=wsb, pj=pj):
                        def g(h):
                            last = None
                            for kc in range(KC):
                                last = h.matmul(
                                    pj[0],
                                    wsb[:, kc * 256 + mi * 128: kc * 256 + (mi + 1) * 128],
                                    xtb[:, par * KC * TTD + kc * TTD:
                                        par * KC * TTD + (kc + 1) * TTD],
                                    start=(kc == 0), stop=(kc == KC - 1))
                            return last
                        return g
                    p.add("tensor", mm_p(), reads=[wname, f"xtb{par}"] + pj[1],
                          writes=pj[1], tag=f"mmp{wname}{tt}_{mi}")
                    if wname == "wq":
                        p.add("vector", (lambda mi=mi, t0=t0, pj=pj: lambda h:
                              h.tensor_scalar(Q[:, mi * T + t0: mi * T + t0 + TTD],
                                              pj[0], rs_t[:, 0:1], bqs[:, mi:mi + 1],
                                              ALU.mult, ALU.add))(),
                              reads=pj[1] + ["bqs", "rs_t"], writes=["Q"],
                              tag=f"evq{tt}_{mi}")
                    else:
                        p.add("vector", (lambda mi=mi, t0=t0, pj=pj: lambda h:
                              h.tensor_scalar(Kt[:, mi * T + t0: mi * T + t0 + TTD],
                                              pj[0], bk[:, mi:mi + 1], None,
                                              ALU.add))(),
                              reads=pj[1] + ["bk"], writes=["Kt"],
                              tag=f"evk{tt}_{mi}")

            for ts in range(2):
                pj = ps_pj[4 + ts]

                def mm_v(par=par, ts=ts, pj=pj):
                    def g(h):
                        last = None
                        for kc in range(KC):
                            last = h.matmul(
                                pj[0],
                                xtb[:, par * KC * TTD + kc * TTD + ts * 128:
                                    par * KC * TTD + kc * TTD + (ts + 1) * 128],
                                wv[:, kc * 256:(kc + 1) * 256],
                                start=(kc == 0), stop=(kc == KC - 1))
                        return last
                    return g
                p.add("tensor", mm_v(), reads=["wv", f"xtb{par}"] + pj[1],
                      writes=pj[1] + ([f"xtu{par}"] if ts == 1 else []),
                      tag=f"mmv{tt}_{ts}")
                ch = (t0 + ts * 128) // 128
                p.add("vector", (lambda ch=ch, pj=pj: lambda h: h.tensor_tensor(
                    V[:, ch * 256:(ch + 1) * 256], pj[0], bvb[:], ALU.add))(),
                    reads=pj[1] + ["bvb"], writes=["V"], tag=f"evv{tt}_{ts}")

        for b in range(2):
            for hh in range(2):
                base = hh * T + b * S
                vbase = (b * S) // 128

                for ig in range(NIG):
                    i0 = ig * 512
                    njc = (i0 + 512) // 128
                    pvp = ig % 2
                    lp = ig % 2

                    def emit_st(jc, par, i0=i0, base=base):
                        def g(h, jc=jc, par=par, i0=i0, base=base):
                            return h.matmul(
                                ps_st[par][0],
                                Kt[:, base + jc * 128: base + (jc + 1) * 128],
                                Q[:, base + i0: base + i0 + 512],
                                start=True, stop=True)
                        p.add("tensor", g, reads=["Q", "Kt"] + ps_st[par][1],
                              writes=ps_st[par][1], tag=f"sb{b}{hh}_{ig}_{jc}")
                        rj = jc - ig * 4
                        if rj >= 0:
                            p.add("vector", (lambda par=par, rj=rj: lambda h:
                                  h.tensor_tensor(
                                      ps_st[par][0], ps_st[par][0],
                                      maskcT[:, rj * 512:(rj + 1) * 512], ALU.add))(),
                                  reads=ps_st[par][1] + ["maskcT"],
                                  writes=ps_st[par][1], tag=f"mskb{b}{hh}_{ig}_{jc}")

                    def emit_tail(jc, par, pvp=pvp, lp=lp, vbase=vbase, njc=njc):
                        pp = jc % 2
                        p.add("scalar", (lambda par=par, pp=pp: lambda h: h.activation(
                            ptile[:, pp * 512:(pp + 1) * 512], ps_st[par][0], AF.Exp))(),
                            reads=ps_st[par][1] + [f"ptu{pp}"], writes=[f"ptile{pp}"],
                            tag=f"bexp{b}{hh}_{ig}_{jc}")
                        p.add("tensor", (lambda hh=hh, jc=jc, pp=pp, pvp=pvp,
                              vbase=vbase, njc=njc: lambda h: h.matmul(
                            ps_pv[pvp][0],
                            V[:, (vbase + jc) * 256 + hh * 128:
                              (vbase + jc) * 256 + (hh + 1) * 128],
                            ptile[:, pp * 512:(pp + 1) * 512],
                            start=(jc == 0), stop=(jc == njc - 1)))(),
                            reads=["V", f"ptile{pp}"] + ps_pv[pvp][1],
                            writes=ps_pv[pvp][1], tag=f"pv{b}{hh}_{ig}_{jc}")
                        p.add("tensor", (lambda jc=jc, pp=pp, lp=lp, njc=njc: lambda h:
                              h.matmul(ps_l[lp][0], onesbc[:, 0:1],
                                       ptile[:, pp * 512:(pp + 1) * 512],
                                       start=(jc == 0), stop=(jc == njc - 1)))(),
                              reads=[f"ptile{pp}", "onesbc"] + ps_l[lp][1],
                              writes=ps_l[lp][1] + [f"ptu{pp}"],
                              tag=f"lmm{b}{hh}_{ig}_{jc}")

                    for jc in range(njc):
                        emit_st(jc, jc % 2)
                        if jc >= 1:
                            emit_tail(jc - 1, (jc - 1) % 2)
                    emit_tail(njc - 1, (njc - 1) % 2)

                    p.add("scalar", (lambda lp=lp: lambda h: h.activation(
                        lrow[0:1, lp * 512:(lp + 1) * 512], ps_l[lp][0], AF.Copy))(),
                        reads=ps_l[lp][1] + [f"lrowu{lp}"], writes=[f"lrow{lp}"],
                        tag=f"lev{b}{hh}_{ig}")
                    p.add("vector", (lambda lp=lp: lambda h: h.reciprocal(
                        rlrow[0:1, lp * 512:(lp + 1) * 512],
                        lrow[0:1, lp * 512:(lp + 1) * 512]))(),
                        reads=[f"lrow{lp}", f"rlru{lp}"],
                        writes=[f"rlrow{lp}", f"lrowu{lp}"],
                        tag=f"rlr{b}{hh}_{ig}")
                    p.add("tensor", (lambda lp=lp: lambda h: h.matmul(
                        ps_rl[0], onesrow[:], rlrow[0:1, lp * 512:(lp + 1) * 512],
                        start=True, stop=True))(),
                        reads=[f"rlrow{lp}", "onesrow"] + ps_rl[1],
                        writes=ps_rl[1] + [f"rlru{lp}"],
                        tag=f"rlb{b}{hh}_{ig}")
                    sp = ig % 2
                    p.add("vector", (lambda sp=sp: lambda h: h.tensor_copy(
                        rlb[:, sp * 512:(sp + 1) * 512], ps_rl[0]))(),
                        reads=ps_rl[1] + [f"rlbu{sp}"], writes=[f"rlb{sp}"],
                        tag=f"rlev{b}{hh}_{ig}")
                    p.add("vector", (lambda pvp=pvp, sp=sp: lambda h: h.tensor_tensor(
                        astage[:, sp * 512:(sp + 1) * 512], ps_pv[pvp][0],
                        rlb[:, sp * 512:(sp + 1) * 512], ALU.mult))(),
                        reads=ps_pv[pvp][1] + [f"rlb{sp}", f"asu{sp}"],
                        writes=[f"ast{sp}", f"rlbu{sp}"], tag=f"oev{b}{hh}_{ig}")
                    p.add("gpsimd", (lambda b=b, hh=hh, i0=i0, sp=sp: lambda h:
                          h.dma_start(
                              out=attn_out[b, hh, :, i0:i0 + 512],
                              in_=astage[:, sp * 512:(sp + 1) * 512]))(),
                          reads=[f"ast{sp}"], writes=[f"asu{sp}", "aout"],
                          dma=True, group=f"gao{sp}", tag=f"ao{b}{hh}_{ig}")

        with nc.Block() as block:
            p.emit_with_sems(st, block, final_engine="gpsimd", final_waits=["aout"])
    return nc


def host_D(x2T, w_q, b_q, w_k, b_k, w_v, b_v):
    x2L = np.ascontiguousarray(
        x2T.reshape(KC, 128, T).transpose(1, 0, 2), dtype=np.float32)
    maskc = np.zeros((128, 4, 512), np.float32)
    maskcT = np.zeros((128, 4, 512), np.float32)
    for ph in range(4):
        for pp in range(128):
            i = ph * 128 + pp
            maskc[pp, ph, :] = np.where(np.arange(512) <= i, 0.0, -1e10)
        for pp in range(128):
            maskcT[pp, ph, :] = np.where(np.arange(512) >= ph * 128 + pp, 0.0, -1e10)
    maps = []
    for c in range(NCORES):
        sl = slice(256 * c, 256 * (c + 1))
        wqL = np.ascontiguousarray(
            w_q[:, sl].reshape(KC, 128, 256).transpose(1, 0, 2), dtype=np.float32)
        wkL = np.ascontiguousarray(
            w_k[:, sl].reshape(KC, 128, 256).transpose(1, 0, 2), dtype=np.float32)
        wvL = np.ascontiguousarray(
            w_v[:, sl].reshape(KC, 128, 256).transpose(1, 0, 2), dtype=np.float32)
        bqL = np.ascontiguousarray(b_q[sl].reshape(2, 128).T, dtype=np.float32)
        bkL = np.ascontiguousarray(b_k[sl].reshape(2, 128).T, dtype=np.float32)
        bvL = np.ascontiguousarray(b_v[sl].reshape(1, 256), dtype=np.float32)
        maps.append({"x2T_in": x2L, "wq_in": wqL, "wk_in": wkL, "wv_in": wvL,
                     "bq_in": bqL, "bk_in": bkL, "bv_in": bvL,
                     "maskcT_in": maskcT})
    return maps


def run_D(x2T, w_q, b_q, w_k, b_k, w_v, b_v):
    nc = build_D()
    res = run_bass_kernel_spmd(nc, host_D(x2T, w_q, b_q, w_k, b_k, w_v, b_v),
                               core_ids=list(range(NCORES)))
    attnT = np.zeros((DM, T), dtype=bf16)
    for c in range(NCORES):
        a = res.results[c]["attn_out"]
        for hh in range(2):
            head = 2 * c + hh
            for b in range(2):
                attnT[head * 128:(head + 1) * 128, b * S:(b + 1) * S] = a[b, hh]
    return attnT



HID = 5632
HC = HID // 128


def build_E():
    nc = bass.Bass("TRN2", target_bir_lowering=False)
    attn_in = nc.dram_tensor("attn_in", [128, KC, TPC], BF16, kind="ExternalInput").ap()
    x2_in = nc.dram_tensor("x2_in", [128, KC, TPC], F32, kind="ExternalInput").ap()
    wo_in = nc.dram_tensor("wo_in", [128, KC, DM], F32, kind="ExternalInput").ap()
    bo_in = nc.dram_tensor("bo_in", [128, KC], F32, kind="ExternalInput").ap()
    wqg_in = nc.dram_tensor("wqg_in", [128, KC, HID], I32, kind="ExternalInput").ap()
    wqu_in = nc.dram_tensor("wqu_in", [128, KC, HID], I32, kind="ExternalInput").ap()
    wqd_in = nc.dram_tensor("wqd_in", [128, HC, DM], I32, kind="ExternalInput").ap()
    sg_in = nc.dram_tensor("sg_in", [128, HC], F32, kind="ExternalInput").ap()
    su_in = nc.dram_tensor("su_in", [128, HC], F32, kind="ExternalInput").ap()
    sd_in = nc.dram_tensor("sd_in", [128, KC], F32, kind="ExternalInput").ap()
    x4_out = nc.dram_tensor("x4_out", [128, KC, TPC], F32, kind="ExternalOutput").ap()

    TT2 = 256
    NTT = TPC // TT2
    HT = TPC // 2

    import contextlib
    with contextlib.ExitStack() as st:
        x = st.enter_context(nc.sbuf_tensor("x", [128, KC * TPC], F32))
        attn = st.enter_context(nc.sbuf_tensor("attn", [128, KC * TPC], BF16))
        wo = st.enter_context(nc.sbuf_tensor("wo", [128, KC * DM], BF16))
        bo = st.enter_context(nc.sbuf_tensor("bo", [128, KC], F32))
        sg = st.enter_context(nc.sbuf_tensor("sg", [128, HC], F32))
        su = st.enter_context(nc.sbuf_tensor("su", [128, HC], F32))
        sd = st.enter_context(nc.sbuf_tensor("sd", [128, KC], F32))
        onesbf = st.enter_context(nc.sbuf_tensor("onesbf", [128, 1], BF16))
        csb = st.enter_context(nc.sbuf_tensor("csb", [128, TT2], F32))
        csb2 = st.enter_context(nc.sbuf_tensor("csb2", [128, TT2], F32))
        tscr = st.enter_context(nc.sbuf_tensor("tscr", [128, TT2], F32))
        ga = st.enter_context(nc.sbuf_tensor("ga", [128, TT2], BF16))
        ua = st.enter_context(nc.sbuf_tensor("ua", [128, TT2], BF16))
        pool = st.enter_context(nc.psum_tensor("pool", [128, 4096], F32))

        x3bf = wo[:, 0:KC * TT2]
        wt = [wo[:, 4096 + i * 2048: 4096 + (i + 1) * 2048] for i in range(4)]
        wdt = [wo[:, 12288 + i * HC * 128: 12288 + (i + 1) * HC * 128] for i in range(2)]
        assert 12288 + 2 * HC * 128 <= KC * DM
        hid = attn[:, 0:HC * TT2]

        p = P(nc)
        ct = make_common(nc, st, p, pool, NT=TPC)
        row, ones, onesrow, eps = ct["row"], ct["ones"], ct["onesrow"], ct["eps"]

        XV = lambda: x[:].rearrange("p (c t) -> p c t", c=KC)

        ps_o = [pool[:, 0:512], pool[:, 512:1024]]
        k_o = [psbk(0, 512), psbk(512, 512)]
        ps_cs = pool[0:1, 1024:1024 + TT2]
        k_cs = psbk(1024, TT2)
        ps_cb = pool[:, 1536:1536 + TT2]
        k_cb = psbk(1536, TT2)
        ps_g = [pool[:, 2048:2048 + TT2], pool[:, 2560:2560 + TT2]]
        k_g = [psbk(2048, TT2), psbk(2560, TT2)]
        ps_u = [pool[:, 3072:3072 + TT2], pool[:, 3584:3584 + TT2]]
        k_u = [psbk(3072, TT2), psbk(3584, TT2)]
        ps_d = [pool[:, 0:TT2], pool[:, 512:512 + TT2]]
        k_d = [psbk(0, TT2), psbk(512, TT2)]

        p.add("vector", lambda h: h.memset(onesbf[:], 1.0), writes=["onesbf"])
        p.add("sync", lambda h: h.dma_start(out=XV(), in_=x2_in),
              writes=["x"], dma=True, group="gx2")
        p.add("sync", lambda h: h.dma_start(
            out=attn[:].rearrange("p (c t) -> p c t", c=KC), in_=attn_in),
            writes=["attn"], dma=True, group="gattn")
        p.add("gpsimd", lambda h: h.dma_start(
            out=wo[:].rearrange("p (c d) -> p c d", d=DM), in_=wo_in),
            writes=["wo"], dma=True, group="gwo")
        p.add("sync", lambda h: h.dma_start(out=bo[:], in_=bo_in),
              writes=["bo"], dma=True, group="gsc")
        p.add("sync", lambda h: h.dma_start(out=sg[:], in_=sg_in),
              writes=["sg"], dma=True, group="gsc")
        p.add("sync", lambda h: h.dma_start(out=su[:], in_=su_in),
              writes=["su"], dma=True, group="gsc")
        p.add("sync", lambda h: h.dma_start(out=sd[:], in_=sd_in),
              writes=["sd"], dma=True, group="gsc")

        for th in range(2):
            for oc in range(KC):
                par = (th * KC + oc) % 2

                def mm_o(th=th, oc=oc, par=par):
                    def g(h):
                        last = None
                        for kc in range(KC):
                            last = h.matmul(
                                ps_o[par],
                                wo[:, kc * DM + oc * 128: kc * DM + (oc + 1) * 128],
                                attn[:, kc * TPC + th * HT: kc * TPC + th * HT + HT],
                                start=(kc == 0), stop=(kc == KC - 1))
                        return last
                    return g
                p.add("tensor", mm_o(), reads=["wo", "attn"] + k_o[par],
                      writes=k_o[par], tag=f"mmo{th}_{oc}")
                p.add("vector", (lambda th=th, oc=oc, par=par: lambda h:
                      h.scalar_tensor_tensor(
                          x[:, oc * TPC + th * HT: oc * TPC + th * HT + HT],
                          ps_o[par], bo[:, oc:oc + 1],
                          x[:, oc * TPC + th * HT: oc * TPC + th * HT + HT],
                          ALU.add, ALU.add))(),
                      reads=k_o[par] + ["bo", "x"], writes=["x"], tag=f"evo{th}_{oc}")

        p.add("gpsimd", lambda h: h.nop(nofuse=True),
              writes=["wo", "attn", "woarena", "attnarena"], tag="arena_sw")

        add_ln_fm(p, ct, x, ["x"], tag="E3")

        for tt in range(NTT):
            t0 = tt * TT2

            def cast_x3(tt=tt, t0=t0):
                def g(h):
                    last = None
                    for kc in range(KC):
                        last = h.tensor_copy(
                            x3bf[:, kc * TT2:(kc + 1) * TT2],
                            x[:, kc * TPC + t0: kc * TPC + t0 + TT2])
                    return last
                return g
            p.add("vector", cast_x3(), reads=["x", "woarena"], writes=["x3bf"],
                  tag=f"cx3_{tt}")

            def mm_cs(tt=tt):
                def g(h):
                    last = None
                    for kc in range(KC):
                        last = h.matmul(ps_cs, onesbf[:, 0:1],
                                        x3bf[:, kc * TT2:(kc + 1) * TT2],
                                        start=(kc == 0), stop=(kc == KC - 1))
                    return last
                return g
            p.add("tensor", mm_cs(), reads=["x3bf", "onesbf"] + k_cs, writes=k_cs,
                  tag=f"mmcs{tt}")
            p.add("scalar", (lambda: lambda h: h.activation(
                row("SCR1", 0, TT2), ps_cs, AF.Copy, scale=128.0))(),
                reads=k_cs, writes=["r128cs"], tag=f"cs{tt}")
            p.add("tensor", lambda h: h.matmul(
                ps_cb, onesrow[:], row("SCR1", 0, TT2), start=True, stop=True),
                reads=["r128cs", "c_onesrow"] + k_cb, writes=k_cb, tag=f"bccs{tt}")
            p.add("vector", lambda h: h.tensor_copy(csb[:], ps_cb),
                  reads=k_cb, writes=["csb"], tag=f"evcs{tt}")

            for hc in range(HC):
                par = hc % 2
                p.add("gpsimd", (lambda hc=hc, par=par: lambda h: h.dma_start(
                    out=wt[par].rearrange("p (c m) -> p c m", m=128),
                    in_=wqg_in[:, :, hc * 128:(hc + 1) * 128]))(),
                    reads=[f"wtu{par}", "woarena"], writes=[f"wt{par}"], dma=True,
                    group=f"gwt{par}", tag=f"ldg{tt}_{hc}")
                p.add("gpsimd", (lambda hc=hc, par=par: lambda h: h.dma_start(
                    out=wt[2 + par].rearrange("p (c m) -> p c m", m=128),
                    in_=wqu_in[:, :, hc * 128:(hc + 1) * 128]))(),
                    reads=[f"wuu{par}", "woarena"], writes=[f"wu{par}"], dma=True,
                    group=f"gwu{par}", tag=f"ldu{tt}_{hc}")

                def mm_g(hc=hc, par=par):
                    def g(h):
                        last = None
                        for kc in range(KC):
                            last = h.matmul(
                                ps_g[par], wt[par][:, kc * 128:(kc + 1) * 128],
                                x3bf[:, kc * TT2:(kc + 1) * TT2],
                                start=(kc == 0), stop=(kc == KC - 1))
                        return last
                    return g
                p.add("tensor", mm_g(), reads=[f"wt{par}", "x3bf"] + k_g[par],
                      writes=k_g[par] + [f"wtu{par}"], tag=f"mmg{tt}_{hc}")
                p.add("vector", (lambda par=par: lambda h: h.tensor_tensor(
                    tscr[:], ps_g[par], csb[:], ALU.subtract))(),
                    reads=k_g[par] + ["csb", "tscr_free"], writes=["tscr"],
                    tag=f"gsub{tt}_{hc}")
                p.add("scalar", (lambda hc=hc: lambda h: h.activation(
                    ga[:], tscr[:], AF.Silu, scale=sg[:, hc:hc + 1]))(),
                    reads=["tscr", "sg", "ga_free"], writes=["ga"], tag=f"gact{tt}_{hc}")

                def mm_u(hc=hc, par=par):
                    def g(h):
                        last = None
                        for kc in range(KC):
                            last = h.matmul(
                                ps_u[par], wt[2 + par][:, kc * 128:(kc + 1) * 128],
                                x3bf[:, kc * TT2:(kc + 1) * TT2],
                                start=(kc == 0), stop=(kc == KC - 1))
                        return last
                    return g
                p.add("tensor", mm_u(), reads=[f"wu{par}", "x3bf"] + k_u[par],
                      writes=k_u[par] + [f"wuu{par}"], tag=f"mmu{tt}_{hc}")
                p.add("vector", (lambda par=par: lambda h: h.tensor_tensor(
                    tscr[:], ps_u[par], csb[:], ALU.subtract))(),
                    reads=k_u[par] + ["csb", "ga"], writes=["tscr"],
                    tag=f"usub{tt}_{hc}")
                p.add("scalar", (lambda hc=hc: lambda h: h.activation(
                    ua[:], tscr[:], AF.Copy, scale=su[:, hc:hc + 1]))(),
                    reads=["tscr", "su", "ua_free"], writes=["ua"], tag=f"uact{tt}_{hc}")
                p.add("vector", (lambda hc=hc: lambda h: h.tensor_tensor(
                    hid[:, hc * TT2:(hc + 1) * TT2], ga[:], ua[:], ALU.mult))(),
                    reads=["ga", "ua", "attnarena"], writes=[f"hid{hc}", "ga_free",
                                                             "ua_free", "tscr_free"],
                    tag=f"hmul{tt}_{hc}")
            HID_RES = [f"hid{hc}" for hc in range(HC)]

            def mm_cs2(tt=tt):
                def g(h):
                    last = None
                    for hc in range(HC):
                        last = h.matmul(ps_cs, onesbf[:, 0:1],
                                        hid[:, hc * TT2:(hc + 1) * TT2],
                                        start=(hc == 0), stop=(hc == HC - 1))
                    return last
                return g
            p.add("tensor", mm_cs2(), reads=HID_RES + ["onesbf"] + k_cs, writes=k_cs,
                  tag=f"mmcs2{tt}")
            p.add("scalar", (lambda: lambda h: h.activation(
                row("SCR1", 0, TT2), ps_cs, AF.Copy, scale=128.0))(),
                reads=k_cs + ["r128cs"], writes=["r128cs"], tag=f"cs2{tt}")
            p.add("tensor", lambda h: h.matmul(
                ps_cb, onesrow[:], row("SCR1", 0, TT2), start=True, stop=True),
                reads=["r128cs", "c_onesrow"] + k_cb, writes=k_cb, tag=f"bccs2{tt}")
            p.add("vector", lambda h: h.tensor_copy(csb2[:], ps_cb),
                  reads=k_cb + ["csb"], writes=["csb2"], tag=f"evcs2{tt}")

            for dc in range(KC):
                par = dc % 2
                p.add("gpsimd", (lambda dc=dc, par=par: lambda h: h.dma_start(
                    out=wdt[par].rearrange("p (c m) -> p c m", m=128),
                    in_=wqd_in[:, :, dc * 128:(dc + 1) * 128]))(),
                    reads=[f"wdu{par}", "woarena"], writes=[f"wd{par}"], dma=True,
                    group=f"gwd{par}", tag=f"ldd{tt}_{dc}")

                def mm_d(dc=dc, par=par):
                    def g(h):
                        last = None
                        for hc in range(HC):
                            last = h.matmul(
                                ps_d[par], wdt[par][:, hc * 128:(hc + 1) * 128],
                                hid[:, hc * TT2:(hc + 1) * TT2],
                                start=(hc == 0), stop=(hc == HC - 1))
                        return last
                    return g
                p.add("tensor", mm_d(), reads=[f"wd{par}"] + HID_RES + k_d[par],
                      writes=k_d[par] + [f"wdu{par}"], tag=f"mmd{tt}_{dc}")
                p.add("vector", (lambda dc=dc, par=par: lambda h: h.tensor_tensor(
                    tscr[:], ps_d[par], csb2[:], ALU.subtract))(),
                    reads=k_d[par] + ["csb2", "tscr_free"], writes=["tscr2"],
                    tag=f"dsub{tt}_{dc}")
                p.add("vector", (lambda dc=dc, t0=t0: lambda h:
                      h.scalar_tensor_tensor(
                          x[:, dc * TPC + t0: dc * TPC + t0 + TT2],
                          tscr[:], sd[:, dc:dc + 1],
                          x[:, dc * TPC + t0: dc * TPC + t0 + TT2],
                          ALU.mult, ALU.add))(),
                      reads=["tscr2", "sd", "x"], writes=["x", "tscr_free"],
                      tag=f"dres{tt}_{dc}")

        add_ln_fm(p, ct, x, ["x"], tag="E4")

        p.add("gpsimd", lambda h: h.dma_start(out=x4_out, in_=XV()),
              reads=["x"], writes=["o1"], dma=True, group="gout")

        with nc.Block() as block:
            p.emit_with_sems(st, block, final_engine="gpsimd", final_waits=["o1"])
    return nc


def host_E(attnT, x2T, w_o, b_o, wq_gate, s_gate, wq_up, s_up, wq_down, s_down):
    woL = _f32(w_o).reshape(KC, 128, DM).transpose(1, 0, 2)
    woL = np.ascontiguousarray(woL)
    boL = _f32(b_o).reshape(KC, 128).T
    wqgL = np.ascontiguousarray(
        wq_gate.T.reshape(KC, 128, HID).transpose(1, 0, 2), dtype=np.int32)
    wquL = np.ascontiguousarray(
        wq_up.T.reshape(KC, 128, HID).transpose(1, 0, 2), dtype=np.int32)
    wqdL = np.ascontiguousarray(
        wq_down.T.reshape(HC, 128, DM).transpose(1, 0, 2), dtype=np.int32)
    sgL = _f32(s_gate).reshape(HC, 128).T
    suL = _f32(s_up).reshape(HC, 128).T
    sdL = _f32(s_down).reshape(KC, 128).T
    maps = []
    for c in range(NCORES):
        sl = slice(c * TPC, (c + 1) * TPC)
        at = np.ascontiguousarray(
            attnT[:, sl].reshape(KC, 128, TPC).transpose(1, 0, 2))
        x2c = _f32(x2T[:, sl].reshape(KC, 128, TPC).transpose(1, 0, 2))
        maps.append({"attn_in": at.astype(bf16), "x2_in": x2c, "wo_in": woL,
                     "bo_in": np.ascontiguousarray(boL),
                     "wqg_in": wqgL, "wqu_in": wquL, "wqd_in": wqdL,
                     "sg_in": np.ascontiguousarray(sgL),
                     "su_in": np.ascontiguousarray(suL),
                     "sd_in": np.ascontiguousarray(sdL)})
    return maps


def run_E(attnT, x2T, w_o, b_o, wq_gate, s_gate, wq_up, s_up, wq_down, s_down):
    nc = build_E()
    res = run_bass_kernel_spmd(
        nc, host_E(attnT, x2T, w_o, b_o, wq_gate, s_gate, wq_up, s_up, wq_down, s_down),
        core_ids=list(range(NCORES)))
    return np.concatenate(
        [r["x4_out"].transpose(1, 0, 2).reshape(DM, TPC) for r in res.results], axis=1)


def host_B(partials, sumsqs, x0T):
    maps = []
    for c in range(NCORES):
        sl = slice(c * TPC, (c + 1) * TPC)
        p8 = np.stack([partials[j][:, :, sl] for j in range(8)])
        sq8 = np.stack([sumsqs[j][sl] for j in range(8)])
        x0c = np.ascontiguousarray(
            x0T[:, sl].reshape(KC, 128, TPC).transpose(1, 0, 2))
        maps.append({"p8": _f32(p8), "sq8": _f32(sq8), "x0c": _f32(x0c)})
    return maps


def run_B(partials, sumsqs, x0T):
    nc = build_B()
    res = run_bass_kernel_spmd(nc, host_B(partials, sumsqs, x0T),
                               core_ids=list(range(NCORES)))
    x1T = np.concatenate(
        [r["x1_out"].transpose(1, 0, 2).reshape(DM, TPC) for r in res.results], axis=1)
    srp = [r["srp_out"] for r in res.results]
    return x1T, srp


def kernel(L0_attn_hidden_states, L0_attn_qkv_w, L0_attn_out_w, L0_attn_gate_w,
           L0_attn_norm_w, L0_ffn_sim_matrix, L0_ffn_gates, L0_ffn_w_q,
           L0_ffn_w_k, L0_ffn_w_v, L0_ffn_o_weights, L1_attn_w_q, L1_attn_b_q,
           L1_attn_w_k, L1_attn_b_k, L1_attn_w_v, L1_attn_b_v, L1_attn_w_o,
           L1_attn_b_o, L1_ffn_wq_gate, L1_ffn_s_gate, L1_ffn_wq_up,
           L1_ffn_s_up, L1_ffn_wq_down, L1_ffn_s_down):
    x0 = np.ascontiguousarray(np.asarray(L0_attn_hidden_states), dtype=np.float32)
    x0T = np.ascontiguousarray(x0.reshape(T, DM).T)

    partials, sumsqs = run_A(x0, np.asarray(L0_attn_qkv_w),
                             np.asarray(L0_attn_gate_w), np.asarray(L0_attn_out_w),
                             np.asarray(L0_attn_norm_w))
    x1T, srps = run_B(partials, sumsqs, x0T)
    x2T, _ = run_C(x1T, srps, np.asarray(L0_ffn_sim_matrix),
                   np.asarray(L0_ffn_gates), np.asarray(L0_ffn_w_v),
                   np.asarray(L0_ffn_o_weights))
    attnT = run_D(x2T, np.asarray(L1_attn_w_q), np.asarray(L1_attn_b_q),
                  np.asarray(L1_attn_w_k), np.asarray(L1_attn_b_k),
                  np.asarray(L1_attn_w_v), np.asarray(L1_attn_b_v))
    x4T = run_E(attnT, x2T, np.asarray(L1_attn_w_o), np.asarray(L1_attn_b_o),
                np.asarray(L1_ffn_wq_gate), np.asarray(L1_ffn_s_gate),
                np.asarray(L1_ffn_wq_up), np.asarray(L1_ffn_s_up),
                np.asarray(L1_ffn_wq_down), np.asarray(L1_ffn_s_down))
    return np.ascontiguousarray(x4T.T).reshape(BB, S, DM)


# version 03
# speedup vs baseline: 1.5762x; 1.1369x over previous
import numpy as np
import ml_dtypes
import concourse.bass as bass
import concourse.mybir as mybir
from concourse.bass_utils import run_bass_kernel_spmd

F32 = mybir.dt.float32
BF16 = mybir.dt.bfloat16
I32 = mybir.dt.int32
AF = mybir.ActivationFunctionType
ALU = mybir.AluOpType
AX = mybir.AxisListType

NCORES = 8
BB, S, DM = 2, 4096, 2048
T = BB * S
TPC = T // NCORES
KC = DM // 128

bf16 = ml_dtypes.bfloat16

ENGINES = ("sync", "gpsimd", "vector", "scalar", "tensor")


class P:

    def __init__(self, nc):
        self.nc = nc
        self.ops = {e: [] for e in ENGINES}
        self.last_writer = {}
        self.readers = {}
        self.groups = {}
        self.sems = {}

    def add(self, eng, fn, reads=(), writes=(), dma=False, group=None, tag=""):
        idx = len(self.ops[eng])
        me = (eng, idx)
        assert (group is not None) == bool(dma), f"dma ops need a group ({tag})"
        deps = set()
        for r in reads:
            w = self.last_writer.get(r)
            if w is not None:
                deps.add(w)
        for r in writes:
            w = self.last_writer.get(r)
            if w is not None:
                deps.add(w)
            for rd in self.readers.get(r, ()):
                deps.add(rd)
        if dma and group in self.groups:
            deps.add(self.groups[group])
        deps.discard(me)
        self.ops[eng].append(dict(fn=fn, deps=deps, dma=dma, group=group,
                                  marked=False, tag=tag))
        for r in reads:
            self.readers.setdefault(r, set()).add(me)
        for r in writes:
            self.last_writer[r] = me
            self.readers[r] = set()
        if dma:
            self.groups[group] = me
        return me

    def finalize(self):
        for e in ENGINES:
            for op in self.ops[e]:
                if op["dma"]:
                    op["marked"] = True
                for (de, di) in op["deps"]:
                    self.ops[de][di]["marked"] = True
        self.incval = {}
        counts = {}
        for e in ENGINES:
            for i, op in enumerate(self.ops[e]):
                key = op["group"] if op["dma"] else e
                if op["marked"]:
                    counts[key] = counts.get(key, 0) + (16 if op["dma"] else 1)
                self.incval[(e, i)] = (key, counts.get(key, 0))

    def make_sems(self, stack):
        keys = set()
        for e in ENGINES:
            for op in self.ops[e]:
                keys.add(op["group"] if op["dma"] else e)
        for k in sorted(keys):
            self.sems[k] = stack.enter_context(self.nc.semaphore(f"sp_{k}"))

    def emit(self, eng, h):
        waited = {}
        for i, op in enumerate(self.ops[eng]):
            wants = {}
            for (de, di) in op["deps"]:
                key, v = self.incval[(de, di)]
                wants[key] = max(wants.get(key, 0), v)
            for key in sorted(wants):
                v = wants[key]
                if v > waited.get(key, 0):
                    h.wait_ge(self.sems[key], v)
                    waited[key] = v
            last = op["fn"](h)
            if op["marked"]:
                assert last is not None, f"op {eng}:{i} ({op['tag']}) returned None"
                key = op["group"] if op["dma"] else eng
                last.then_inc(self.sems[key], 16 if op["dma"] else 1)

    def emit_all(self, block, final_engine="gpsimd", final_waits=()):
        def _noop(h):
            return h.nop(nofuse=True)
        self.add(final_engine, _noop, reads=list(final_waits), tag="finalwait")
        self.finalize()

        def mk(e):
            def _(h):
                self.emit(e, h)
            return _
        block.sync(mk("sync"))
        block.gpsimd(mk("gpsimd"))
        block.vector(mk("vector"))
        block.scalar(mk("scalar"))
        block.tensor(mk("tensor"))

    def emit_with_sems(self, stack, block, final_engine="gpsimd", final_waits=()):
        def _noop(h):
            return h.nop(nofuse=True)
        self.add(final_engine, _noop, reads=list(final_waits), tag="finalwait")
        self.finalize()
        self.make_sems(stack)

        def mk(e):
            def _(h):
                self.emit(e, h)
            return _
        block.sync(mk("sync"))
        block.gpsimd(mk("gpsimd"))
        block.vector(mk("vector"))
        block.scalar(mk("scalar"))
        block.tensor(mk("tensor"))


def _f32(x):
    return np.ascontiguousarray(x, dtype=np.float32)



LBS = 256
NBLK = S // LBS


def build_A():
    nc = bass.Bass("TRN2", target_bir_lowering=False)
    id64c = nc.inline_tensor(np.vstack([np.eye(64), np.eye(64)]).astype(bf16), name="identity64a")
    x0T_in = nc.dram_tensor("x0T_in", [128, KC, T], F32, kind="ExternalInput").ap()
    wqkv_in = nc.dram_tensor("wqkv_in", [128, KC, 768], F32, kind="ExternalInput").ap()
    wgate_in = nc.dram_tensor("wgate_in", [128, KC, 256], F32, kind="ExternalInput").ap()
    wout_in = nc.dram_tensor("wout_in", [128, 2, DM], F32, kind="ExternalInput").ap()
    normw_in = nc.dram_tensor("normw_in", [128, 2], F32, kind="ExternalInput").ap()
    ddT_in = nc.dram_tensor("ddT_in", [128, 4, 2, LBS], F32, kind="ExternalInput").ap()
    qd_in = nc.dram_tensor("qd_in", [128, 4, LBS], F32, kind="ExternalInput").ap()
    kd_in = nc.dram_tensor("kd_in", [128, 8], F32, kind="ExternalInput").ap()
    bd_in = nc.dram_tensor("bd_in", [64, 4], F32, kind="ExternalInput").ap()
    partial_out = nc.dram_tensor("partial_out", [128, KC, T], F32, kind="ExternalOutput").ap()
    sumsq_out = nc.dram_tensor("sumsq_out", [1, T], F32, kind="ExternalOutput").ap()

    TTA = 256
    NPT = T // TTA

    import contextlib
    with contextlib.ExitStack() as st:
        xtb = st.enter_context(nc.sbuf_tensor("xtb", [128, 2 * KC * TTA], BF16))
        wqkv = st.enter_context(nc.sbuf_tensor("wqkv", [128, KC * 768], BF16))
        wgate = st.enter_context(nc.sbuf_tensor("wgate", [128, KC * 256], BF16))
        woutw = st.enter_context(nc.sbuf_tensor("woutw", [128, 2 * DM], BF16))
        normw = st.enter_context(nc.sbuf_tensor("normw", [128, 2], F32))
        Q = st.enter_context(nc.sbuf_tensor("Q", [128, 2 * T], BF16))
        Kt = st.enter_context(nc.sbuf_tensor("Kt", [128, 2 * T], BF16))
        V = st.enter_context(nc.sbuf_tensor("V", [128, (T // 128) * 256], BF16))
        ga = st.enter_context(nc.sbuf_tensor("ga", [128, 2 * T], BF16))

        kd = st.enter_context(nc.sbuf_tensor("kd", [128, 8], F32))
        bd = st.enter_context(nc.sbuf_tensor("bd", [64, 4], F32))
        id64 = st.enter_context(nc.sbuf_tensor("id64", [128, 64], BF16))
        onesf = st.enter_context(nc.sbuf_tensor("onesf", [64, 1], F32))
        ktm = st.enter_context(nc.sbuf_tensor("ktm", [128, 8 * 64], BF16))
        states = st.enter_context(nc.sbuf_tensor("states", [64, 8 * 64], F32))
        statesbf = st.enter_context(nc.sbuf_tensor("statesbf", [64, 8 * 64], BF16))
        sqscr = st.enter_context(nc.sbuf_tensor("sqscr", [64, 4 * LBS], F32))
        ssqrow = st.enter_context(nc.sbuf_tensor("ssqrow", [1, 2 * LBS], F32))
        ostage = st.enter_context(nc.sbuf_tensor("ostage", [128, 8 * LBS], F32))
        ddT = wqkv[:, 0:8 * LBS]
        qd = wqkv[:, 8 * LBS:12 * LBS]
        awm = wgate[:, 0:8 * LBS]
        qp = wgate[0:64, 8 * LBS:12 * LBS]
        pool = st.enter_context(nc.psum_tensor("pool", [128, 3584], F32))
        pbf = st.enter_context(nc.psum_tensor("pbf", [128, 1024], BF16))

        p = P(nc)

        def reg(off, width, parts=128):
            return pool[0:parts, off:off + width], psbk(off, width)

        def aset(s):
            return dict(
                qk=[reg(s * 1024 + i * 256, TTA) for i in range(4)],
                v=[reg(3072 + i * 256, TTA) for i in range(2)],
                gt=[reg(2048 + s * 512 + i * 256, TTA) for i in range(2)])
        asets = [aset(0), aset(1)]
        ps_aw = [reg(0, LBS), reg(512, LBS)]
        ps_att = reg(1024, LBS, parts=64)
        ps_state = reg(1536, 64, parts=64)
        ps_sq = reg(2048, LBS, parts=1)
        ps_out = [reg(2560, LBS), reg(3072, LBS)]
        k_pbf = ["psB7"]

        p.add("vector", lambda h: h.memset(onesf[:], 1.0), writes=["onesf"])
        p.add("sync", lambda h: h.dma_start(out=id64[:], in_=id64c.ap()),
              writes=["id64"], dma=True, group="gc")
        p.add("sync", lambda h: h.dma_start(out=kd[:], in_=kd_in),
              writes=["kd"], dma=True, group="gc")
        p.add("sync", lambda h: h.dma_start(out=bd[:], in_=bd_in),
              writes=["bd"], dma=True, group="gc")
        p.add("sync", lambda h: h.dma_start(out=normw[:], in_=normw_in),
              writes=["normw"], dma=True, group="gc")
        p.add("gpsimd", lambda h: h.dma_start(
            out=woutw[:].rearrange("p (c d) -> p c d", c=2), in_=wout_in),
            writes=["woutw"], dma=True, group="gc2")
        p.add("gpsimd", lambda h: h.dma_start(
            out=wqkv[:].rearrange("p (c m) -> p c m", m=768), in_=wqkv_in),
            writes=["wqkv"], dma=True, group="gw")
        p.add("gpsimd", lambda h: h.dma_start(
            out=wgate[:].rearrange("p (c m) -> p c m", m=256), in_=wgate_in),
            writes=["wgate"], dma=True, group="gw")
        for c2 in range(2):
            p.add("scalar", (lambda c2=c2: lambda h: h.activation(
                woutw[:, c2 * DM:(c2 + 1) * DM], woutw[:, c2 * DM:(c2 + 1) * DM],
                AF.Copy, scale=normw[:, c2:c2 + 1]))(),
                reads=["woutw", "normw"], writes=["woutw"], tag=f"woscale{c2}")

        for tt in range(NPT):
            par = tt % 2
            t0 = tt * TTA
            xv = lambda par=par: xtb[:, par * KC * TTA:(par + 1) * KC * TTA]
            p.add("gpsimd", (lambda tt=tt, par=par: lambda h: h.dma_start(
                out=xtb[:, par * KC * TTA:(par + 1) * KC * TTA]
                    .rearrange("p (c t) -> p c t", c=KC),
                in_=x0T_in[:, :, tt * TTA:(tt + 1) * TTA]))(),
                reads=[f"xtu{par}"], writes=[f"xtb{par}"], dma=True, group=f"gx{par}",
                tag=f"xld{tt}")

            ps_qk = asets[par]["qk"]
            ps_v = asets[par]["v"]
            ps_gt = asets[par]["gt"]
            for mi in range(4):
                def mm_qk(tt=tt, par=par, mi=mi, ps_qk=ps_qk):
                    def g(h):
                        last = None
                        for kc in range(KC):
                            last = h.matmul(
                                ps_qk[mi][0],
                                wqkv[:, kc * 768 + mi * 128: kc * 768 + (mi + 1) * 128],
                                xtb[:, par * KC * TTA + kc * TTA:
                                    par * KC * TTA + (kc + 1) * TTA],
                                start=(kc == 0), stop=(kc == KC - 1))
                        return last
                    return g
                p.add("tensor", mm_qk(), reads=["wqkv", f"xtb{par}"] + ps_qk[mi][1],
                      writes=ps_qk[mi][1], tag=f"mmqk{tt}_{mi}")
                dst = Q if mi < 2 else Kt
                dsn = "Q" if mi < 2 else "Kt"
                qi = mi % 2
                p.add("scalar", (lambda dst=dst, qi=qi, t0=t0, mi=mi, ps_qk=ps_qk:
                      lambda h: h.activation(dst[:, qi * T + t0: qi * T + t0 + TTA],
                                             ps_qk[mi][0], AF.Silu))(),
                      reads=ps_qk[mi][1], writes=[dsn], tag=f"evqk{tt}_{mi}")

            for ts in range(2):
                def mm_v(tt=tt, par=par, ts=ts, ps_v=ps_v):
                    def g(h):
                        last = None
                        for kc in range(KC):
                            last = h.matmul(
                                ps_v[ts][0],
                                xtb[:, par * KC * TTA + kc * TTA + ts * 128:
                                    par * KC * TTA + kc * TTA + (ts + 1) * 128],
                                wqkv[:, kc * 768 + 512: kc * 768 + 768],
                                start=(kc == 0), stop=(kc == KC - 1))
                        return last
                    return g
                p.add("tensor", mm_v(), reads=["wqkv", f"xtb{par}"] + ps_v[ts][1],
                      writes=ps_v[ts][1], tag=f"mmv{tt}_{ts}")
                ch = (t0 + ts * 128) // 128
                p.add("scalar", (lambda ch=ch, ts=ts, ps_v=ps_v: lambda h: h.activation(
                    V[:, ch * 256:(ch + 1) * 256], ps_v[ts][0], AF.Silu))(),
                    reads=ps_v[ts][1], writes=["V"], tag=f"evv{tt}_{ts}")

            for gi in range(2):
                def mm_gt(tt=tt, par=par, gi=gi, ps_gt=ps_gt):
                    def g(h):
                        last = None
                        for kc in range(KC):
                            last = h.matmul(
                                ps_gt[gi][0],
                                wgate[:, kc * 256 + gi * 128: kc * 256 + (gi + 1) * 128],
                                xtb[:, par * KC * TTA + kc * TTA:
                                    par * KC * TTA + (kc + 1) * TTA],
                                start=(kc == 0), stop=(kc == KC - 1))
                        return last
                    return g
                p.add("tensor", mm_gt(), reads=["wgate", f"xtb{par}"] + ps_gt[gi][1],
                      writes=ps_gt[gi][1] + ([f"xtu{par}"] if gi == 1 else []),
                      tag=f"mmgt{tt}_{gi}")
                p.add("scalar", (lambda gi=gi, t0=t0, ps_gt=ps_gt: lambda h: h.activation(
                    ga[:, gi * T + t0: gi * T + t0 + TTA], ps_gt[gi][0], AF.Sigmoid))(),
                    reads=ps_gt[gi][1], writes=["gate"], tag=f"evgt{tt}_{gi}")


        ps_aw = [[reg(i * 512 + jc * 256, LBS) for jc in range(2)] for i in range(4)]
        ps_att = [reg(2048 + i * 256, LBS, parts=64) for i in range(4)]
        ps_state = [reg(3072 + i * 64, 64, parts=64) for i in range(4)]
        ps_sq = reg(3328, LBS, parts=1)
        ps_out = [reg((oc % 4) * 512 + 256 * 0, LBS) for oc in range(4)]

        p.add("gpsimd", lambda h: h.nop(nofuse=True),
              writes=["wqkv", "wgate", "warena"], tag="a_arena_sw")
        p.add("gpsimd", lambda h: h.dma_start(
            out=ddT.rearrange("p (i jc l) -> p i jc l", i=4, jc=2), in_=ddT_in),
            reads=["warena"], writes=["ddT"], dma=True, group="gcc")
        p.add("gpsimd", lambda h: h.dma_start(
            out=qd.rearrange("p (i l) -> p i l", i=4), in_=qd_in),
            reads=["warena"], writes=["qd"], dma=True, group="gcc")

        for b2 in range(2):
            for blk in range(NBLK):
                bt0 = b2 * S + blk * LBS
                gch = bt0 // 128
                gt = b2 * NBLK + blk

                for i in range(4):
                    qi, pr = i // 2, i % 2
                    rows = slice(pr * 64, pr * 64 + 64)
                    for jc in range(2):
                        p.add("tensor", (lambda qi=qi, rows=rows, bt0=bt0, jc=jc, i=i:
                              lambda h: h.matmul(
                                  ps_aw[i][jc][0],
                                  Kt[rows, qi * T + bt0 + jc * 128:
                                     qi * T + bt0 + (jc + 1) * 128],
                                  Q[rows, qi * T + bt0: qi * T + bt0 + LBS],
                                  start=True, stop=True))(),
                              reads=["Q", "Kt"] + ps_aw[i][jc][1],
                              writes=ps_aw[i][jc][1],
                              tag=f"aw{b2}_{blk}_{i}_{jc}")
                for i in range(4):
                    qi, pr = i // 2, i % 2
                    rows = slice(pr * 64, pr * 64 + 64)
                    for jc in range(2):
                        p.add("vector", (lambda i=i, jc=jc: lambda h: h.tensor_tensor(
                            awm[:, (i * 2 + jc) * LBS:(i * 2 + jc + 1) * LBS],
                            ps_aw[i][jc][0],
                            ddT[:, (i * 2 + jc) * LBS:(i * 2 + jc + 1) * LBS],
                            ALU.mult))(),
                            reads=ps_aw[i][jc][1] + ["ddT", "warena", f"awmu{i}_{jc}"],
                            writes=[f"awm{i}_{jc}"] + ps_aw[i][jc][1],
                            tag=f"awm{b2}_{blk}_{i}_{jc}")
                    if blk > 0:
                        p.add("vector", (lambda i=i, qi=qi, rows=rows, bt0=bt0:
                              lambda h: h.tensor_tensor(
                                  qp[0:64, i * LBS:(i + 1) * LBS],
                                  Q[rows, qi * T + bt0: qi * T + bt0 + LBS],
                                  qd[rows, i * LBS:(i + 1) * LBS], ALU.mult))(),
                              reads=["Q", "qd", "warena", f"qpu{i}"], writes=[f"qp{i}"],
                              tag=f"qp{b2}_{blk}_{i}")
                for i in range(4):
                    pr = i % 2
                    rows = slice(pr * 64, pr * 64 + 64)
                    sslot = slice((b2 * 4 + i) * 64, (b2 * 4 + i) * 64 + 64)

                    def mm_att(b2=b2, blk=blk, i=i, gch=gch, sslot=sslot, rows=rows):
                        def g(h):
                            last = None
                            for jc in range(2):
                                last = h.matmul(
                                    ps_att[i][0], V[:, (gch + jc) * 256 + i * 64:
                                                    (gch + jc) * 256 + (i + 1) * 64],
                                    awm[:, (i * 2 + jc) * LBS:(i * 2 + jc + 1) * LBS],
                                    start=(jc == 0), stop=(blk == 0 and jc == 1))
                            if blk > 0:
                                last = h.matmul(
                                    ps_att[i][0], statesbf[:, sslot],
                                    qp[0:64, i * LBS:(i + 1) * LBS],
                                    start=False, stop=True)
                            return last
                        return g
                    rd = ["V", f"awm{i}_0", f"awm{i}_1"] + ps_att[i][1]
                    wr = ps_att[i][1] + [f"awmu{i}_0", f"awmu{i}_1"]
                    if blk > 0:
                        rd += [f"qp{i}", f"stbf{b2}{i}"]
                        wr += [f"qpu{i}"]
                    p.add("tensor", mm_att(), reads=rd, writes=wr,
                          tag=f"att{b2}_{blk}_{i}")
                for i in range(4):
                    qi, pr = i // 2, i % 2
                    rows = slice(pr * 64, pr * 64 + 64)
                    for jc in range(2):
                        p.add("tensor", (lambda qi=qi, rows=rows, bt0=bt0, jc=jc, i=i:
                              lambda h: h.transpose(
                                  pbf[:, (i * 2 + jc) * 64:(i * 2 + jc + 1) * 64],
                                  Kt[rows, qi * T + bt0 + jc * 128:
                                     qi * T + bt0 + (jc + 1) * 128], id64[rows, :]))(),
                              reads=["Kt", "id64"] + k_pbf, writes=k_pbf,
                              tag=f"ktr{b2}_{blk}_{i}_{jc}")
                for i in range(4):
                    for jc in range(2):
                        p.add("scalar", (lambda i=i, jc=jc: lambda h: h.activation(
                            ktm[:, (i * 2 + jc) * 64:(i * 2 + jc + 1) * 64],
                            pbf[:, (i * 2 + jc) * 64:(i * 2 + jc + 1) * 64],
                            AF.Copy, scale=kd[:, i * 2 + jc: i * 2 + jc + 1]))(),
                            reads=k_pbf + ["kd", f"ktmu{i}_{jc}"],
                            writes=[f"ktm{i}_{jc}"] + k_pbf,
                            tag=f"ktev{b2}_{blk}_{i}_{jc}")
                for i in range(4):
                    def mm_st(i=i, gch=gch):
                        def g(h):
                            last = None
                            for jc in range(2):
                                last = h.matmul(
                                    ps_state[i][0],
                                    ktm[:, (i * 2 + jc) * 64:(i * 2 + jc + 1) * 64],
                                    V[:, (gch + jc) * 256 + i * 64:
                                      (gch + jc) * 256 + (i + 1) * 64],
                                    start=(jc == 0), stop=(jc == 1))
                            return last
                        return g
                    p.add("tensor", mm_st(), reads=[f"ktm{i}_0", f"ktm{i}_1", "V"]
                          + ps_state[i][1],
                          writes=ps_state[i][1] + [f"ktmu{i}_0", f"ktmu{i}_1"],
                          tag=f"stmm{b2}_{blk}_{i}")
                for i in range(4):
                    sslot = slice((b2 * 4 + i) * 64, (b2 * 4 + i) * 64 + 64)
                    if blk == 0:
                        p.add("vector", (lambda sslot=sslot, i=i: lambda h:
                              h.tensor_copy(states[:, sslot], ps_state[i][0]))(),
                              reads=ps_state[i][1], writes=[f"st{b2}{i}"] + ps_state[i][1],
                              tag=f"stini{b2}_{i}")
                    else:
                        p.add("vector", (lambda sslot=sslot, i=i: lambda h:
                              h.scalar_tensor_tensor(
                                  states[:, sslot], states[:, sslot],
                                  bd[:, i:i + 1], ps_state[i][0],
                                  ALU.mult, ALU.add))(),
                              reads=ps_state[i][1] + [f"st{b2}{i}", "bd"],
                              writes=[f"st{b2}{i}"] + ps_state[i][1],
                              tag=f"stup{b2}_{blk}_{i}")
                    p.add("scalar", (lambda sslot=sslot: lambda h: h.copy(
                        statesbf[:, sslot], states[:, sslot]))(),
                        reads=[f"st{b2}{i}"], writes=[f"stbf{b2}{i}"],
                        tag=f"stbf{b2}_{blk}_{i}")
                for i in range(4):
                    p.add("scalar", (lambda i=i: lambda h: h.activation(
                        sqscr[:, i * LBS:(i + 1) * LBS], ps_att[i][0], AF.Square))(),
                        reads=ps_att[i][1] + [f"sqsu{i}"], writes=[f"sqscr{i}"],
                        tag=f"sq{b2}_{blk}_{i}")
                for i in range(4):
                    p.add("tensor", (lambda i=i: lambda h: h.matmul(
                        ps_sq[0], onesf[:, 0:1], sqscr[:, i * LBS:(i + 1) * LBS],
                        start=(i == 0), stop=(i == 3)))(),
                        reads=[f"sqscr{i}", "onesf"] + ps_sq[1],
                        writes=ps_sq[1] + [f"sqsu{i}"],
                        tag=f"sqmm{b2}_{blk}_{i}")
                for i in range(4):
                    qi, pr = i // 2, i % 2
                    rows = slice(pr * 64, pr * 64 + 64)
                    p.add("vector", (lambda qi=qi, rows=rows, bt0=bt0, i=i: lambda h:
                          h.tensor_tensor(
                              ga[rows, qi * T + bt0: qi * T + bt0 + LBS],
                              ps_att[i][0],
                              ga[rows, qi * T + bt0: qi * T + bt0 + LBS],
                              ALU.mult))(),
                          reads=ps_att[i][1] + [f"ga{gt}", f"sqscr{i}"],
                          writes=[f"ga{gt}"] + ps_att[i][1],
                          tag=f"gmul{b2}_{blk}_{i}")
                sqp = blk % 2
                p.add("scalar", (lambda sqp=sqp: lambda h: h.activation(
                    ssqrow[0:1, sqp * LBS:(sqp + 1) * LBS], ps_sq[0], AF.Copy))(),
                    reads=ps_sq[1] + [f"ssqu{sqp}"], writes=[f"ssqrow{sqp}"] + ps_sq[1],
                    tag=f"ssqev{b2}_{blk}")
                p.add("gpsimd", (lambda bt0=bt0, sqp=sqp: lambda h: h.dma_start(
                    out=sumsq_out[0:1, bt0:bt0 + LBS],
                    in_=ssqrow[0:1, sqp * LBS:(sqp + 1) * LBS]))(),
                    reads=[f"ssqrow{sqp}"], writes=[f"ssqu{sqp}", "ssqout"],
                    dma=True, group="gsq", tag=f"ssqst{b2}_{blk}")

                stp = blk % 2
                for oc in range(KC):
                    opar = oc % 4
                    grp = oc // 4

                    def mm_o(oc=oc, opar=opar, bt0=bt0):
                        def g(h):
                            last = None
                            for c2 in range(2):
                                last = h.matmul(
                                    ps_aw[opar][0][0],
                                    woutw[:, c2 * DM + oc * 128: c2 * DM + (oc + 1) * 128],
                                    ga[:, c2 * T + bt0: c2 * T + bt0 + LBS],
                                    start=(c2 == 0), stop=(c2 == 1))
                            return last
                        return g
                    p.add("tensor", mm_o(), reads=["woutw", f"ga{gt}"]
                          + ps_aw[opar][0][1] + ps_aw[opar][1][1],
                          writes=ps_aw[opar][0][1] + ps_aw[opar][1][1],
                          tag=f"mmo{b2}_{blk}_{oc}")
                    p.add("scalar", (lambda oc=oc, opar=opar, stp=stp: lambda h:
                          h.activation(ostage[:, (stp * 4 + oc % 4) * LBS:
                                              (stp * 4 + oc % 4 + 1) * LBS],
                                       ps_aw[opar][0][0], AF.Copy))(),
                          reads=ps_aw[opar][0][1] + ps_aw[opar][1][1]
                          + [f"osu{stp}_{grp}"],
                          writes=[f"ost{stp}_{oc % 4}"] + ps_aw[opar][0][1]
                          + ps_aw[opar][1][1],
                          tag=f"oev{b2}_{blk}_{oc}")
                    if oc % 4 == 3:
                        p.add("gpsimd", (lambda bt0=bt0, stp=stp, grp=grp: lambda h:
                              h.dma_start(
                                  out=partial_out[:, grp * 4:(grp + 1) * 4,
                                                  bt0:bt0 + LBS],
                                  in_=ostage[:, stp * 4 * LBS:(stp + 1) * 4 * LBS]
                                      .rearrange("p (c t) -> p c t", c=4)))(),
                              reads=[f"ost{stp}_{g4}" for g4 in range(4)],
                              writes=[f"osu{stp}_{grp}", "pout"]
                              + [f"ost{stp}_{g4}" for g4 in range(4)],
                              dma=True, group=f"go{stp}", tag=f"ost{b2}_{blk}_{grp}")

        with nc.Block() as block:
            p.emit_with_sems(st, block, final_engine="gpsimd",
                             final_waits=["pout", "ssqu"])
    return nc


def lightning_consts(heads):
    Hh = 32
    base = 1.0 / 2.0 ** (8.0 / Hh)
    rate = np.float32(base) ** (np.arange(Hh, dtype=np.float32) + 1.0)
    factor = np.float32(1.0 - 0.0 / (32 - 1 + 1e-5) + 1e-5)
    slope = (rate * factor).astype(np.float32)
    r = np.arange(LBS, dtype=np.float32) + 1.0
    ddT = np.zeros((128, 4, 2, LBS), np.float32)
    qdv = np.zeros((128, 4, LBS), np.float32)
    kdv = np.zeros((128, 8), np.float32)
    bdv = np.zeros((64, 4), np.float32)
    for ii, h in enumerate(heads):
        s = slope[h]
        qdv[:, ii, :] = np.exp(-s * r)[None, :]
        dd = r[:, None] - r[None, :]
        ddm = np.where(dd >= 0, np.exp(-s * dd), 0.0).astype(np.float32)
        ddTm = ddm.T
        for jc in range(2):
            ddT[:, ii, jc, :] = ddTm[jc * 128:(jc + 1) * 128, :]
        kdfull = np.exp(-s * (LBS - r)).astype(np.float32)
        for jc in range(2):
            kdv[:, ii * 2 + jc] = kdfull[jc * 128:(jc + 1) * 128]
        bdv[:, ii] = np.exp(np.float32(-s * LBS))
    return ddT, qdv, kdv, bdv


def host_A(x0, qkv_w, gate_w, out_w, norm_w):
    x0T = np.ascontiguousarray(
        x0.reshape(T, DM).T.reshape(KC, 128, T).transpose(1, 0, 2), dtype=np.float32)
    maps = []
    for c in range(NCORES):
        heads = [4 * c + i for i in range(4)]
        cols = []
        for grp in range(3):
            for i in range(4):
                h0 = heads[i] * 192 + grp * 64
                cols.extend(range(h0, h0 + 64))
        wq = qkv_w[cols, :]
        wqkv = np.ascontiguousarray(
            wq.T.reshape(KC, 128, 768).transpose(1, 0, 2), dtype=np.float32)
        gsl = gate_w[256 * c:256 * c + 256, :]
        wgate = np.ascontiguousarray(
            gsl.T.reshape(KC, 128, 256).transpose(1, 0, 2), dtype=np.float32)
        osl = out_w[:, 256 * c:256 * c + 256]
        wout = np.ascontiguousarray(
            osl.T.reshape(2, 128, DM).transpose(1, 0, 2), dtype=np.float32)
        nsl = np.ascontiguousarray(
            norm_w[256 * c:256 * c + 256].reshape(2, 128).T, dtype=np.float32)
        ddT, qdv, kdv, bdv = lightning_consts(heads)
        maps.append({"x0T_in": x0T, "wqkv_in": wqkv, "wgate_in": wgate,
                     "wout_in": wout, "normw_in": nsl, "ddT_in": ddT,
                     "qd_in": qdv, "kd_in": kdv, "bd_in": bdv})
    return maps


def run_A(x0, qkv_w, gate_w, out_w, norm_w):
    nc = build_A()
    res = run_bass_kernel_spmd(nc, host_A(x0, qkv_w, gate_w, out_w, norm_w),
                               core_ids=list(range(NCORES)))
    partials = [r["partial_out"] for r in res.results]
    sumsqs = [r["sumsq_out"][0] for r in res.results]
    return partials, sumsqs



PSBANK = 512


def psbk(off, width):
    return [f"psB{k}" for k in range(off // PSBANK, (off + width - 1) // PSBANK + 1)]


COMMON_ROWSLOTS = ("MEAN", "VAR", "AROW", "BROW", "ZERO", "SCR1")


def make_common(nc, st, p, pool, NT=TPC, offs=(0, 1024, 2048, 2560)):
    HT = NT // 2
    ct = dict(NT=NT, HT=HT)
    ct["ones"] = st.enter_context(nc.sbuf_tensor("c_ones", [128, 1], F32))
    ct["onesrow"] = st.enter_context(nc.sbuf_tensor("c_onesrow", [1, 128], F32))
    ct["eps"] = st.enter_context(nc.sbuf_tensor("c_eps", [1, 1], F32))
    ct["rows"] = st.enter_context(nc.sbuf_tensor("c_rows", [1, NT * len(COMMON_ROWSLOTS)], F32))
    ct["sqscr"] = st.enter_context(nc.sbuf_tensor("c_sqscr", [128, 2 * NT], F32))
    o_r, o_r2, o_a, o_b = offs
    ct["ps_row"] = pool[0:1, o_r:o_r + NT]
    ct["ps_row2"] = pool[0:1, o_r2:o_r2 + NT]
    ct["ps_a"] = pool[:, o_a:o_a + HT]
    ct["ps_b"] = pool[:, o_b:o_b + HT]
    ct["k_row"] = psbk(o_r, NT)
    ct["k_row2"] = psbk(o_r2, NT)
    ct["k_a"] = psbk(o_a, HT)
    ct["k_b"] = psbk(o_b, HT)
    for i, nm in enumerate(COMMON_ROWSLOTS):
        ct[nm] = i * NT
    p.add("vector", lambda h: h.memset(ct["ones"][:], 1.0), writes=["c_ones"])
    p.add("vector", lambda h: h.memset(ct["onesrow"][:], 1.0), writes=["c_onesrow"])
    p.add("vector", lambda h: h.memset(ct["eps"][:], 1e-5), writes=["c_eps"])
    p.add("vector", lambda h: h.memset(ct["rows"][0:1, ct["ZERO"] * 1: ct["ZERO"] + NT], 0.0),
          writes=["c_zero"])

    def row(slot, lo=0, hi=None):
        off = ct[slot]
        return ct["rows"][0:1, off + lo: off + (NT if hi is None else hi)]
    ct["row"] = row
    return ct


def add_ln_fm(p, ct, acc, acc_res, tag=""):
    NT, HT = ct["NT"], ct["HT"]
    row, ones, onesrow, eps = ct["row"], ct["ones"], ct["onesrow"], ct["eps"]
    sqscr, ps_row, ps_row2, ps_a, ps_b = (
        ct["sqscr"], ct["ps_row"], ct["ps_row2"], ct["ps_a"], ct["ps_b"])
    k_row, k_row2, k_a, k_b = ct["k_row"], ct["k_row2"], ct["k_a"], ct["k_b"]

    def mm_sum(h):
        last = None
        for c in range(KC):
            for hh in range(2):
                last = h.matmul(
                    ps_row[0:1, hh * HT:(hh + 1) * HT], ones[:, 0:1],
                    acc[:, c * NT + hh * HT: c * NT + (hh + 1) * HT],
                    start=(c == 0), stop=(c == KC - 1))
        return last
    p.add("tensor", mm_sum, reads=list(acc_res) + ["c_ones"], writes=k_row,
          tag=f"lnsum{tag}")

    for c in range(KC):
        buf = c % 2
        p.add("scalar",
              (lambda c=c, buf=buf: lambda h: h.activation(
                  sqscr[:, buf * NT:(buf + 1) * NT],
                  acc[:, c * NT:(c + 1) * NT], AF.Square))(),
              reads=list(acc_res), writes=[f"c_sqscr{buf}"], tag=f"sq{tag}{c}")

        def mmq(c=c, buf=buf):
            def g(h):
                last = None
                for hh in range(2):
                    last = h.matmul(
                        ps_row2[0:1, hh * HT:(hh + 1) * HT], ones[:, 0:1],
                        sqscr[:, buf * NT + hh * HT: buf * NT + (hh + 1) * HT],
                        start=(c == 0), stop=(c == KC - 1))
                return last
            return g
        p.add("tensor", mmq(), reads=[f"c_sqscr{buf}", "c_ones"],
              writes=k_row2, tag=f"mmq{tag}{c}")

    p.add("scalar", lambda h: h.activation(
        row("MEAN"), ps_row[0:1, :], AF.Copy, scale=1.0 / DM),
        reads=k_row, writes=["c_rMEAN"])
    p.add("scalar", lambda h: h.activation(
        row("VAR"), ps_row2[0:1, :], AF.Copy, scale=1.0 / DM),
        reads=k_row2, writes=["c_rVAR"])
    p.add("scalar", lambda h: h.activation(row("SCR1"), row("MEAN"), AF.Square),
          reads=["c_rMEAN"], writes=["c_rSCR1"])
    p.add("vector", lambda h: h.tensor_tensor(
        row("VAR"), row("VAR"), row("SCR1"), ALU.subtract),
        reads=["c_rVAR", "c_rSCR1"], writes=["c_rVAR"])
    p.add("scalar", lambda h: h.activation(
        row("VAR"), row("VAR"), AF.Sqrt, bias=eps[0:1, 0:1]),
        reads=["c_rVAR", "c_eps"], writes=["c_rVAR"])
    p.add("vector", lambda h: h.reciprocal(row("AROW"), row("VAR")),
          reads=["c_rVAR"], writes=["c_rAROW"])
    p.add("vector", lambda h: h.tensor_tensor(
        row("BROW"), row("MEAN"), row("AROW"), ALU.mult),
        reads=["c_rMEAN", "c_rAROW"], writes=["c_rBROW"])
    p.add("vector", lambda h: h.tensor_tensor(
        row("BROW"), row("ZERO"), row("BROW"), ALU.subtract),
        reads=["c_rBROW", "c_zero"], writes=["c_rBROW"])

    for hh in range(2):
        lo = hh * HT
        p.add("tensor",
              (lambda lo=lo: lambda h: h.matmul(
                  ps_a[:], onesrow[:], row("AROW", lo, lo + HT),
                  start=True, stop=True))(),
              reads=["c_rAROW", "c_onesrow"], writes=k_a, tag=f"bca{tag}{hh}")
        p.add("tensor",
              (lambda lo=lo: lambda h: h.matmul(
                  ps_b[:], onesrow[:], row("BROW", lo, lo + HT),
                  start=True, stop=True))(),
              reads=["c_rBROW", "c_onesrow"], writes=k_b, tag=f"bcb{tag}{hh}")
        for c in range(KC):
            p.add("vector",
                  (lambda c=c, lo=lo: lambda h: h.tensor_tensor(
                      acc[:, c * NT + lo: c * NT + lo + HT],
                      acc[:, c * NT + lo: c * NT + lo + HT],
                      ps_a[:], ALU.mult))(),
                  reads=list(acc_res) + k_a, writes=list(acc_res),
                  tag=f"lnm{tag}{hh}_{c}")
            p.add("vector",
                  (lambda c=c, lo=lo: lambda h: h.tensor_tensor(
                      acc[:, c * NT + lo: c * NT + lo + HT],
                      acc[:, c * NT + lo: c * NT + lo + HT],
                      ps_b[:], ALU.add))(),
                  reads=list(acc_res) + k_b, writes=list(acc_res),
                  tag=f"lna{tag}{hh}_{c}")


def build_B(upto=99):
    nc = bass.Bass("TRN2", target_bir_lowering=False)
    p8 = nc.dram_tensor("p8", [8, 128, KC, TPC], F32, kind="ExternalInput").ap()
    sq8 = nc.dram_tensor("sq8", [8, TPC], F32, kind="ExternalInput").ap()
    x0c = nc.dram_tensor("x0c", [128, KC, TPC], F32, kind="ExternalInput").ap()
    x1_out = nc.dram_tensor("x1_out", [128, KC, TPC], F32, kind="ExternalOutput").ap()
    srp_out = nc.dram_tensor("srp_out", [128, KC], F32, kind="ExternalOutput").ap()

    HT = TPC // 2

    import contextlib
    with contextlib.ExitStack() as st:
        acc = st.enter_context(nc.sbuf_tensor("acc", [128, KC * TPC], F32))
        x0 = st.enter_context(nc.sbuf_tensor("x0", [128, KC * TPC], F32))
        sqs = st.enter_context(nc.sbuf_tensor("sq", [8, TPC], F32))
        srp = st.enter_context(nc.sbuf_tensor("srp", [128, KC], F32))
        pool = st.enter_context(nc.psum_tensor("pool", [128, 4096], F32))

        p = P(nc)
        ct = make_common(nc, st, p, pool, NT=TPC)
        row, ones, onesrow, eps = ct["row"], ct["ones"], ct["onesrow"], ct["eps"]
        ps_row, ps_a = ct["ps_row"], ct["ps_a"]
        k_row, k_a = ct["k_row"], ct["k_a"]
        ACCV = lambda: acc[:].rearrange("p (c t) -> p c t", c=KC)

        p.add("vector", lambda h: h.memset(srp[:], 0.0), writes=["srp"])

        p.add("sync", lambda h: h.dma_start(
            out=x0[:].rearrange("p (c t) -> p c t", c=KC), in_=x0c),
            writes=["x0"], dma=True, group="gx0")
        p.add("sync", lambda h: h.dma_start(out=sqs[:], in_=sq8),
              writes=["sq"], dma=True, group="gsq")

        NSL = (KC * TPC) // 2048
        for j in range(8):
            for s in range(NSL):
                sl_out = acc[:, s * 2048:(s + 1) * 2048]
                if j == 0:
                    p.add("gpsimd", (lambda s=s, sl_out=sl_out: lambda h: h.dma_start(
                        out=sl_out, in_=p8[0, :, 2 * s:2 * s + 2, :]))(),
                        writes=[f"acc{s}"], dma=True, group=f"gacc{s}")
                else:
                    p.add("gpsimd", (lambda j=j, s=s, sl_out=sl_out: lambda h:
                          h.dma_start(out=sl_out, in_=p8[j, :, 2 * s:2 * s + 2, :],
                                      accum_op=ALU.add))(),
                          reads=[f"acc{s}"], writes=[f"acc{s}"], dma=True,
                          group=f"gacc{s}")
        ACC_RES = [f"acc{s}" for s in range(NSL)]

        if upto >= 2:
            def mm_sqsum(h):
                h.matmul(ps_row[0:1, 0:HT], ones[0:8, 0:1], sqs[:, 0:HT],
                         start=True, stop=True)
                return h.matmul(ps_row[0:1, HT:TPC], ones[0:8, 0:1], sqs[:, HT:TPC],
                                start=True, stop=True)
            p.add("tensor", mm_sqsum, reads=["sq", "c_ones"], writes=k_row)
            p.add("scalar", lambda h: h.activation(
                row("VAR"), ps_row[0:1, :], AF.Sqrt, bias=eps[0:1, 0:1], scale=1.0 / DM),
                reads=k_row + ["c_eps"], writes=["c_rVAR"])
            p.add("vector", lambda h: h.reciprocal(row("MEAN"), row("VAR")),
                  reads=["c_rVAR"], writes=["c_rRSTD1"])

            for hh in range(2):
                lo = hh * HT
                p.add("tensor",
                      (lambda lo=lo: lambda h: h.matmul(
                          ps_a[:], onesrow[:], row("MEAN", lo, lo + HT),
                          start=True, stop=True))(),
                      reads=["c_rRSTD1", "c_onesrow"], writes=k_a)
                for c in range(KC):
                    p.add("vector",
                          (lambda c=c, lo=lo: lambda h: h.tensor_tensor(
                              acc[:, c * TPC + lo: c * TPC + lo + HT],
                              acc[:, c * TPC + lo: c * TPC + lo + HT],
                              ps_a[:], ALU.mult))(),
                          reads=ACC_RES + k_a, writes=ACC_RES, tag=f"sc{hh}_{c}")
                    p.add("vector",
                          (lambda c=c, lo=lo: lambda h: h.tensor_tensor(
                              acc[:, c * TPC + lo: c * TPC + lo + HT],
                              acc[:, c * TPC + lo: c * TPC + lo + HT],
                              x0[:, c * TPC + lo: c * TPC + lo + HT], ALU.add))(),
                          reads=ACC_RES + ["x0"], writes=ACC_RES, tag=f"ax{hh}_{c}")

        if upto >= 3:
            add_ln_fm(p, ct, acc, ACC_RES, tag="B")

        if upto >= 4:
            def srp_f(h):
                last = None
                for c in range(KC):
                    last = h.reduce_sum(srp[:, c:c + 1],
                                        acc[:, c * TPC:(c + 1) * TPC], axis=AX.X)
                return last
            p.add("vector", srp_f, reads=ACC_RES, writes=["srp"])

        p.add("gpsimd", lambda h: h.dma_start(out=x1_out, in_=ACCV()),
              reads=ACC_RES, writes=["out1"], dma=True, group="gout")
        p.add("gpsimd", lambda h: h.dma_start(out=srp_out, in_=srp[:]),
              reads=["srp"], writes=["out2"], dma=True, group="gout")

        with nc.Block() as block:
            p.emit_with_sems(st, block, final_engine="gpsimd",
                             final_waits=["out1", "out2"])
    return nc



def build_C():
    nc = bass.Bass("TRN2", target_bir_lowering=False)
    ident64 = nc.inline_tensor(np.eye(64, dtype=bf16), name="identity64c")
    x1_in = nc.dram_tensor("x1_in", [128, KC, TPC], F32, kind="ExternalInput").ap()
    srp4 = nc.dram_tensor("srp4", [128, 4 * KC], F32, kind="ExternalInput").ap()
    simm = nc.dram_tensor("simm", [128, KC, 32], F32, kind="ExternalInput").ap()
    gates_in = nc.dram_tensor("gates_in", [1, 32], F32, kind="ExternalInput").ap()
    wv_in = nc.dram_tensor("wv_in", [32, 64, DM], F32, kind="ExternalInput").ap()
    ow_in = nc.dram_tensor("ow_in", [32, 64, DM], F32, kind="ExternalInput").ap()
    x2_out = nc.dram_tensor("x2_out", [128, KC, TPC], F32, kind="ExternalOutput").ap()
    rt_out = nc.dram_tensor("rt_out", [1, 32], F32, kind="ExternalOutput").ap()

    HT = TPC // 2
    NE = 32

    import contextlib
    with contextlib.ExitStack() as st:
        x1 = st.enter_context(nc.sbuf_tensor("x1", [128, KC * TPC], F32))
        x1b = st.enter_context(nc.sbuf_tensor("x1b", [128, KC * TPC], BF16))
        srp_sb = st.enter_context(nc.sbuf_tensor("srp_sb", [128, 4 * KC], F32))
        s_sb = st.enter_context(nc.sbuf_tensor("s_sb", [128, KC], F32))
        s2_sb = st.enter_context(nc.sbuf_tensor("s2_sb", [128, KC], F32))
        sim_sb = st.enter_context(nc.sbuf_tensor("sim_sb", [128, KC * 32], F32))
        sim2_sb = st.enter_context(nc.sbuf_tensor("sim2_sb", [128, KC * 32], F32))
        id64 = st.enter_context(nc.sbuf_tensor("id64", [64, 64], BF16))
        g_sb = st.enter_context(nc.sbuf_tensor("g_sb", [1, 32 * 12], F32))
        r_bc = st.enter_context(nc.sbuf_tensor("r_bc", [64, 32], F32))
        rIe = st.enter_context(nc.sbuf_tensor("rIe", [64, 2 * 64], BF16))
        wvb = st.enter_context(nc.sbuf_tensor("wvb", [64, 2 * DM], BF16))
        owb = st.enter_context(nc.sbuf_tensor("owb", [64, 2 * DM], BF16))
        weT = st.enter_context(nc.sbuf_tensor("weT", [128, KC * 64], BF16))
        weTacc = st.enter_context(nc.sbuf_tensor("weTacc", [128, KC * 64], F32))
        wod = st.enter_context(nc.sbuf_tensor("wod", [64, DM], BF16))
        comb = st.enter_context(nc.sbuf_tensor("comb", [64, TPC], BF16))
        pool = st.enter_context(nc.psum_tensor("pool", [128, 4096], F32))
        ps_wo = pool[0:64, 1024:1024 + DM]
        k_wo = psbk(1024, DM)
        ps_c = pool[0:64, 3072:3072 + HT]
        k_c = psbk(3072, HT)
        ps_f = pool[:, 3584:3584 + HT]
        k_f = psbk(3584, HT)
        ps_g = pool[0:1, 3072:3072 + 128]
        k_g = psbk(3072, 128)
        ps_rb = pool[0:64, 3584:3584 + 32]
        k_rb = psbk(3584, 32)

        p = P(nc)
        ct = make_common(nc, st, p, pool, NT=TPC)
        row, ones, onesrow, eps = ct["row"], ct["ones"], ct["onesrow"], ct["eps"]

        X2V = lambda: x1[:].rearrange("p (c t) -> p c t", c=KC)

        GS = {nm: i * 32 for i, nm in enumerate(
            ["AFF", "LOG", "GATED", "MASKA", "WORK", "FB", "IND", "SCR",
             "SEL", "ROUT", "EXP", "SC1"])}

        def grow(nm, lo=0, hi=32):
            off = GS[nm]
            return g_sb[0:1, off + lo: off + hi]

        p.add("sync", lambda h: h.dma_start(out=X2V(), in_=x1_in),
              writes=["x1"], dma=True, group="gx1")
        p.add("gpsimd", lambda h: h.dma_start(
            out=x1b[:].rearrange("p (c t) -> p c t", c=KC), in_=x1_in),
            writes=["x1b"], dma=True, group="gx1b")
        p.add("sync", lambda h: h.dma_start(out=srp_sb[:], in_=srp4),
              writes=["srp_sb"], dma=True, group="gsrp")
        p.add("sync", lambda h: h.dma_start(
            out=sim_sb[:].rearrange("p (c e) -> p c e", e=32), in_=simm),
            writes=["sim_sb"], dma=True, group="gsim")
        p.add("sync", lambda h: h.dma_start(out=grow("LOG"), in_=gates_in),
              writes=["ggates", "gLOG"], dma=True, group="ggate")
        p.add("sync", lambda h: h.dma_start(out=id64[:], in_=ident64.ap()),
              writes=["id64"], dma=True, group="gid")

        def T(*slots):
            return [f"g{s}" for s in slots]

        p.add("vector", lambda h: h.tensor_tensor(
            s_sb[:], srp_sb[:, 0:KC], srp_sb[:, KC:2 * KC], ALU.add),
            reads=["srp_sb"], writes=["s_sb"])
        p.add("vector", lambda h: h.tensor_tensor(
            s2_sb[:], srp_sb[:, 2 * KC:3 * KC], srp_sb[:, 3 * KC:4 * KC], ALU.add),
            reads=["srp_sb"], writes=["s2_sb"])
        p.add("vector", lambda h: h.tensor_tensor(
            s_sb[:], s_sb[:], s2_sb[:], ALU.add),
            reads=["s_sb", "s2_sb"], writes=["s_sb"])
        p.add("scalar", lambda h: h.activation(s2_sb[:], s_sb[:], AF.Square),
              reads=["s_sb"], writes=["s2_sb"])
        p.add("tensor", lambda h: h.matmul(ps_g[0:1, 0:KC], ones[:, 0:1], s2_sb[:],
                                           start=True, stop=True),
              reads=["s2_sb", "c_ones"], writes=k_g)
        p.add("vector", lambda h: h.reduce_sum(grow("SC1", 0, 1), ps_g[0:1, 0:KC], axis=AX.X),
              reads=k_g, writes=T("SC1_0"))
        p.add("scalar", lambda h: h.activation(sim2_sb[:], sim_sb[:], AF.Square),
              reads=["sim_sb"], writes=["sim2_sb"])

        def mm_cn(h):
            last = None
            for c in range(KC):
                last = h.matmul(ps_g[0:1, 32:64], ones[:, 0:1],
                                sim2_sb[:, c * 32:(c + 1) * 32],
                                start=(c == 0), stop=(c == KC - 1))
            return last
        p.add("tensor", mm_cn, reads=["sim2_sb", "c_ones"] + k_g, writes=k_g)

        def mm_num(h):
            last = None
            for c in range(KC):
                last = h.matmul(ps_g[0:1, 64:96], s_sb[:, c:c + 1],
                                sim_sb[:, c * 32:(c + 1) * 32],
                                start=(c == 0), stop=(c == KC - 1))
            return last
        p.add("tensor", mm_num, reads=["sim_sb", "s_sb"] + k_g, writes=k_g)

        p.add("vector", lambda h: h.memset(grow("SC1", 3, 4), 0.0), writes=T("SC1_3"))
        p.add("vector", lambda h: h.memset(grow("SC1", 5, 6), 1.0e30), writes=T("SC1_5"))
        p.add("vector", lambda h: h.memset(grow("SC1", 8, 9), 1.0), writes=T("SC1_8"))
        p.add("vector", lambda h: h.memset(grow("SC1", 9, 10), 3.4e38), writes=T("SC1_9"))

        p.add("scalar", lambda h: h.activation(grow("SC1", 1, 2), grow("SC1", 0, 1), AF.Sqrt),
              reads=T("SC1_0"), writes=T("SC1_1"))
        p.add("vector", lambda h: h.reciprocal(grow("SC1", 2, 3), grow("SC1", 1, 2)),
              reads=T("SC1_1"), writes=T("SC1_2"))
        p.add("scalar", lambda h: h.activation(grow("SCR"), ps_g[0:1, 32:64], AF.Sqrt),
              reads=k_g, writes=T("SCR"))
        p.add("vector", lambda h: h.reciprocal(grow("IND"), grow("SCR")),
              reads=T("SCR"), writes=T("IND"))
        p.add("vector", lambda h: h.tensor_tensor(
            grow("AFF"), ps_g[0:1, 64:96], grow("IND"), ALU.mult),
            reads=k_g + T("IND"), writes=T("AFF"))
        p.add("vector", lambda h: h.tensor_single_scalar(
            grow("AFF"), grow("AFF"), grow("SC1", 2, 3), ALU.mult),
            reads=T("AFF", "SC1_2"), writes=T("AFF"))
        p.add("scalar", lambda h: h.activation(grow("SCR"), grow("LOG"), AF.Sigmoid),
              reads=["ggates"] + T("LOG"), writes=T("SCR"))
        p.add("vector", lambda h: h.tensor_tensor(
            grow("LOG"), grow("AFF"), grow("SCR"), ALU.subtract),
            reads=T("AFF", "SCR", "LOG") + ["ggates"], writes=T("LOG"))
        p.add("vector", lambda h: h.tensor_relu(grow("GATED"), grow("LOG")),
              reads=T("LOG"), writes=T("GATED"))
        p.add("vector", lambda h: h.tensor_single_scalar(
            grow("MASKA"), grow("GATED"), grow("SC1", 3, 4), ALU.is_gt),
            reads=T("GATED", "SC1_3"), writes=T("MASKA"))

        p.add("vector", lambda h: h.tensor_copy(grow("WORK"), grow("LOG")),
              reads=T("LOG"), writes=T("WORK"))
        p.add("vector", lambda h: h.memset(grow("FB"), 0.0), writes=T("FB"))
        for it in range(16):
            p.add("vector", lambda h: h.reduce_max(grow("SC1", 4, 5), grow("WORK"), axis=AX.X),
                  reads=T("WORK"), writes=T("SC1_4"), tag=f"tk{it}")
            p.add("vector", lambda h: h.tensor_single_scalar(
                grow("IND"), grow("WORK"), grow("SC1", 4, 5), ALU.is_ge),
                reads=T("WORK", "SC1_4"), writes=T("IND"))
            p.add("vector", lambda h: h.tensor_tensor(
                grow("FB"), grow("FB"), grow("IND"), ALU.max),
                reads=T("FB", "IND"), writes=T("FB"))
            p.add("vector", lambda h: h.tensor_single_scalar(
                grow("IND"), grow("IND"), grow("SC1", 5, 6), ALU.mult),
                reads=T("IND", "SC1_5"), writes=T("IND"))
            p.add("vector", lambda h: h.tensor_tensor(
                grow("WORK"), grow("WORK"), grow("IND"), ALU.subtract),
                reads=T("WORK", "IND"), writes=T("WORK"))
        p.add("vector", lambda h: h.reduce_sum(grow("SC1", 6, 7), grow("MASKA"), axis=AX.X),
              reads=T("MASKA"), writes=T("SC1_6"))
        p.add("vector", lambda h: h.tensor_single_scalar(
            grow("SC1", 7, 8), grow("SC1", 6, 7), grow("SC1", 3, 4), ALU.is_le),
            reads=T("SC1_6", "SC1_3"), writes=T("SC1_7"))
        p.add("vector", lambda h: h.tensor_tensor(
            grow("SCR"), grow("MASKA"), grow("FB"), ALU.max),
            reads=T("MASKA", "FB"), writes=T("SCR"))
        p.add("vector", lambda h: h.tensor_tensor(
            grow("SCR"), grow("SCR"), grow("MASKA"), ALU.subtract),
            reads=T("SCR", "MASKA"), writes=T("SCR"))
        p.add("vector", lambda h: h.tensor_single_scalar(
            grow("SCR"), grow("SCR"), grow("SC1", 7, 8), ALU.mult),
            reads=T("SCR", "SC1_7"), writes=T("SCR"))
        p.add("vector", lambda h: h.tensor_tensor(
            grow("MASKA"), grow("MASKA"), grow("SCR"), ALU.add),
            reads=T("MASKA", "SCR"), writes=T("MASKA"))
        p.add("vector", lambda h: h.tensor_tensor(
            grow("SEL"), grow("GATED"), grow("MASKA"), ALU.mult),
            reads=T("GATED", "MASKA"), writes=T("SEL"))
        p.add("vector", lambda h: h.tensor_single_scalar(
            grow("SCR"), grow("MASKA"), grow("SC1", 8, 9), ALU.subtract),
            reads=T("MASKA", "SC1_8"), writes=T("SCR"))
        p.add("vector", lambda h: h.tensor_single_scalar(
            grow("SCR"), grow("SCR"), grow("SC1", 9, 10), ALU.mult),
            reads=T("SCR", "SC1_9"), writes=T("SCR"))
        p.add("vector", lambda h: h.tensor_tensor(
            grow("SEL"), grow("SEL"), grow("SCR"), ALU.add),
            reads=T("SEL", "SCR"), writes=T("SEL"))
        p.add("vector", lambda h: h.reduce_max(grow("SC1", 10, 11), grow("SEL"),
                                               axis=AX.X, negate=True),
              reads=T("SEL"), writes=T("SC1_10"))
        p.add("scalar", lambda h: h.activation(
            grow("EXP"), grow("SEL"), AF.Exp, bias=grow("SC1", 10, 11),
            accum_out=grow("SC1", 11, 12)),
            reads=T("SEL", "SC1_10"), writes=T("EXP", "SC1_11"))
        p.add("vector", lambda h: h.reciprocal(grow("SC1", 10, 11), grow("SC1", 11, 12)),
              reads=T("SC1_11"), writes=T("SC1_10"))
        p.add("vector", lambda h: h.tensor_single_scalar(
            grow("ROUT"), grow("EXP"), grow("SC1", 10, 11), ALU.mult),
            reads=T("EXP", "SC1_10"), writes=T("ROUT") + ["grout"])
        p.add("tensor", lambda h: h.matmul(
            ps_rb, onesrow[0:1, 0:64], grow("ROUT"), start=True, stop=True),
            reads=T("ROUT") + ["c_onesrow"] + k_rb, writes=k_rb, tag="rbc")
        p.add("vector", lambda h: h.tensor_copy(r_bc[:], ps_rb),
            reads=k_rb, writes=["r_bc"])

        p.add("vector", lambda h: h.memset(weTacc[:], 0.0), writes=["weTacc"])
        for e in range(NE):
            par = e % 2
            p.add("gpsimd", (lambda e=e, par=par: lambda h: h.dma_start(
                out=wvb[:, par * DM:(par + 1) * DM], in_=wv_in[e]))(),
                reads=[f"wvuse{par}"], writes=[f"wvb{par}"], dma=True, group="gwv",
                tag=f"wvld{e}")
            p.add("gpsimd", (lambda e=e, par=par: lambda h: h.dma_start(
                out=owb[:, par * DM:(par + 1) * DM], in_=ow_in[e]))(),
                reads=[f"owuse{par}"], writes=[f"owb{par}"], dma=True, group="gow",
                tag=f"owld{e}")
            p.add("vector", (lambda e=e, par=par: lambda h: h.tensor_single_scalar(
                rIe[:, par * 64:(par + 1) * 64], id64[:], r_bc[0:64, e:e + 1], ALU.mult))(),
                reads=["id64", "r_bc", f"rIuse{par}"], writes=[f"rIe{par}"],
                tag=f"rIe{e}")

            for cc in range(KC):
                slot_off = (cc % 2) * 512 + ((cc // 2) % 8) * 64
                kslot = [f"psB{slot_off // PSBANK}"]
                p.add("tensor", (lambda e=e, par=par, cc=cc, slot_off=slot_off: lambda h: h.matmul(
                    pool[:, slot_off:slot_off + 64],
                    wvb[:, par * DM + cc * 128: par * DM + (cc + 1) * 128],
                    rIe[:, par * 64:(par + 1) * 64],
                    start=True, stop=True))(),
                    reads=[f"wvb{par}", f"rIe{par}"] + kslot, writes=kslot,
                    tag=f"mmwe{e}_{cc}")
                p.add("vector", (lambda cc=cc, slot_off=slot_off: lambda h: h.tensor_tensor(
                    weTacc[:, cc * 64:(cc + 1) * 64],
                    weTacc[:, cc * 64:(cc + 1) * 64],
                    pool[:, slot_off:slot_off + 64], ALU.add))(),
                    reads=["weTacc"] + kslot, writes=["weTacc"] + kslot,
                    tag=f"weadd{e}_{cc}")
            p.add("gpsimd", (lambda par=par: lambda h: h.nop(nofuse=True))(),
                  reads=[f"wvb{par}"], writes=[f"wvuse{par}"], tag=f"wvfree{e}")

            def mm_wo(e=e, par=par):
                def g(h):
                    last = None
                    for oc in range(4):
                        last = h.matmul(
                            ps_wo[:, oc * 512:(oc + 1) * 512],
                            rIe[:, par * 64:(par + 1) * 64],
                            owb[:, par * DM + oc * 512: par * DM + (oc + 1) * 512],
                            start=(e == 0), stop=(e == NE - 1))
                    return last
                return g
            p.add("tensor", mm_wo(), reads=[f"owb{par}", f"rIe{par}"] + k_wo,
                  writes=k_wo + [f"owuse{par}", f"rIuse{par}"],
                  tag=f"mmwo{e}")

        p.add("scalar", lambda h: h.activation(weT[:], weTacc[:], AF.Copy),
              reads=["weTacc"], writes=["weT"])
        p.add("scalar", lambda h: h.activation(wod[:], ps_wo, AF.Copy),
              reads=k_wo, writes=["wod"])

        for th in range(2):
            lo = th * HT

            def mm_c(th=th, lo=lo):
                def g(h):
                    last = None
                    for cc in range(KC):
                        last = h.matmul(
                            ps_c, weT[:, cc * 64:(cc + 1) * 64],
                            x1b[:, cc * TPC + lo: cc * TPC + lo + HT],
                            start=(cc == 0), stop=(cc == KC - 1))
                    return last
                return g
            p.add("tensor", mm_c(), reads=["weT", "x1b"] + k_c, writes=k_c,
                  tag=f"mmc{th}")
            p.add("scalar", (lambda lo=lo: lambda h: h.activation(
                comb[:, lo:lo + HT], ps_c, AF.Copy))(),
                reads=k_c, writes=["comb"], tag=f"combev{th}")

            for oc in range(KC):
                p.add("tensor", (lambda oc=oc, lo=lo: lambda h: h.matmul(
                    ps_f, wod[:, oc * 128:(oc + 1) * 128],
                    comb[:, lo:lo + HT], start=True, stop=True))(),
                    reads=["wod", "comb"] + k_f, writes=k_f, tag=f"mmf{th}_{oc}")
                p.add("vector", (lambda oc=oc, lo=lo: lambda h: h.tensor_tensor(
                    x1[:, oc * TPC + lo: oc * TPC + lo + HT],
                    ps_f,
                    x1[:, oc * TPC + lo: oc * TPC + lo + HT], ALU.add))(),
                    reads=k_f + ["x1"], writes=["x1"], tag=f"resf{th}_{oc}")

        add_ln_fm(p, ct, x1, ["x1"], tag="C")

        p.add("gpsimd", lambda h: h.dma_start(out=x2_out, in_=X2V()),
              reads=["x1"], writes=["o1"], dma=True, group="gout")
        p.add("gpsimd", lambda h: h.dma_start(out=rt_out, in_=grow("ROUT")),
              reads=["grout"], writes=["o2"], dma=True, group="gout")

        with nc.Block() as block:
            p.emit_with_sems(st, block, final_engine="gpsimd",
                             final_waits=["o1", "o2"])
    return nc


def host_C(x1T, srps, sim_matrix, gates, w_v, o_weights):
    maps = []
    simm = _f32(sim_matrix.reshape(KC, 128, 32).transpose(1, 0, 2))
    wv = _f32(w_v)
    ow = _f32(o_weights)
    g1 = _f32(np.asarray(gates).reshape(1, 32))
    for c in range(NCORES):
        b = c // 4
        sl = slice(c * TPC, (c + 1) * TPC)
        x1c = _f32(x1T[:, sl].reshape(KC, 128, TPC).transpose(1, 0, 2))
        srp4 = np.concatenate([srps[4 * b + j] for j in range(4)], axis=1)
        maps.append({"x1_in": x1c, "srp4": _f32(srp4), "simm": simm,
                     "gates_in": g1, "wv_in": wv, "ow_in": ow})
    return maps


def run_C(x1T, srps, sim_matrix, gates, w_v, o_weights):
    nc = build_C()
    res = run_bass_kernel_spmd(nc, host_C(x1T, srps, sim_matrix, gates, w_v, o_weights),
                               core_ids=list(range(NCORES)))
    x2T = np.concatenate(
        [r["x2_out"].transpose(1, 0, 2).reshape(DM, TPC) for r in res.results], axis=1)
    return x2T, [r["rt_out"] for r in res.results]




def build_D():
    nc = bass.Bass("TRN2", target_bir_lowering=False)
    x2T_in = nc.dram_tensor("x2T_in", [128, KC, T], F32, kind="ExternalInput").ap()
    wq_in = nc.dram_tensor("wq_in", [128, KC, 256], F32, kind="ExternalInput").ap()
    wk_in = nc.dram_tensor("wk_in", [128, KC, 256], F32, kind="ExternalInput").ap()
    wv_in = nc.dram_tensor("wv_in", [128, KC, 256], F32, kind="ExternalInput").ap()
    bq_in = nc.dram_tensor("bq_in", [128, 2], F32, kind="ExternalInput").ap()
    bk_in = nc.dram_tensor("bk_in", [128, 2], F32, kind="ExternalInput").ap()
    bv_in = nc.dram_tensor("bv_in", [1, 256], F32, kind="ExternalInput").ap()
    maskcT_in = nc.dram_tensor("maskcT_in", [128, 4, 512], F32, kind="ExternalInput").ap()
    attn_out = nc.dram_tensor("attn_out", [2, 2, 128, S], BF16, kind="ExternalOutput").ap()

    TTD = 256
    NPT = T // TTD
    NIG = S // 512
    RS = 0.08838834764831845

    import contextlib
    with contextlib.ExitStack() as st:
        xtb = st.enter_context(nc.sbuf_tensor("xtb", [128, 2 * KC * TTD], BF16))
        wq = st.enter_context(nc.sbuf_tensor("wq", [128, KC * 256], BF16))
        wk = st.enter_context(nc.sbuf_tensor("wk", [128, KC * 256], BF16))
        wv = st.enter_context(nc.sbuf_tensor("wv", [128, KC * 256], BF16))
        bq = st.enter_context(nc.sbuf_tensor("bq", [128, 2], F32))
        bk = st.enter_context(nc.sbuf_tensor("bk", [128, 2], F32))
        bqs = st.enter_context(nc.sbuf_tensor("bqs", [128, 2], F32))
        rs_t = st.enter_context(nc.sbuf_tensor("rs_t", [128, 1], F32))
        rsneg = st.enter_context(nc.sbuf_tensor("rsneg", [1, 1], F32))
        bvb = st.enter_context(nc.sbuf_tensor("bvb", [128, 256], F32))
        onesrow = st.enter_context(nc.sbuf_tensor("onesrow", [1, 128], F32))
        onesbf1 = st.enter_context(nc.sbuf_tensor("onesbf1", [1, 128], BF16))
        onesbc = st.enter_context(nc.sbuf_tensor("onesbc", [128, 1], BF16))
        maskcT = st.enter_context(nc.sbuf_tensor("maskcT", [128, 4 * 512], F32))
        Q = st.enter_context(nc.sbuf_tensor("Q", [128, 2 * T], BF16))
        Kt = st.enter_context(nc.sbuf_tensor("Kt", [128, 2 * T], BF16))
        V = st.enter_context(nc.sbuf_tensor("V", [128, (T // 128) * 256], BF16))
        ptile = st.enter_context(nc.sbuf_tensor("ptile", [128, 2 * 512], BF16))
        sqt = st.enter_context(nc.sbuf_tensor("sqt", [128, 2 * 512], BF16))
        rowq = st.enter_context(nc.sbuf_tensor("rowq", [1, S], F32))
        rowk = st.enter_context(nc.sbuf_tensor("rowk", [1, 512], F32))
        kmx = st.enter_context(nc.sbuf_tensor("kmx", [1, 4], F32))
        nbb = st.enter_context(nc.sbuf_tensor("nbb", [1, S], BF16))
        lrow = st.enter_context(nc.sbuf_tensor("lrow", [1, 2 * 512], F32))
        rlrow = st.enter_context(nc.sbuf_tensor("rlrow", [1, 2 * 512], F32))
        rlb = st.enter_context(nc.sbuf_tensor("rlb", [128, 2 * 512], F32))
        astage = st.enter_context(nc.sbuf_tensor("astage", [128, 2 * 512], BF16))
        pool = st.enter_context(nc.psum_tensor("pool", [128, 4096], F32))

        p = P(nc)

        def reg(off, width, parts=128):
            return pool[0:parts, off:off + width], psbk(off, width)

        ps_st = [reg(0, 512), reg(512, 512)]
        ps_pv = [reg(1024, 512), reg(1536, 512)]
        ps_l = [reg(2048, 512, parts=1), reg(2560, 512, parts=1)]
        ps_rl = reg(3072, 512)
        ps_qn = [reg(3584, 512, parts=1)]
        k_qn = ps_qn[0][1]
        ps_pj = [reg(0, 256), reg(512, 256), reg(1024, 256),
                 reg(1536, 256), reg(2048, 256), reg(2560, 256)]

        p.add("vector", lambda h: h.memset(onesrow[:], 1.0), writes=["onesrow"])
        p.add("vector", lambda h: h.memset(onesbf1[:], 1.0), writes=["onesbf1"])
        p.add("vector", lambda h: h.memset(onesbc[:], 1.0), writes=["onesbc"])
        p.add("vector", lambda h: h.memset(rs_t[:], RS), writes=["rs_t"])
        p.add("vector", lambda h: h.memset(rsneg[:], -RS), writes=["rsneg"])
        p.add("sync", lambda h: h.dma_start(out=bq[:], in_=bq_in),
              writes=["bq"], dma=True, group="gc")
        p.add("sync", lambda h: h.dma_start(out=bk[:], in_=bk_in),
              writes=["bk"], dma=True, group="gc")
        p.add("sync", lambda h: h.dma_start(
            out=maskcT[:].rearrange("p (f j) -> p f j", f=4), in_=maskcT_in),
            writes=["maskcT"], dma=True, group="gc")
        p.add("gpsimd", lambda h: h.dma_start(
            out=wq[:].rearrange("p (c m) -> p c m", m=256), in_=wq_in),
            writes=["wq"], dma=True, group="gw")
        p.add("gpsimd", lambda h: h.dma_start(
            out=wk[:].rearrange("p (c m) -> p c m", m=256), in_=wk_in),
            writes=["wk"], dma=True, group="gw")
        p.add("gpsimd", lambda h: h.dma_start(
            out=wv[:].rearrange("p (c m) -> p c m", m=256), in_=wv_in),
            writes=["wv"], dma=True, group="gw")
        p.add("vector", lambda h: h.tensor_scalar(
            bqs[:], bq[:], rs_t[:, 0:1], None, ALU.mult),
            reads=["bq", "rs_t"], writes=["bqs"])
        p.add("sync", lambda h: h.dma_start(out=rowq[0:1, 0:256], in_=bv_in),
              writes=["bvrow"], dma=True, group="gc")
        p.add("tensor", lambda h: h.matmul(
            pool[:, 3072:3072 + 256], onesrow[:], rowq[0:1, 0:256],
            start=True, stop=True),
            reads=["bvrow", "onesrow"] + psbk(3072, 256), writes=psbk(3072, 256),
            tag="bvbc")
        p.add("vector", lambda h: h.tensor_copy(bvb[:], pool[:, 3072:3072 + 256]),
              reads=psbk(3072, 256), writes=["bvb", "rowq_free"])

        for tt in range(NPT):
            par = tt % 2
            t0 = tt * TTD
            p.add("gpsimd", (lambda tt=tt, par=par: lambda h: h.dma_start(
                out=xtb[:, par * KC * TTD:(par + 1) * KC * TTD]
                    .rearrange("p (c t) -> p c t", c=KC),
                in_=x2T_in[:, :, tt * TTD:(tt + 1) * TTD]))(),
                reads=[f"xtu{par}"], writes=[f"xtb{par}"], dma=True, group=f"gx{par}",
                tag=f"xld{tt}")

            for mi in range(2):
                for wname, wsb in (("wq", wq), ("wk", wk)):
                    pj = ps_pj[mi if wname == "wq" else 2 + mi]

                    def mm_p(par=par, mi=mi, wsb=wsb, pj=pj):
                        def g(h):
                            last = None
                            for kc in range(KC):
                                last = h.matmul(
                                    pj[0],
                                    wsb[:, kc * 256 + mi * 128: kc * 256 + (mi + 1) * 128],
                                    xtb[:, par * KC * TTD + kc * TTD:
                                        par * KC * TTD + (kc + 1) * TTD],
                                    start=(kc == 0), stop=(kc == KC - 1))
                            return last
                        return g
                    p.add("tensor", mm_p(), reads=[wname, f"xtb{par}"] + pj[1],
                          writes=pj[1], tag=f"mmp{wname}{tt}_{mi}")
                    if wname == "wq":
                        p.add("vector", (lambda mi=mi, t0=t0, pj=pj: lambda h:
                              h.tensor_scalar(Q[:, mi * T + t0: mi * T + t0 + TTD],
                                              pj[0], rs_t[:, 0:1], bqs[:, mi:mi + 1],
                                              ALU.mult, ALU.add))(),
                              reads=pj[1] + ["bqs", "rs_t"], writes=["Q"],
                              tag=f"evq{tt}_{mi}")
                    else:
                        p.add("vector", (lambda mi=mi, t0=t0, pj=pj: lambda h:
                              h.tensor_scalar(Kt[:, mi * T + t0: mi * T + t0 + TTD],
                                              pj[0], bk[:, mi:mi + 1], None,
                                              ALU.add))(),
                              reads=pj[1] + ["bk"], writes=["Kt"],
                              tag=f"evk{tt}_{mi}")

            for ts in range(2):
                pj = ps_pj[4 + ts]

                def mm_v(par=par, ts=ts, pj=pj):
                    def g(h):
                        last = None
                        for kc in range(KC):
                            last = h.matmul(
                                pj[0],
                                xtb[:, par * KC * TTD + kc * TTD + ts * 128:
                                    par * KC * TTD + kc * TTD + (ts + 1) * 128],
                                wv[:, kc * 256:(kc + 1) * 256],
                                start=(kc == 0), stop=(kc == KC - 1))
                        return last
                    return g
                p.add("tensor", mm_v(), reads=["wv", f"xtb{par}"] + pj[1],
                      writes=pj[1] + ([f"xtu{par}"] if ts == 1 else []),
                      tag=f"mmv{tt}_{ts}")
                ch = (t0 + ts * 128) // 128
                p.add("vector", (lambda ch=ch, pj=pj: lambda h: h.tensor_tensor(
                    V[:, ch * 256:(ch + 1) * 256], pj[0], bvb[:], ALU.add))(),
                    reads=pj[1] + ["bvb"], writes=["V"], tag=f"evv{tt}_{ts}")

        for b in range(2):
            for hh in range(2):
                base = hh * T + b * S
                vbase = (b * S) // 128

                for ig in range(NIG):
                    i0 = ig * 512
                    njc = (i0 + 512) // 128
                    pvp = ig % 2
                    lp = ig % 2

                    def emit_st(jc, par, i0=i0, base=base):
                        def g(h, jc=jc, par=par, i0=i0, base=base):
                            return h.matmul(
                                ps_st[par][0],
                                Kt[:, base + jc * 128: base + (jc + 1) * 128],
                                Q[:, base + i0: base + i0 + 512],
                                start=True, stop=True)
                        p.add("tensor", g, reads=["Q", "Kt"] + ps_st[par][1],
                              writes=ps_st[par][1], tag=f"sb{b}{hh}_{ig}_{jc}")
                        rj = jc - ig * 4
                        if rj >= 0:
                            p.add("vector", (lambda par=par, rj=rj: lambda h:
                                  h.tensor_tensor(
                                      ps_st[par][0], ps_st[par][0],
                                      maskcT[:, rj * 512:(rj + 1) * 512], ALU.add))(),
                                  reads=ps_st[par][1] + ["maskcT"],
                                  writes=ps_st[par][1], tag=f"mskb{b}{hh}_{ig}_{jc}")

                    def emit_tail(jc, par, pvp=pvp, lp=lp, vbase=vbase, njc=njc):
                        pp = jc % 2
                        p.add("scalar", (lambda par=par, pp=pp: lambda h: h.activation(
                            ptile[:, pp * 512:(pp + 1) * 512], ps_st[par][0], AF.Exp))(),
                            reads=ps_st[par][1] + [f"ptu{pp}"], writes=[f"ptile{pp}"],
                            tag=f"bexp{b}{hh}_{ig}_{jc}")
                        p.add("tensor", (lambda hh=hh, jc=jc, pp=pp, pvp=pvp,
                              vbase=vbase, njc=njc: lambda h: h.matmul(
                            ps_pv[pvp][0],
                            V[:, (vbase + jc) * 256 + hh * 128:
                              (vbase + jc) * 256 + (hh + 1) * 128],
                            ptile[:, pp * 512:(pp + 1) * 512],
                            start=(jc == 0), stop=(jc == njc - 1)))(),
                            reads=["V", f"ptile{pp}"] + ps_pv[pvp][1],
                            writes=ps_pv[pvp][1], tag=f"pv{b}{hh}_{ig}_{jc}")
                        p.add("tensor", (lambda jc=jc, pp=pp, lp=lp, njc=njc: lambda h:
                              h.matmul(ps_l[lp][0], onesbc[:, 0:1],
                                       ptile[:, pp * 512:(pp + 1) * 512],
                                       start=(jc == 0), stop=(jc == njc - 1)))(),
                              reads=[f"ptile{pp}", "onesbc"] + ps_l[lp][1],
                              writes=ps_l[lp][1] + [f"ptu{pp}"],
                              tag=f"lmm{b}{hh}_{ig}_{jc}")

                    for jc in range(njc):
                        emit_st(jc, jc % 2)
                        if jc >= 1:
                            emit_tail(jc - 1, (jc - 1) % 2)
                    emit_tail(njc - 1, (njc - 1) % 2)

                    p.add("scalar", (lambda lp=lp: lambda h: h.activation(
                        lrow[0:1, lp * 512:(lp + 1) * 512], ps_l[lp][0], AF.Copy))(),
                        reads=ps_l[lp][1] + [f"lrowu{lp}"], writes=[f"lrow{lp}"],
                        tag=f"lev{b}{hh}_{ig}")
                    p.add("vector", (lambda lp=lp: lambda h: h.reciprocal(
                        rlrow[0:1, lp * 512:(lp + 1) * 512],
                        lrow[0:1, lp * 512:(lp + 1) * 512]))(),
                        reads=[f"lrow{lp}", f"rlru{lp}"],
                        writes=[f"rlrow{lp}", f"lrowu{lp}"],
                        tag=f"rlr{b}{hh}_{ig}")
                    p.add("tensor", (lambda lp=lp: lambda h: h.matmul(
                        ps_rl[0], onesrow[:], rlrow[0:1, lp * 512:(lp + 1) * 512],
                        start=True, stop=True))(),
                        reads=[f"rlrow{lp}", "onesrow"] + ps_rl[1],
                        writes=ps_rl[1] + [f"rlru{lp}"],
                        tag=f"rlb{b}{hh}_{ig}")
                    sp = ig % 2
                    p.add("vector", (lambda sp=sp: lambda h: h.tensor_copy(
                        rlb[:, sp * 512:(sp + 1) * 512], ps_rl[0]))(),
                        reads=ps_rl[1] + [f"rlbu{sp}"], writes=[f"rlb{sp}"],
                        tag=f"rlev{b}{hh}_{ig}")
                    p.add("vector", (lambda pvp=pvp, sp=sp: lambda h: h.tensor_tensor(
                        astage[:, sp * 512:(sp + 1) * 512], ps_pv[pvp][0],
                        rlb[:, sp * 512:(sp + 1) * 512], ALU.mult))(),
                        reads=ps_pv[pvp][1] + [f"rlb{sp}", f"asu{sp}"],
                        writes=[f"ast{sp}", f"rlbu{sp}"], tag=f"oev{b}{hh}_{ig}")
                    p.add("gpsimd", (lambda b=b, hh=hh, i0=i0, sp=sp: lambda h:
                          h.dma_start(
                              out=attn_out[b, hh, :, i0:i0 + 512],
                              in_=astage[:, sp * 512:(sp + 1) * 512]))(),
                          reads=[f"ast{sp}"], writes=[f"asu{sp}", "aout"],
                          dma=True, group=f"gao{sp}", tag=f"ao{b}{hh}_{ig}")

        with nc.Block() as block:
            p.emit_with_sems(st, block, final_engine="gpsimd", final_waits=["aout"])
    return nc


def host_D(x2T, w_q, b_q, w_k, b_k, w_v, b_v):
    x2L = np.ascontiguousarray(
        x2T.reshape(KC, 128, T).transpose(1, 0, 2), dtype=np.float32)
    maskc = np.zeros((128, 4, 512), np.float32)
    maskcT = np.zeros((128, 4, 512), np.float32)
    for ph in range(4):
        for pp in range(128):
            i = ph * 128 + pp
            maskc[pp, ph, :] = np.where(np.arange(512) <= i, 0.0, -1e10)
        for pp in range(128):
            maskcT[pp, ph, :] = np.where(np.arange(512) >= ph * 128 + pp, 0.0, -1e10)
    maps = []
    for c in range(NCORES):
        sl = slice(256 * c, 256 * (c + 1))
        wqL = np.ascontiguousarray(
            w_q[:, sl].reshape(KC, 128, 256).transpose(1, 0, 2), dtype=np.float32)
        wkL = np.ascontiguousarray(
            w_k[:, sl].reshape(KC, 128, 256).transpose(1, 0, 2), dtype=np.float32)
        wvL = np.ascontiguousarray(
            w_v[:, sl].reshape(KC, 128, 256).transpose(1, 0, 2), dtype=np.float32)
        bqL = np.ascontiguousarray(b_q[sl].reshape(2, 128).T, dtype=np.float32)
        bkL = np.ascontiguousarray(b_k[sl].reshape(2, 128).T, dtype=np.float32)
        bvL = np.ascontiguousarray(b_v[sl].reshape(1, 256), dtype=np.float32)
        maps.append({"x2T_in": x2L, "wq_in": wqL, "wk_in": wkL, "wv_in": wvL,
                     "bq_in": bqL, "bk_in": bkL, "bv_in": bvL,
                     "maskcT_in": maskcT})
    return maps


def run_D(x2T, w_q, b_q, w_k, b_k, w_v, b_v):
    nc = build_D()
    res = run_bass_kernel_spmd(nc, host_D(x2T, w_q, b_q, w_k, b_k, w_v, b_v),
                               core_ids=list(range(NCORES)))
    attnT = np.zeros((DM, T), dtype=bf16)
    for c in range(NCORES):
        a = res.results[c]["attn_out"]
        for hh in range(2):
            head = 2 * c + hh
            for b in range(2):
                attnT[head * 128:(head + 1) * 128, b * S:(b + 1) * S] = a[b, hh]
    return attnT



HID = 5632
HC = HID // 128


def build_E():
    nc = bass.Bass("TRN2", target_bir_lowering=False)
    attn_in = nc.dram_tensor("attn_in", [128, KC, TPC], BF16, kind="ExternalInput").ap()
    x2_in = nc.dram_tensor("x2_in", [128, KC, TPC], F32, kind="ExternalInput").ap()
    wo_in = nc.dram_tensor("wo_in", [128, KC, DM], F32, kind="ExternalInput").ap()
    bo_in = nc.dram_tensor("bo_in", [128, KC], F32, kind="ExternalInput").ap()
    wqg_in = nc.dram_tensor("wqg_in", [128, KC, HID], I32, kind="ExternalInput").ap()
    wqu_in = nc.dram_tensor("wqu_in", [128, KC, HID], I32, kind="ExternalInput").ap()
    wqd_in = nc.dram_tensor("wqd_in", [128, HC, DM], I32, kind="ExternalInput").ap()
    sg_in = nc.dram_tensor("sg_in", [128, HC], F32, kind="ExternalInput").ap()
    su_in = nc.dram_tensor("su_in", [128, HC], F32, kind="ExternalInput").ap()
    sd_in = nc.dram_tensor("sd_in", [128, KC], F32, kind="ExternalInput").ap()
    x4_out = nc.dram_tensor("x4_out", [128, KC, TPC], F32, kind="ExternalOutput").ap()

    TT2 = 512
    NTT = TPC // TT2
    HT = TPC // 2

    import contextlib
    with contextlib.ExitStack() as st:
        x = st.enter_context(nc.sbuf_tensor("x", [128, KC * TPC], F32))
        attn = st.enter_context(nc.sbuf_tensor("attn", [128, KC * TPC], BF16))
        wo = st.enter_context(nc.sbuf_tensor("wo", [128, KC * DM], BF16))
        bo = st.enter_context(nc.sbuf_tensor("bo", [128, KC], F32))
        sg = st.enter_context(nc.sbuf_tensor("sg", [128, HC], F32))
        su = st.enter_context(nc.sbuf_tensor("su", [128, HC], F32))
        sd = st.enter_context(nc.sbuf_tensor("sd", [128, KC], F32))
        onesbf = st.enter_context(nc.sbuf_tensor("onesbf", [128, 1], BF16))
        csb = st.enter_context(nc.sbuf_tensor("csb", [128, TT2], F32))
        csb2 = st.enter_context(nc.sbuf_tensor("csb2", [128, TT2], F32))
        tscr = st.enter_context(nc.sbuf_tensor("tscr", [128, TT2], F32))
        ga = st.enter_context(nc.sbuf_tensor("ga", [128, TT2], BF16))
        ua = st.enter_context(nc.sbuf_tensor("ua", [128, TT2], BF16))
        pool = st.enter_context(nc.psum_tensor("pool", [128, 4096], F32))

        x3bf = attn[:, 0:KC * TT2]
        wt = [attn[:, KC * TT2 + i * 2048: KC * TT2 + (i + 1) * 2048] for i in range(4)]
        assert KC * TT2 + 4 * 2048 <= KC * TPC
        hid = wo[:, 0:HC * TT2]
        wdt = [wo[:, HC * TT2: HC * TT2 + HC * 128]]
        assert HC * TT2 + HC * 128 <= KC * DM

        p = P(nc)
        ct = make_common(nc, st, p, pool, NT=TPC)
        row, ones, onesrow, eps = ct["row"], ct["ones"], ct["onesrow"], ct["eps"]

        XV = lambda: x[:].rearrange("p (c t) -> p c t", c=KC)

        ps_o = [pool[:, 0:512], pool[:, 512:1024]]
        k_o = [psbk(0, 512), psbk(512, 512)]
        ps_cs = pool[0:1, 1024:1024 + TT2]
        k_cs = psbk(1024, TT2)
        ps_cb = pool[:, 1536:1536 + TT2]
        k_cb = psbk(1536, TT2)

        ps_g = [pool[:, 2048:2048 + TT2], pool[:, 2560:2560 + TT2]]
        k_g = [psbk(2048, TT2), psbk(2560, TT2)]
        ps_u = [pool[:, 3072:3072 + TT2], pool[:, 3584:3584 + TT2]]
        k_u = [psbk(3072, TT2), psbk(3584, TT2)]
        ps_d = [pool[:, 0:TT2], pool[:, 512:512 + TT2]]
        k_d = [psbk(0, TT2), psbk(512, TT2)]
        assert TT2 == 512

        p.add("vector", lambda h: h.memset(onesbf[:], 1.0), writes=["onesbf"])
        p.add("sync", lambda h: h.dma_start(out=XV(), in_=x2_in),
              writes=["x"], dma=True, group="gx2")
        p.add("sync", lambda h: h.dma_start(
            out=attn[:].rearrange("p (c t) -> p c t", c=KC), in_=attn_in),
            writes=["attn"], dma=True, group="gattn")
        p.add("gpsimd", lambda h: h.dma_start(
            out=wo[:].rearrange("p (c d) -> p c d", d=DM), in_=wo_in),
            writes=["wo"], dma=True, group="gwo")
        p.add("sync", lambda h: h.dma_start(out=bo[:], in_=bo_in),
              writes=["bo"], dma=True, group="gsc")
        p.add("sync", lambda h: h.dma_start(out=sg[:], in_=sg_in),
              writes=["sg"], dma=True, group="gsc")
        p.add("sync", lambda h: h.dma_start(out=su[:], in_=su_in),
              writes=["su"], dma=True, group="gsc")
        p.add("sync", lambda h: h.dma_start(out=sd[:], in_=sd_in),
              writes=["sd"], dma=True, group="gsc")

        for th in range(2):
            for oc in range(KC):
                par = (th * KC + oc) % 2

                def mm_o(th=th, oc=oc, par=par):
                    def g(h):
                        last = None
                        for kc in range(KC):
                            last = h.matmul(
                                ps_o[par],
                                wo[:, kc * DM + oc * 128: kc * DM + (oc + 1) * 128],
                                attn[:, kc * TPC + th * HT: kc * TPC + th * HT + HT],
                                start=(kc == 0), stop=(kc == KC - 1))
                        return last
                    return g
                p.add("tensor", mm_o(), reads=["wo", "attn"] + k_o[par],
                      writes=k_o[par], tag=f"mmo{th}_{oc}")
                p.add("vector", (lambda th=th, oc=oc, par=par: lambda h:
                      h.scalar_tensor_tensor(
                          x[:, oc * TPC + th * HT: oc * TPC + th * HT + HT],
                          ps_o[par], bo[:, oc:oc + 1],
                          x[:, oc * TPC + th * HT: oc * TPC + th * HT + HT],
                          ALU.add, ALU.add))(),
                      reads=k_o[par] + ["bo", "x"], writes=["x"], tag=f"evo{th}_{oc}")

        p.add("gpsimd", lambda h: h.nop(nofuse=True),
              writes=["wo", "attn", "woarena", "attnarena"], tag="arena_sw")

        add_ln_fm(p, ct, x, ["x"], tag="E3")

        for tt in range(NTT):
            t0 = tt * TT2

            def cast_x3(tt=tt, t0=t0):
                def g(h):
                    last = None
                    for kc in range(KC):
                        last = h.tensor_copy(
                            x3bf[:, kc * TT2:(kc + 1) * TT2],
                            x[:, kc * TPC + t0: kc * TPC + t0 + TT2])
                    return last
                return g
            p.add("vector", cast_x3(), reads=["x", "attnarena"], writes=["x3bf"],
                  tag=f"cx3_{tt}")

            def mm_cs(tt=tt):
                def g(h):
                    last = None
                    for kc in range(KC):
                        last = h.matmul(ps_cs, onesbf[:, 0:1],
                                        x3bf[:, kc * TT2:(kc + 1) * TT2],
                                        start=(kc == 0), stop=(kc == KC - 1))
                    return last
                return g
            p.add("tensor", mm_cs(), reads=["x3bf", "onesbf"] + k_cs, writes=k_cs,
                  tag=f"mmcs{tt}")
            p.add("scalar", (lambda: lambda h: h.activation(
                row("SCR1", 0, TT2), ps_cs, AF.Copy, scale=128.0))(),
                reads=k_cs, writes=["r128cs"], tag=f"cs{tt}")
            p.add("tensor", lambda h: h.matmul(
                ps_cb, onesrow[:], row("SCR1", 0, TT2), start=True, stop=True),
                reads=["r128cs", "c_onesrow"] + k_cb, writes=k_cb, tag=f"bccs{tt}")
            p.add("vector", lambda h: h.tensor_copy(csb[:], ps_cb),
                  reads=k_cb, writes=["csb"], tag=f"evcs{tt}")

            for hc in range(HC):
                par = hc % 2
                p.add("gpsimd", (lambda hc=hc, par=par: lambda h: h.dma_start(
                    out=wt[par].rearrange("p (c m) -> p c m", m=128),
                    in_=wqg_in[:, :, hc * 128:(hc + 1) * 128]))(),
                    reads=[f"wtu{par}", "attnarena"], writes=[f"wt{par}"], dma=True,
                    group=f"gwt{par}", tag=f"ldg{tt}_{hc}")
                p.add("gpsimd", (lambda hc=hc, par=par: lambda h: h.dma_start(
                    out=wt[2 + par].rearrange("p (c m) -> p c m", m=128),
                    in_=wqu_in[:, :, hc * 128:(hc + 1) * 128]))(),
                    reads=[f"wuu{par}", "attnarena"], writes=[f"wu{par}"], dma=True,
                    group=f"gwu{par}", tag=f"ldu{tt}_{hc}")

                def mm_g(hc=hc, par=par):
                    def g(h):
                        last = None
                        for kc in range(KC):
                            last = h.matmul(
                                ps_g[par], wt[par][:, kc * 128:(kc + 1) * 128],
                                x3bf[:, kc * TT2:(kc + 1) * TT2],
                                start=(kc == 0), stop=(kc == KC - 1))
                        return last
                    return g
                p.add("tensor", mm_g(), reads=[f"wt{par}", "x3bf"] + k_g[par],
                      writes=k_g[par] + [f"wtu{par}"], tag=f"mmg{tt}_{hc}")
                p.add("vector", (lambda par=par: lambda h: h.tensor_tensor(
                    tscr[:], ps_g[par], csb[:], ALU.subtract))(),
                    reads=k_g[par] + ["csb", "tscr_free"], writes=["tscr"],
                    tag=f"gsub{tt}_{hc}")
                p.add("scalar", (lambda hc=hc: lambda h: h.activation(
                    ga[:], tscr[:], AF.Silu, scale=sg[:, hc:hc + 1]))(),
                    reads=["tscr", "sg", "ga_free"], writes=["ga"], tag=f"gact{tt}_{hc}")

                def mm_u(hc=hc, par=par):
                    def g(h):
                        last = None
                        for kc in range(KC):
                            last = h.matmul(
                                ps_u[par], wt[2 + par][:, kc * 128:(kc + 1) * 128],
                                x3bf[:, kc * TT2:(kc + 1) * TT2],
                                start=(kc == 0), stop=(kc == KC - 1))
                        return last
                    return g
                p.add("tensor", mm_u(), reads=[f"wu{par}", "x3bf"] + k_u[par],
                      writes=k_u[par] + [f"wuu{par}"], tag=f"mmu{tt}_{hc}")
                p.add("vector", (lambda par=par: lambda h: h.tensor_tensor(
                    tscr[:], ps_u[par], csb[:], ALU.subtract))(),
                    reads=k_u[par] + ["csb", "ga"], writes=["tscr"],
                    tag=f"usub{tt}_{hc}")
                p.add("scalar", (lambda hc=hc: lambda h: h.activation(
                    ua[:], tscr[:], AF.Copy, scale=su[:, hc:hc + 1]))(),
                    reads=["tscr", "su", "ua_free"], writes=["ua"], tag=f"uact{tt}_{hc}")
                p.add("vector", (lambda hc=hc: lambda h: h.tensor_tensor(
                    hid[:, hc * TT2:(hc + 1) * TT2], ga[:], ua[:], ALU.mult))(),
                    reads=["ga", "ua", "woarena"], writes=[f"hid{hc}", "ga_free",
                                                           "ua_free", "tscr_free"],
                    tag=f"hmul{tt}_{hc}")
            HID_RES = [f"hid{hc}" for hc in range(HC)]

            def mm_cs2(tt=tt):
                def g(h):
                    last = None
                    for hc in range(HC):
                        last = h.matmul(ps_cs, onesbf[:, 0:1],
                                        hid[:, hc * TT2:(hc + 1) * TT2],
                                        start=(hc == 0), stop=(hc == HC - 1))
                    return last
                return g
            p.add("tensor", mm_cs2(), reads=HID_RES + ["onesbf"] + k_cs, writes=k_cs,
                  tag=f"mmcs2{tt}")
            p.add("scalar", (lambda: lambda h: h.activation(
                row("SCR1", 0, TT2), ps_cs, AF.Copy, scale=128.0))(),
                reads=k_cs + ["r128cs"], writes=["r128cs"], tag=f"cs2{tt}")
            p.add("tensor", lambda h: h.matmul(
                ps_cb, onesrow[:], row("SCR1", 0, TT2), start=True, stop=True),
                reads=["r128cs", "c_onesrow"] + k_cb, writes=k_cb, tag=f"bccs2{tt}")
            p.add("vector", lambda h: h.tensor_copy(csb2[:], ps_cb),
                  reads=k_cb + ["csb"], writes=["csb2"], tag=f"evcs2{tt}")

            for dc in range(KC):
                par = dc % 2
                p.add("gpsimd", (lambda dc=dc: lambda h: h.dma_start(
                    out=wdt[0].rearrange("p (c m) -> p c m", m=128),
                    in_=wqd_in[:, :, dc * 128:(dc + 1) * 128]))(),
                    reads=["wdu0", "woarena"], writes=["wd0"], dma=True,
                    group="gwd0", tag=f"ldd{tt}_{dc}")

                def mm_d(dc=dc, par=par):
                    def g(h):
                        last = None
                        for hc in range(HC):
                            last = h.matmul(
                                ps_d[par], wdt[0][:, hc * 128:(hc + 1) * 128],
                                hid[:, hc * TT2:(hc + 1) * TT2],
                                start=(hc == 0), stop=(hc == HC - 1))
                        return last
                    return g
                p.add("tensor", mm_d(), reads=["wd0"] + HID_RES + k_d[par],
                      writes=k_d[par] + ["wdu0"], tag=f"mmd{tt}_{dc}")
                p.add("vector", (lambda dc=dc, par=par: lambda h: h.tensor_tensor(
                    tscr[:], ps_d[par], csb2[:], ALU.subtract))(),
                    reads=k_d[par] + ["csb2", "tscr_free"], writes=["tscr2"],
                    tag=f"dsub{tt}_{dc}")
                p.add("vector", (lambda dc=dc, t0=t0: lambda h:
                      h.scalar_tensor_tensor(
                          x[:, dc * TPC + t0: dc * TPC + t0 + TT2],
                          tscr[:], sd[:, dc:dc + 1],
                          x[:, dc * TPC + t0: dc * TPC + t0 + TT2],
                          ALU.mult, ALU.add))(),
                      reads=["tscr2", "sd", "x"], writes=["x", "tscr_free"],
                      tag=f"dres{tt}_{dc}")

        add_ln_fm(p, ct, x, ["x"], tag="E4")

        p.add("gpsimd", lambda h: h.dma_start(out=x4_out, in_=XV()),
              reads=["x"], writes=["o1"], dma=True, group="gout")

        with nc.Block() as block:
            p.emit_with_sems(st, block, final_engine="gpsimd", final_waits=["o1"])
    return nc


def host_E(attnT, x2T, w_o, b_o, wq_gate, s_gate, wq_up, s_up, wq_down, s_down):
    woL = _f32(w_o).reshape(KC, 128, DM).transpose(1, 0, 2)
    woL = np.ascontiguousarray(woL)
    boL = _f32(b_o).reshape(KC, 128).T
    wqgL = np.ascontiguousarray(
        wq_gate.T.reshape(KC, 128, HID).transpose(1, 0, 2), dtype=np.int32)
    wquL = np.ascontiguousarray(
        wq_up.T.reshape(KC, 128, HID).transpose(1, 0, 2), dtype=np.int32)
    wqdL = np.ascontiguousarray(
        wq_down.T.reshape(HC, 128, DM).transpose(1, 0, 2), dtype=np.int32)
    sgL = _f32(s_gate).reshape(HC, 128).T
    suL = _f32(s_up).reshape(HC, 128).T
    sdL = _f32(s_down).reshape(KC, 128).T
    maps = []
    for c in range(NCORES):
        sl = slice(c * TPC, (c + 1) * TPC)
        at = np.ascontiguousarray(
            attnT[:, sl].reshape(KC, 128, TPC).transpose(1, 0, 2))
        x2c = _f32(x2T[:, sl].reshape(KC, 128, TPC).transpose(1, 0, 2))
        maps.append({"attn_in": at.astype(bf16), "x2_in": x2c, "wo_in": woL,
                     "bo_in": np.ascontiguousarray(boL),
                     "wqg_in": wqgL, "wqu_in": wquL, "wqd_in": wqdL,
                     "sg_in": np.ascontiguousarray(sgL),
                     "su_in": np.ascontiguousarray(suL),
                     "sd_in": np.ascontiguousarray(sdL)})
    return maps


def run_E(attnT, x2T, w_o, b_o, wq_gate, s_gate, wq_up, s_up, wq_down, s_down):
    nc = build_E()
    res = run_bass_kernel_spmd(
        nc, host_E(attnT, x2T, w_o, b_o, wq_gate, s_gate, wq_up, s_up, wq_down, s_down),
        core_ids=list(range(NCORES)))
    return np.concatenate(
        [r["x4_out"].transpose(1, 0, 2).reshape(DM, TPC) for r in res.results], axis=1)


def host_B(partials, sumsqs, x0T):
    maps = []
    for c in range(NCORES):
        sl = slice(c * TPC, (c + 1) * TPC)
        p8 = np.stack([partials[j][:, :, sl] for j in range(8)])
        sq8 = np.stack([sumsqs[j][sl] for j in range(8)])
        x0c = np.ascontiguousarray(
            x0T[:, sl].reshape(KC, 128, TPC).transpose(1, 0, 2))
        maps.append({"p8": _f32(p8), "sq8": _f32(sq8), "x0c": _f32(x0c)})
    return maps


def run_B(partials, sumsqs, x0T):
    nc = build_B()
    res = run_bass_kernel_spmd(nc, host_B(partials, sumsqs, x0T),
                               core_ids=list(range(NCORES)))
    x1T = np.concatenate(
        [r["x1_out"].transpose(1, 0, 2).reshape(DM, TPC) for r in res.results], axis=1)
    srp = [r["srp_out"] for r in res.results]
    return x1T, srp


def kernel(L0_attn_hidden_states, L0_attn_qkv_w, L0_attn_out_w, L0_attn_gate_w,
           L0_attn_norm_w, L0_ffn_sim_matrix, L0_ffn_gates, L0_ffn_w_q,
           L0_ffn_w_k, L0_ffn_w_v, L0_ffn_o_weights, L1_attn_w_q, L1_attn_b_q,
           L1_attn_w_k, L1_attn_b_k, L1_attn_w_v, L1_attn_b_v, L1_attn_w_o,
           L1_attn_b_o, L1_ffn_wq_gate, L1_ffn_s_gate, L1_ffn_wq_up,
           L1_ffn_s_up, L1_ffn_wq_down, L1_ffn_s_down):
    x0 = np.ascontiguousarray(np.asarray(L0_attn_hidden_states), dtype=np.float32)
    x0T = np.ascontiguousarray(x0.reshape(T, DM).T)

    partials, sumsqs = run_A(x0, np.asarray(L0_attn_qkv_w),
                             np.asarray(L0_attn_gate_w), np.asarray(L0_attn_out_w),
                             np.asarray(L0_attn_norm_w))
    x1T, srps = run_B(partials, sumsqs, x0T)
    x2T, _ = run_C(x1T, srps, np.asarray(L0_ffn_sim_matrix),
                   np.asarray(L0_ffn_gates), np.asarray(L0_ffn_w_v),
                   np.asarray(L0_ffn_o_weights))
    attnT = run_D(x2T, np.asarray(L1_attn_w_q), np.asarray(L1_attn_b_q),
                  np.asarray(L1_attn_w_k), np.asarray(L1_attn_b_k),
                  np.asarray(L1_attn_w_v), np.asarray(L1_attn_b_v))
    x4T = run_E(attnT, x2T, np.asarray(L1_attn_w_o), np.asarray(L1_attn_b_o),
                np.asarray(L1_ffn_wq_gate), np.asarray(L1_ffn_s_gate),
                np.asarray(L1_ffn_wq_up), np.asarray(L1_ffn_s_up),
                np.asarray(L1_ffn_wq_down), np.asarray(L1_ffn_s_down))
    return np.ascontiguousarray(x4T.T).reshape(BB, S, DM)


# version 04
# speedup vs baseline: 1.6900x; 1.0722x over previous
import numpy as np
import ml_dtypes
import concourse.bass as bass
import concourse.mybir as mybir
from concourse.bass_utils import run_bass_kernel_spmd

F32 = mybir.dt.float32
BF16 = mybir.dt.bfloat16
I32 = mybir.dt.int32
AF = mybir.ActivationFunctionType
ALU = mybir.AluOpType
AX = mybir.AxisListType

NCORES = 8
BB, S, DM = 2, 4096, 2048
T = BB * S
TPC = T // NCORES
KC = DM // 128

bf16 = ml_dtypes.bfloat16

ENGINES = ("sync", "gpsimd", "vector", "scalar", "tensor")


class P:

    def __init__(self, nc):
        self.nc = nc
        self.ops = {e: [] for e in ENGINES}
        self.last_writer = {}
        self.readers = {}
        self.groups = {}
        self.sems = {}

    def add(self, eng, fn, reads=(), writes=(), dma=False, group=None, tag=""):
        idx = len(self.ops[eng])
        me = (eng, idx)
        assert (group is not None) == bool(dma), f"dma ops need a group ({tag})"
        deps = set()
        for r in reads:
            w = self.last_writer.get(r)
            if w is not None:
                deps.add(w)
        for r in writes:
            w = self.last_writer.get(r)
            if w is not None:
                deps.add(w)
            for rd in self.readers.get(r, ()):
                deps.add(rd)
        if dma and group in self.groups:
            deps.add(self.groups[group])
        deps.discard(me)
        self.ops[eng].append(dict(fn=fn, deps=deps, dma=dma, group=group,
                                  marked=False, tag=tag))
        for r in reads:
            self.readers.setdefault(r, set()).add(me)
        for r in writes:
            self.last_writer[r] = me
            self.readers[r] = set()
        if dma:
            self.groups[group] = me
        return me

    def finalize(self):
        for e in ENGINES:
            for op in self.ops[e]:
                if op["dma"]:
                    op["marked"] = True
                for (de, di) in op["deps"]:
                    self.ops[de][di]["marked"] = True
        self.incval = {}
        counts = {}
        for e in ENGINES:
            for i, op in enumerate(self.ops[e]):
                key = op["group"] if op["dma"] else e
                if op["marked"]:
                    counts[key] = counts.get(key, 0) + (16 if op["dma"] else 1)
                self.incval[(e, i)] = (key, counts.get(key, 0))

    def make_sems(self, stack):
        keys = set()
        for e in ENGINES:
            for op in self.ops[e]:
                keys.add(op["group"] if op["dma"] else e)
        for k in sorted(keys):
            self.sems[k] = stack.enter_context(self.nc.semaphore(f"sp_{k}"))

    def emit(self, eng, h):
        waited = {}
        for i, op in enumerate(self.ops[eng]):
            wants = {}
            for (de, di) in op["deps"]:
                key, v = self.incval[(de, di)]
                wants[key] = max(wants.get(key, 0), v)
            for key in sorted(wants):
                v = wants[key]
                if v > waited.get(key, 0):
                    h.wait_ge(self.sems[key], v)
                    waited[key] = v
            last = op["fn"](h)
            if op["marked"]:
                assert last is not None, f"op {eng}:{i} ({op['tag']}) returned None"
                key = op["group"] if op["dma"] else eng
                last.then_inc(self.sems[key], 16 if op["dma"] else 1)

    def emit_all(self, block, final_engine="gpsimd", final_waits=()):
        def _noop(h):
            return h.nop(nofuse=True)
        self.add(final_engine, _noop, reads=list(final_waits), tag="finalwait")
        self.finalize()

        def mk(e):
            def _(h):
                self.emit(e, h)
            return _
        block.sync(mk("sync"))
        block.gpsimd(mk("gpsimd"))
        block.vector(mk("vector"))
        block.scalar(mk("scalar"))
        block.tensor(mk("tensor"))

    def emit_with_sems(self, stack, block, final_engine="gpsimd", final_waits=()):
        def _noop(h):
            return h.nop(nofuse=True)
        self.add(final_engine, _noop, reads=list(final_waits), tag="finalwait")
        self.finalize()
        self.make_sems(stack)

        def mk(e):
            def _(h):
                self.emit(e, h)
            return _
        block.sync(mk("sync"))
        block.gpsimd(mk("gpsimd"))
        block.vector(mk("vector"))
        block.scalar(mk("scalar"))
        block.tensor(mk("tensor"))


def _f32(x):
    return np.ascontiguousarray(x, dtype=np.float32)



LBS = 256
NBLK = S // LBS


def build_A():
    nc = bass.Bass("TRN2", target_bir_lowering=False)
    id64c = nc.inline_tensor(np.vstack([np.eye(64), np.eye(64)]).astype(bf16), name="identity64a")
    x0T_in = nc.dram_tensor("x0T_in", [128, KC, T], F32, kind="ExternalInput").ap()
    wqkv_in = nc.dram_tensor("wqkv_in", [128, KC, 768], F32, kind="ExternalInput").ap()
    wgate_in = nc.dram_tensor("wgate_in", [128, KC, 256], F32, kind="ExternalInput").ap()
    wout_in = nc.dram_tensor("wout_in", [128, 2, DM], F32, kind="ExternalInput").ap()
    normw_in = nc.dram_tensor("normw_in", [128, 2], F32, kind="ExternalInput").ap()
    ddT_in = nc.dram_tensor("ddT_in", [128, 4, 2, LBS], F32, kind="ExternalInput").ap()
    qd_in = nc.dram_tensor("qd_in", [128, 4, LBS], F32, kind="ExternalInput").ap()
    kd_in = nc.dram_tensor("kd_in", [128, 8], F32, kind="ExternalInput").ap()
    bd_in = nc.dram_tensor("bd_in", [64, 4], F32, kind="ExternalInput").ap()
    partial_out = nc.dram_tensor("partial_out", [128, KC, T], F32, kind="ExternalOutput").ap()
    sumsq_out = nc.dram_tensor("sumsq_out", [1, T], F32, kind="ExternalOutput").ap()

    TTA = 256
    NPT = T // TTA

    import contextlib
    with contextlib.ExitStack() as st:
        xtb = st.enter_context(nc.sbuf_tensor("xtb", [128, 2 * KC * TTA], BF16))
        wqkv = st.enter_context(nc.sbuf_tensor("wqkv", [128, KC * 768], BF16))
        wgate = st.enter_context(nc.sbuf_tensor("wgate", [128, KC * 256], BF16))
        woutw = st.enter_context(nc.sbuf_tensor("woutw", [128, 2 * DM], BF16))
        normw = st.enter_context(nc.sbuf_tensor("normw", [128, 2], F32))
        Q = st.enter_context(nc.sbuf_tensor("Q", [128, 2 * T], BF16))
        Kt = st.enter_context(nc.sbuf_tensor("Kt", [128, 2 * T], BF16))
        V = st.enter_context(nc.sbuf_tensor("V", [128, (T // 128) * 256], BF16))
        ga = st.enter_context(nc.sbuf_tensor("ga", [128, 2 * T], BF16))

        kd = st.enter_context(nc.sbuf_tensor("kd", [128, 8], F32))
        bd = st.enter_context(nc.sbuf_tensor("bd", [64, 4], F32))
        id64 = st.enter_context(nc.sbuf_tensor("id64", [128, 64], BF16))
        onesf = st.enter_context(nc.sbuf_tensor("onesf", [64, 1], F32))
        ktm = st.enter_context(nc.sbuf_tensor("ktm", [128, 8 * 64], BF16))
        states = st.enter_context(nc.sbuf_tensor("states", [64, 8 * 64], F32))
        statesbf = st.enter_context(nc.sbuf_tensor("statesbf", [64, 8 * 64], BF16))
        sqscr = st.enter_context(nc.sbuf_tensor("sqscr", [64, 4 * LBS], F32))
        ssqrow = st.enter_context(nc.sbuf_tensor("ssqrow", [1, 2 * LBS], F32))
        ostage = st.enter_context(nc.sbuf_tensor("ostage", [128, 8 * LBS], F32))
        ddT = wqkv[:, 0:8 * LBS]
        qd = wqkv[:, 8 * LBS:12 * LBS]
        awm = wgate[:, 0:8 * LBS]
        qp = wgate[0:64, 8 * LBS:12 * LBS]
        pool = st.enter_context(nc.psum_tensor("pool", [128, 3584], F32))
        pbf = st.enter_context(nc.psum_tensor("pbf", [128, 1024], BF16))

        p = P(nc)

        def reg(off, width, parts=128):
            return pool[0:parts, off:off + width], psbk(off, width)

        def aset(s):
            return dict(
                qk=[reg(s * 1024 + i * 256, TTA) for i in range(4)],
                v=[reg(3072 + i * 256, TTA) for i in range(2)],
                gt=[reg(2048 + s * 512 + i * 256, TTA) for i in range(2)])
        asets = [aset(0), aset(1)]
        ps_aw = [reg(0, LBS), reg(512, LBS)]
        ps_att = reg(1024, LBS, parts=64)
        ps_state = reg(1536, 64, parts=64)
        ps_sq = reg(2048, LBS, parts=1)
        ps_out = [reg(2560, LBS), reg(3072, LBS)]
        k_pbf = ["psB7"]

        p.add("vector", lambda h: h.memset(onesf[:], 1.0), writes=["onesf"])
        p.add("sync", lambda h: h.dma_start(out=id64[:], in_=id64c.ap()),
              writes=["id64"], dma=True, group="gc")
        p.add("sync", lambda h: h.dma_start(out=kd[:], in_=kd_in),
              writes=["kd"], dma=True, group="gc")
        p.add("sync", lambda h: h.dma_start(out=bd[:], in_=bd_in),
              writes=["bd"], dma=True, group="gc")
        p.add("sync", lambda h: h.dma_start(out=normw[:], in_=normw_in),
              writes=["normw"], dma=True, group="gc")
        p.add("gpsimd", lambda h: h.dma_start(
            out=woutw[:].rearrange("p (c d) -> p c d", c=2), in_=wout_in),
            writes=["woutw"], dma=True, group="gc2")
        p.add("gpsimd", lambda h: h.dma_start(
            out=wqkv[:].rearrange("p (c m) -> p c m", m=768), in_=wqkv_in),
            writes=["wqkv"], dma=True, group="gw")
        p.add("gpsimd", lambda h: h.dma_start(
            out=wgate[:].rearrange("p (c m) -> p c m", m=256), in_=wgate_in),
            writes=["wgate"], dma=True, group="gw")
        for c2 in range(2):
            p.add("scalar", (lambda c2=c2: lambda h: h.activation(
                woutw[:, c2 * DM:(c2 + 1) * DM], woutw[:, c2 * DM:(c2 + 1) * DM],
                AF.Copy, scale=normw[:, c2:c2 + 1]))(),
                reads=["woutw", "normw"], writes=["woutw"], tag=f"woscale{c2}")

        for tt in range(NPT):
            par = tt % 2
            t0 = tt * TTA
            xv = lambda par=par: xtb[:, par * KC * TTA:(par + 1) * KC * TTA]
            p.add("gpsimd", (lambda tt=tt, par=par: lambda h: h.dma_start(
                out=xtb[:, par * KC * TTA:(par + 1) * KC * TTA]
                    .rearrange("p (c t) -> p c t", c=KC),
                in_=x0T_in[:, :, tt * TTA:(tt + 1) * TTA]))(),
                reads=[f"xtu{par}"], writes=[f"xtb{par}"], dma=True, group=f"gx{par}",
                tag=f"xld{tt}")

            ps_qk = asets[par]["qk"]
            ps_v = asets[par]["v"]
            ps_gt = asets[par]["gt"]
            for mi in range(4):
                def mm_qk(tt=tt, par=par, mi=mi, ps_qk=ps_qk):
                    def g(h):
                        last = None
                        for kc in range(KC):
                            last = h.matmul(
                                ps_qk[mi][0],
                                wqkv[:, kc * 768 + mi * 128: kc * 768 + (mi + 1) * 128],
                                xtb[:, par * KC * TTA + kc * TTA:
                                    par * KC * TTA + (kc + 1) * TTA],
                                start=(kc == 0), stop=(kc == KC - 1))
                        return last
                    return g
                p.add("tensor", mm_qk(), reads=["wqkv", f"xtb{par}"] + ps_qk[mi][1],
                      writes=ps_qk[mi][1], tag=f"mmqk{tt}_{mi}")
                dst = Q if mi < 2 else Kt
                dsn = "Q" if mi < 2 else "Kt"
                qi = mi % 2
                p.add("scalar", (lambda dst=dst, qi=qi, t0=t0, mi=mi, ps_qk=ps_qk:
                      lambda h: h.activation(dst[:, qi * T + t0: qi * T + t0 + TTA],
                                             ps_qk[mi][0], AF.Silu))(),
                      reads=ps_qk[mi][1], writes=[dsn], tag=f"evqk{tt}_{mi}")

            for ts in range(2):
                def mm_v(tt=tt, par=par, ts=ts, ps_v=ps_v):
                    def g(h):
                        last = None
                        for kc in range(KC):
                            last = h.matmul(
                                ps_v[ts][0],
                                xtb[:, par * KC * TTA + kc * TTA + ts * 128:
                                    par * KC * TTA + kc * TTA + (ts + 1) * 128],
                                wqkv[:, kc * 768 + 512: kc * 768 + 768],
                                start=(kc == 0), stop=(kc == KC - 1))
                        return last
                    return g
                p.add("tensor", mm_v(), reads=["wqkv", f"xtb{par}"] + ps_v[ts][1],
                      writes=ps_v[ts][1], tag=f"mmv{tt}_{ts}")
                ch = (t0 + ts * 128) // 128
                p.add("scalar", (lambda ch=ch, ts=ts, ps_v=ps_v: lambda h: h.activation(
                    V[:, ch * 256:(ch + 1) * 256], ps_v[ts][0], AF.Silu))(),
                    reads=ps_v[ts][1], writes=["V"], tag=f"evv{tt}_{ts}")

            for gi in range(2):
                def mm_gt(tt=tt, par=par, gi=gi, ps_gt=ps_gt):
                    def g(h):
                        last = None
                        for kc in range(KC):
                            last = h.matmul(
                                ps_gt[gi][0],
                                wgate[:, kc * 256 + gi * 128: kc * 256 + (gi + 1) * 128],
                                xtb[:, par * KC * TTA + kc * TTA:
                                    par * KC * TTA + (kc + 1) * TTA],
                                start=(kc == 0), stop=(kc == KC - 1))
                        return last
                    return g
                p.add("tensor", mm_gt(), reads=["wgate", f"xtb{par}"] + ps_gt[gi][1],
                      writes=ps_gt[gi][1] + ([f"xtu{par}"] if gi == 1 else []),
                      tag=f"mmgt{tt}_{gi}")
                p.add("scalar", (lambda gi=gi, t0=t0, ps_gt=ps_gt: lambda h: h.activation(
                    ga[:, gi * T + t0: gi * T + t0 + TTA], ps_gt[gi][0], AF.Sigmoid))(),
                    reads=ps_gt[gi][1], writes=["gate"], tag=f"evgt{tt}_{gi}")


        ps_aw = [[reg(i * 512 + jc * 256, LBS) for jc in range(2)] for i in range(4)]
        ps_att = [reg(2048 + i * 256, LBS, parts=64) for i in range(4)]
        ps_state = [reg(3072 + i * 64, 64, parts=64) for i in range(4)]
        ps_sq = reg(3328, LBS, parts=1)
        ps_out = [reg((oc % 4) * 512 + 256 * 0, LBS) for oc in range(4)]

        p.add("gpsimd", lambda h: h.nop(nofuse=True),
              writes=["wqkv", "wgate", "warena"], tag="a_arena_sw")
        p.add("gpsimd", lambda h: h.dma_start(
            out=ddT.rearrange("p (i jc l) -> p i jc l", i=4, jc=2), in_=ddT_in),
            reads=["warena"], writes=["ddT"], dma=True, group="gcc")
        p.add("gpsimd", lambda h: h.dma_start(
            out=qd.rearrange("p (i l) -> p i l", i=4), in_=qd_in),
            reads=["warena"], writes=["qd"], dma=True, group="gcc")

        for b2 in range(2):
            for blk in range(NBLK):
                bt0 = b2 * S + blk * LBS
                gch = bt0 // 128
                gt = b2 * NBLK + blk

                for i in range(4):
                    qi, pr = i // 2, i % 2
                    rows = slice(pr * 64, pr * 64 + 64)
                    for jc in range(2):
                        p.add("tensor", (lambda qi=qi, rows=rows, bt0=bt0, jc=jc, i=i:
                              lambda h: h.matmul(
                                  ps_aw[i][jc][0],
                                  Kt[rows, qi * T + bt0 + jc * 128:
                                     qi * T + bt0 + (jc + 1) * 128],
                                  Q[rows, qi * T + bt0: qi * T + bt0 + LBS],
                                  start=True, stop=True))(),
                              reads=["Q", "Kt"] + ps_aw[i][jc][1],
                              writes=ps_aw[i][jc][1],
                              tag=f"aw{b2}_{blk}_{i}_{jc}")
                for i in range(4):
                    qi, pr = i // 2, i % 2
                    rows = slice(pr * 64, pr * 64 + 64)
                    for jc in range(2):
                        p.add("vector", (lambda i=i, jc=jc: lambda h: h.tensor_tensor(
                            awm[:, (i * 2 + jc) * LBS:(i * 2 + jc + 1) * LBS],
                            ps_aw[i][jc][0],
                            ddT[:, (i * 2 + jc) * LBS:(i * 2 + jc + 1) * LBS],
                            ALU.mult))(),
                            reads=ps_aw[i][jc][1] + ["ddT", "warena", f"awmu{i}_{jc}"],
                            writes=[f"awm{i}_{jc}"] + ps_aw[i][jc][1],
                            tag=f"awm{b2}_{blk}_{i}_{jc}")
                    if blk > 0:
                        p.add("vector", (lambda i=i, qi=qi, rows=rows, bt0=bt0:
                              lambda h: h.tensor_tensor(
                                  qp[0:64, i * LBS:(i + 1) * LBS],
                                  Q[rows, qi * T + bt0: qi * T + bt0 + LBS],
                                  qd[rows, i * LBS:(i + 1) * LBS], ALU.mult))(),
                              reads=["Q", "qd", "warena", f"qpu{i}"], writes=[f"qp{i}"],
                              tag=f"qp{b2}_{blk}_{i}")
                for i in range(4):
                    pr = i % 2
                    rows = slice(pr * 64, pr * 64 + 64)
                    sslot = slice((b2 * 4 + i) * 64, (b2 * 4 + i) * 64 + 64)

                    def mm_att(b2=b2, blk=blk, i=i, gch=gch, sslot=sslot, rows=rows):
                        def g(h):
                            last = None
                            for jc in range(2):
                                last = h.matmul(
                                    ps_att[i][0], V[:, (gch + jc) * 256 + i * 64:
                                                    (gch + jc) * 256 + (i + 1) * 64],
                                    awm[:, (i * 2 + jc) * LBS:(i * 2 + jc + 1) * LBS],
                                    start=(jc == 0), stop=(blk == 0 and jc == 1))
                            if blk > 0:
                                last = h.matmul(
                                    ps_att[i][0], statesbf[:, sslot],
                                    qp[0:64, i * LBS:(i + 1) * LBS],
                                    start=False, stop=True)
                            return last
                        return g
                    rd = ["V", f"awm{i}_0", f"awm{i}_1"] + ps_att[i][1]
                    wr = ps_att[i][1] + [f"awmu{i}_0", f"awmu{i}_1"]
                    if blk > 0:
                        rd += [f"qp{i}", f"stbf{b2}{i}"]
                        wr += [f"qpu{i}"]
                    p.add("tensor", mm_att(), reads=rd, writes=wr,
                          tag=f"att{b2}_{blk}_{i}")
                for i in range(4):
                    qi, pr = i // 2, i % 2
                    rows = slice(pr * 64, pr * 64 + 64)
                    for jc in range(2):
                        p.add("tensor", (lambda qi=qi, rows=rows, bt0=bt0, jc=jc, i=i:
                              lambda h: h.transpose(
                                  pbf[:, (i * 2 + jc) * 64:(i * 2 + jc + 1) * 64],
                                  Kt[rows, qi * T + bt0 + jc * 128:
                                     qi * T + bt0 + (jc + 1) * 128], id64[rows, :]))(),
                              reads=["Kt", "id64"] + k_pbf, writes=k_pbf,
                              tag=f"ktr{b2}_{blk}_{i}_{jc}")
                for i in range(4):
                    for jc in range(2):
                        p.add("scalar", (lambda i=i, jc=jc: lambda h: h.activation(
                            ktm[:, (i * 2 + jc) * 64:(i * 2 + jc + 1) * 64],
                            pbf[:, (i * 2 + jc) * 64:(i * 2 + jc + 1) * 64],
                            AF.Copy, scale=kd[:, i * 2 + jc: i * 2 + jc + 1]))(),
                            reads=k_pbf + ["kd", f"ktmu{i}_{jc}"],
                            writes=[f"ktm{i}_{jc}"] + k_pbf,
                            tag=f"ktev{b2}_{blk}_{i}_{jc}")
                for i in range(4):
                    def mm_st(i=i, gch=gch):
                        def g(h):
                            last = None
                            for jc in range(2):
                                last = h.matmul(
                                    ps_state[i][0],
                                    ktm[:, (i * 2 + jc) * 64:(i * 2 + jc + 1) * 64],
                                    V[:, (gch + jc) * 256 + i * 64:
                                      (gch + jc) * 256 + (i + 1) * 64],
                                    start=(jc == 0), stop=(jc == 1))
                            return last
                        return g
                    p.add("tensor", mm_st(), reads=[f"ktm{i}_0", f"ktm{i}_1", "V"]
                          + ps_state[i][1],
                          writes=ps_state[i][1] + [f"ktmu{i}_0", f"ktmu{i}_1"],
                          tag=f"stmm{b2}_{blk}_{i}")
                for i in range(4):
                    sslot = slice((b2 * 4 + i) * 64, (b2 * 4 + i) * 64 + 64)
                    if blk == 0:
                        p.add("vector", (lambda sslot=sslot, i=i: lambda h:
                              h.tensor_copy(states[:, sslot], ps_state[i][0]))(),
                              reads=ps_state[i][1], writes=[f"st{b2}{i}"] + ps_state[i][1],
                              tag=f"stini{b2}_{i}")
                    else:
                        p.add("vector", (lambda sslot=sslot, i=i: lambda h:
                              h.scalar_tensor_tensor(
                                  states[:, sslot], states[:, sslot],
                                  bd[:, i:i + 1], ps_state[i][0],
                                  ALU.mult, ALU.add))(),
                              reads=ps_state[i][1] + [f"st{b2}{i}", "bd"],
                              writes=[f"st{b2}{i}"] + ps_state[i][1],
                              tag=f"stup{b2}_{blk}_{i}")
                    p.add("scalar", (lambda sslot=sslot: lambda h: h.copy(
                        statesbf[:, sslot], states[:, sslot]))(),
                        reads=[f"st{b2}{i}"], writes=[f"stbf{b2}{i}"],
                        tag=f"stbf{b2}_{blk}_{i}")
                for i in range(4):
                    p.add("scalar", (lambda i=i: lambda h: h.activation(
                        sqscr[:, i * LBS:(i + 1) * LBS], ps_att[i][0], AF.Square))(),
                        reads=ps_att[i][1] + [f"sqsu{i}"], writes=[f"sqscr{i}"],
                        tag=f"sq{b2}_{blk}_{i}")
                for i in range(4):
                    p.add("tensor", (lambda i=i: lambda h: h.matmul(
                        ps_sq[0], onesf[:, 0:1], sqscr[:, i * LBS:(i + 1) * LBS],
                        start=(i == 0), stop=(i == 3)))(),
                        reads=[f"sqscr{i}", "onesf"] + ps_sq[1],
                        writes=ps_sq[1] + [f"sqsu{i}"],
                        tag=f"sqmm{b2}_{blk}_{i}")
                for i in range(4):
                    qi, pr = i // 2, i % 2
                    rows = slice(pr * 64, pr * 64 + 64)
                    p.add("vector", (lambda qi=qi, rows=rows, bt0=bt0, i=i: lambda h:
                          h.tensor_tensor(
                              ga[rows, qi * T + bt0: qi * T + bt0 + LBS],
                              ps_att[i][0],
                              ga[rows, qi * T + bt0: qi * T + bt0 + LBS],
                              ALU.mult))(),
                          reads=ps_att[i][1] + [f"ga{gt}", f"sqscr{i}"],
                          writes=[f"ga{gt}"] + ps_att[i][1],
                          tag=f"gmul{b2}_{blk}_{i}")
                sqp = blk % 2
                p.add("scalar", (lambda sqp=sqp: lambda h: h.activation(
                    ssqrow[0:1, sqp * LBS:(sqp + 1) * LBS], ps_sq[0], AF.Copy))(),
                    reads=ps_sq[1] + [f"ssqu{sqp}"], writes=[f"ssqrow{sqp}"] + ps_sq[1],
                    tag=f"ssqev{b2}_{blk}")
                p.add("gpsimd", (lambda bt0=bt0, sqp=sqp: lambda h: h.dma_start(
                    out=sumsq_out[0:1, bt0:bt0 + LBS],
                    in_=ssqrow[0:1, sqp * LBS:(sqp + 1) * LBS]))(),
                    reads=[f"ssqrow{sqp}"], writes=[f"ssqu{sqp}", "ssqout"],
                    dma=True, group="gsq", tag=f"ssqst{b2}_{blk}")

                if blk % 2 == 1:
                    pt0 = b2 * S + (blk - 1) * LBS
                    gts = [gt - 1, gt]
                    stp = 0
                    for oc in range(KC):
                        opar = oc % 4
                        grp = oc // 4

                        po = pool[:, opar * 512: opar * 512 + 512]
                        kpo = psbk(opar * 512, 512)

                        def mm_o(oc=oc, opar=opar, pt0=pt0, po=po):
                            def g(h):
                                last = None
                                for c2 in range(2):
                                    last = h.matmul(
                                        po,
                                        woutw[:, c2 * DM + oc * 128:
                                              c2 * DM + (oc + 1) * 128],
                                        ga[:, c2 * T + pt0: c2 * T + pt0 + 2 * LBS],
                                        start=(c2 == 0), stop=(c2 == 1))
                                return last
                            return g
                        p.add("tensor", mm_o(),
                              reads=["woutw"] + [f"ga{g2}" for g2 in gts] + kpo,
                              writes=kpo,
                              tag=f"mmo{b2}_{blk}_{oc}")
                        p.add("scalar", (lambda oc=oc, stp=stp, po=po: lambda h:
                              h.activation(ostage[:, (stp * 4 + oc % 4) * 2 * LBS:
                                                  (stp * 4 + oc % 4 + 1) * 2 * LBS],
                                           po, AF.Copy))(),
                              reads=kpo + [f"osu{stp}_{grp}"],
                              writes=[f"ost{stp}_{oc % 4}"] + kpo,
                              tag=f"oev{b2}_{blk}_{oc}")
                        if oc % 4 == 3:
                            p.add("gpsimd", (lambda pt0=pt0, stp=stp, grp=grp:
                                  lambda h: h.dma_start(
                                      out=partial_out[:, grp * 4:(grp + 1) * 4,
                                                      pt0:pt0 + 2 * LBS],
                                      in_=ostage[:, stp * 8 * LBS:(stp + 1) * 8 * LBS]
                                          .rearrange("p (c t) -> p c t", c=4)))(),
                                  reads=[f"ost{stp}_{g4}" for g4 in range(4)],
                                  writes=[f"osu{stp}_{grp}", "pout"]
                                  + [f"ost{stp}_{g4}" for g4 in range(4)],
                                  dma=True, group=f"go{stp}", tag=f"ost{b2}_{blk}_{grp}")

        with nc.Block() as block:
            p.emit_with_sems(st, block, final_engine="gpsimd",
                             final_waits=["pout", "ssqout"])
    return nc


def lightning_consts(heads):
    Hh = 32
    base = 1.0 / 2.0 ** (8.0 / Hh)
    rate = np.float32(base) ** (np.arange(Hh, dtype=np.float32) + 1.0)
    factor = np.float32(1.0 - 0.0 / (32 - 1 + 1e-5) + 1e-5)
    slope = (rate * factor).astype(np.float32)
    r = np.arange(LBS, dtype=np.float32) + 1.0
    ddT = np.zeros((128, 4, 2, LBS), np.float32)
    qdv = np.zeros((128, 4, LBS), np.float32)
    kdv = np.zeros((128, 8), np.float32)
    bdv = np.zeros((64, 4), np.float32)
    for ii, h in enumerate(heads):
        s = slope[h]
        qdv[:, ii, :] = np.exp(-s * r)[None, :]
        dd = r[:, None] - r[None, :]
        ddm = np.where(dd >= 0, np.exp(-s * dd), 0.0).astype(np.float32)
        ddTm = ddm.T
        for jc in range(2):
            ddT[:, ii, jc, :] = ddTm[jc * 128:(jc + 1) * 128, :]
        kdfull = np.exp(-s * (LBS - r)).astype(np.float32)
        for jc in range(2):
            kdv[:, ii * 2 + jc] = kdfull[jc * 128:(jc + 1) * 128]
        bdv[:, ii] = np.exp(np.float32(-s * LBS))
    return ddT, qdv, kdv, bdv


def host_A(x0, qkv_w, gate_w, out_w, norm_w):
    x0T = np.ascontiguousarray(
        x0.reshape(T, DM).T.reshape(KC, 128, T).transpose(1, 0, 2), dtype=np.float32)
    maps = []
    for c in range(NCORES):
        heads = [4 * c + i for i in range(4)]
        cols = []
        for grp in range(3):
            for i in range(4):
                h0 = heads[i] * 192 + grp * 64
                cols.extend(range(h0, h0 + 64))
        wq = qkv_w[cols, :]
        wqkv = np.ascontiguousarray(
            wq.T.reshape(KC, 128, 768).transpose(1, 0, 2), dtype=np.float32)
        gsl = gate_w[256 * c:256 * c + 256, :]
        wgate = np.ascontiguousarray(
            gsl.T.reshape(KC, 128, 256).transpose(1, 0, 2), dtype=np.float32)
        osl = out_w[:, 256 * c:256 * c + 256]
        wout = np.ascontiguousarray(
            osl.T.reshape(2, 128, DM).transpose(1, 0, 2), dtype=np.float32)
        nsl = np.ascontiguousarray(
            norm_w[256 * c:256 * c + 256].reshape(2, 128).T, dtype=np.float32)
        ddT, qdv, kdv, bdv = lightning_consts(heads)
        maps.append({"x0T_in": x0T, "wqkv_in": wqkv, "wgate_in": wgate,
                     "wout_in": wout, "normw_in": nsl, "ddT_in": ddT,
                     "qd_in": qdv, "kd_in": kdv, "bd_in": bdv})
    return maps


def run_A(x0, qkv_w, gate_w, out_w, norm_w):
    nc = build_A()
    res = run_bass_kernel_spmd(nc, host_A(x0, qkv_w, gate_w, out_w, norm_w),
                               core_ids=list(range(NCORES)))
    partials = [r["partial_out"] for r in res.results]
    sumsqs = [r["sumsq_out"][0] for r in res.results]
    return partials, sumsqs



PSBANK = 512


def psbk(off, width):
    return [f"psB{k}" for k in range(off // PSBANK, (off + width - 1) // PSBANK + 1)]


COMMON_ROWSLOTS = ("MEAN", "VAR", "AROW", "BROW", "SCR1")


def make_common(nc, st, p, pool, NT=TPC, offs=(0, 1024, 2048, 2560)):
    HT = NT // 2
    ct = dict(NT=NT, HT=HT)
    ct["ones"] = st.enter_context(nc.sbuf_tensor("c_ones", [128, 1], F32))
    ct["onesrow"] = st.enter_context(nc.sbuf_tensor("c_onesrow", [1, 128], F32))
    ct["eps"] = st.enter_context(nc.sbuf_tensor("c_eps", [1, 1], F32))
    ct["rows"] = st.enter_context(nc.sbuf_tensor("c_rows", [1, NT * len(COMMON_ROWSLOTS)], F32))
    ct["sqscr"] = st.enter_context(nc.sbuf_tensor("c_sqscr", [128, NT], F32))
    o_r, o_r2, o_a, o_b = offs
    ct["ps_row"] = pool[0:1, o_r:o_r + NT]
    ct["ps_row2"] = pool[0:1, o_r2:o_r2 + NT]
    ct["ps_a"] = pool[:, o_a:o_a + HT]
    ct["ps_b"] = pool[:, o_b:o_b + HT]
    ct["k_row"] = psbk(o_r, NT)
    ct["k_row2"] = psbk(o_r2, NT)
    ct["k_a"] = psbk(o_a, HT)
    ct["k_b"] = psbk(o_b, HT)
    for i, nm in enumerate(COMMON_ROWSLOTS):
        ct[nm] = i * NT
    p.add("vector", lambda h: h.memset(ct["ones"][:], 1.0), writes=["c_ones"])
    p.add("vector", lambda h: h.memset(ct["onesrow"][:], 1.0), writes=["c_onesrow"])
    p.add("vector", lambda h: h.memset(ct["eps"][:], 1e-5), writes=["c_eps"])


    def row(slot, lo=0, hi=None):
        off = ct[slot]
        return ct["rows"][0:1, off + lo: off + (NT if hi is None else hi)]
    ct["row"] = row
    return ct


def add_ln_fm(p, ct, acc, acc_res, tag=""):
    NT, HT = ct["NT"], ct["HT"]
    row, ones, onesrow, eps = ct["row"], ct["ones"], ct["onesrow"], ct["eps"]
    sqscr, ps_row, ps_row2, ps_a, ps_b = (
        ct["sqscr"], ct["ps_row"], ct["ps_row2"], ct["ps_a"], ct["ps_b"])
    k_row, k_row2, k_a, k_b = ct["k_row"], ct["k_row2"], ct["k_a"], ct["k_b"]

    def mm_sum(h):
        last = None
        for c in range(KC):
            for hh in range(2):
                last = h.matmul(
                    ps_row[0:1, hh * HT:(hh + 1) * HT], ones[:, 0:1],
                    acc[:, c * NT + hh * HT: c * NT + (hh + 1) * HT],
                    start=(c == 0), stop=(c == KC - 1))
        return last
    p.add("tensor", mm_sum, reads=list(acc_res) + ["c_ones"], writes=k_row,
          tag=f"lnsum{tag}")

    for c in range(KC):
        p.add("scalar",
              (lambda c=c: lambda h: h.activation(
                  sqscr[:, 0:NT],
                  acc[:, c * NT:(c + 1) * NT], AF.Square))(),
              reads=list(acc_res) + ["c_sqscru"], writes=["c_sqscr0"],
              tag=f"sq{tag}{c}")

        def mmq(c=c):
            def g(h):
                last = None
                for hh in range(2):
                    last = h.matmul(
                        ps_row2[0:1, hh * HT:(hh + 1) * HT], ones[:, 0:1],
                        sqscr[:, hh * HT:(hh + 1) * HT],
                        start=(c == 0), stop=(c == KC - 1))
                return last
            return g
        p.add("tensor", mmq(), reads=["c_sqscr0", "c_ones"],
              writes=k_row2 + ["c_sqscru"], tag=f"mmq{tag}{c}")

    p.add("scalar", lambda h: h.activation(
        row("MEAN"), ps_row[0:1, :], AF.Copy, scale=1.0 / DM),
        reads=k_row, writes=["c_rMEAN"])
    p.add("scalar", lambda h: h.activation(
        row("VAR"), ps_row2[0:1, :], AF.Copy, scale=1.0 / DM),
        reads=k_row2, writes=["c_rVAR"])
    p.add("scalar", lambda h: h.activation(row("SCR1"), row("MEAN"), AF.Square),
          reads=["c_rMEAN"], writes=["c_rSCR1"])
    p.add("vector", lambda h: h.tensor_tensor(
        row("VAR"), row("VAR"), row("SCR1"), ALU.subtract),
        reads=["c_rVAR", "c_rSCR1"], writes=["c_rVAR"])
    p.add("scalar", lambda h: h.activation(
        row("VAR"), row("VAR"), AF.Sqrt, bias=eps[0:1, 0:1]),
        reads=["c_rVAR", "c_eps"], writes=["c_rVAR"])
    p.add("vector", lambda h: h.reciprocal(row("AROW"), row("VAR")),
          reads=["c_rVAR"], writes=["c_rAROW"])
    p.add("vector", lambda h: h.tensor_tensor(
        row("BROW"), row("MEAN"), row("AROW"), ALU.mult),
        reads=["c_rMEAN", "c_rAROW"], writes=["c_rBROW"])
    p.add("scalar", lambda h: h.mul(row("BROW"), row("BROW"), -1.0),
          reads=["c_rBROW"], writes=["c_rBROW"])

    for hh in range(2):
        lo = hh * HT
        p.add("tensor",
              (lambda lo=lo: lambda h: h.matmul(
                  ps_a[:], onesrow[:], row("AROW", lo, lo + HT),
                  start=True, stop=True))(),
              reads=["c_rAROW", "c_onesrow"], writes=k_a, tag=f"bca{tag}{hh}")
        p.add("tensor",
              (lambda lo=lo: lambda h: h.matmul(
                  ps_b[:], onesrow[:], row("BROW", lo, lo + HT),
                  start=True, stop=True))(),
              reads=["c_rBROW", "c_onesrow"], writes=k_b, tag=f"bcb{tag}{hh}")
        for c in range(KC):
            p.add("vector",
                  (lambda c=c, lo=lo: lambda h: h.tensor_tensor(
                      acc[:, c * NT + lo: c * NT + lo + HT],
                      acc[:, c * NT + lo: c * NT + lo + HT],
                      ps_a[:], ALU.mult))(),
                  reads=list(acc_res) + k_a, writes=list(acc_res),
                  tag=f"lnm{tag}{hh}_{c}")
            p.add("vector",
                  (lambda c=c, lo=lo: lambda h: h.tensor_tensor(
                      acc[:, c * NT + lo: c * NT + lo + HT],
                      acc[:, c * NT + lo: c * NT + lo + HT],
                      ps_b[:], ALU.add))(),
                  reads=list(acc_res) + k_b, writes=list(acc_res),
                  tag=f"lna{tag}{hh}_{c}")


def build_B(upto=99):
    nc = bass.Bass("TRN2", target_bir_lowering=False)
    p8 = nc.dram_tensor("p8", [8, 128, KC, TPC], F32, kind="ExternalInput").ap()
    sq8 = nc.dram_tensor("sq8", [8, TPC], F32, kind="ExternalInput").ap()
    x0c = nc.dram_tensor("x0c", [128, KC, TPC], F32, kind="ExternalInput").ap()
    x1_out = nc.dram_tensor("x1_out", [128, KC, TPC], F32, kind="ExternalOutput").ap()
    srp_out = nc.dram_tensor("srp_out", [128, KC], F32, kind="ExternalOutput").ap()

    HT = TPC // 2

    import contextlib
    with contextlib.ExitStack() as st:
        acc = st.enter_context(nc.sbuf_tensor("acc", [128, KC * TPC], F32))
        x0 = st.enter_context(nc.sbuf_tensor("x0", [128, KC * TPC], F32))
        sqs = st.enter_context(nc.sbuf_tensor("sq", [8, TPC], F32))
        srp = st.enter_context(nc.sbuf_tensor("srp", [128, KC], F32))
        pool = st.enter_context(nc.psum_tensor("pool", [128, 4096], F32))

        p = P(nc)
        ct = make_common(nc, st, p, pool, NT=TPC)
        row, ones, onesrow, eps = ct["row"], ct["ones"], ct["onesrow"], ct["eps"]
        ps_row, ps_a = ct["ps_row"], ct["ps_a"]
        k_row, k_a = ct["k_row"], ct["k_a"]
        ACCV = lambda: acc[:].rearrange("p (c t) -> p c t", c=KC)

        p.add("vector", lambda h: h.memset(srp[:], 0.0), writes=["srp"])

        p.add("sync", lambda h: h.dma_start(
            out=x0[:].rearrange("p (c t) -> p c t", c=KC), in_=x0c),
            writes=["x0"], dma=True, group="gx0")
        p.add("sync", lambda h: h.dma_start(out=sqs[:], in_=sq8),
              writes=["sq"], dma=True, group="gsq")

        NSL = (KC * TPC) // 2048
        for j in range(8):
            for s in range(NSL):
                sl_out = acc[:, s * 2048:(s + 1) * 2048]
                if j == 0:
                    p.add("gpsimd", (lambda s=s, sl_out=sl_out: lambda h: h.dma_start(
                        out=sl_out, in_=p8[0, :, 2 * s:2 * s + 2, :]))(),
                        writes=[f"acc{s}"], dma=True, group=f"gacc{s}")
                else:
                    p.add("gpsimd", (lambda j=j, s=s, sl_out=sl_out: lambda h:
                          h.dma_start(out=sl_out, in_=p8[j, :, 2 * s:2 * s + 2, :],
                                      accum_op=ALU.add))(),
                          reads=[f"acc{s}"], writes=[f"acc{s}"], dma=True,
                          group=f"gacc{s}")
        ACC_RES = [f"acc{s}" for s in range(NSL)]

        if upto >= 2:
            def mm_sqsum(h):
                h.matmul(ps_row[0:1, 0:HT], ones[0:8, 0:1], sqs[:, 0:HT],
                         start=True, stop=True)
                return h.matmul(ps_row[0:1, HT:TPC], ones[0:8, 0:1], sqs[:, HT:TPC],
                                start=True, stop=True)
            p.add("tensor", mm_sqsum, reads=["sq", "c_ones"], writes=k_row)
            p.add("scalar", lambda h: h.activation(
                row("VAR"), ps_row[0:1, :], AF.Sqrt, bias=eps[0:1, 0:1], scale=1.0 / DM),
                reads=k_row + ["c_eps"], writes=["c_rVAR"])
            p.add("vector", lambda h: h.reciprocal(row("MEAN"), row("VAR")),
                  reads=["c_rVAR"], writes=["c_rRSTD1"])

            for hh in range(2):
                lo = hh * HT
                p.add("tensor",
                      (lambda lo=lo: lambda h: h.matmul(
                          ps_a[:], onesrow[:], row("MEAN", lo, lo + HT),
                          start=True, stop=True))(),
                      reads=["c_rRSTD1", "c_onesrow"], writes=k_a)
                for c in range(KC):
                    p.add("vector",
                          (lambda c=c, lo=lo: lambda h: h.tensor_tensor(
                              acc[:, c * TPC + lo: c * TPC + lo + HT],
                              acc[:, c * TPC + lo: c * TPC + lo + HT],
                              ps_a[:], ALU.mult))(),
                          reads=ACC_RES + k_a, writes=ACC_RES, tag=f"sc{hh}_{c}")
                    p.add("vector",
                          (lambda c=c, lo=lo: lambda h: h.tensor_tensor(
                              acc[:, c * TPC + lo: c * TPC + lo + HT],
                              acc[:, c * TPC + lo: c * TPC + lo + HT],
                              x0[:, c * TPC + lo: c * TPC + lo + HT], ALU.add))(),
                          reads=ACC_RES + ["x0"], writes=ACC_RES, tag=f"ax{hh}_{c}")

        if upto >= 3:
            add_ln_fm(p, ct, acc, ACC_RES, tag="B")

        if upto >= 4:
            def srp_f(h):
                last = None
                for c in range(KC):
                    last = h.reduce_sum(srp[:, c:c + 1],
                                        acc[:, c * TPC:(c + 1) * TPC], axis=AX.X)
                return last
            p.add("vector", srp_f, reads=ACC_RES, writes=["srp"])

        p.add("gpsimd", lambda h: h.dma_start(out=x1_out, in_=ACCV()),
              reads=ACC_RES, writes=["out1"], dma=True, group="gout")
        p.add("gpsimd", lambda h: h.dma_start(out=srp_out, in_=srp[:]),
              reads=["srp"], writes=["out2"], dma=True, group="gout")

        with nc.Block() as block:
            p.emit_with_sems(st, block, final_engine="gpsimd",
                             final_waits=["out1", "out2"])
    return nc



def build_C():
    nc = bass.Bass("TRN2", target_bir_lowering=False)
    ident64 = nc.inline_tensor(np.eye(64, dtype=bf16), name="identity64c")
    x1_in = nc.dram_tensor("x1_in", [128, KC, TPC], F32, kind="ExternalInput").ap()
    srp4 = nc.dram_tensor("srp4", [128, 4 * KC], F32, kind="ExternalInput").ap()
    simm = nc.dram_tensor("simm", [128, KC, 32], F32, kind="ExternalInput").ap()
    gates_in = nc.dram_tensor("gates_in", [1, 32], F32, kind="ExternalInput").ap()
    wv_in = nc.dram_tensor("wv_in", [32, 64, DM], F32, kind="ExternalInput").ap()
    ow_in = nc.dram_tensor("ow_in", [32, 64, DM], F32, kind="ExternalInput").ap()
    x2_out = nc.dram_tensor("x2_out", [128, KC, TPC], F32, kind="ExternalOutput").ap()
    rt_out = nc.dram_tensor("rt_out", [1, 32], F32, kind="ExternalOutput").ap()

    HT = TPC // 2
    NE = 32

    import contextlib
    with contextlib.ExitStack() as st:
        x1 = st.enter_context(nc.sbuf_tensor("x1", [128, KC * TPC], F32))
        x1b = st.enter_context(nc.sbuf_tensor("x1b", [128, KC * TPC], BF16))
        srp_sb = st.enter_context(nc.sbuf_tensor("srp_sb", [128, 4 * KC], F32))
        s_sb = st.enter_context(nc.sbuf_tensor("s_sb", [128, KC], F32))
        s2_sb = st.enter_context(nc.sbuf_tensor("s2_sb", [128, KC], F32))
        sim_sb = st.enter_context(nc.sbuf_tensor("sim_sb", [128, KC * 32], F32))
        sim2_sb = st.enter_context(nc.sbuf_tensor("sim2_sb", [128, KC * 32], F32))
        id64 = st.enter_context(nc.sbuf_tensor("id64", [64, 64], BF16))
        g_sb = st.enter_context(nc.sbuf_tensor("g_sb", [1, 32 * 12], F32))
        r_bc = st.enter_context(nc.sbuf_tensor("r_bc", [64, 32], F32))
        rIe = st.enter_context(nc.sbuf_tensor("rIe", [64, 2 * 64], BF16))
        wvb = st.enter_context(nc.sbuf_tensor("wvb", [64, 2 * DM], BF16))
        owb = st.enter_context(nc.sbuf_tensor("owb", [64, 2 * DM], BF16))
        weT = st.enter_context(nc.sbuf_tensor("weT", [128, KC * 64], BF16))
        weTacc = st.enter_context(nc.sbuf_tensor("weTacc", [128, KC * 64], F32))
        wod = st.enter_context(nc.sbuf_tensor("wod", [64, DM], BF16))
        comb = st.enter_context(nc.sbuf_tensor("comb", [64, TPC], BF16))
        pool = st.enter_context(nc.psum_tensor("pool", [128, 4096], F32))
        ps_wo = pool[0:64, 1024:1024 + DM]
        k_wo = psbk(1024, DM)
        ps_c = pool[0:64, 3072:3072 + HT]
        k_c = psbk(3072, HT)
        ps_f = pool[:, 3584:3584 + HT]
        k_f = psbk(3584, HT)
        ps_g = pool[0:1, 3072:3072 + 128]
        k_g = psbk(3072, 128)
        ps_rb = pool[0:64, 3584:3584 + 32]
        k_rb = psbk(3584, 32)

        p = P(nc)
        ct = make_common(nc, st, p, pool, NT=TPC)
        row, ones, onesrow, eps = ct["row"], ct["ones"], ct["onesrow"], ct["eps"]

        X2V = lambda: x1[:].rearrange("p (c t) -> p c t", c=KC)

        GS = {nm: i * 32 for i, nm in enumerate(
            ["AFF", "LOG", "GATED", "MASKA", "WORK", "FB", "IND", "SCR",
             "SEL", "ROUT", "EXP", "SC1"])}

        def grow(nm, lo=0, hi=32):
            off = GS[nm]
            return g_sb[0:1, off + lo: off + hi]

        p.add("sync", lambda h: h.dma_start(out=X2V(), in_=x1_in),
              writes=["x1"], dma=True, group="gx1")
        p.add("gpsimd", lambda h: h.dma_start(
            out=x1b[:].rearrange("p (c t) -> p c t", c=KC), in_=x1_in),
            writes=["x1b"], dma=True, group="gx1b")
        p.add("sync", lambda h: h.dma_start(out=srp_sb[:], in_=srp4),
              writes=["srp_sb"], dma=True, group="gsrp")
        p.add("sync", lambda h: h.dma_start(
            out=sim_sb[:].rearrange("p (c e) -> p c e", e=32), in_=simm),
            writes=["sim_sb"], dma=True, group="gsim")
        p.add("sync", lambda h: h.dma_start(out=grow("LOG"), in_=gates_in),
              writes=["ggates", "gLOG"], dma=True, group="ggate")
        p.add("sync", lambda h: h.dma_start(out=id64[:], in_=ident64.ap()),
              writes=["id64"], dma=True, group="gid")

        def T(*slots):
            return [f"g{s}" for s in slots]

        p.add("vector", lambda h: h.tensor_tensor(
            s_sb[:], srp_sb[:, 0:KC], srp_sb[:, KC:2 * KC], ALU.add),
            reads=["srp_sb"], writes=["s_sb"])
        p.add("vector", lambda h: h.tensor_tensor(
            s2_sb[:], srp_sb[:, 2 * KC:3 * KC], srp_sb[:, 3 * KC:4 * KC], ALU.add),
            reads=["srp_sb"], writes=["s2_sb"])
        p.add("vector", lambda h: h.tensor_tensor(
            s_sb[:], s_sb[:], s2_sb[:], ALU.add),
            reads=["s_sb", "s2_sb"], writes=["s_sb"])
        p.add("scalar", lambda h: h.activation(s2_sb[:], s_sb[:], AF.Square),
              reads=["s_sb"], writes=["s2_sb"])
        p.add("tensor", lambda h: h.matmul(ps_g[0:1, 0:KC], ones[:, 0:1], s2_sb[:],
                                           start=True, stop=True),
              reads=["s2_sb", "c_ones"], writes=k_g)
        p.add("vector", lambda h: h.reduce_sum(grow("SC1", 0, 1), ps_g[0:1, 0:KC], axis=AX.X),
              reads=k_g, writes=T("SC1_0"))
        p.add("scalar", lambda h: h.activation(sim2_sb[:], sim_sb[:], AF.Square),
              reads=["sim_sb"], writes=["sim2_sb"])

        def mm_cn(h):
            last = None
            for c in range(KC):
                last = h.matmul(ps_g[0:1, 32:64], ones[:, 0:1],
                                sim2_sb[:, c * 32:(c + 1) * 32],
                                start=(c == 0), stop=(c == KC - 1))
            return last
        p.add("tensor", mm_cn, reads=["sim2_sb", "c_ones"] + k_g, writes=k_g)

        def mm_num(h):
            last = None
            for c in range(KC):
                last = h.matmul(ps_g[0:1, 64:96], s_sb[:, c:c + 1],
                                sim_sb[:, c * 32:(c + 1) * 32],
                                start=(c == 0), stop=(c == KC - 1))
            return last
        p.add("tensor", mm_num, reads=["sim_sb", "s_sb"] + k_g, writes=k_g)

        p.add("vector", lambda h: h.memset(grow("SC1", 3, 4), 0.0), writes=T("SC1_3"))
        p.add("vector", lambda h: h.memset(grow("SC1", 5, 6), 1.0e30), writes=T("SC1_5"))
        p.add("vector", lambda h: h.memset(grow("SC1", 8, 9), 1.0), writes=T("SC1_8"))
        p.add("vector", lambda h: h.memset(grow("SC1", 9, 10), 3.4e38), writes=T("SC1_9"))

        p.add("scalar", lambda h: h.activation(grow("SC1", 1, 2), grow("SC1", 0, 1), AF.Sqrt),
              reads=T("SC1_0"), writes=T("SC1_1"))
        p.add("vector", lambda h: h.reciprocal(grow("SC1", 2, 3), grow("SC1", 1, 2)),
              reads=T("SC1_1"), writes=T("SC1_2"))
        p.add("scalar", lambda h: h.activation(grow("SCR"), ps_g[0:1, 32:64], AF.Sqrt),
              reads=k_g, writes=T("SCR"))
        p.add("vector", lambda h: h.reciprocal(grow("IND"), grow("SCR")),
              reads=T("SCR"), writes=T("IND"))
        p.add("vector", lambda h: h.tensor_tensor(
            grow("AFF"), ps_g[0:1, 64:96], grow("IND"), ALU.mult),
            reads=k_g + T("IND"), writes=T("AFF"))
        p.add("vector", lambda h: h.tensor_single_scalar(
            grow("AFF"), grow("AFF"), grow("SC1", 2, 3), ALU.mult),
            reads=T("AFF", "SC1_2"), writes=T("AFF"))
        p.add("scalar", lambda h: h.activation(grow("SCR"), grow("LOG"), AF.Sigmoid),
              reads=["ggates"] + T("LOG"), writes=T("SCR"))
        p.add("vector", lambda h: h.tensor_tensor(
            grow("LOG"), grow("AFF"), grow("SCR"), ALU.subtract),
            reads=T("AFF", "SCR", "LOG") + ["ggates"], writes=T("LOG"))
        p.add("vector", lambda h: h.tensor_relu(grow("GATED"), grow("LOG")),
              reads=T("LOG"), writes=T("GATED"))
        p.add("vector", lambda h: h.tensor_single_scalar(
            grow("MASKA"), grow("GATED"), grow("SC1", 3, 4), ALU.is_gt),
            reads=T("GATED", "SC1_3"), writes=T("MASKA"))

        p.add("vector", lambda h: h.tensor_copy(grow("WORK"), grow("LOG")),
              reads=T("LOG"), writes=T("WORK"))
        p.add("vector", lambda h: h.memset(grow("FB"), 0.0), writes=T("FB"))
        for it in range(16):
            p.add("vector", lambda h: h.reduce_max(grow("SC1", 4, 5), grow("WORK"), axis=AX.X),
                  reads=T("WORK"), writes=T("SC1_4"), tag=f"tk{it}")
            p.add("vector", lambda h: h.tensor_single_scalar(
                grow("IND"), grow("WORK"), grow("SC1", 4, 5), ALU.is_ge),
                reads=T("WORK", "SC1_4"), writes=T("IND"))
            p.add("vector", lambda h: h.tensor_tensor(
                grow("FB"), grow("FB"), grow("IND"), ALU.max),
                reads=T("FB", "IND"), writes=T("FB"))
            p.add("vector", lambda h: h.tensor_single_scalar(
                grow("IND"), grow("IND"), grow("SC1", 5, 6), ALU.mult),
                reads=T("IND", "SC1_5"), writes=T("IND"))
            p.add("vector", lambda h: h.tensor_tensor(
                grow("WORK"), grow("WORK"), grow("IND"), ALU.subtract),
                reads=T("WORK", "IND"), writes=T("WORK"))
        p.add("vector", lambda h: h.reduce_sum(grow("SC1", 6, 7), grow("MASKA"), axis=AX.X),
              reads=T("MASKA"), writes=T("SC1_6"))
        p.add("vector", lambda h: h.tensor_single_scalar(
            grow("SC1", 7, 8), grow("SC1", 6, 7), grow("SC1", 3, 4), ALU.is_le),
            reads=T("SC1_6", "SC1_3"), writes=T("SC1_7"))
        p.add("vector", lambda h: h.tensor_tensor(
            grow("SCR"), grow("MASKA"), grow("FB"), ALU.max),
            reads=T("MASKA", "FB"), writes=T("SCR"))
        p.add("vector", lambda h: h.tensor_tensor(
            grow("SCR"), grow("SCR"), grow("MASKA"), ALU.subtract),
            reads=T("SCR", "MASKA"), writes=T("SCR"))
        p.add("vector", lambda h: h.tensor_single_scalar(
            grow("SCR"), grow("SCR"), grow("SC1", 7, 8), ALU.mult),
            reads=T("SCR", "SC1_7"), writes=T("SCR"))
        p.add("vector", lambda h: h.tensor_tensor(
            grow("MASKA"), grow("MASKA"), grow("SCR"), ALU.add),
            reads=T("MASKA", "SCR"), writes=T("MASKA"))
        p.add("vector", lambda h: h.tensor_tensor(
            grow("SEL"), grow("GATED"), grow("MASKA"), ALU.mult),
            reads=T("GATED", "MASKA"), writes=T("SEL"))
        p.add("vector", lambda h: h.tensor_single_scalar(
            grow("SCR"), grow("MASKA"), grow("SC1", 8, 9), ALU.subtract),
            reads=T("MASKA", "SC1_8"), writes=T("SCR"))
        p.add("vector", lambda h: h.tensor_single_scalar(
            grow("SCR"), grow("SCR"), grow("SC1", 9, 10), ALU.mult),
            reads=T("SCR", "SC1_9"), writes=T("SCR"))
        p.add("vector", lambda h: h.tensor_tensor(
            grow("SEL"), grow("SEL"), grow("SCR"), ALU.add),
            reads=T("SEL", "SCR"), writes=T("SEL"))
        p.add("vector", lambda h: h.reduce_max(grow("SC1", 10, 11), grow("SEL"),
                                               axis=AX.X, negate=True),
              reads=T("SEL"), writes=T("SC1_10"))
        p.add("scalar", lambda h: h.activation(
            grow("EXP"), grow("SEL"), AF.Exp, bias=grow("SC1", 10, 11),
            accum_out=grow("SC1", 11, 12)),
            reads=T("SEL", "SC1_10"), writes=T("EXP", "SC1_11"))
        p.add("vector", lambda h: h.reciprocal(grow("SC1", 10, 11), grow("SC1", 11, 12)),
              reads=T("SC1_11"), writes=T("SC1_10"))
        p.add("vector", lambda h: h.tensor_single_scalar(
            grow("ROUT"), grow("EXP"), grow("SC1", 10, 11), ALU.mult),
            reads=T("EXP", "SC1_10"), writes=T("ROUT") + ["grout"])
        p.add("tensor", lambda h: h.matmul(
            ps_rb, onesrow[0:1, 0:64], grow("ROUT"), start=True, stop=True),
            reads=T("ROUT") + ["c_onesrow"] + k_rb, writes=k_rb, tag="rbc")
        p.add("vector", lambda h: h.tensor_copy(r_bc[:], ps_rb),
            reads=k_rb, writes=["r_bc"])

        p.add("vector", lambda h: h.memset(weTacc[:], 0.0), writes=["weTacc"])
        for e in range(NE):
            par = e % 2
            p.add("gpsimd", (lambda e=e, par=par: lambda h: h.dma_start(
                out=wvb[:, par * DM:(par + 1) * DM], in_=wv_in[e]))(),
                reads=[f"wvuse{par}"], writes=[f"wvb{par}"], dma=True, group="gwv",
                tag=f"wvld{e}")
            p.add("gpsimd", (lambda e=e, par=par: lambda h: h.dma_start(
                out=owb[:, par * DM:(par + 1) * DM], in_=ow_in[e]))(),
                reads=[f"owuse{par}"], writes=[f"owb{par}"], dma=True, group="gow",
                tag=f"owld{e}")
            p.add("vector", (lambda e=e, par=par: lambda h: h.tensor_single_scalar(
                rIe[:, par * 64:(par + 1) * 64], id64[:], r_bc[0:64, e:e + 1], ALU.mult))(),
                reads=["id64", "r_bc", f"rIuse{par}"], writes=[f"rIe{par}"],
                tag=f"rIe{e}")

            for cc in range(KC):
                slot_off = (cc % 2) * 512 + ((cc // 2) % 8) * 64
                kslot = [f"psB{slot_off // PSBANK}"]
                p.add("tensor", (lambda e=e, par=par, cc=cc, slot_off=slot_off: lambda h: h.matmul(
                    pool[:, slot_off:slot_off + 64],
                    wvb[:, par * DM + cc * 128: par * DM + (cc + 1) * 128],
                    rIe[:, par * 64:(par + 1) * 64],
                    start=True, stop=True))(),
                    reads=[f"wvb{par}", f"rIe{par}"] + kslot, writes=kslot,
                    tag=f"mmwe{e}_{cc}")
                p.add("vector", (lambda cc=cc, slot_off=slot_off: lambda h: h.tensor_tensor(
                    weTacc[:, cc * 64:(cc + 1) * 64],
                    weTacc[:, cc * 64:(cc + 1) * 64],
                    pool[:, slot_off:slot_off + 64], ALU.add))(),
                    reads=["weTacc"] + kslot, writes=["weTacc"] + kslot,
                    tag=f"weadd{e}_{cc}")
            p.add("gpsimd", (lambda par=par: lambda h: h.nop(nofuse=True))(),
                  reads=[f"wvb{par}"], writes=[f"wvuse{par}"], tag=f"wvfree{e}")

            def mm_wo(e=e, par=par):
                def g(h):
                    last = None
                    for oc in range(4):
                        last = h.matmul(
                            ps_wo[:, oc * 512:(oc + 1) * 512],
                            rIe[:, par * 64:(par + 1) * 64],
                            owb[:, par * DM + oc * 512: par * DM + (oc + 1) * 512],
                            start=(e == 0), stop=(e == NE - 1))
                    return last
                return g
            p.add("tensor", mm_wo(), reads=[f"owb{par}", f"rIe{par}"] + k_wo,
                  writes=k_wo + [f"owuse{par}", f"rIuse{par}"],
                  tag=f"mmwo{e}")

        p.add("scalar", lambda h: h.activation(weT[:], weTacc[:], AF.Copy),
              reads=["weTacc"], writes=["weT"])
        p.add("scalar", lambda h: h.activation(wod[:], ps_wo, AF.Copy),
              reads=k_wo, writes=["wod"])

        for th in range(2):
            lo = th * HT

            def mm_c(th=th, lo=lo):
                def g(h):
                    last = None
                    for cc in range(KC):
                        last = h.matmul(
                            ps_c, weT[:, cc * 64:(cc + 1) * 64],
                            x1b[:, cc * TPC + lo: cc * TPC + lo + HT],
                            start=(cc == 0), stop=(cc == KC - 1))
                    return last
                return g
            p.add("tensor", mm_c(), reads=["weT", "x1b"] + k_c, writes=k_c,
                  tag=f"mmc{th}")
            p.add("scalar", (lambda lo=lo: lambda h: h.activation(
                comb[:, lo:lo + HT], ps_c, AF.Copy))(),
                reads=k_c, writes=["comb"], tag=f"combev{th}")

            for oc in range(KC):
                p.add("tensor", (lambda oc=oc, lo=lo: lambda h: h.matmul(
                    ps_f, wod[:, oc * 128:(oc + 1) * 128],
                    comb[:, lo:lo + HT], start=True, stop=True))(),
                    reads=["wod", "comb"] + k_f, writes=k_f, tag=f"mmf{th}_{oc}")
                p.add("vector", (lambda oc=oc, lo=lo: lambda h: h.tensor_tensor(
                    x1[:, oc * TPC + lo: oc * TPC + lo + HT],
                    ps_f,
                    x1[:, oc * TPC + lo: oc * TPC + lo + HT], ALU.add))(),
                    reads=k_f + ["x1"], writes=["x1"], tag=f"resf{th}_{oc}")

        add_ln_fm(p, ct, x1, ["x1"], tag="C")

        p.add("gpsimd", lambda h: h.dma_start(out=x2_out, in_=X2V()),
              reads=["x1"], writes=["o1"], dma=True, group="gout")
        p.add("gpsimd", lambda h: h.dma_start(out=rt_out, in_=grow("ROUT")),
              reads=["grout"], writes=["o2"], dma=True, group="gout")

        with nc.Block() as block:
            p.emit_with_sems(st, block, final_engine="gpsimd",
                             final_waits=["o1", "o2"])
    return nc


def host_C(x1T, srps, sim_matrix, gates, w_v, o_weights):
    maps = []
    simm = _f32(sim_matrix.reshape(KC, 128, 32).transpose(1, 0, 2))
    wv = _f32(w_v)
    ow = _f32(o_weights)
    g1 = _f32(np.asarray(gates).reshape(1, 32))
    for c in range(NCORES):
        b = c // 4
        sl = slice(c * TPC, (c + 1) * TPC)
        x1c = _f32(x1T[:, sl].reshape(KC, 128, TPC).transpose(1, 0, 2))
        srp4 = np.concatenate([srps[4 * b + j] for j in range(4)], axis=1)
        maps.append({"x1_in": x1c, "srp4": _f32(srp4), "simm": simm,
                     "gates_in": g1, "wv_in": wv, "ow_in": ow})
    return maps


def run_C(x1T, srps, sim_matrix, gates, w_v, o_weights):
    nc = build_C()
    res = run_bass_kernel_spmd(nc, host_C(x1T, srps, sim_matrix, gates, w_v, o_weights),
                               core_ids=list(range(NCORES)))
    x2T = np.concatenate(
        [r["x2_out"].transpose(1, 0, 2).reshape(DM, TPC) for r in res.results], axis=1)
    return x2T, [r["rt_out"] for r in res.results]




def build_D():
    nc = bass.Bass("TRN2", target_bir_lowering=False)
    x2T_in = nc.dram_tensor("x2T_in", [128, KC, T], F32, kind="ExternalInput").ap()
    wq_in = nc.dram_tensor("wq_in", [128, KC, 256], F32, kind="ExternalInput").ap()
    wk_in = nc.dram_tensor("wk_in", [128, KC, 256], F32, kind="ExternalInput").ap()
    wv_in = nc.dram_tensor("wv_in", [128, KC, 256], F32, kind="ExternalInput").ap()
    bq_in = nc.dram_tensor("bq_in", [128, 2], F32, kind="ExternalInput").ap()
    bk_in = nc.dram_tensor("bk_in", [128, 2], F32, kind="ExternalInput").ap()
    bv_in = nc.dram_tensor("bv_in", [1, 256], F32, kind="ExternalInput").ap()
    maskcT_in = nc.dram_tensor("maskcT_in", [128, 4, 512], F32, kind="ExternalInput").ap()
    attn_out = nc.dram_tensor("attn_out", [2, 2, 128, S], BF16, kind="ExternalOutput").ap()

    TTD = 256
    NPT = T // TTD
    NIG = S // 512
    RS = 0.08838834764831845

    import contextlib
    with contextlib.ExitStack() as st:
        xtb = st.enter_context(nc.sbuf_tensor("xtb", [128, 2 * KC * TTD], BF16))
        wq = st.enter_context(nc.sbuf_tensor("wq", [128, KC * 256], BF16))
        wk = st.enter_context(nc.sbuf_tensor("wk", [128, KC * 256], BF16))
        wv = st.enter_context(nc.sbuf_tensor("wv", [128, KC * 256], BF16))
        bq = st.enter_context(nc.sbuf_tensor("bq", [128, 2], F32))
        bk = st.enter_context(nc.sbuf_tensor("bk", [128, 2], F32))
        bqs = st.enter_context(nc.sbuf_tensor("bqs", [128, 2], F32))
        rs_t = st.enter_context(nc.sbuf_tensor("rs_t", [128, 1], F32))
        rsneg = st.enter_context(nc.sbuf_tensor("rsneg", [1, 1], F32))
        bvb = st.enter_context(nc.sbuf_tensor("bvb", [128, 256], F32))
        onesrow = st.enter_context(nc.sbuf_tensor("onesrow", [1, 128], F32))
        onesbf1 = st.enter_context(nc.sbuf_tensor("onesbf1", [1, 128], BF16))
        onesbc = st.enter_context(nc.sbuf_tensor("onesbc", [128, 1], BF16))
        maskcT = st.enter_context(nc.sbuf_tensor("maskcT", [128, 4 * 512], F32))
        Q = st.enter_context(nc.sbuf_tensor("Q", [128, 2 * T], BF16))
        Kt = st.enter_context(nc.sbuf_tensor("Kt", [128, 2 * T], BF16))
        V = st.enter_context(nc.sbuf_tensor("V", [128, (T // 128) * 256], BF16))
        ptile = st.enter_context(nc.sbuf_tensor("ptile", [128, 2 * 512], BF16))
        sqt = st.enter_context(nc.sbuf_tensor("sqt", [128, 2 * 512], BF16))
        rowq = st.enter_context(nc.sbuf_tensor("rowq", [1, S], F32))
        rowk = st.enter_context(nc.sbuf_tensor("rowk", [1, 512], F32))
        kmx = st.enter_context(nc.sbuf_tensor("kmx", [1, 4], F32))
        nbb = st.enter_context(nc.sbuf_tensor("nbb", [1, S], BF16))
        lrow = st.enter_context(nc.sbuf_tensor("lrow", [1, 2 * 512], F32))
        rlrow = st.enter_context(nc.sbuf_tensor("rlrow", [1, 2 * 512], F32))
        rlb = st.enter_context(nc.sbuf_tensor("rlb", [128, 2 * 512], F32))
        astage = st.enter_context(nc.sbuf_tensor("astage", [128, 2 * 512], BF16))
        pool = st.enter_context(nc.psum_tensor("pool", [128, 4096], F32))

        p = P(nc)

        def reg(off, width, parts=128):
            return pool[0:parts, off:off + width], psbk(off, width)

        ps_st = [reg(0, 512), reg(512, 512)]
        ps_pv = [reg(1024, 512), reg(1536, 512)]
        ps_l = [reg(2048, 512, parts=1), reg(2560, 512, parts=1)]
        ps_rl = reg(3072, 512)
        ps_qn = [reg(3584, 512, parts=1)]
        k_qn = ps_qn[0][1]
        ps_pj = [reg(0, 256), reg(512, 256), reg(1024, 256),
                 reg(1536, 256), reg(2048, 256), reg(2560, 256)]

        p.add("vector", lambda h: h.memset(onesrow[:], 1.0), writes=["onesrow"])
        p.add("vector", lambda h: h.memset(onesbf1[:], 1.0), writes=["onesbf1"])
        p.add("vector", lambda h: h.memset(onesbc[:], 1.0), writes=["onesbc"])
        p.add("vector", lambda h: h.memset(rs_t[:], RS), writes=["rs_t"])
        p.add("vector", lambda h: h.memset(rsneg[:], -RS), writes=["rsneg"])
        p.add("sync", lambda h: h.dma_start(out=bq[:], in_=bq_in),
              writes=["bq"], dma=True, group="gc")
        p.add("sync", lambda h: h.dma_start(out=bk[:], in_=bk_in),
              writes=["bk"], dma=True, group="gc")
        p.add("sync", lambda h: h.dma_start(
            out=maskcT[:].rearrange("p (f j) -> p f j", f=4), in_=maskcT_in),
            writes=["maskcT"], dma=True, group="gc")
        p.add("gpsimd", lambda h: h.dma_start(
            out=wq[:].rearrange("p (c m) -> p c m", m=256), in_=wq_in),
            writes=["wq"], dma=True, group="gw")
        p.add("gpsimd", lambda h: h.dma_start(
            out=wk[:].rearrange("p (c m) -> p c m", m=256), in_=wk_in),
            writes=["wk"], dma=True, group="gw")
        p.add("gpsimd", lambda h: h.dma_start(
            out=wv[:].rearrange("p (c m) -> p c m", m=256), in_=wv_in),
            writes=["wv"], dma=True, group="gw")
        p.add("vector", lambda h: h.tensor_scalar(
            bqs[:], bq[:], rs_t[:, 0:1], None, ALU.mult),
            reads=["bq", "rs_t"], writes=["bqs"])
        p.add("sync", lambda h: h.dma_start(out=rowq[0:1, 0:256], in_=bv_in),
              writes=["bvrow"], dma=True, group="gc")
        p.add("tensor", lambda h: h.matmul(
            pool[:, 3072:3072 + 256], onesrow[:], rowq[0:1, 0:256],
            start=True, stop=True),
            reads=["bvrow", "onesrow"] + psbk(3072, 256), writes=psbk(3072, 256),
            tag="bvbc")
        p.add("vector", lambda h: h.tensor_copy(bvb[:], pool[:, 3072:3072 + 256]),
              reads=psbk(3072, 256), writes=["bvb", "rowq_free"])

        for tt in range(NPT):
            par = tt % 2
            t0 = tt * TTD
            p.add("gpsimd", (lambda tt=tt, par=par: lambda h: h.dma_start(
                out=xtb[:, par * KC * TTD:(par + 1) * KC * TTD]
                    .rearrange("p (c t) -> p c t", c=KC),
                in_=x2T_in[:, :, tt * TTD:(tt + 1) * TTD]))(),
                reads=[f"xtu{par}"], writes=[f"xtb{par}"], dma=True, group=f"gx{par}",
                tag=f"xld{tt}")

            for mi in range(2):
                for wname, wsb in (("wq", wq), ("wk", wk)):
                    pj = ps_pj[mi if wname == "wq" else 2 + mi]

                    def mm_p(par=par, mi=mi, wsb=wsb, pj=pj):
                        def g(h):
                            last = None
                            for kc in range(KC):
                                last = h.matmul(
                                    pj[0],
                                    wsb[:, kc * 256 + mi * 128: kc * 256 + (mi + 1) * 128],
                                    xtb[:, par * KC * TTD + kc * TTD:
                                        par * KC * TTD + (kc + 1) * TTD],
                                    start=(kc == 0), stop=(kc == KC - 1))
                            return last
                        return g
                    p.add("tensor", mm_p(), reads=[wname, f"xtb{par}"] + pj[1],
                          writes=pj[1], tag=f"mmp{wname}{tt}_{mi}")
                    if wname == "wq":
                        p.add("vector", (lambda mi=mi, t0=t0, pj=pj: lambda h:
                              h.tensor_scalar(Q[:, mi * T + t0: mi * T + t0 + TTD],
                                              pj[0], rs_t[:, 0:1], bqs[:, mi:mi + 1],
                                              ALU.mult, ALU.add))(),
                              reads=pj[1] + ["bqs", "rs_t"], writes=["Q"],
                              tag=f"evq{tt}_{mi}")
                    else:
                        p.add("vector", (lambda mi=mi, t0=t0, pj=pj: lambda h:
                              h.tensor_scalar(Kt[:, mi * T + t0: mi * T + t0 + TTD],
                                              pj[0], bk[:, mi:mi + 1], None,
                                              ALU.add))(),
                              reads=pj[1] + ["bk"], writes=["Kt"],
                              tag=f"evk{tt}_{mi}")

            for ts in range(2):
                pj = ps_pj[4 + ts]

                def mm_v(par=par, ts=ts, pj=pj):
                    def g(h):
                        last = None
                        for kc in range(KC):
                            last = h.matmul(
                                pj[0],
                                xtb[:, par * KC * TTD + kc * TTD + ts * 128:
                                    par * KC * TTD + kc * TTD + (ts + 1) * 128],
                                wv[:, kc * 256:(kc + 1) * 256],
                                start=(kc == 0), stop=(kc == KC - 1))
                        return last
                    return g
                p.add("tensor", mm_v(), reads=["wv", f"xtb{par}"] + pj[1],
                      writes=pj[1] + ([f"xtu{par}"] if ts == 1 else []),
                      tag=f"mmv{tt}_{ts}")
                ch = (t0 + ts * 128) // 128
                p.add("vector", (lambda ch=ch, pj=pj: lambda h: h.tensor_tensor(
                    V[:, ch * 256:(ch + 1) * 256], pj[0], bvb[:], ALU.add))(),
                    reads=pj[1] + ["bvb"], writes=["V"], tag=f"evv{tt}_{ts}")

        for b in range(2):
            for hh in range(2):
                base = hh * T + b * S
                vbase = (b * S) // 128

                for ig in range(NIG):
                    i0 = ig * 512
                    njc = (i0 + 512) // 128
                    pvp = ig % 2
                    lp = ig % 2

                    def emit_st(jc, par, i0=i0, base=base):
                        def g(h, jc=jc, par=par, i0=i0, base=base):
                            return h.matmul(
                                ps_st[par][0],
                                Kt[:, base + jc * 128: base + (jc + 1) * 128],
                                Q[:, base + i0: base + i0 + 512],
                                start=True, stop=True)
                        p.add("tensor", g, reads=["Q", "Kt"] + ps_st[par][1],
                              writes=ps_st[par][1], tag=f"sb{b}{hh}_{ig}_{jc}")
                        rj = jc - ig * 4
                        if rj >= 0:
                            p.add("vector", (lambda par=par, rj=rj: lambda h:
                                  h.tensor_tensor(
                                      ps_st[par][0], ps_st[par][0],
                                      maskcT[:, rj * 512:(rj + 1) * 512], ALU.add))(),
                                  reads=ps_st[par][1] + ["maskcT"],
                                  writes=ps_st[par][1], tag=f"mskb{b}{hh}_{ig}_{jc}")

                    def emit_tail(jc, par, pvp=pvp, lp=lp, vbase=vbase, njc=njc):
                        pp = jc % 2
                        p.add("scalar", (lambda par=par, pp=pp: lambda h: h.activation(
                            ptile[:, pp * 512:(pp + 1) * 512], ps_st[par][0], AF.Exp))(),
                            reads=ps_st[par][1] + [f"ptu{pp}"], writes=[f"ptile{pp}"],
                            tag=f"bexp{b}{hh}_{ig}_{jc}")
                        p.add("tensor", (lambda hh=hh, jc=jc, pp=pp, pvp=pvp,
                              vbase=vbase, njc=njc: lambda h: h.matmul(
                            ps_pv[pvp][0],
                            V[:, (vbase + jc) * 256 + hh * 128:
                              (vbase + jc) * 256 + (hh + 1) * 128],
                            ptile[:, pp * 512:(pp + 1) * 512],
                            start=(jc == 0), stop=(jc == njc - 1)))(),
                            reads=["V", f"ptile{pp}"] + ps_pv[pvp][1],
                            writes=ps_pv[pvp][1], tag=f"pv{b}{hh}_{ig}_{jc}")
                        p.add("tensor", (lambda jc=jc, pp=pp, lp=lp, njc=njc: lambda h:
                              h.matmul(ps_l[lp][0], onesbc[:, 0:1],
                                       ptile[:, pp * 512:(pp + 1) * 512],
                                       start=(jc == 0), stop=(jc == njc - 1)))(),
                              reads=[f"ptile{pp}", "onesbc"] + ps_l[lp][1],
                              writes=ps_l[lp][1] + [f"ptu{pp}"],
                              tag=f"lmm{b}{hh}_{ig}_{jc}")

                    for jc in range(njc):
                        emit_st(jc, jc % 2)
                        if jc >= 1:
                            emit_tail(jc - 1, (jc - 1) % 2)
                    emit_tail(njc - 1, (njc - 1) % 2)

                    p.add("scalar", (lambda lp=lp: lambda h: h.activation(
                        lrow[0:1, lp * 512:(lp + 1) * 512], ps_l[lp][0], AF.Copy))(),
                        reads=ps_l[lp][1] + [f"lrowu{lp}"], writes=[f"lrow{lp}"],
                        tag=f"lev{b}{hh}_{ig}")
                    p.add("vector", (lambda lp=lp: lambda h: h.reciprocal(
                        rlrow[0:1, lp * 512:(lp + 1) * 512],
                        lrow[0:1, lp * 512:(lp + 1) * 512]))(),
                        reads=[f"lrow{lp}", f"rlru{lp}"],
                        writes=[f"rlrow{lp}", f"lrowu{lp}"],
                        tag=f"rlr{b}{hh}_{ig}")
                    p.add("tensor", (lambda lp=lp: lambda h: h.matmul(
                        ps_rl[0], onesrow[:], rlrow[0:1, lp * 512:(lp + 1) * 512],
                        start=True, stop=True))(),
                        reads=[f"rlrow{lp}", "onesrow"] + ps_rl[1],
                        writes=ps_rl[1] + [f"rlru{lp}"],
                        tag=f"rlb{b}{hh}_{ig}")
                    sp = ig % 2
                    p.add("vector", (lambda sp=sp: lambda h: h.tensor_copy(
                        rlb[:, sp * 512:(sp + 1) * 512], ps_rl[0]))(),
                        reads=ps_rl[1] + [f"rlbu{sp}"], writes=[f"rlb{sp}"],
                        tag=f"rlev{b}{hh}_{ig}")
                    p.add("vector", (lambda pvp=pvp, sp=sp: lambda h: h.tensor_tensor(
                        astage[:, sp * 512:(sp + 1) * 512], ps_pv[pvp][0],
                        rlb[:, sp * 512:(sp + 1) * 512], ALU.mult))(),
                        reads=ps_pv[pvp][1] + [f"rlb{sp}", f"asu{sp}"],
                        writes=[f"ast{sp}", f"rlbu{sp}"], tag=f"oev{b}{hh}_{ig}")
                    p.add("gpsimd", (lambda b=b, hh=hh, i0=i0, sp=sp: lambda h:
                          h.dma_start(
                              out=attn_out[b, hh, :, i0:i0 + 512],
                              in_=astage[:, sp * 512:(sp + 1) * 512]))(),
                          reads=[f"ast{sp}"], writes=[f"asu{sp}", "aout"],
                          dma=True, group=f"gao{sp}", tag=f"ao{b}{hh}_{ig}")

        with nc.Block() as block:
            p.emit_with_sems(st, block, final_engine="gpsimd", final_waits=["aout"])
    return nc


def host_D(x2T, w_q, b_q, w_k, b_k, w_v, b_v):
    x2L = np.ascontiguousarray(
        x2T.reshape(KC, 128, T).transpose(1, 0, 2), dtype=np.float32)
    maskc = np.zeros((128, 4, 512), np.float32)
    maskcT = np.zeros((128, 4, 512), np.float32)
    for ph in range(4):
        for pp in range(128):
            i = ph * 128 + pp
            maskc[pp, ph, :] = np.where(np.arange(512) <= i, 0.0, -1e10)
        for pp in range(128):
            maskcT[pp, ph, :] = np.where(np.arange(512) >= ph * 128 + pp, 0.0, -1e10)
    maps = []
    for c in range(NCORES):
        sl = slice(256 * c, 256 * (c + 1))
        wqL = np.ascontiguousarray(
            w_q[:, sl].reshape(KC, 128, 256).transpose(1, 0, 2), dtype=np.float32)
        wkL = np.ascontiguousarray(
            w_k[:, sl].reshape(KC, 128, 256).transpose(1, 0, 2), dtype=np.float32)
        wvL = np.ascontiguousarray(
            w_v[:, sl].reshape(KC, 128, 256).transpose(1, 0, 2), dtype=np.float32)
        bqL = np.ascontiguousarray(b_q[sl].reshape(2, 128).T, dtype=np.float32)
        bkL = np.ascontiguousarray(b_k[sl].reshape(2, 128).T, dtype=np.float32)
        bvL = np.ascontiguousarray(b_v[sl].reshape(1, 256), dtype=np.float32)
        maps.append({"x2T_in": x2L, "wq_in": wqL, "wk_in": wkL, "wv_in": wvL,
                     "bq_in": bqL, "bk_in": bkL, "bv_in": bvL,
                     "maskcT_in": maskcT})
    return maps


def run_D(x2T, w_q, b_q, w_k, b_k, w_v, b_v):
    nc = build_D()
    res = run_bass_kernel_spmd(nc, host_D(x2T, w_q, b_q, w_k, b_k, w_v, b_v),
                               core_ids=list(range(NCORES)))
    attnT = np.zeros((DM, T), dtype=bf16)
    for c in range(NCORES):
        a = res.results[c]["attn_out"]
        for hh in range(2):
            head = 2 * c + hh
            for b in range(2):
                attnT[head * 128:(head + 1) * 128, b * S:(b + 1) * S] = a[b, hh]
    return attnT



HID = 5632
HC = HID // 128


def build_E():
    nc = bass.Bass("TRN2", target_bir_lowering=False)
    attn_in = nc.dram_tensor("attn_in", [128, KC, TPC], BF16, kind="ExternalInput").ap()
    x2_in = nc.dram_tensor("x2_in", [128, KC, TPC], F32, kind="ExternalInput").ap()
    wo_in = nc.dram_tensor("wo_in", [128, KC, DM], F32, kind="ExternalInput").ap()
    bo_in = nc.dram_tensor("bo_in", [128, KC], F32, kind="ExternalInput").ap()
    wqg_in = nc.dram_tensor("wqg_in", [128, KC, HID], I32, kind="ExternalInput").ap()
    wqu_in = nc.dram_tensor("wqu_in", [128, KC, HID], I32, kind="ExternalInput").ap()
    wqd_in = nc.dram_tensor("wqd_in", [128, HC, DM], I32, kind="ExternalInput").ap()
    sg_in = nc.dram_tensor("sg_in", [128, HC], F32, kind="ExternalInput").ap()
    su_in = nc.dram_tensor("su_in", [128, HC], F32, kind="ExternalInput").ap()
    sd_in = nc.dram_tensor("sd_in", [128, KC], F32, kind="ExternalInput").ap()
    x4_out = nc.dram_tensor("x4_out", [128, KC, TPC], F32, kind="ExternalOutput").ap()

    TT2 = 512
    NTT = TPC // TT2
    HT = TPC // 2

    import contextlib
    with contextlib.ExitStack() as st:
        x = st.enter_context(nc.sbuf_tensor("x", [128, KC * TPC], F32))
        attn = st.enter_context(nc.sbuf_tensor("attn", [128, KC * TPC], BF16))
        wo = st.enter_context(nc.sbuf_tensor("wo", [128, KC * DM], BF16))
        wdt2 = st.enter_context(nc.sbuf_tensor("wdt2", [128, HC * 128], BF16))
        bo = st.enter_context(nc.sbuf_tensor("bo", [128, KC], F32))
        sg = st.enter_context(nc.sbuf_tensor("sg", [128, HC], F32))
        su = st.enter_context(nc.sbuf_tensor("su", [128, HC], F32))
        sd = st.enter_context(nc.sbuf_tensor("sd", [128, KC], F32))
        onesbf = st.enter_context(nc.sbuf_tensor("onesbf", [128, 1], BF16))
        csb = st.enter_context(nc.sbuf_tensor("csb", [128, TT2], F32))
        csb2 = st.enter_context(nc.sbuf_tensor("csb2", [128, TT2], F32))
        tscr = st.enter_context(nc.sbuf_tensor("tscr", [128, TT2], F32))
        ga = st.enter_context(nc.sbuf_tensor("ga", [128, TT2], BF16))
        ua = st.enter_context(nc.sbuf_tensor("ua", [128, TT2], BF16))
        pool = st.enter_context(nc.psum_tensor("pool", [128, 4096], F32))

        x3bf = attn[:, 0:KC * TT2]
        wt = [attn[:, KC * TT2 + i * 2048: KC * TT2 + (i + 1) * 2048] for i in range(4)]
        assert KC * TT2 + 4 * 2048 <= KC * TPC
        hid = wo[:, 0:HC * TT2]
        wdt = [wo[:, HC * TT2: HC * TT2 + HC * 128],
               wdt2[:, 0:HC * 128]]
        assert HC * TT2 + HC * 128 <= KC * DM

        p = P(nc)
        ct = make_common(nc, st, p, pool, NT=TPC)
        row, ones, onesrow, eps = ct["row"], ct["ones"], ct["onesrow"], ct["eps"]

        XV = lambda: x[:].rearrange("p (c t) -> p c t", c=KC)

        ps_o = [pool[:, 0:512], pool[:, 512:1024]]
        k_o = [psbk(0, 512), psbk(512, 512)]
        ps_cs = pool[0:1, 1024:1024 + TT2]
        k_cs = psbk(1024, TT2)
        ps_cb = pool[:, 1536:1536 + TT2]
        k_cb = psbk(1536, TT2)

        ps_g = [pool[:, 2048:2048 + TT2], pool[:, 2560:2560 + TT2]]
        k_g = [psbk(2048, TT2), psbk(2560, TT2)]
        ps_u = [pool[:, 3072:3072 + TT2], pool[:, 3584:3584 + TT2]]
        k_u = [psbk(3072, TT2), psbk(3584, TT2)]
        ps_d = [pool[:, 0:TT2], pool[:, 512:512 + TT2]]
        k_d = [psbk(0, TT2), psbk(512, TT2)]
        assert TT2 == 512

        p.add("vector", lambda h: h.memset(onesbf[:], 1.0), writes=["onesbf"])
        p.add("sync", lambda h: h.dma_start(out=XV(), in_=x2_in),
              writes=["x"], dma=True, group="gx2")
        p.add("sync", lambda h: h.dma_start(
            out=attn[:].rearrange("p (c t) -> p c t", c=KC), in_=attn_in),
            writes=["attn"], dma=True, group="gattn")
        p.add("gpsimd", lambda h: h.dma_start(
            out=wo[:].rearrange("p (c d) -> p c d", d=DM), in_=wo_in),
            writes=["wo"], dma=True, group="gwo")
        p.add("sync", lambda h: h.dma_start(out=bo[:], in_=bo_in),
              writes=["bo"], dma=True, group="gsc")
        p.add("sync", lambda h: h.dma_start(out=sg[:], in_=sg_in),
              writes=["sg"], dma=True, group="gsc")
        p.add("sync", lambda h: h.dma_start(out=su[:], in_=su_in),
              writes=["su"], dma=True, group="gsc")
        p.add("sync", lambda h: h.dma_start(out=sd[:], in_=sd_in),
              writes=["sd"], dma=True, group="gsc")

        for th in range(2):
            for oc in range(KC):
                par = (th * KC + oc) % 2

                def mm_o(th=th, oc=oc, par=par):
                    def g(h):
                        last = None
                        for kc in range(KC):
                            last = h.matmul(
                                ps_o[par],
                                wo[:, kc * DM + oc * 128: kc * DM + (oc + 1) * 128],
                                attn[:, kc * TPC + th * HT: kc * TPC + th * HT + HT],
                                start=(kc == 0), stop=(kc == KC - 1))
                        return last
                    return g
                p.add("tensor", mm_o(), reads=["wo", "attn"] + k_o[par],
                      writes=k_o[par], tag=f"mmo{th}_{oc}")
                p.add("vector", (lambda th=th, oc=oc, par=par: lambda h:
                      h.scalar_tensor_tensor(
                          x[:, oc * TPC + th * HT: oc * TPC + th * HT + HT],
                          ps_o[par], bo[:, oc:oc + 1],
                          x[:, oc * TPC + th * HT: oc * TPC + th * HT + HT],
                          ALU.add, ALU.add))(),
                      reads=k_o[par] + ["bo", "x"], writes=["x"], tag=f"evo{th}_{oc}")

        p.add("gpsimd", lambda h: h.nop(nofuse=True),
              writes=["wo", "attn", "woarena", "attnarena"], tag="arena_sw")

        add_ln_fm(p, ct, x, ["x"], tag="E3")

        for tt in range(NTT):
            t0 = tt * TT2

            def cast_x3(tt=tt, t0=t0):
                def g(h):
                    last = None
                    for kc in range(KC):
                        last = h.tensor_copy(
                            x3bf[:, kc * TT2:(kc + 1) * TT2],
                            x[:, kc * TPC + t0: kc * TPC + t0 + TT2])
                    return last
                return g
            p.add("vector", cast_x3(), reads=["x", "attnarena"], writes=["x3bf"],
                  tag=f"cx3_{tt}")

            def mm_cs(tt=tt):
                def g(h):
                    last = None
                    for kc in range(KC):
                        last = h.matmul(ps_cs, onesbf[:, 0:1],
                                        x3bf[:, kc * TT2:(kc + 1) * TT2],
                                        start=(kc == 0), stop=(kc == KC - 1))
                    return last
                return g
            p.add("tensor", mm_cs(), reads=["x3bf", "onesbf"] + k_cs, writes=k_cs,
                  tag=f"mmcs{tt}")
            p.add("scalar", (lambda: lambda h: h.activation(
                row("SCR1", 0, TT2), ps_cs, AF.Copy, scale=128.0))(),
                reads=k_cs, writes=["r128cs"], tag=f"cs{tt}")
            p.add("tensor", lambda h: h.matmul(
                ps_cb, onesrow[:], row("SCR1", 0, TT2), start=True, stop=True),
                reads=["r128cs", "c_onesrow"] + k_cb, writes=k_cb, tag=f"bccs{tt}")
            p.add("vector", lambda h: h.tensor_copy(csb[:], ps_cb),
                  reads=k_cb, writes=["csb"], tag=f"evcs{tt}")

            for hc in range(HC):
                par = hc % 2
                p.add("gpsimd", (lambda hc=hc, par=par: lambda h: h.dma_start(
                    out=wt[par].rearrange("p (c m) -> p c m", m=128),
                    in_=wqg_in[:, :, hc * 128:(hc + 1) * 128]))(),
                    reads=[f"wtu{par}", "attnarena"], writes=[f"wt{par}"], dma=True,
                    group=f"gwt{par}", tag=f"ldg{tt}_{hc}")
                p.add("gpsimd", (lambda hc=hc, par=par: lambda h: h.dma_start(
                    out=wt[2 + par].rearrange("p (c m) -> p c m", m=128),
                    in_=wqu_in[:, :, hc * 128:(hc + 1) * 128]))(),
                    reads=[f"wuu{par}", "attnarena"], writes=[f"wu{par}"], dma=True,
                    group=f"gwu{par}", tag=f"ldu{tt}_{hc}")

                def mm_g(hc=hc, par=par):
                    def g(h):
                        last = None
                        for kc in range(KC):
                            last = h.matmul(
                                ps_g[par], wt[par][:, kc * 128:(kc + 1) * 128],
                                x3bf[:, kc * TT2:(kc + 1) * TT2],
                                start=(kc == 0), stop=(kc == KC - 1))
                        return last
                    return g
                p.add("tensor", mm_g(), reads=[f"wt{par}", "x3bf"] + k_g[par],
                      writes=k_g[par] + [f"wtu{par}"], tag=f"mmg{tt}_{hc}")
                p.add("vector", (lambda par=par: lambda h: h.tensor_tensor(
                    tscr[:], ps_g[par], csb[:], ALU.subtract))(),
                    reads=k_g[par] + ["csb", "tscr_free"], writes=["tscr"],
                    tag=f"gsub{tt}_{hc}")
                p.add("scalar", (lambda hc=hc: lambda h: h.activation(
                    ga[:], tscr[:], AF.Silu, scale=sg[:, hc:hc + 1]))(),
                    reads=["tscr", "sg", "ga_free"], writes=["ga"], tag=f"gact{tt}_{hc}")

                def mm_u(hc=hc, par=par):
                    def g(h):
                        last = None
                        for kc in range(KC):
                            last = h.matmul(
                                ps_u[par], wt[2 + par][:, kc * 128:(kc + 1) * 128],
                                x3bf[:, kc * TT2:(kc + 1) * TT2],
                                start=(kc == 0), stop=(kc == KC - 1))
                        return last
                    return g
                p.add("tensor", mm_u(), reads=[f"wu{par}", "x3bf"] + k_u[par],
                      writes=k_u[par] + [f"wuu{par}"], tag=f"mmu{tt}_{hc}")
                p.add("vector", (lambda par=par: lambda h: h.tensor_tensor(
                    tscr[:], ps_u[par], csb[:], ALU.subtract))(),
                    reads=k_u[par] + ["csb", "ga"], writes=["tscr"],
                    tag=f"usub{tt}_{hc}")
                p.add("scalar", (lambda hc=hc: lambda h: h.activation(
                    ua[:], tscr[:], AF.Copy, scale=su[:, hc:hc + 1]))(),
                    reads=["tscr", "su", "ua_free"], writes=["ua"], tag=f"uact{tt}_{hc}")
                p.add("vector", (lambda hc=hc: lambda h: h.tensor_tensor(
                    hid[:, hc * TT2:(hc + 1) * TT2], ga[:], ua[:], ALU.mult))(),
                    reads=["ga", "ua", "woarena"], writes=[f"hid{hc}", "ga_free",
                                                           "ua_free", "tscr_free"],
                    tag=f"hmul{tt}_{hc}")
            HID_RES = [f"hid{hc}" for hc in range(HC)]

            def mm_cs2(tt=tt):
                def g(h):
                    last = None
                    for hc in range(HC):
                        last = h.matmul(ps_cs, onesbf[:, 0:1],
                                        hid[:, hc * TT2:(hc + 1) * TT2],
                                        start=(hc == 0), stop=(hc == HC - 1))
                    return last
                return g
            p.add("tensor", mm_cs2(), reads=HID_RES + ["onesbf"] + k_cs, writes=k_cs,
                  tag=f"mmcs2{tt}")
            p.add("scalar", (lambda: lambda h: h.activation(
                row("SCR1", 0, TT2), ps_cs, AF.Copy, scale=128.0))(),
                reads=k_cs + ["r128cs"], writes=["r128cs"], tag=f"cs2{tt}")
            p.add("tensor", lambda h: h.matmul(
                ps_cb, onesrow[:], row("SCR1", 0, TT2), start=True, stop=True),
                reads=["r128cs", "c_onesrow"] + k_cb, writes=k_cb, tag=f"bccs2{tt}")
            p.add("vector", lambda h: h.tensor_copy(csb2[:], ps_cb),
                  reads=k_cb + ["csb"], writes=["csb2"], tag=f"evcs2{tt}")

            for dc in range(KC):
                par = dc % 2
                p.add("gpsimd", (lambda dc=dc, par=par: lambda h: h.dma_start(
                    out=wdt[par].rearrange("p (c m) -> p c m", m=128),
                    in_=wqd_in[:, :, dc * 128:(dc + 1) * 128]))(),
                    reads=[f"wdu{par}", "woarena"], writes=[f"wd{par}"], dma=True,
                    group=f"gwd{par}", tag=f"ldd{tt}_{dc}")

                def mm_d(dc=dc, par=par):
                    def g(h):
                        last = None
                        for hc in range(HC):
                            last = h.matmul(
                                ps_d[par], wdt[par][:, hc * 128:(hc + 1) * 128],
                                hid[:, hc * TT2:(hc + 1) * TT2],
                                start=(hc == 0), stop=(hc == HC - 1))
                        return last
                    return g
                p.add("tensor", mm_d(), reads=[f"wd{par}"] + HID_RES + k_d[par],
                      writes=k_d[par] + [f"wdu{par}"], tag=f"mmd{tt}_{dc}")
                p.add("vector", (lambda dc=dc, par=par: lambda h: h.tensor_tensor(
                    tscr[:], ps_d[par], csb2[:], ALU.subtract))(),
                    reads=k_d[par] + ["csb2", "tscr_free"], writes=["tscr2"],
                    tag=f"dsub{tt}_{dc}")
                p.add("vector", (lambda dc=dc, t0=t0: lambda h:
                      h.scalar_tensor_tensor(
                          x[:, dc * TPC + t0: dc * TPC + t0 + TT2],
                          tscr[:], sd[:, dc:dc + 1],
                          x[:, dc * TPC + t0: dc * TPC + t0 + TT2],
                          ALU.mult, ALU.add))(),
                      reads=["tscr2", "sd", "x"], writes=["x", "tscr_free"],
                      tag=f"dres{tt}_{dc}")

        add_ln_fm(p, ct, x, ["x"], tag="E4")

        p.add("gpsimd", lambda h: h.dma_start(out=x4_out, in_=XV()),
              reads=["x"], writes=["o1"], dma=True, group="gout")

        with nc.Block() as block:
            p.emit_with_sems(st, block, final_engine="gpsimd", final_waits=["o1"])
    return nc


def host_E(attnT, x2T, w_o, b_o, wq_gate, s_gate, wq_up, s_up, wq_down, s_down):
    woL = _f32(w_o).reshape(KC, 128, DM).transpose(1, 0, 2)
    woL = np.ascontiguousarray(woL)
    boL = _f32(b_o).reshape(KC, 128).T
    wqgL = np.ascontiguousarray(
        wq_gate.T.reshape(KC, 128, HID).transpose(1, 0, 2), dtype=np.int32)
    wquL = np.ascontiguousarray(
        wq_up.T.reshape(KC, 128, HID).transpose(1, 0, 2), dtype=np.int32)
    wqdL = np.ascontiguousarray(
        wq_down.T.reshape(HC, 128, DM).transpose(1, 0, 2), dtype=np.int32)
    sgL = _f32(s_gate).reshape(HC, 128).T
    suL = _f32(s_up).reshape(HC, 128).T
    sdL = _f32(s_down).reshape(KC, 128).T
    maps = []
    for c in range(NCORES):
        sl = slice(c * TPC, (c + 1) * TPC)
        at = np.ascontiguousarray(
            attnT[:, sl].reshape(KC, 128, TPC).transpose(1, 0, 2))
        x2c = _f32(x2T[:, sl].reshape(KC, 128, TPC).transpose(1, 0, 2))
        maps.append({"attn_in": at.astype(bf16), "x2_in": x2c, "wo_in": woL,
                     "bo_in": np.ascontiguousarray(boL),
                     "wqg_in": wqgL, "wqu_in": wquL, "wqd_in": wqdL,
                     "sg_in": np.ascontiguousarray(sgL),
                     "su_in": np.ascontiguousarray(suL),
                     "sd_in": np.ascontiguousarray(sdL)})
    return maps


def run_E(attnT, x2T, w_o, b_o, wq_gate, s_gate, wq_up, s_up, wq_down, s_down):
    nc = build_E()
    res = run_bass_kernel_spmd(
        nc, host_E(attnT, x2T, w_o, b_o, wq_gate, s_gate, wq_up, s_up, wq_down, s_down),
        core_ids=list(range(NCORES)))
    return np.concatenate(
        [r["x4_out"].transpose(1, 0, 2).reshape(DM, TPC) for r in res.results], axis=1)


def host_B(partials, sumsqs, x0T):
    maps = []
    for c in range(NCORES):
        sl = slice(c * TPC, (c + 1) * TPC)
        p8 = np.stack([partials[j][:, :, sl] for j in range(8)])
        sq8 = np.stack([sumsqs[j][sl] for j in range(8)])
        x0c = np.ascontiguousarray(
            x0T[:, sl].reshape(KC, 128, TPC).transpose(1, 0, 2))
        maps.append({"p8": _f32(p8), "sq8": _f32(sq8), "x0c": _f32(x0c)})
    return maps


def run_B(partials, sumsqs, x0T):
    nc = build_B()
    res = run_bass_kernel_spmd(nc, host_B(partials, sumsqs, x0T),
                               core_ids=list(range(NCORES)))
    x1T = np.concatenate(
        [r["x1_out"].transpose(1, 0, 2).reshape(DM, TPC) for r in res.results], axis=1)
    srp = [r["srp_out"] for r in res.results]
    return x1T, srp


def kernel(L0_attn_hidden_states, L0_attn_qkv_w, L0_attn_out_w, L0_attn_gate_w,
           L0_attn_norm_w, L0_ffn_sim_matrix, L0_ffn_gates, L0_ffn_w_q,
           L0_ffn_w_k, L0_ffn_w_v, L0_ffn_o_weights, L1_attn_w_q, L1_attn_b_q,
           L1_attn_w_k, L1_attn_b_k, L1_attn_w_v, L1_attn_b_v, L1_attn_w_o,
           L1_attn_b_o, L1_ffn_wq_gate, L1_ffn_s_gate, L1_ffn_wq_up,
           L1_ffn_s_up, L1_ffn_wq_down, L1_ffn_s_down):
    x0 = np.ascontiguousarray(np.asarray(L0_attn_hidden_states), dtype=np.float32)
    x0T = np.ascontiguousarray(x0.reshape(T, DM).T)

    partials, sumsqs = run_A(x0, np.asarray(L0_attn_qkv_w),
                             np.asarray(L0_attn_gate_w), np.asarray(L0_attn_out_w),
                             np.asarray(L0_attn_norm_w))
    x1T, srps = run_B(partials, sumsqs, x0T)
    x2T, _ = run_C(x1T, srps, np.asarray(L0_ffn_sim_matrix),
                   np.asarray(L0_ffn_gates), np.asarray(L0_ffn_w_v),
                   np.asarray(L0_ffn_o_weights))
    attnT = run_D(x2T, np.asarray(L1_attn_w_q), np.asarray(L1_attn_b_q),
                  np.asarray(L1_attn_w_k), np.asarray(L1_attn_b_k),
                  np.asarray(L1_attn_w_v), np.asarray(L1_attn_b_v))
    x4T = run_E(attnT, x2T, np.asarray(L1_attn_w_o), np.asarray(L1_attn_b_o),
                np.asarray(L1_ffn_wq_gate), np.asarray(L1_ffn_s_gate),
                np.asarray(L1_ffn_wq_up), np.asarray(L1_ffn_s_up),
                np.asarray(L1_ffn_wq_down), np.asarray(L1_ffn_s_down))
    return np.ascontiguousarray(x4T.T).reshape(BB, S, DM)
